# version 20
# baseline (speedup 1.0000x reference)
import numpy as np
import ml_dtypes
from contextlib import ExitStack

import concourse.bass as bass
import concourse.mybir as mybir
from concourse.bass_utils import run_bass_kernel_spmd
from concourse.dve_spec import Spec, Src0, Src1, C0, C1, C2, One, eq, lower
from concourse.dve_op import DveOp
from concourse.dve_ops import OPS as _DVE_OPS
from concourse.dve_table_gen import dve_ver_for


def _make_lut2acc():
    for op in _DVE_OPS:
        if op.name == "LUT2ACC":
            return op
    body = Src1 + eq(Src0, C2) * C0 + eq(Src0 - C2, One) * C1
    spec = Spec(body=body)
    ver = dve_ver_for("TRN2")
    sha = lower(spec, ver=ver, perf_en=False, name="LUT2ACC").sha(ver)
    op = DveOp("LUT2ACC", spec, subdim=False, uops_sha={ver: sha},
               reference=lambda in0, in1, s0, s1, imm2:
                   in1 + (in0 == imm2) * s0 + ((in0 - imm2) == 1.0) * s1)
    _DVE_OPS.append(op)
    from concourse import dve_ops as _dvo
    _dvo._SUB_OPCODE_FOR_NAME[op.name] = _dvo._CUSTOM_DVE_ROW_BASE + len(_DVE_OPS) - 1
    return op


LUT2ACC = _make_lut2acc()

F32 = mybir.dt.float32
BF16 = mybir.dt.bfloat16
I32 = mybir.dt.int32
AF = mybir.ActivationFunctionType
ALU = mybir.AluOpType
BF = ml_dtypes.bfloat16

D = 2048
H = 32
DH = 64
M = 4
S = 64
L = 2048
NB = 32
NL = 128
INTER = 5504
NKC = 16
NTC = 16
NSC = 4
SCALE = 0.125
ROPE_BASE = 10000.0
NOB = 43

DEBUG = False
BUILD_UPTO = 99
SW1 = {"a", "b", "v", "g", "psc", "loc", "lm2", "grep"}
PUMP_PHASES = {"s1a", "s1b", "s1c", "lm", "s2a", "s2b"}


def _ps_perm():
    perm = np.zeros(D, np.int64)
    for r in range(D):
        p = r // 1024
        rem = r % 1024
        h = rem // 32
        i = rem % 32
        perm[r] = h * DH + 2 * i + p
    return perm


def _ptile(W, ow):
    Din, O = W.shape
    return np.ascontiguousarray(
        W.reshape(Din // 128, 128, O // ow, ow).transpose(2, 1, 0, 3),
        dtype=np.float32)


def host_prep(inputs):
    g = lambda n: np.asarray(inputs["L0_attn_" + n])
    perm = _ps_perm()

    w_qkv = g("w_qkv")
    b_qkv = g("b_qkv")
    wq = w_qkv[:, 0:D][:, perm]
    wk = w_qkv[:, D:2 * D][:, perm]
    wqk = np.ascontiguousarray(np.concatenate([wq, wk], 1), dtype=np.float32)
    bqk = np.concatenate([b_qkv[0:D][perm], b_qkv[D:2 * D][perm]])
    wv = np.ascontiguousarray(w_qkv[:, 2 * D:3 * D], dtype=np.float32)
    bv = b_qkv[2 * D:3 * D]

    pq = g("pool_q")
    PQ = np.zeros((D, NL), np.float32)
    for r in range(D):
        p = r // 1024
        rem = r % 1024
        h = rem // 32
        i = rem % 32
        PQ[r, 4 * h: 4 * h + 4] = pq[:, h * DH + 2 * i + p]

    w_lm_qkv = g("w_lm_qkv")
    b_lm_qkv = g("b_lm_qkv")

    def bias_cp(b):
        return np.ascontiguousarray(b.reshape(-1, 128), dtype=np.float32)

    i_of_row = np.arange(128) % 32
    inv = ROPE_BASE ** (-(2.0 * i_of_row) / DH)
    ang = inv[:, None] * np.arange(L, dtype=np.float64)[None, :]
    cosT = np.cos(ang).astype(np.float32)
    sinT = np.sin(ang).astype(np.float32)

    tril = np.tril(np.ones((S, S), np.float32))
    mask_local = np.concatenate([tril, tril], 0).astype(BF)
    blk = np.repeat(np.arange(NB), M)
    mask_lm = (blk[:, None] >= blk[None, :]).astype(np.float32).astype(BF)
    mra = np.zeros((128, 512), np.float32)
    for r in range(128):
        nrel = (r % 32) // 4
        mra[r, nrel * 64:(nrel + 1) * 64] = 1.0
    mask_ra = mra.astype(BF)
    tok = np.arange(L)
    col = np.arange(NL)
    mask_g = ((col[None, :] // 4) < (tok[:, None] // 64)).astype(np.float32).astype(BF)

    lut = np.asarray(inputs["L0_ffn_lut"]).astype(np.float32)
    lutb = np.broadcast_to(lut, (128, 16)).copy()

    cst = {
        "wqk": _ptile(wqk, 128), "bqk": bias_cp(bqk),
        "wv": _ptile(wv, 256), "bv": np.ascontiguousarray(bv, np.float32).reshape(1, D),
        "PQ": PQ,
        "wlq": _ptile(np.ascontiguousarray(w_lm_qkv[:, 0:D]), 128),
        "wlk": _ptile(np.ascontiguousarray(w_lm_qkv[:, D:2 * D]), 128),
        "wlv": _ptile(np.ascontiguousarray(w_lm_qkv[:, 2 * D:3 * D]), 256),
        "blq": bias_cp(b_lm_qkv[0:D]), "blk": bias_cp(b_lm_qkv[D:2 * D]),
        "blv": np.ascontiguousarray(b_lm_qkv[2 * D:3 * D], np.float32).reshape(1, D),
        "wlmout": _ptile(np.asarray(g("w_lm_out"), np.float32), 128),
        "blmout": bias_cp(g("b_lm_out")),
        "wl2q": _ptile(np.ascontiguousarray(g("w_l2t_q")[:, perm]), 128),
        "bl2q": bias_cp(g("b_l2t_q")[perm]),
        "wl2out": _ptile(np.asarray(g("w_l2t_out"), np.float32), 128),
        "bl2out": bias_cp(g("b_l2t_out")),
        "wgk": _ptile(np.ascontiguousarray(g("w_lm_kv")[:, 0:D][:, perm]), 128),
        "bgk": bias_cp(g("b_lm_kv")[0:D][perm]),
        "wgv": _ptile(np.ascontiguousarray(g("w_lm_kv")[:, D:2 * D]), 256),
        "bgv": np.ascontiguousarray(g("b_lm_kv")[D:2 * D], np.float32).reshape(1, D),
        "wout": _ptile(np.asarray(g("w_out"), np.float32), 128),
        "bout": bias_cp(g("b_out")),
        "wgate": np.ascontiguousarray(g("w_gate"), np.float32),
        "bgate": np.ascontiguousarray(g("b_gate"), np.float32).reshape(H, 1),
        "cosT": cosT, "sinT": sinT,
        "mask_local": mask_local, "mask_lm": mask_lm,
        "mask_ra": mask_ra, "mask_g": mask_g,
        "lutb": lutb,
        "ident": np.eye(128).astype(BF),
        "ones": np.ones((1, 128)).astype(BF),
        "bqk_r": np.ascontiguousarray(bqk, np.float32).reshape(1, 2 * D),
        "gidx": np.ascontiguousarray(inputs["L0_ffn_gate_idx"]),
        "uidx": np.ascontiguousarray(inputs["L0_ffn_up_idx"]),
        "didx": np.ascontiguousarray(inputs["L0_ffn_down_idx"]),
    }
    return cst


class Ctx:
    ENGS = ("sync", "tensor", "vector", "scalar", "gpsimd")

    def __init__(self, nc, stack, block):
        self.nc = nc
        self.stack = stack
        self.block = block
        self.h = {}
        self.n = {}
        for e in self.ENGS:
            self.h[e] = stack.enter_context(nc.semaphore("prg_" + e))
            self.n[e] = 0
        self.sb = {}
        self.scopes = []
        self.pf_free = {}
        self.wqb_free = [0, 0]
        self.wvb_free = [0, 0]

    def mark(self, inst, eng, amt=1):
        inst.then_inc(self.h[eng], amt)
        self.n[eng] += amt
        return self.n[eng]

    def wait(self, engobj, eng, val=None):
        v = self.n[eng] if val is None else val
        if v and v > 0:
            engobj.wait_ge(self.h[eng], v)

    def snap(self):
        return dict(self.n)

    def alloc(self, name, shape, dtype):
        st = self.scopes[-1] if self.scopes else self.stack
        t = st.enter_context(self.nc.sbuf_tensor(name, shape, dtype))
        self.sb[name] = t
        return t

    def push_scope(self):
        self.scopes.append(ExitStack())

    def pop_scope(self):
        self.scopes.pop().close()

    def vguard(self, v, inst):
        self.mark(inst, "vector")
        self.wait(v, "vector")

    def bank_wait(self, engobj, bank):
        fr = self.pf_free.get(bank)
        if fr is not None:
            self.wait(engobj, fr[0], fr[1])

    def bank_read(self, bank, eng):
        self.pf_free[bank] = (eng, self.n[eng])


def barrier(ctx):
    snap = ctx.snap()
    for name in Ctx.ENGS:
        def _f(e, snap=snap, me=name):
            for other, v in snap.items():
                if v > 0 and other != me:
                    e.wait_ge(ctx.h[other], v)
        getattr(ctx.block, name)(_f)


def build_program():
    nc = bass.Bass("TRN2", target_bir_lowering=False)
    dd = {}

    def din(name, shape, dtype):
        dd[name] = nc.dram_tensor(name, list(shape), dtype, kind="ExternalInput")

    def dint(name, shape, dtype):
        kind = "ExternalOutput" if DEBUG else "Internal"
        dd[name] = nc.dram_tensor(name, list(shape), dtype, kind=kind)

    din("x", (L, D), F32)
    din("wqk", (32, 128, 16, 128), F32); din("bqk", (32, 128), F32)
    din("wv", (8, 128, 16, 256), F32); din("bv", (1, D), F32)
    din("PQ", (D, NL), F32)
    din("wlq", (16, 128, 16, 128), F32); din("wlk", (16, 128, 16, 128), F32)
    din("wlv", (8, 128, 16, 256), F32)
    din("blq", (16, 128), F32); din("blk", (16, 128), F32); din("blv", (1, D), F32)
    din("wlmout", (16, 128, 16, 128), F32); din("blmout", (16, 128), F32)
    din("wl2q", (16, 128, 16, 128), F32); din("bl2q", (16, 128), F32)
    din("wl2out", (16, 128, 16, 128), F32); din("bl2out", (16, 128), F32)
    din("wgk", (16, 128, 16, 128), F32); din("bgk", (16, 128), F32)
    din("wgv", (8, 128, 16, 256), F32); din("bgv", (1, D), F32)
    din("wout", (16, 128, 16, 128), F32); din("bout", (16, 128), F32)
    din("wgate", (D, H), F32); din("bgate", (H, 1), F32)
    din("cosT", (128, L), F32); din("sinT", (128, L), F32)
    din("mask_local", (128, 64), BF16)
    din("mask_lm", (128, 128), BF16)
    din("mask_ra", (128, 512), BF16)
    din("mask_g", (L, NL), BF16)
    din("lutb", (128, 16), F32)
    din("gidx", (INTER, D), I32)
    din("uidx", (INTER, D), I32)
    din("didx", (D, INTER), I32)
    din("ident", (128, 128), BF16)
    din("ones", (1, 128), BF16)
    din("bqk_r", (1, 2 * D), F32)

    dd["out"] = nc.dram_tensor("out", [L, D], F32, kind="ExternalOutput")

    dint("dxn", (NSC, 128, NKC, 512), BF16)
    dint("dq", (NSC, 128, NKC, 512), BF16)
    dint("dk", (NSC, 128, NKC, 512), BF16)
    dint("dv", (NTC, 128, D), BF16)
    dint("dol", (NSC, NKC, 128, 512), BF16)
    dint("ds", (NSC, 128, NKC, 512), BF16)
    dint("dgrep", (H, L), BF16)
    dd["wg_nat"] = nc.dram_tensor("wg_nat", [INTER, D], BF16, kind="Internal")
    dd["wu_nat"] = nc.dram_tensor("wu_nat", [INTER, D], BF16, kind="Internal")
    dd["wd_nat"] = nc.dram_tensor("wd_nat", [D, INTER], BF16, kind="Internal")
    dint("wgT", (NOB, 128, NKC, 128), BF16)
    dint("wuT", (NOB, 128, NKC, 128), BF16)
    dint("wdT", (NKC, 128, NOB, 128), BF16)
    if DEBUG:
        dint("dbg_lqt", (128, NKC, 128), BF16)
        dint("dbg_lkt", (128, NKC, 128), BF16)
        dint("dbg_lvt", (128, D), BF16)
        dint("dbg_lot", (128, NKC, 128), BF16)
        dint("dbg_sge", (128, 512), F32)
        dint("dbg_scrB", (128, 512), F32)
        dint("dbg_mg", (128, 4, 128), BF16)
        dint("dbg_red", (128, 32), F32)
        dint("dbg_agb", (128, 512), BF16)
        dint("dbg_agt", (128, 512), BF16)
        dint("dbg_olt", (128, 512), BF16)
        dint("dbg_grep", (128, 512), BF16)
        dint("dbg_og", (128, 512), F32)
        dint("dbg_lmf", (128, NKC, NL), BF16)
        dint("dbg_lmf2", (128, NKC, NL), BF16)
        dint("dbg_lmf3", (128, NKC, NL), BF16)
        dint("dbg_gk", (128, NKC, NL), BF16)
        dint("dbg_gv", (128, D), BF16)
        dint("dbg_gated", (128, NKC, 512), BF16)

    with ExitStack() as stack:
        block = stack.enter_context(nc.Block())
        ctx = Ctx(nc, stack, block)
        ctx.d = dd
        ctx.PF = [stack.enter_context(nc.psum_tensor(f"pf{i}", [128, 512], F32))
                  for i in range(7)]
        ctx.PB = stack.enter_context(nc.psum_tensor("pbt", [128, 1024], BF16))
        emit(ctx)

    return nc, dd


def p_consts(ctx):
    block, d = ctx.block, ctx.d
    a = ctx.alloc
    a("mloc", [128, 64], BF16)
    a("lut_sb", [128, 16], F32)
    a("luth_sb", [128, 16], F32)
    a("ident_sb", [128, 128], BF16)
    a("identf", [128, 128], F32)
    a("ones_sb", [1, 128], BF16)
    a("bqkr", [1, 2 * D], BF16)
    a("wgate_sb", [128, NKC, H], BF16)
    a("bgate_sb", [H, 1], F32)
    a("bstage", [48, 128], F32)
    a("scrA", [128, 512], F32)
    a("scrB", [128, 512], F32)
    a("stat", [128, 8], F32)
    a("red", [128, 32], F32)
    a("epsc", [128, 1], F32)
    bias_c = {"blq": 16, "blk": 16, "blmout": 16, "bl2q": 16,
              "bl2out": 16, "bgk": 16, "bout": 16}
    for nm, c in bias_c.items():
        a("b_" + nm, [128, c], F32)
    a("b_bout_h", [128, 16], F32)

    sb = ctx.sb

    @block.sync
    def _(s):
        ctx.mark(s.dma_start(out=sb["mloc"][:], in_=d["mask_local"][:]), "sync", 16)
        ctx.mark(s.dma_start(out=sb["lut_sb"][:], in_=d["lutb"][:]), "sync", 16)
        ctx.mark(s.dma_start(out=sb["ident_sb"][:], in_=d["ident"][:]), "sync", 16)
        ctx.mark(s.dma_start(out=sb["ones_sb"][:], in_=d["ones"][:]), "sync", 16)
        ctx.mark(s.dma_start(out=sb["bgate_sb"][:], in_=d["bgate"][:]), "sync", 16)

    @block.gpsimd
    def _(g):
        ctx.mark(g.dma_start(out=sb["wgate_sb"][:], in_=d["wgate"][:].rearrange(
            "(c p) o -> p c o", p=128)), "gpsimd", 16)
        ctx.mark(g.dma_start(out=sb["bqkr"][:], in_=d["bqk_r"][:]), "gpsimd", 16)

    @block.vector
    def _(v):
        ctx.wait(v, "sync")
        ctx.mark(v.tensor_scalar_mul(sb["luth_sb"][:], sb["lut_sb"][:], 0.5), "vector")
        ctx.mark(v.tensor_copy(out=sb["identf"][:], in_=sb["ident_sb"][:]), "vector")
        ctx.mark(v.memset(sb["epsc"][:], 1e-5), "vector")

    for nm, C in bias_c.items():
        tgt = sb["b_" + nm]

        @block.sync
        def _(s, nm=nm, C=C):
            ctx.wait(s, "tensor")
            ctx.mark(s.dma_start(out=sb["bstage"][0:C, :], in_=d[nm][:]), "sync", 16)

        @block.tensor
        def _(t, C=C):
            ctx.wait(t, "sync")
            ctx.wait(t, "vector")
            ctx.mark(t.transpose(out=ctx.PF[6][:, 0:C], in_=sb["bstage"][0:C, :],
                                 identity=sb["identf"][0:C, 0:C]), "tensor")

        @block.vector
        def _(v, tgt=tgt, C=C):
            ctx.wait(v, "tensor")
            ctx.mark(v.tensor_copy(out=tgt[:], in_=ctx.PF[6][:, 0:C]), "vector")

    @block.vector
    def _(v):
        ctx.mark(v.tensor_scalar_mul(sb["b_bout_h"][:], sb["b_bout"][:], 0.5),
                 "vector")


class Expander:

    def __init__(self, ctx):
        a = ctx.alloc
        self.ctx = ctx
        self.idxbuf = [a(f"idx{i}", [128, 2752], I32) for i in range(2)]
        self.idxb = a("idxb", [128, 2752], BF16)
        self.sel = a("sel", [128, 16, 512], BF16)
        self.wt = [a(f"wt{i}", [128, 2752], BF16) for i in range(2)]
        d = ctx.d
        sb = ctx.sb
        jobs = []
        for b in range(NOB):
            jobs.append((d["gidx"], d["wg_nat"], b, 0, D, sb["lut_sb"]))
        for b in range(NOB):
            jobs.append((d["uidx"], d["wu_nat"], b, 0, D, sb["lut_sb"]))
        for b in range(NKC):
            jobs.append((d["didx"], d["wd_nat"], b, 0, 2752, sb["luth_sb"]))
            jobs.append((d["didx"], d["wd_nat"], b, 2752, 2752, sb["luth_sb"]))
        self.jobs = jobs
        self.li = 0
        self.ci = 0
        self.si = 0
        self.job_spill = {}
        self.conv_done = [0, 0]
        self.spill_done = [0, 0]
        self.sel_free = [0, 0]

    def _load(self):
        ctx = self.ctx
        if self.li >= len(self.jobs):
            return
        i = self.li
        (dsrc, ddst, b, c0, ncols, lsb) = self.jobs[i]
        slot = i % 2

        @ctx.block.sync
        def _(s):
            if i >= 2:
                ctx.wait(s, "vector", self.conv_done[slot])
            ctx.mark(s.dma_start(
                out=self.idxbuf[slot][:, 0:ncols],
                in_=dsrc[b * 128:(b + 1) * 128, c0:c0 + ncols]), "sync", 16)
        self.li += 1

    def pump(self):
        ctx = self.ctx
        if self.ci == 0 and self.si == 0:
            self._load()
            self._load()
        if self.ci >= len(self.jobs):
            tr = getattr(ctx, "tr", None)
            if tr is not None:
                tr.step()
            return
        i = self.ci
        (dsrc, ddst, b, c0, ncols, lsb) = self.jobs[i]
        slot = i % 2
        nsub = (ncols + 511) // 512
        sub = self.si

        if sub == 0:
            @ctx.block.vector
            def _(v):
                ctx.wait(v, "sync")
                if i >= 2:
                    ctx.wait(v, "sync", self.spill_done[slot])
                self.conv_done[slot] = ctx.mark(
                    v.tensor_copy(out=self.idxb[:, 0:ncols],
                                  in_=self.idxbuf[slot][:, 0:ncols]), "vector")

        @ctx.block.vector
        def _(v):
            sel = self.sel
            w = min(512, ncols - sub * 512)
            srcv = self.idxb[:, sub * 512: sub * 512 + w]
            for t in range(16):
                v.tensor_scalar(out=sel[:, t, 0:w], in0=srcv,
                                scalar1=float(t), scalar2=lsb[:, t:t + 1],
                                op0=ALU.is_equal, op1=ALU.mult)
            if w == 512:
                f = sel[:].rearrange("p c f -> p (c f)")
                v.tensor_add(f[:, 0:4096], f[:, 0:4096], f[:, 4096:8192])
                v.tensor_add(f[:, 0:2048], f[:, 0:2048], f[:, 2048:4096])
                v.tensor_add(f[:, 0:1024], f[:, 0:1024], f[:, 1024:2048])
                inst = v.tensor_add(
                    self.wt[slot][:, sub * 512: sub * 512 + 512],
                    f[:, 0:512], f[:, 512:1024])
            else:
                stride = 1
                while stride < 16:
                    for t0 in range(0, 16, 2 * stride):
                        dst = sel[:, t0, 0:w]
                        if 2 * stride == 16:
                            dst = self.wt[slot][:, sub * 512: sub * 512 + w]
                        inst = v.tensor_add(dst, sel[:, t0, 0:w],
                                            sel[:, t0 + stride, 0:w])
                    stride *= 2
            ctx.mark(inst, "vector")

        self.si += 1
        if self.si < nsub:
            return

        @ctx.block.sync
        def _(s):
            ctx.wait(s, "vector")
            self.spill_done[slot] = ctx.mark(
                s.dma_start(out=ddst[b * 128:(b + 1) * 128, c0:c0 + ncols],
                            in_=self.wt[slot][:, 0:ncols]), "sync", 16)
            self.job_spill[i] = self.spill_done[slot]
        self.si = 0
        self.ci += 1
        self._load()

    def finish(self):
        while self.ci < len(self.jobs):
            self.pump()


class TrRound:

    def __init__(self, ctx):
        a = ctx.alloc
        self.ctx = ctx
        self.tb = [a(f"trb{i}", [128, 512], BF16) for i in range(2)]
        d = ctx.d
        steps = []
        fams = [(d["wg_nat"], d["wgT"], INTER, NKC, 0),
                (d["wu_nat"], d["wuT"], INTER, NKC, 43),
                (d["wd_nat"], d["wdT"], D, NOB, 86)]
        for (src_, dst, O, HB, base) in fams:
            nog = (O + 511) // 512
            for hb in range(HB):
                for og in range(nog):
                    rows = min(512, O - og * 512)
                    ntile = (rows + 127) // 128
                    if base < 86:
                        ready = base + min(4 * og + 3, 42)
                    else:
                        ready = 86 + 8 * og + 7
                    steps.append((src_, dst, og, hb, rows, ntile, ready))
        self.steps = steps
        self.i = 0
        self.store_done = [0, 0]
        self.load_done = [0, 0]

    def step(self):
        ctx = self.ctx
        if self.i >= len(self.steps):
            return False
        (src_, dst, og, hb, rows, ntile, ready) = self.steps[self.i]
        slot = self.i % 2
        rmark = ctx.exp.job_spill.get(ready)
        if rmark is None:
            return False

        @ctx.block.sync
        def _(s):
            ctx.wait(s, "sync", rmark)
            if self.store_done[slot]:
                ctx.wait(s, "gpsimd", self.store_done[slot])
            self.load_done[slot] = ctx.mark(s.dma_start_transpose(
                out=self.tb[slot][:, 0:rows],
                in_=src_[og * 512: og * 512 + rows,
                         hb * 128:(hb + 1) * 128]), "sync", 16)

        @ctx.block.gpsimd
        def _(g):
            ctx.wait(g, "sync", self.load_done[slot])
            for j in range(ntile):
                w = min(128, rows - j * 128)
                inst = g.dma_start(
                    out=dst[og * 4 + j, :, hb, 0:w],
                    in_=self.tb[slot][:, j * 128: j * 128 + w])
                self.store_done[slot] = ctx.mark(inst, "gpsimd", 16)
        self.i += 1
        return True

    def finish(self):
        while self.i < len(self.steps):
            if not self.step():
                raise RuntimeError("trround source not spilled at finish")


def p_lm_scope_alloc(ctx):
    block, d, sb = ctx.block, ctx.d, ctx.sb
    a = ctx.alloc
    a("lmfT", [128, NKC, NL], BF16)
    a("gz", [H, L], BF16)
    a("kbias", [1, 3 * D], BF16)
    for i in range(2):
        a(f"wqb{i}", [128, NKC, 128], BF16)
        a(f"wvb{i}", [128, NKC, 256], BF16)

    @block.gpsimd
    def _(g):
        ctx.mark(g.dma_start(out=sb["kbias"][:, 0:D], in_=d["bv"][:]), "gpsimd", 16)
        ctx.mark(g.dma_start(out=sb["kbias"][:, D:2 * D], in_=d["blv"][:]), "gpsimd", 16)
        ctx.mark(g.dma_start(out=sb["kbias"][:, 2 * D:3 * D], in_=d["bgv"][:]), "gpsimd", 16)


def p_post1_alloc(ctx):
    ctx.tr = TrRound(ctx)
    _a0 = ctx.alloc
    _a0("mlm", [128, 128], BF16)
    _a0("mra", [128, 512], BF16)

    @ctx.block.sync
    def _(s):
        ctx.mark(s.dma_start(out=ctx.sb["mlm"][:], in_=ctx.d["mask_lm"][:]),
                 "sync", 16)
        ctx.mark(s.dma_start(out=ctx.sb["mra"][:], in_=ctx.d["mask_ra"][:]),
                 "sync", 16)

    a = ctx.alloc
    a("lmf2T", [128, NKC, NL], BF16)
    a("lmf3T", [128, NKC, NL], BF16)
    a("l2qT", [128, NKC, NL], BF16)
    a("gkT", [128, NKC, NL], BF16)
    a("lqT", [128, NKC, NL], BF16)
    a("lkT", [128, NKC, NL], BF16)
    a("lvtok", [128, D], BF16)
    a("loT", [128, NKC, NL], BF16)
    a("roT", [128, NKC, NL], BF16)
    a("gv_sb", [128, D], BF16)


def p_sweep1(ctx):
    block, d, sb = ctx.block, ctx.d, ctx.sb
    a = ctx.alloc
    xbuf = [a(f"xb{i}", [128, D], F32) for i in range(2)]
    xnb = [a("xnb0", [128, D], BF16)] * 2
    xnT = a("xnT", [128, NKC, 512], BF16)
    qT = a("qT_s", [128, NKC, 512], BF16)
    kT = a("kT_s", [128, NKC, 512], BF16)
    vsc = a("vsc", [128, 4, D], BF16)
    cos_sb = a("cos_sb", [128, 512], BF16)
    sin_sb = a("sin_sb", [128, 512], BF16)
    pq_sb = a("pq_sb", [128, NKC, 128], BF16)
    wqb = [ctx.sb["wqb0"], ctx.sb["wqb1"]]
    wvb = [ctx.sb["wvb0"], ctx.sb["wvb1"]]
    scrA = sb["scrA"]
    scrB = sb["scrB"]
    stat = sb["stat"]
    red = sb["red"]
    A_sb = a("A_sb", [128, 4, 64], BF16)
    AT_sb = a("AT_sb", [128, 4, 128], BF16)
    pa_sb = a("pa_sb", [128, 512], BF16)
    paT_sb = a("paT_sb", [128, 4, 128], BF16)
    olst = a("olst", [128, 512], BF16)

    @block.gpsimd
    def _(g):
        ctx.mark(g.dma_start(out=pq_sb[:], in_=d["PQ"][:].rearrange(
            "(c p) o -> p c o", p=128)), "gpsimd", 16)

    xb_free = [0, 0]
    xnb_free = [0, 0]
    wq_free = ctx.wqb_free
    wv_free = ctx.wvb_free
    for sc in range(NSC):
        @block.gpsimd
        def _(g, sc=sc):
            ctx.wait(g, "vector", ctx.n["vector"])
            ctx.mark(g.dma_start(out=cos_sb[:],
                                 in_=d["cosT"][:, sc * 512:(sc + 1) * 512]),
                     "gpsimd", 16)
            ctx.mark(g.dma_start(out=sin_sb[:],
                                 in_=d["sinT"][:, sc * 512:(sc + 1) * 512]),
                     "gpsimd", 16)

        prev_lnstats = [ctx.n["vector"]]
        for tc_rel in range(4) if "a" in SW1 else []:
            tc = sc * 4 + tc_rel
            slot = tc % 2

            @block.sync
            def _(s, tc=tc, slot=slot):
                if tc >= 2:
                    ctx.wait(s, "vector", xb_free[slot])
                ctx.mark(s.dma_start(out=xbuf[slot][:],
                                     in_=d["x"][tc * 128:(tc + 1) * 128, :]),
                         "sync", 16)

            @block.scalar
            def _(sce, slot=slot):
                ctx.wait(sce, "sync")
                ctx.wait(sce, "vector", prev_lnstats[0])
                x = xbuf[slot][:]
                sce.activation(out=scrA[:], in_=x[:, 0:512], func=AF.Square)
                sce.activation(out=scrB[:], in_=x[:, 512:1024], func=AF.Square)
                ctx.mark(sce.activation(out=xnb[0][:, 0:512], in_=x[:, 0:512],
                                        func=AF.Square), "scalar")

            @block.vector
            def _(v):
                ctx.wait(v, "scalar")
                v.reduce_sum(red[:, 12:13], scrA[:], axis=mybir.AxisListType.X)
                ctx.mark(v.reduce_sum(red[:, 13:14], scrB[:],
                                      axis=mybir.AxisListType.X), "vector")

            @block.scalar
            def _(sce, slot=slot):
                ctx.wait(sce, "vector")
                x = xbuf[slot][:]
                sce.activation(out=scrA[:], in_=x[:, 1024:1536], func=AF.Square)
                sce.activation(out=scrB[:], in_=x[:, 1536:2048], func=AF.Square)
                ctx.mark(sce.activation(out=xnb[0][:, 0:512],
                                        in_=x[:, 1024:1536],
                                        func=AF.Square), "scalar")

            @block.vector
            def _(v, slot=slot):
                ctx.wait(v, "scalar")
                ctx.wait(v, "tensor")
                x = xbuf[slot][:]
                v.reduce_sum(red[:, 14:15], scrA[:], axis=mybir.AxisListType.X)
                v.reduce_sum(red[:, 15:16], scrB[:], axis=mybir.AxisListType.X)
                ctx.vguard(v, v.reduce_sum(stat[:, 1:2], x,
                                           axis=mybir.AxisListType.X))
                v.tensor_add(red[:, 8:9], red[:, 12:13], red[:, 13:14])
                ctx.vguard(v, v.tensor_add(red[:, 9:10], red[:, 14:15],
                                           red[:, 15:16]))
                v.tensor_add(red[:, 10:11], red[:, 8:9], red[:, 9:10])
                ctx.vguard(v, v.tensor_scalar_mul(stat[:, 2:3], stat[:, 1:2],
                                                  1.0 / D))
                v.tensor_scalar_mul(stat[:, 0:1], red[:, 10:11], 1.0 / D)
                ctx.vguard(v, v.tensor_scalar(out=stat[:, 3:4], in0=stat[:, 2:3],
                                              scalar1=stat[:, 2:3], scalar2=-1.0,
                                              op0=ALU.mult, op1=ALU.mult))
                v.tensor_add(stat[:, 4:5], stat[:, 0:1], stat[:, 3:4])
                ctx.mark(v.tensor_copy(out=red[:, 30:31], in_=stat[:, 4:5]),
                         "vector")
                prev_lnstats[0] = ctx.n["vector"]

            @block.scalar
            def _(sce):
                ctx.wait(sce, "vector")
                sce.activation(out=stat[:, 6:7], in_=stat[:, 4:5], func=AF.Ln,
                               bias=sb["epsc"][:], scale=1.0)
                ctx.mark(sce.activation(out=stat[:, 7:8], in_=stat[:, 4:5],
                                        func=AF.Ln, bias=sb["epsc"][:],
                                        scale=1.0), "scalar")
                ctx.wait(sce, "scalar")
                sce.activation(out=stat[:, 5:6], in_=stat[:, 6:7],
                               func=AF.Exp, scale=-0.5)
                ctx.mark(sce.activation(out=stat[:, 7:8], in_=stat[:, 6:7],
                                        func=AF.Exp, scale=-0.5), "scalar")

            @block.vector
            def _(v, slot=slot, tc=tc):
                ctx.wait(v, "scalar")
                xb_free[slot] = ctx.mark(
                    v.tensor_scalar(out=xnb[slot][:], in0=xbuf[slot][:],
                                    scalar1=stat[:, 2:3], scalar2=stat[:, 5:6],
                                    op0=ALU.subtract, op1=ALU.mult), "vector")

            if "s1a" in PUMP_PHASES:
                ctx.exp.pump()
            for half in range(2):
                @block.tensor
                def _(t, slot=slot, half=half):
                    ctx.wait(t, "vector", xb_free[slot])
                    ctx.bank_wait(t, "PB")
                    for j in range(8):
                        c = half * 8 + j
                        inst = t.transpose(out=ctx.PB[:, j * 128:(j + 1) * 128],
                                           in_=xnb[slot][:, c * 128:(c + 1) * 128],
                                           identity=sb["ident_sb"][:])
                    xnb_free[slot] = ctx.mark(inst, "tensor")

                @block.vector
                def _(v, tc_rel=tc_rel, half=half):
                    ctx.wait(v, "tensor")
                    ctx.mark(v.tensor_copy(
                        out=xnT[:, half * 8:(half + 1) * 8,
                                tc_rel * 128:(tc_rel + 1) * 128],
                        in_=ctx.PB[:].rearrange("p (c f) -> p c f", c=8)),
                        "vector")
                    ctx.bank_read("PB", "vector")



        xnT_ready = ctx.n["vector"]
        if "a" in SW1:
            @block.sync
            def _(s, sc=sc):
                ctx.wait(s, "vector")
                ctx.mark(s.dma_start(out=d["dxn"][sc], in_=xnT[:]), "sync", 16)

        rope_prev = [max(xnT_ready, prev_lnstats[0])]
        for grp in range(16) if "b" in SW1 else []:
            is_q = grp < 8
            j = grp % 8
            ce = j if is_q else 16 + j
            co = ce + 8
            dstT = qT if is_q else kT
            ddst = d["dq"] if is_q else d["dk"]
            for par, oc in ((0, ce), (1, co)):
                slot = par

                @block.gpsimd
                def _(g, oc=oc, slot=slot, grp=grp):
                    if wq_free[slot]:
                        ctx.wait(g, "tensor", wq_free[slot])
                    ctx.mark(g.dma_start(out=wqb[slot][:],
                                         in_=d["wqk"][oc]), "gpsimd", 16)

                @block.tensor
                def _(t, slot=slot, par=par, oc=oc):
                    ctx.wait(t, "gpsimd")
                    ctx.wait(t, "vector", xnT_ready)
                    ctx.bank_wait(t, par)
                    for kc in range(NKC):
                        t.matmul(out=ctx.PF[par][:],
                                 lhsT=wqb[slot][:, kc, :],
                                 rhs=xnT[:, kc, :],
                                 start=(kc == 0), stop=False)
                    for q4 in range(4):
                        inst = t.matmul(
                            out=ctx.PF[par][:, q4 * 128:(q4 + 1) * 128],
                            lhsT=sb["bqkr"][0:1, oc * 128:(oc + 1) * 128],
                            rhs=sb["ones_sb"][:],
                            start=False, stop=True)
                    wq_free[slot] = ctx.mark(inst, "tensor")

            @block.vector
            def _(v, j=j, dstT=dstT, sc=sc):
                ctx.wait(v, "tensor")
                ctx.wait(v, "gpsimd")
                ctx.wait(v, "scalar")
                cs = cos_sb[:]
                sn = sin_sb[:]
                v.tensor_mul(scrA[:], ctx.PF[0][:], cs)
                v.tensor_mul(scrB[:], ctx.PF[1][:], sn)
                v.tensor_sub(dstT[:, j, :], scrA[:], scrB[:])
                v.tensor_mul(scrA[:], ctx.PF[0][:], sn)
                v.tensor_mul(scrB[:], ctx.PF[1][:], cs)
                ctx.mark(v.tensor_add(dstT[:, j + 8, :], scrA[:], scrB[:]),
                         "vector")
                rope_prev[0] = ctx.n["vector"]
                ctx.bank_read(0, "vector")
                ctx.bank_read(1, "vector")

            @block.sync
            def _(s, j=j, dstT=dstT, ddst=ddst, sc=sc):
                ctx.wait(s, "vector")
                ctx.mark(s.dma_start(out=ddst[sc, :, j, :],
                                     in_=dstT[:, j, :]), "sync", 16)
                ctx.mark(s.dma_start(out=ddst[sc, :, j + 8, :],
                                     in_=dstT[:, j + 8, :]), "sync", 16)

            if "s1b" in PUMP_PHASES:
                ctx.exp.pump()

        for vb in range(8) if "v" in SW1 else []:
            slot = vb % 2

            @block.gpsimd
            def _(g, vb=vb, slot=slot, sc=sc):
                if wv_free[slot]:
                    ctx.wait(g, "tensor", wv_free[slot])
                ctx.mark(g.dma_start(out=wvb[slot][:],
                                     in_=d["wv"][vb]), "gpsimd", 16)

            for tc_rel in range(4):
                bank = 2 + (tc_rel % 2)

                @block.tensor
                def _(t, vb=vb, slot=slot, tc_rel=tc_rel, bank=bank):
                    if tc_rel == 0:
                        ctx.wait(t, "gpsimd")
                    ctx.wait(t, "vector", xnT_ready)
                    ctx.bank_wait(t, bank)
                    for kc in range(NKC):
                        t.matmul(out=ctx.PF[bank][:, 0:256],
                                 lhsT=xnT[:, kc, tc_rel * 128:(tc_rel + 1) * 128],
                                 rhs=wvb[slot][:, kc, :],
                                 start=(kc == 0), stop=False)
                    inst = t.matmul(out=ctx.PF[bank][:, 0:256], lhsT=sb["ones_sb"][:],
                                    rhs=sb["kbias"][0:1, vb * 256:(vb + 1) * 256],
                                    start=False, stop=True)
                    w = ctx.mark(inst, "tensor")
                    if tc_rel == 3:
                        wv_free[slot] = w

                @block.vector
                def _(v, vb=vb, tc_rel=tc_rel, bank=bank):
                    ctx.wait(v, "tensor")
                    ctx.mark(v.tensor_copy(
                        out=vsc[:, tc_rel, vb * 256:(vb + 1) * 256],
                        in_=ctx.PF[bank][:, 0:256]), "vector")
                    ctx.bank_read(bank, "vector")

        @block.sync
        def _(s, sc=sc):
            ctx.wait(s, "vector")
            for tc_rel in range(4):
                ctx.mark(s.dma_start(out=d["dv"][sc * 4 + tc_rel, :, :],
                                     in_=vsc[:, tc_rel, :]), "sync", 16)

        if "g" not in SW1:
            continue
        @block.tensor
        def _(t):
            ctx.wait(t, "vector", xnT_ready)
            ctx.wait(t, "gpsimd")
            ctx.bank_wait(t, 4)
            for kc in range(NKC):
                inst = t.matmul(out=ctx.PF[4][0:32, :],
                                lhsT=sb["wgate_sb"][:, kc, :],
                                rhs=xnT[:, kc, :],
                                start=(kc == 0), stop=(kc == NKC - 1))
            ctx.mark(inst, "tensor")

        @block.scalar
        def _(sce, sc=sc):
            ctx.wait(sce, "tensor")
            ctx.mark(sce.activation(out=sb["gz"][:, sc * 512:(sc + 1) * 512],
                                    in_=ctx.PF[4][0:32, :], func=AF.Identity,
                                    bias=sb["bgate_sb"][:], scale=1.0), "scalar")
            ctx.bank_read(4, "scalar")

        qk_done = ctx.n["vector"]
        if "psc" not in SW1:
            continue
        @block.tensor
        def _(t):
            ctx.wait(t, "vector", qk_done)
            ctx.wait(t, "gpsimd")
            ctx.bank_wait(t, 5)
            for kc in range(NKC):
                inst = t.matmul(out=ctx.PF[5][:], lhsT=pq_sb[:, kc, :],
                                rhs=kT[:, kc, :],
                                start=(kc == 0), stop=(kc == NKC - 1))
            ctx.mark(inst, "tensor")

        @block.vector
        def _(v):
            ctx.wait(v, "tensor")
            ctx.wait(v, "scalar")
            s3 = ctx.PF[5][:].rearrange("p (s f) -> p s f", s=8)
            ctx.vguard(v, v.reduce_max(red[:, 0:8], s3, axis=mybir.AxisListType.X))
            ctx.mark(v.tensor_sub(scrA[:].rearrange("p (s f) -> p s f", s=8), s3,
                                  red[:, 0:8].unsqueeze(2).broadcast_to(
                                      [128, 8, 64])), "vector")
            ctx.bank_read(5, "vector")
        pool_sub = ctx.n["vector"]

        @block.scalar
        def _(sce):
            ctx.wait(sce, "vector", pool_sub)
            ctx.mark(sce.activation(out=scrB[:], in_=scrA[:], func=AF.Exp,
                                    scale=SCALE), "scalar")

        @block.vector
        def _(v):
            ctx.wait(v, "scalar")
            e3 = scrB[:].rearrange("p (s f) -> p s f", s=8)
            ctx.vguard(v, v.reduce_sum(red[:, 8:16], e3, axis=mybir.AxisListType.X))
            ctx.vguard(v, v.reciprocal(red[:, 16:24], red[:, 8:16]))
            ctx.mark(v.tensor_mul(pa_sb[:].rearrange("p (s f) -> p s f", s=8),
                                  e3, red[:, 16:24].unsqueeze(2).broadcast_to(
                                      [128, 8, 64])), "vector")
        pa_mark = ctx.n["vector"]

        @block.tensor
        def _(t):
            ctx.wait(t, "vector", pa_mark)
            ctx.bank_wait(t, "PB")
            for tc_rel in range(4):
                inst = t.transpose(out=ctx.PB[:, tc_rel * 128:(tc_rel + 1) * 128],
                                   in_=pa_sb[:, tc_rel * 128:(tc_rel + 1) * 128],
                                   identity=sb["ident_sb"][:])
            ctx.mark(inst, "tensor")

        @block.vector
        def _(v):
            ctx.wait(v, "tensor")
            ctx.mark(v.tensor_copy(out=paT_sb[:],
                                   in_=ctx.PB[:, 0:512].rearrange(
                                       "p (c f) -> p c f", c=4)), "vector")
            ctx.bank_read("PB", "vector")
        paT_mark = ctx.n["vector"]

        nhp = 16 if "loc" in SW1 else (1 if "loc1" in SW1 else 0)
        for hp in range(nhp):
            for hh in range(2):
                h = 2 * hp + hh
                bank = hh

                @block.tensor
                def _(t, h=h, bank=bank):
                    ctx.wait(t, "vector", qk_done)
                    ctx.bank_wait(t, bank)
                    for nrel in range(8):
                        for par in range(2):
                            ch = par * 8 + h // 4
                            rb = (h % 4) * 32
                            inst = t.matmul(
                                out=ctx.PF[bank][(nrel % 2) * 64:(nrel % 2) * 64 + 64,
                                                 (nrel // 2) * 64:(nrel // 2) * 64 + 64],
                                lhsT=qT[rb:rb + 32, ch, nrel * 64:(nrel + 1) * 64],
                                rhs=kT[rb:rb + 32, ch, nrel * 64:(nrel + 1) * 64],
                                start=(par == 0), stop=(par == 1),
                                tile_position=(rb, (nrel % 2) * 64))
                    ctx.mark(inst, "tensor")

                @block.vector
                def _(v, bank=bank):
                    ctx.wait(v, "tensor")
                    ctx.wait(v, "scalar")
                    s3 = ctx.PF[bank][:, 0:256].rearrange("p (s f) -> p s f", s=4)
                    ctx.vguard(v, v.reduce_max(red[:, 0:4], s3,
                                               axis=mybir.AxisListType.X))
                    ctx.mark(v.tensor_sub(
                        scrA[:, 0:256].rearrange("p (s f) -> p s f", s=4), s3,
                        red[:, 0:4].unsqueeze(2).broadcast_to([128, 4, 64])),
                        "vector")
                    ctx.bank_read(bank, "vector")
                lsub_mark = ctx.n["vector"]

                @block.scalar
                def _(sce, lsub_mark=lsub_mark):
                    ctx.wait(sce, "vector", lsub_mark)
                    ctx.mark(sce.activation(out=scrB[:, 0:256], in_=scrA[:, 0:256],
                                            func=AF.Exp, scale=SCALE), "scalar")

                @block.vector
                def _(v):
                    ctx.wait(v, "scalar")
                    ctx.wait(v, "tensor")
                    e3 = scrB[:, 0:256].rearrange("p (s f) -> p s f", s=4)
                    v.tensor_mul(scrA[:, 0:256].rearrange("p (s f) -> p s f", s=4),
                                 e3, sb["mloc"][:].unsqueeze(1).broadcast_to(
                                     [128, 4, 64]))
                    m3 = scrA[:, 0:256].rearrange("p (s f) -> p s f", s=4)
                    ctx.vguard(v, v.reduce_sum(red[:, 4:8], m3,
                                               axis=mybir.AxisListType.X))
                    ctx.vguard(v, v.reciprocal(red[:, 16:20], red[:, 4:8]))
                    ctx.mark(v.tensor_mul(A_sb[:], m3,
                                          red[:, 16:20].unsqueeze(2).broadcast_to(
                                              [128, 4, 64])), "vector")
                A_mark = ctx.n["vector"]

                @block.tensor
                def _(t, A_mark=A_mark):
                    ctx.wait(t, "vector", A_mark)
                    ctx.bank_wait(t, "PB")
                    for g_ in range(4):
                        t.transpose(out=ctx.PB[0:64, g_ * 128:(g_ + 1) * 128],
                                    in_=A_sb[:, g_, :],
                                    identity=sb["ident_sb"][:])
                        inst = t.transpose(
                            out=ctx.PB[64:128, g_ * 128:(g_ + 1) * 128],
                            in_=A_sb[:, g_, :], identity=sb["ident_sb"][:],
                            tile_position=(0, 64))
                    ctx.mark(inst, "tensor")

                @block.vector
                def _(v):
                    ctx.wait(v, "tensor")
                    ctx.mark(v.tensor_copy(out=AT_sb[:],
                                           in_=ctx.PB[:, 0:512].rearrange(
                                               "p (c f) -> p c f", c=4)), "vector")
                    ctx.bank_read("PB", "vector")
                AT_mark = ctx.n["vector"]

                @block.tensor
                def _(t, h=h, hh=hh, AT_mark=AT_mark):
                    ctx.wait(t, "vector", AT_mark)
                    if hh == 0:
                        ctx.bank_wait(t, 2)
                        ctx.bank_wait(t, 6)
                    for nrel in range(8):
                        rb = (nrel % 2) * 64
                        bnk = 2 if nrel % 2 == 0 else 6
                        inst = t.matmul(
                            out=ctx.PF[bnk][hh * 64:hh * 64 + 64,
                                            (nrel // 2) * 64:(nrel // 2) * 64 + 64],
                            lhsT=vsc[rb:rb + 64, nrel // 2, h * 64:(h + 1) * 64],
                            rhs=AT_sb[rb:rb + 64, nrel // 2, rb:rb + 64],
                            start=True, stop=True,
                            tile_position=(rb, hh * 64))
                    ctx.mark(inst, "tensor")

            @block.vector
            def _(v):
                ctx.wait(v, "tensor")
                ctx.wait(v, "sync")
                oview = olst[:].rearrange("p (n2 two f) -> p n2 two f", two=2, f=64)
                v.tensor_copy(out=oview[:, :, 0, :],
                              in_=ctx.PF[2][:, 0:256].rearrange(
                                  "p (n f) -> p n f", f=64))
                ctx.mark(v.tensor_copy(out=oview[:, :, 1, :],
                                       in_=ctx.PF[6][:, 0:256].rearrange(
                                           "p (n f) -> p n f", f=64)), "vector")
                ctx.bank_read(2, "vector")
                ctx.bank_read(6, "vector")

            @block.sync
            def _(s, hp=hp, sc=sc):
                ctx.wait(s, "vector")
                ctx.mark(s.dma_start(out=d["dol"][sc, hp], in_=olst[:]),
                         "sync", 16)

            if "s1c" in PUMP_PHASES:
                ctx.exp.pump()

        if "lm2" not in SW1:
            continue
        @block.tensor
        def _(t):
            ctx.wait(t, "vector", paT_mark)
            ctx.bank_wait(t, 3)
            ctx.bank_wait(t, 5)
            for h in range(H):
                for nrel in range(8):
                    rb = (nrel % 2) * 64
                    bnk = 3 if nrel % 2 == 0 else 5
                    inst = t.matmul(
                        out=ctx.PF[bnk][(h % 2) * 64:(h % 2) * 64 + 64,
                                        (h // 2) * 16 + (nrel // 2) * 4:
                                        (h // 2) * 16 + (nrel // 2) * 4 + 4],
                        lhsT=vsc[rb:rb + 64, nrel // 2, h * 64:(h + 1) * 64],
                        rhs=paT_sb[rb:rb + 64, nrel // 2, 4 * h:4 * h + 4],
                        start=True, stop=True,
                        tile_position=(rb, (h % 2) * 64))
            ctx.mark(inst, "tensor")

        @block.vector
        def _(v, sc=sc):
            ctx.wait(v, "tensor")
            dview = sb["lmfT"][:, :, sc * 32:(sc + 1) * 32].rearrange(
                "p c (n2 two m) -> p c n2 two m", two=2, m=4)
            v.tensor_copy(out=dview[:, :, :, 0, :],
                          in_=ctx.PF[3][:, 0:256].rearrange(
                              "p (c n2 m) -> p c n2 m", n2=4, m=4))
            ctx.mark(v.tensor_copy(out=dview[:, :, :, 1, :],
                                   in_=ctx.PF[5][:, 0:256].rearrange(
                                       "p (c n2 m) -> p c n2 m", n2=4, m=4)),
                     "vector")
            ctx.bank_read(3, "vector")
            ctx.bank_read(5, "vector")

    barrier(ctx)
    if "grep" not in SW1:
        return

    for qq in range(4):
        @block.scalar
        def _(sce, qq=qq):
            ctx.wait(sce, "vector")
            ctx.mark(sce.activation(out=scrA[0:32, :],
                                    in_=sb["gz"][:, qq * 512:(qq + 1) * 512],
                                    func=AF.Tanh, scale=0.5), "scalar")

        @block.vector
        def _(v, qq=qq):
            ctx.wait(v, "scalar")
            ctx.mark(v.tensor_scalar(out=sb["gz"][:, qq * 512:(qq + 1) * 512],
                                     in0=scrA[0:32, :], scalar1=0.5,
                                     scalar2=0.5, op0=ALU.mult, op1=ALU.add),
                     "vector")

    @block.sync
    def _(s):
        ctx.wait(s, "vector")
        ctx.mark(s.dma_start(out=d["dgrep"][:], in_=sb["gz"][:]), "sync", 16)


def _wproj_to_chunks(ctx, wname, bias_sb, rhs_sb, out_sb, resid_sb=None,
                     tag=""):
    block, d, sb = ctx.block, ctx.d, ctx.sb
    wqb = [sb["wqb0"], sb["wqb1"]]
    wl_free = ctx.wqb_free
    for oc in range(NKC):
        slot = oc % 2

        @block.gpsimd
        def _(g, oc=oc, slot=slot):
            if wl_free[slot]:
                ctx.wait(g, "tensor", wl_free[slot])
            ctx.mark(g.dma_start(out=wqb[slot][:], in_=d[wname][oc]),
                     "gpsimd", 16)

        bank = oc % 2

        @block.tensor
        def _(t, slot=slot, bank=bank, oc=oc):
            ctx.wait(t, "gpsimd")
            if oc == 0:
                ctx.wait(t, "vector")
                ctx.wait(t, "scalar")
            ctx.bank_wait(t, bank)
            for kc in range(NKC):
                inst = t.matmul(out=ctx.PF[bank][:, 0:NL],
                                lhsT=wqb[slot][:, kc, :],
                                rhs=rhs_sb[:, kc, :],
                                start=(kc == 0), stop=(kc == NKC - 1))
            wl_free[slot] = ctx.mark(inst, "tensor")

        if resid_sb is None:
            @block.scalar
            def _(sce, oc=oc, bank=bank):
                ctx.wait(sce, "tensor")
                ctx.mark(sce.activation(out=out_sb[:, oc, :],
                                        in_=ctx.PF[bank][:, 0:NL],
                                        func=AF.Identity,
                                        bias=bias_sb[:, oc:oc + 1], scale=1.0),
                         "scalar")
                ctx.bank_read(bank, "scalar")
        else:
            @block.vector
            def _(v, oc=oc, bank=bank):
                ctx.wait(v, "tensor")
                ctx.mark(v.scalar_tensor_tensor(
                    out=out_sb[:, oc, :], in0=ctx.PF[bank][:, 0:NL],
                    scalar=bias_sb[:, oc:oc + 1], in1=resid_sb[:, oc, :],
                    op0=ALU.add, op1=ALU.add), "vector")
                ctx.bank_read(bank, "vector")


def _vproj_tok(ctx, wname, kbias_off, lhsT_sb, out_sb):
    block, d, sb = ctx.block, ctx.d, ctx.sb
    wvb = [sb["wvb0"], sb["wvb1"]]
    wv_free = ctx.wvb_free
    for vb in range(8):
        slot = vb % 2
        bank = 2 + vb % 2

        @block.gpsimd
        def _(g, vb=vb, slot=slot):
            if wv_free[slot]:
                ctx.wait(g, "tensor", wv_free[slot])
            ctx.mark(g.dma_start(out=wvb[slot][:], in_=d[wname][vb]),
                     "gpsimd", 16)

        @block.tensor
        def _(t, vb=vb, slot=slot, bank=bank):
            ctx.wait(t, "gpsimd")
            ctx.wait(t, "vector")
            ctx.bank_wait(t, bank)
            for kc in range(NKC):
                t.matmul(out=ctx.PF[bank][:, 0:256], lhsT=lhsT_sb[:, kc, :],
                         rhs=wvb[slot][:, kc, :],
                         start=(kc == 0), stop=False)
            inst = t.matmul(out=ctx.PF[bank][:, 0:256], lhsT=sb["ones_sb"][:],
                            rhs=sb["kbias"][0:1, kbias_off + vb * 256:
                                            kbias_off + (vb + 1) * 256],
                            start=False, stop=True)
            wv_free[slot] = ctx.mark(inst, "tensor")

        @block.vector
        def _(v, vb=vb, bank=bank):
            ctx.wait(v, "tensor")
            ctx.mark(v.tensor_copy(out=out_sb[:, vb * 256:(vb + 1) * 256],
                                   in_=ctx.PF[bank][:, 0:256]), "vector")
            ctx.bank_read(bank, "vector")


def p_landmark(ctx):
    block, d, sb = ctx.block, ctx.d, ctx.sb
    scrA, scrB = sb["scrA"], sb["scrB"]
    red = sb["red"]
    barrier(ctx)
    _wproj_to_chunks(ctx, "wlq", sb["b_blq"], sb["lmfT"], sb["lqT"])
    _wproj_to_chunks(ctx, "wlk", sb["b_blk"], sb["lmfT"], sb["lkT"])
    _vproj_tok(ctx, "wlv", D, sb["lmfT"], sb["lvtok"])
    la_sb = ctx.alloc("la_sb", [128, 128], BF16)
    laT_sb = ctx.alloc("laT_sb", [128, 128], BF16)

    for h in range(H):
        hb = (h % 2) * 64
        bank = h % 2

        @block.tensor
        def _(t, h=h, hb=hb, bank=bank):
            ctx.bank_wait(t, bank)
            ctx.mark(t.matmul(out=ctx.PF[bank][:, 0:128],
                              lhsT=sb["lqT"][hb:hb + 64, h // 2, :],
                              rhs=sb["lkT"][hb:hb + 64, h // 2, :],
                              start=True, stop=True,
                              tile_position=(hb, 0)), "tensor")

        @block.vector
        def _(v, bank=bank):
            ctx.wait(v, "tensor")
            ctx.wait(v, "scalar")
            ctx.vguard(v, v.reduce_max(red[:, 0:1], ctx.PF[bank][:, 0:128],
                                       axis=mybir.AxisListType.X))
            v.tensor_scalar_mul(red[:, 1:2], red[:, 0:1], -SCALE)
            ctx.mark(v.tensor_copy(out=red[:, 30:31], in_=red[:, 0:1]),
                     "vector")
        mneg_mark = ctx.n["vector"]

        @block.scalar
        def _(sce, bank=bank, mneg_mark=mneg_mark):
            ctx.wait(sce, "vector", mneg_mark)
            ctx.mark(sce.activation(out=scrA[:, 0:128], in_=ctx.PF[bank][:, 0:128],
                                    func=AF.Exp, scale=SCALE,
                                    bias=red[:, 1:2]), "scalar")
            ctx.bank_read(bank, "scalar")

        @block.vector
        def _(v):
            ctx.wait(v, "scalar")
            v.tensor_mul(scrB[:, 0:128], scrA[:, 0:128], sb["mlm"][:])
            ctx.vguard(v, v.reduce_sum(red[:, 2:3], scrB[:, 0:128],
                                       axis=mybir.AxisListType.X))
            ctx.vguard(v, v.reciprocal(red[:, 3:4], red[:, 2:3]))
            ctx.mark(v.tensor_scalar_mul(la_sb[:], scrB[:, 0:128],
                                         red[:, 3:4]), "vector")
        la_mark = ctx.n["vector"]

        @block.tensor
        def _(t, la_mark=la_mark):
            ctx.wait(t, "vector", la_mark)
            ctx.bank_wait(t, "PB")
            ctx.mark(t.transpose(out=ctx.PB[:, 0:128], in_=la_sb[:],
                                 identity=sb["ident_sb"][:]), "tensor")

        @block.vector
        def _(v):
            ctx.wait(v, "tensor")
            ctx.mark(v.tensor_copy(out=laT_sb[:], in_=ctx.PB[:, 0:128]), "vector")
            ctx.bank_read("PB", "vector")
        laT_mark = ctx.n["vector"]

        @block.tensor
        def _(t, h=h, hb=hb, laT_mark=laT_mark):
            ctx.wait(t, "vector", laT_mark)
            if hb == 0:
                ctx.bank_wait(t, 2)
            ctx.mark(t.matmul(out=ctx.PF[2][hb:hb + 64, 0:128],
                              lhsT=sb["lvtok"][:, h * 64:(h + 1) * 64],
                              rhs=laT_sb[:], start=True, stop=True,
                              tile_position=(0, hb)), "tensor")

        if h % 2 == 1:
            @block.vector
            def _(v, h=h):
                ctx.wait(v, "tensor")
                ctx.mark(v.tensor_copy(out=sb["loT"][:, h // 2, :],
                                       in_=ctx.PF[2][:, 0:128]), "vector")
                ctx.bank_read(2, "vector")
        if "lm" in PUMP_PHASES:
            ctx.exp.pump()
            ctx.exp.pump()

    _wproj_to_chunks(ctx, "wlmout", sb["b_blmout"], sb["loT"], sb["lmf2T"],
                     resid_sb=sb["lmfT"])
    _wproj_to_chunks(ctx, "wl2q", sb["b_bl2q"], sb["lmf2T"], sb["l2qT"])
    if DEBUG:
        @block.sync
        def _(s):
            ctx.wait(s, "vector")
            ctx.wait(s, "scalar")
            ctx.mark(s.dma_start(out=d["dbg_lmf"][:], in_=sb["lmfT"][:]), "sync", 16)
            ctx.mark(s.dma_start(out=d["dbg_lmf2"][:], in_=sb["lmf2T"][:]), "sync", 16)
            ctx.mark(s.dma_start(out=d["dbg_lqt"][:], in_=sb["lqT"][:]), "sync", 16)
            ctx.mark(s.dma_start(out=d["dbg_lkt"][:], in_=sb["lkT"][:]), "sync", 16)
            ctx.mark(s.dma_start(out=d["dbg_lvt"][:], in_=sb["lvtok"][:]), "sync", 16)
            ctx.mark(s.dma_start(out=d["dbg_lot"][:], in_=sb["loT"][:]), "sync", 16)


def p_sweep2a(ctx):
    block, d, sb = ctx.block, ctx.d, ctx.sb
    kT = ctx.alloc("kT2", [128, NKC, 512], BF16)
    vsc = ctx.alloc("vsc2", [128, 4, D], BF16)
    scrA, scrB, red = sb["scrA"], sb["scrB"], sb["red"]
    raA = ctx.alloc("raA", [128, 512], BF16)
    raT = ctx.alloc("raT", [128, 4, 128], BF16)
    for sc in range(NSC):
        barrier(ctx)

        @block.sync
        def _(s, sc=sc):
            ctx.mark(s.dma_start(out=kT[:], in_=d["dk"][sc]), "sync", 16)
            for tc_rel in range(4):
                ctx.mark(s.dma_start(out=vsc[:, tc_rel, :],
                                     in_=d["dv"][sc * 4 + tc_rel, :, :]),
                         "sync", 16)

        for hg in range(8):
            bank = hg % 2

            @block.tensor
            def _(t, hg=hg, sc=sc, bank=bank):
                ctx.wait(t, "sync")
                ctx.bank_wait(t, bank)
                for hsub in range(4):
                    h = 4 * hg + hsub
                    for par in range(2):
                        ch = par * 8 + h // 4
                        rb = (h % 4) * 32
                        inst = t.matmul(
                            out=ctx.PF[bank][hsub * 32:(hsub + 1) * 32, :],
                            lhsT=sb["l2qT"][rb:rb + 32, ch,
                                            sc * 32:(sc + 1) * 32],
                            rhs=kT[rb:rb + 32, ch, :],
                            start=(par == 0), stop=(par == 1),
                            tile_position=(rb, hsub * 32))
                ctx.mark(inst, "tensor")

            @block.vector
            def _(v, bank=bank):
                ctx.wait(v, "tensor")
                ctx.wait(v, "scalar")
                ctx.vguard(v, v.reduce_max(red[:, 0:1], ctx.PF[bank][:],
                                           axis=mybir.AxisListType.X))
                v.tensor_scalar_mul(red[:, 1:2], red[:, 0:1], -SCALE)
                ctx.mark(v.tensor_copy(out=red[:, 30:31], in_=red[:, 0:1]),
                         "vector")
            mneg2_mark = ctx.n["vector"]

            @block.scalar
            def _(sce, bank=bank, mneg2_mark=mneg2_mark):
                ctx.wait(sce, "vector", mneg2_mark)
                ctx.mark(sce.activation(out=scrA[:], in_=ctx.PF[bank][:],
                                        func=AF.Exp, scale=SCALE,
                                        bias=red[:, 1:2]), "scalar")
                ctx.bank_read(bank, "scalar")

            @block.vector
            def _(v):
                ctx.wait(v, "scalar")
                v.tensor_mul(scrB[:], scrA[:], sb["mra"][:])
                ctx.vguard(v, v.reduce_sum(red[:, 2:3], scrB[:],
                                           axis=mybir.AxisListType.X))
                ctx.vguard(v, v.reciprocal(red[:, 3:4], red[:, 2:3]))
                ctx.mark(v.tensor_scalar_mul(raA[:], scrB[:], red[:, 3:4]),
                         "vector")
            raA_mark = ctx.n["vector"]

            @block.tensor
            def _(t, raA_mark=raA_mark):
                ctx.wait(t, "vector", raA_mark)
                ctx.bank_wait(t, "PB")
                for pp in range(4):
                    inst = t.transpose(out=ctx.PB[:, pp * 128:(pp + 1) * 128],
                                       in_=raA[:, pp * 128:(pp + 1) * 128],
                                       identity=sb["ident_sb"][:])
                ctx.mark(inst, "tensor")

            @block.vector
            def _(v):
                ctx.wait(v, "tensor")
                ctx.mark(v.tensor_copy(out=raT[:],
                                       in_=ctx.PB[:, 0:512].rearrange(
                                           "p (c f) -> p c f", c=4)), "vector")
                ctx.bank_read("PB", "vector")
            raT_mark = ctx.n["vector"]

            @block.tensor
            def _(t, hg=hg, raT_mark=raT_mark):
                ctx.wait(t, "vector", raT_mark)
                if hg == 0:
                    ctx.bank_wait(t, 2)
                    ctx.bank_wait(t, 3)
                for hsub in range(4):
                    h = 4 * hg + hsub
                    for nrel in range(8):
                        rb = (nrel % 2) * 64
                        bnk = 2 if nrel % 2 == 0 else 3
                        inst = t.matmul(
                            out=ctx.PF[bnk][(h % 2) * 64:(h % 2) * 64 + 64,
                                            (h // 2) * 16 + (nrel // 2) * 4:
                                            (h // 2) * 16 + (nrel // 2) * 4 + 4],
                            lhsT=vsc[rb:rb + 64, nrel // 2, h * 64:(h + 1) * 64],
                            rhs=raT[rb:rb + 64, nrel // 2,
                                    hsub * 32 + nrel * 4:hsub * 32 + nrel * 4 + 4],
                            start=True, stop=True,
                            tile_position=(rb, (h % 2) * 64))
                ctx.mark(inst, "tensor")

            if "s2a" in PUMP_PHASES:
                ctx.exp.pump()
                ctx.exp.pump()
                ctx.exp.pump()

        @block.vector
        def _(v, sc=sc):
            ctx.wait(v, "tensor")
            dview = sb["roT"][:, :, sc * 32:(sc + 1) * 32].rearrange(
                "p c (n2 two m) -> p c n2 two m", two=2, m=4)
            v.tensor_copy(out=dview[:, :, :, 0, :],
                          in_=ctx.PF[2][:, 0:256].rearrange(
                              "p (c n2 m) -> p c n2 m", n2=4, m=4))
            ctx.mark(v.tensor_copy(out=dview[:, :, :, 1, :],
                                   in_=ctx.PF[3][:, 0:256].rearrange(
                                       "p (c n2 m) -> p c n2 m", n2=4, m=4)),
                     "vector")
            ctx.bank_read(2, "vector")
            ctx.bank_read(3, "vector")


def p_lmtail(ctx):
    block, d, sb = ctx.block, ctx.d, ctx.sb
    barrier(ctx)
    _wproj_to_chunks(ctx, "wl2out", sb["b_bl2out"], sb["roT"], sb["lmf3T"],
                     resid_sb=sb["lmf2T"])
    _wproj_to_chunks(ctx, "wgk", sb["b_bgk"], sb["lmf3T"], sb["gkT"])
    _vproj_tok(ctx, "wgv", 2 * D, sb["lmf3T"], sb["gv_sb"])
    if DEBUG:
        @block.sync
        def _(s):
            ctx.wait(s, "vector")
            ctx.wait(s, "scalar")
            ctx.mark(s.dma_start(out=d["dbg_lmf3"][:], in_=sb["lmf3T"][:]), "sync", 16)
            ctx.mark(s.dma_start(out=d["dbg_gk"][:], in_=sb["gkT"][:]), "sync", 16)
            ctx.mark(s.dma_start(out=d["dbg_gv"][:], in_=sb["gv_sb"][:]), "sync", 16)


def p_sweep2b(ctx):
    block, d, sb = ctx.block, ctx.d, ctx.sb
    qT = ctx.alloc("qT2", [128, NKC, 512], BF16)
    scrA, scrB, red = sb["scrA"], sb["scrB"], sb["red"]
    mg = ctx.alloc("mg", [128, NTC, 128], BF16)
    agb = ctx.alloc("agb", [128, 512], BF16)
    agT = ctx.alloc("agT", [128, 4, 128], BF16)

    @block.sync
    def _(s):
        ctx.mark(s.dma_start(out=mg[:], in_=d["mask_g"][:].rearrange(
            "(c p) l -> p c l", p=128)), "sync", 16)
    mg_mark = ctx.n["sync"]
    olT = [ctx.alloc(f"olT{i}", [128, 512], BF16) for i in range(2)]
    grep = [ctx.alloc(f"grep{i}", [128, 512], BF16) for i in range(2)]
    gated = ctx.alloc("gated", [128, NKC, 512], BF16)
    sstg = ctx.alloc("sstg", [128, 512], BF16)
    ol_used = [0, 0]

    for sc in range(NSC):
        barrier(ctx)

        @block.sync
        def _(s, sc=sc):
            ctx.mark(s.dma_start(out=qT[:], in_=d["dq"][sc]), "sync", 16)

        for hp in range(16):
            slot = hp % 2

            @block.sync
            def _(s, hp=hp, sc=sc, slot=slot):
                if hp >= 2:
                    ctx.wait(s, "vector", ol_used[slot])
                ctx.mark(s.dma_start(out=olT[slot][:],
                                     in_=d["dol"][sc, hp]), "sync", 16)
                gsrc = d["dgrep"][2 * hp:2 * hp + 2, sc * 512:(sc + 1) * 512]
                ctx.mark(s.dma_start(out=grep[slot][:],
                                     in_=gsrc.unsqueeze(1).broadcast_to(
                                         [2, 64, 512])), "sync", 16)

            for hh in range(2):
                h = 2 * hp + hh
                bank = hh

                @block.tensor
                def _(t, h=h, bank=bank):
                    ctx.wait(t, "sync")
                    ctx.bank_wait(t, bank)
                    for tc_rel in range(4):
                        for par in range(2):
                            ch = par * 8 + h // 4
                            rb = (h % 4) * 32
                            inst = t.matmul(
                                out=ctx.PF[bank][:, tc_rel * 128:(tc_rel + 1) * 128],
                                lhsT=qT[rb:rb + 32, ch,
                                        tc_rel * 128:(tc_rel + 1) * 128],
                                rhs=sb["gkT"][rb:rb + 32, ch, :],
                                start=(par == 0), stop=(par == 1),
                                tile_position=(rb, 0))
                    ctx.mark(inst, "tensor")

                @block.vector
                def _(v, bank=bank):
                    ctx.wait(v, "tensor")
                    ctx.wait(v, "scalar")
                    s3 = ctx.PF[bank][:].rearrange("p (s f) -> p s f", s=4)
                    ctx.vguard(v, v.reduce_max(red[:, 0:4], s3,
                                               axis=mybir.AxisListType.X))
                    v.tensor_scalar_mul(red[:, 4:8], red[:, 0:4], -SCALE)
                    ctx.mark(v.tensor_copy(out=red[:, 30:31], in_=red[:, 0:1]),
                             "vector")
                mneg3_mark = ctx.n["vector"]

                @block.scalar
                def _(sce, bank=bank, mneg3_mark=mneg3_mark):
                    ctx.wait(sce, "vector", mneg3_mark)
                    for tc_rel in range(4):
                        inst = sce.activation(
                            out=scrA[:, tc_rel * 128:(tc_rel + 1) * 128],
                            in_=ctx.PF[bank][:, tc_rel * 128:(tc_rel + 1) * 128],
                            func=AF.Exp, scale=SCALE,
                            bias=red[:, 4 + tc_rel:5 + tc_rel])
                    ctx.mark(inst, "scalar")
                    ctx.bank_read(bank, "scalar")

                @block.vector
                def _(v, sc=sc):
                    ctx.wait(v, "scalar")
                    ctx.wait(v, "sync", mg_mark)
                    e3 = scrA[:].rearrange("p (s f) -> p s f", s=4)
                    v.tensor_mul(scrB[:].rearrange("p (s f) -> p s f", s=4),
                                 e3, mg[:, sc * 4:(sc + 1) * 4, :])
                    m3 = scrB[:].rearrange("p (s f) -> p s f", s=4)
                    ctx.vguard(v, v.reduce_sum(red[:, 8:12], m3,
                                               axis=mybir.AxisListType.X))
                    ctx.vguard(v, v.tensor_scalar_max(red[:, 12:16],
                                                      red[:, 8:12], 1e-9))
                    ctx.vguard(v, v.reciprocal(red[:, 16:20], red[:, 12:16]))
                    ctx.mark(v.tensor_mul(agb[:].rearrange("p (s f) -> p s f", s=4),
                                          m3,
                                          red[:, 16:20].unsqueeze(2).broadcast_to(
                                              [128, 4, 128])), "vector")
                agb_mark = ctx.n["vector"]

                @block.tensor
                def _(t, agb_mark=agb_mark):
                    ctx.wait(t, "vector", agb_mark)
                    ctx.bank_wait(t, "PB")
                    for tc_rel in range(4):
                        inst = t.transpose(
                            out=ctx.PB[:, tc_rel * 128:(tc_rel + 1) * 128],
                            in_=agb[:, tc_rel * 128:(tc_rel + 1) * 128],
                            identity=sb["ident_sb"][:])
                    ctx.mark(inst, "tensor")

                @block.vector
                def _(v):
                    ctx.wait(v, "tensor")
                    ctx.mark(v.tensor_copy(out=agT[:],
                                           in_=ctx.PB[:, 0:512].rearrange(
                                               "p (c f) -> p c f", c=4)), "vector")
                    ctx.bank_read("PB", "vector")
                agT_mark = ctx.n["vector"]

                @block.tensor
                def _(t, h=h, hh=hh, agT_mark=agT_mark):
                    ctx.wait(t, "vector", agT_mark)
                    if hh == 0:
                        ctx.bank_wait(t, 2)
                    for tc_rel in range(4):
                        inst = t.matmul(
                            out=ctx.PF[2][hh * 64:hh * 64 + 64,
                                          tc_rel * 128:(tc_rel + 1) * 128],
                            lhsT=sb["gv_sb"][:, h * 64:(h + 1) * 64],
                            rhs=agT[:, tc_rel, :], start=True, stop=True,
                            tile_position=(0, hh * 64))
                    ctx.mark(inst, "tensor")

            if DEBUG and sc == 0 and hp == 0:
                @block.sync
                def _(s, slot=slot):
                    ctx.wait(s, "tensor")
                    ctx.wait(s, "vector")
                    ctx.mark(s.dma_start(out=d["dbg_sge"][:], in_=scrA[:]), "sync", 16)
                    ctx.mark(s.dma_start(out=d["dbg_scrB"][:], in_=scrB[:]), "sync", 16)
                    ctx.mark(s.dma_start(out=d["dbg_mg"][:], in_=mg[:, 0:4, :]), "sync", 16)
                    ctx.mark(s.dma_start(out=d["dbg_red"][:], in_=red[:]), "sync", 16)
                    ctx.mark(s.dma_start(out=d["dbg_agb"][:], in_=agb[:]), "sync", 16)
                    ctx.mark(s.dma_start(out=d["dbg_agt"][:],
                                         in_=agT[:].rearrange("p c f -> p (c f)")), "sync", 16)
                    ctx.mark(s.dma_start(out=d["dbg_olt"][:], in_=olT[slot][:]), "sync", 16)
                    ctx.mark(s.dma_start(out=d["dbg_grep"][:], in_=grep[slot][:]), "sync", 16)

            @block.vector
            def _(v, hp=hp, slot=slot):
                ctx.wait(v, "tensor")
                ctx.wait(v, "sync")
                v.tensor_sub(scrA[:], ctx.PF[2][:], olT[slot][:])
                v.tensor_mul(scrB[:], scrA[:], grep[slot][:])
                ol_used[slot] = ctx.mark(
                    v.tensor_add(gated[:, hp, :], scrB[:], olT[slot][:]),
                    "vector")
                ctx.bank_read(2, "vector")

            ctx.exp.pump()
            ctx.exp.pump()
            ctx.exp.pump()

        if DEBUG and sc == 0:
            @block.sync
            def _(s):
                ctx.wait(s, "vector")
                ctx.mark(s.dma_start(out=d["dbg_gated"][:], in_=gated[:]),
                         "sync", 16)

        gated_mark = ctx.n["vector"]
        wqb = [sb["wqb0"], sb["wqb1"]]
        wo_free = ctx.wqb_free
        for oc in range(NKC):
            slot = oc % 2
            bank = oc % 2

            @block.gpsimd
            def _(g, oc=oc, slot=slot):
                if wo_free[slot]:
                    ctx.wait(g, "tensor", wo_free[slot])
                ctx.mark(g.dma_start(out=wqb[slot][:], in_=d["wout"][oc]),
                         "gpsimd", 16)

            @block.tensor
            def _(t, slot=slot, bank=bank, oc=oc, gated_mark=gated_mark):
                ctx.wait(t, "gpsimd")
                ctx.bank_wait(t, bank)
                if oc < 2:
                    ctx.wait(t, "vector", gated_mark)
                for kc in range(NKC):
                    inst = t.matmul(out=ctx.PF[bank][:],
                                    lhsT=wqb[slot][:, kc, :],
                                    rhs=gated[:, kc, :],
                                    start=(kc == 0), stop=(kc == NKC - 1))
                wo_free[slot] = ctx.mark(inst, "tensor")

            @block.scalar
            def _(sce, oc=oc, bank=bank):
                ctx.wait(sce, "tensor")
                ctx.wait(sce, "sync")
                ctx.mark(sce.activation(out=sstg[:], in_=ctx.PF[bank][:],
                                        func=AF.Identity, scale=0.5,
                                        bias=sb["b_bout_h"][:, oc:oc + 1]),
                         "scalar")
                ctx.bank_read(bank, "scalar")

            @block.sync
            def _(s, oc=oc, sc=sc):
                ctx.wait(s, "scalar")
                ctx.mark(s.dma_start(out=d["ds"][sc, :, oc, :],
                                     in_=sstg[:]), "sync", 16)


def p_ffn(ctx):
    block, d, sb = ctx.block, ctx.d, ctx.sb
    a = ctx.alloc
    xnf = a("xnf", [128, NKC, 512], BF16)
    sTf = a("sTf", [128, NKC, 512], BF16)
    hT = a("hT", [128, NOB, 512], BF16)
    pT = sTf
    gbuf = [a(f"gbuf{i}", [128, NKC, 128], BF16) for i in range(2)]
    ubuf = [a(f"ubuf{i}", [128, NKC, 128], BF16) for i in range(2)]
    dbuf = [a(f"dbuf{i}", [128, NOB, 128], BF16) for i in range(2)]
    xbuf2 = [a("xbf0", [128, D], F32)] * 2
    ostg = [a("ostg0", [128, D], F32)] * 2
    scr2A = a("scr2A", [128, 512], F32)
    scr2B = a("scr2B", [128, 512], F32)
    scrp = [(sb["scrA"], sb["scrB"]), (scr2A, scr2B)]

    gu_free = [0, 0]
    dn_free = [0, 0]
    x2_free = [0, 0]
    os_free = [0, 0]
    for sc in range(NSC):
        barrier(ctx)

        @block.sync
        def _(s, sc=sc):
            ctx.mark(s.dma_start(out=xnf[:], in_=d["dxn"][sc]), "sync", 16)
            ctx.mark(s.dma_start(out=sTf[:], in_=d["ds"][sc]), "sync", 16)

        for oc in range(NOB):
            slot = oc % 2

            @block.sync
            def _(s, oc=oc, slot=slot, sc=sc):
                if oc >= 2 or sc > 0:
                    ctx.wait(s, "tensor", gu_free[slot])
                ctx.mark(s.dma_start(out=gbuf[slot][:], in_=d["wgT"][oc]),
                         "sync", 16)
                ctx.mark(s.dma_start(out=ubuf[slot][:], in_=d["wuT"][oc]),
                         "sync", 16)

            bg = 0 if oc % 2 == 0 else 4
            bu = 1 if oc % 2 == 0 else 5

            @block.tensor
            def _(t, oc=oc, slot=slot, bg=bg, bu=bu):
                ctx.wait(t, "sync")
                ctx.bank_wait(t, bg)
                ctx.bank_wait(t, bu)
                for kc in range(NKC):
                    t.matmul(out=ctx.PF[bg][:], lhsT=gbuf[slot][:, kc, :],
                             rhs=xnf[:, kc, :], start=(kc == 0),
                             stop=(kc == NKC - 1))
                for kc in range(NKC):
                    inst = t.matmul(out=ctx.PF[bu][:], lhsT=ubuf[slot][:, kc, :],
                                    rhs=xnf[:, kc, :], start=(kc == 0),
                                    stop=(kc == NKC - 1))
                gu_free[slot] = ctx.mark(inst, "tensor")

            sA, sB = scrp[oc % 2]

            @block.scalar
            def _(sce, bg=bg, sA=sA):
                ctx.wait(sce, "tensor")
                ctx.mark(sce.activation(out=sA[:], in_=ctx.PF[bg][:],
                                        func=AF.Tanh, scale=0.5), "scalar")

            @block.vector
            def _(v, oc=oc, bg=bg, bu=bu, sA=sA, sB=sB):
                ctx.wait(v, "scalar")
                v.scalar_tensor_tensor(out=sB[:], in0=sA[:], scalar=1.0,
                                       in1=ctx.PF[bg][:], op0=ALU.add,
                                       op1=ALU.mult)
                ctx.mark(v.tensor_mul(hT[:, oc, :], sB[:], ctx.PF[bu][:]),
                         "vector")
                ctx.bank_read(bg, "vector")
                ctx.bank_read(bu, "vector")

        for oc2 in range(NKC):
            slot = oc2 % 2
            bank = 2 + oc2 % 2

            @block.sync
            def _(s, oc2=oc2, slot=slot, sc=sc):
                if oc2 >= 2 or sc > 0:
                    ctx.wait(s, "tensor", dn_free[slot])
                ctx.mark(s.dma_start(out=dbuf[slot][:], in_=d["wdT"][oc2]),
                         "sync", 16)

            @block.tensor
            def _(t, oc2=oc2, slot=slot, bank=bank):
                ctx.wait(t, "sync")
                if oc2 == 0:
                    ctx.wait(t, "vector")
                ctx.bank_wait(t, bank)
                for ic in range(NOB):
                    inst = t.matmul(out=ctx.PF[bank][:],
                                    lhsT=dbuf[slot][:, ic, :],
                                    rhs=hT[:, ic, :],
                                    start=(ic == 0), stop=(ic == NOB - 1))
                dn_free[slot] = ctx.mark(inst, "tensor")

            sA, sB = scrp[oc2 % 2]

            @block.scalar
            def _(sce, bank=bank, sA=sA):
                ctx.wait(sce, "tensor")
                ctx.mark(sce.activation(out=sA[:], in_=ctx.PF[bank][:],
                                        func=AF.Tanh, scale=0.5), "scalar")

            @block.vector
            def _(v, oc2=oc2, bank=bank, sA=sA, sB=sB):
                ctx.wait(v, "scalar")
                ctx.wait(v, "sync")
                v.scalar_tensor_tensor(out=sB[:], in0=sA[:], scalar=1.0,
                                       in1=ctx.PF[bank][:], op0=ALU.add,
                                       op1=ALU.mult)
                ctx.mark(v.tensor_mul(pT[:, oc2, :], sB[:], sTf[:, oc2, :]),
                         "vector")
                ctx.bank_read(bank, "vector")

        for tc_rel in range(4):
            tc = sc * 4 + tc_rel
            slot = 0

            @block.sync
            def _(s, tc=tc, slot=slot):
                if tc >= 1:
                    ctx.wait(s, "vector", x2_free[slot])
                ctx.mark(s.dma_start(out=xbuf2[slot][:],
                                     in_=d["x"][tc * 128:(tc + 1) * 128, :]),
                         "sync", 16)

            for half in range(2):
                @block.tensor
                def _(t, tc_rel=tc_rel, half=half):
                    if half == 0 and tc_rel == 0:
                        ctx.wait(t, "vector")
                    ctx.bank_wait(t, "PB")
                    for j in range(8):
                        oc2 = half * 8 + j
                        inst = t.transpose(
                            out=ctx.PB[:, j * 128:(j + 1) * 128],
                            in_=pT[:, oc2, tc_rel * 128:(tc_rel + 1) * 128],
                            identity=sb["ident_sb"][:])
                    ctx.mark(inst, "tensor")

                @block.vector
                def _(v, half=half, slot=slot):
                    ctx.wait(v, "tensor")
                    if half == 0:
                        ctx.wait(v, "sync", os_free[slot])
                        ctx.wait(v, "sync")
                    inst = v.tensor_add(
                        ostg[slot][:, half * 1024:(half + 1) * 1024],
                        xbuf2[slot][:, half * 1024:(half + 1) * 1024],
                        ctx.PB[:, 0:1024])
                    w = ctx.mark(inst, "vector")
                    if half == 1:
                        x2_free[slot] = w
                    ctx.bank_read("PB", "vector")

            @block.sync
            def _(s, tc=tc, slot=slot):
                ctx.wait(s, "vector")
                os_free[slot] = ctx.mark(
                    s.dma_start(out=d["out"][tc * 128:(tc + 1) * 128, :],
                                in_=ostg[slot][:]), "sync", 16)


def emit(ctx):
    p_consts(ctx)
    ctx.push_scope()
    p_lm_scope_alloc(ctx)
    ctx.exp = Expander(ctx)
    ctx.push_scope()
    p_sweep1(ctx)
    barrier(ctx)
    ctx.pop_scope()
    ctx.push_scope()
    p_post1_alloc(ctx)
    p_landmark(ctx)
    ctx.push_scope()
    p_sweep2a(ctx)
    barrier(ctx)
    ctx.pop_scope()
    p_lmtail(ctx)
    ctx.push_scope()
    p_sweep2b(ctx)
    ctx.exp.finish()
    ctx.tr.finish()
    barrier(ctx)
    ctx.pop_scope()
    barrier(ctx)
    ctx.pop_scope()
    barrier(ctx)
    ctx.pop_scope()
    ctx.push_scope()
    p_ffn(ctx)
    ctx.pop_scope()
    block = ctx.block

    @block.sync
    def _(s):
        for e in ("tensor", "vector", "scalar", "gpsimd"):
            ctx.wait(s, e)
        ctx.wait(s, "sync")


def kernel(**inputs):
    cst = host_prep(inputs)
    x = np.asarray(inputs["L0_attn_x"])
    nc, dd = build_program()
    in_maps = []
    for c in range(8):
        m = dict(cst)
        m["x"] = np.ascontiguousarray(x[c])
        in_maps.append(m)
    res = run_bass_kernel_spmd(nc, in_maps, core_ids=list(range(8)))
    return np.stack([res.results[c]["out"] for c in range(8)], 0)


# version 21
# speedup vs baseline: 1.0643x; 1.0643x over previous
import numpy as np
import ml_dtypes
from contextlib import ExitStack

import concourse.bass as bass
import concourse.mybir as mybir
from concourse.bass_utils import run_bass_kernel_spmd
from concourse.dve_spec import Spec, Src0, Src1, C0, C1, C2, One, eq, lower
from concourse.dve_op import DveOp
from concourse.dve_ops import OPS as _DVE_OPS
from concourse.dve_table_gen import dve_ver_for


def _make_lut2acc():
    for op in _DVE_OPS:
        if op.name == "LUT2ACC":
            return op
    body = Src1 + eq(Src0, C2) * C0 + eq(Src0 - C2, One) * C1
    spec = Spec(body=body)
    ver = dve_ver_for("TRN2")
    sha = lower(spec, ver=ver, perf_en=False, name="LUT2ACC").sha(ver)
    op = DveOp("LUT2ACC", spec, subdim=False, uops_sha={ver: sha},
               reference=lambda in0, in1, s0, s1, imm2:
                   in1 + (in0 == imm2) * s0 + ((in0 - imm2) == 1.0) * s1)
    _DVE_OPS.append(op)
    from concourse import dve_ops as _dvo
    _dvo._SUB_OPCODE_FOR_NAME[op.name] = _dvo._CUSTOM_DVE_ROW_BASE + len(_DVE_OPS) - 1
    return op


LUT2ACC = _make_lut2acc()

F32 = mybir.dt.float32
BF16 = mybir.dt.bfloat16
I32 = mybir.dt.int32
AF = mybir.ActivationFunctionType
ALU = mybir.AluOpType
BF = ml_dtypes.bfloat16

D = 2048
H = 32
DH = 64
M = 4
S = 64
L = 2048
NB = 32
NL = 128
INTER = 5504
NKC = 16
NTC = 16
NSC = 4
SCALE = 0.125
ROPE_BASE = 10000.0
NOB = 43

DEBUG = False
BUILD_UPTO = 99
SW1 = {"a", "b", "v", "g", "psc", "loc", "lm2", "grep"}
PUMP_PHASES = {"s1a", "s1b", "s1c", "lm", "s2a", "s2b"}


def _ps_perm():
    perm = np.zeros(D, np.int64)
    for r in range(D):
        p = r // 1024
        rem = r % 1024
        h = rem // 32
        i = rem % 32
        perm[r] = h * DH + 2 * i + p
    return perm


def _ptile(W, ow):
    Din, O = W.shape
    return np.ascontiguousarray(
        W.reshape(Din // 128, 128, O // ow, ow).transpose(2, 1, 0, 3),
        dtype=np.float32)


def host_prep(inputs):
    g = lambda n: np.asarray(inputs["L0_attn_" + n])
    perm = _ps_perm()

    w_qkv = g("w_qkv")
    b_qkv = g("b_qkv")
    wq = w_qkv[:, 0:D][:, perm]
    wk = w_qkv[:, D:2 * D][:, perm]
    wqk = np.ascontiguousarray(np.concatenate([wq, wk], 1), dtype=np.float32)
    bqk = np.concatenate([b_qkv[0:D][perm], b_qkv[D:2 * D][perm]])
    wv = np.ascontiguousarray(w_qkv[:, 2 * D:3 * D], dtype=np.float32)
    bv = b_qkv[2 * D:3 * D]

    pq = g("pool_q")
    PQ = np.zeros((D, NL), np.float32)
    for r in range(D):
        p = r // 1024
        rem = r % 1024
        h = rem // 32
        i = rem % 32
        PQ[r, 4 * h: 4 * h + 4] = pq[:, h * DH + 2 * i + p]

    w_lm_qkv = g("w_lm_qkv")
    b_lm_qkv = g("b_lm_qkv")

    def bias_cp(b):
        return np.ascontiguousarray(b.reshape(-1, 128), dtype=np.float32)

    i_of_row = np.arange(128) % 32
    inv = ROPE_BASE ** (-(2.0 * i_of_row) / DH)
    ang = inv[:, None] * np.arange(L, dtype=np.float64)[None, :]
    cosT = np.cos(ang).astype(np.float32)
    sinT = np.sin(ang).astype(np.float32)

    tril = np.tril(np.ones((S, S), np.float32))
    mask_local = np.concatenate([tril, tril], 0).astype(BF)
    blk = np.repeat(np.arange(NB), M)
    mask_lm = (blk[:, None] >= blk[None, :]).astype(np.float32).astype(BF)
    mra = np.zeros((128, 512), np.float32)
    for r in range(128):
        nrel = (r % 32) // 4
        mra[r, nrel * 64:(nrel + 1) * 64] = 1.0
    mask_ra = mra.astype(BF)
    tok = np.arange(L)
    col = np.arange(NL)
    mask_g = ((col[None, :] // 4) < (tok[:, None] // 64)).astype(np.float32).astype(BF)

    lut = np.asarray(inputs["L0_ffn_lut"]).astype(np.float32)
    lutb = np.broadcast_to(lut, (128, 16)).copy()

    cst = {
        "wqk": _ptile(wqk, 128), "bqk": bias_cp(bqk),
        "wv": _ptile(wv, 256), "bv": np.ascontiguousarray(bv, np.float32).reshape(1, D),
        "PQ": PQ,
        "wlq": _ptile(np.ascontiguousarray(w_lm_qkv[:, 0:D]), 128),
        "wlk": _ptile(np.ascontiguousarray(w_lm_qkv[:, D:2 * D]), 128),
        "wlv": _ptile(np.ascontiguousarray(w_lm_qkv[:, 2 * D:3 * D]), 256),
        "blq": bias_cp(b_lm_qkv[0:D]), "blk": bias_cp(b_lm_qkv[D:2 * D]),
        "blv": np.ascontiguousarray(b_lm_qkv[2 * D:3 * D], np.float32).reshape(1, D),
        "wlmout": _ptile(np.asarray(g("w_lm_out"), np.float32), 128),
        "blmout": bias_cp(g("b_lm_out")),
        "wl2q": _ptile(np.ascontiguousarray(g("w_l2t_q")[:, perm]), 128),
        "bl2q": bias_cp(g("b_l2t_q")[perm]),
        "wl2out": _ptile(np.asarray(g("w_l2t_out"), np.float32), 128),
        "bl2out": bias_cp(g("b_l2t_out")),
        "wgk": _ptile(np.ascontiguousarray(g("w_lm_kv")[:, 0:D][:, perm]), 128),
        "bgk": bias_cp(g("b_lm_kv")[0:D][perm]),
        "wgv": _ptile(np.ascontiguousarray(g("w_lm_kv")[:, D:2 * D]), 256),
        "bgv": np.ascontiguousarray(g("b_lm_kv")[D:2 * D], np.float32).reshape(1, D),
        "wout": _ptile(np.asarray(g("w_out"), np.float32), 128),
        "bout": bias_cp(g("b_out")),
        "wgate": np.ascontiguousarray(g("w_gate"), np.float32),
        "bgate": np.ascontiguousarray(g("b_gate"), np.float32).reshape(H, 1),
        "cosT": cosT, "sinT": sinT,
        "mask_local": mask_local, "mask_lm": mask_lm,
        "mask_ra": mask_ra, "mask_g": mask_g,
        "lutb": lutb,
        "ident": np.eye(128).astype(BF),
        "ones": np.ones((1, 128)).astype(BF),
        "bqk_r": np.ascontiguousarray(bqk, np.float32).reshape(1, 2 * D),
        "gidx": np.ascontiguousarray(inputs["L0_ffn_gate_idx"]),
        "uidx": np.ascontiguousarray(inputs["L0_ffn_up_idx"]),
        "didx": np.ascontiguousarray(inputs["L0_ffn_down_idx"]),
    }
    return cst


class Ctx:
    ENGS = ("sync", "tensor", "vector", "scalar", "gpsimd")

    def __init__(self, nc, stack, block):
        self.nc = nc
        self.stack = stack
        self.block = block
        self.h = {}
        self.n = {}
        for e in self.ENGS:
            self.h[e] = stack.enter_context(nc.semaphore("prg_" + e))
            self.n[e] = 0
        self.sb = {}
        self.scopes = []
        self.pf_free = {}
        self.wqb_free = [0, 0]
        self.wvb_free = [0, 0]

    def mark(self, inst, eng, amt=1):
        inst.then_inc(self.h[eng], amt)
        self.n[eng] += amt
        return self.n[eng]

    def wait(self, engobj, eng, val=None):
        v = self.n[eng] if val is None else val
        if v and v > 0:
            engobj.wait_ge(self.h[eng], v)

    def snap(self):
        return dict(self.n)

    def alloc(self, name, shape, dtype):
        st = self.scopes[-1] if self.scopes else self.stack
        t = st.enter_context(self.nc.sbuf_tensor(name, shape, dtype))
        self.sb[name] = t
        return t

    def push_scope(self):
        self.scopes.append(ExitStack())

    def pop_scope(self):
        self.scopes.pop().close()

    def vguard(self, v, inst):
        self.mark(inst, "vector")
        self.wait(v, "vector")

    def bank_wait(self, engobj, bank):
        fr = self.pf_free.get(bank)
        if fr is not None:
            self.wait(engobj, fr[0], fr[1])

    def bank_read(self, bank, eng):
        self.pf_free[bank] = (eng, self.n[eng])


def barrier(ctx):
    snap = ctx.snap()
    for name in Ctx.ENGS:
        def _f(e, snap=snap, me=name):
            for other, v in snap.items():
                if v > 0 and other != me:
                    e.wait_ge(ctx.h[other], v)
        getattr(ctx.block, name)(_f)


def build_program():
    nc = bass.Bass("TRN2", target_bir_lowering=False)
    dd = {}

    def din(name, shape, dtype):
        dd[name] = nc.dram_tensor(name, list(shape), dtype, kind="ExternalInput")

    def dint(name, shape, dtype):
        kind = "ExternalOutput" if DEBUG else "Internal"
        dd[name] = nc.dram_tensor(name, list(shape), dtype, kind=kind)

    din("x", (L, D), F32)
    din("wqk", (32, 128, 16, 128), F32); din("bqk", (32, 128), F32)
    din("wv", (8, 128, 16, 256), F32); din("bv", (1, D), F32)
    din("PQ", (D, NL), F32)
    din("wlq", (16, 128, 16, 128), F32); din("wlk", (16, 128, 16, 128), F32)
    din("wlv", (8, 128, 16, 256), F32)
    din("blq", (16, 128), F32); din("blk", (16, 128), F32); din("blv", (1, D), F32)
    din("wlmout", (16, 128, 16, 128), F32); din("blmout", (16, 128), F32)
    din("wl2q", (16, 128, 16, 128), F32); din("bl2q", (16, 128), F32)
    din("wl2out", (16, 128, 16, 128), F32); din("bl2out", (16, 128), F32)
    din("wgk", (16, 128, 16, 128), F32); din("bgk", (16, 128), F32)
    din("wgv", (8, 128, 16, 256), F32); din("bgv", (1, D), F32)
    din("wout", (16, 128, 16, 128), F32); din("bout", (16, 128), F32)
    din("wgate", (D, H), F32); din("bgate", (H, 1), F32)
    din("cosT", (128, L), F32); din("sinT", (128, L), F32)
    din("mask_local", (128, 64), BF16)
    din("mask_lm", (128, 128), BF16)
    din("mask_ra", (128, 512), BF16)
    din("mask_g", (L, NL), BF16)
    din("lutb", (128, 16), F32)
    din("gidx", (INTER, D), I32)
    din("uidx", (INTER, D), I32)
    din("didx", (D, INTER), I32)
    din("ident", (128, 128), BF16)
    din("ones", (1, 128), BF16)
    din("bqk_r", (1, 2 * D), F32)

    dd["out"] = nc.dram_tensor("out", [L, D], F32, kind="ExternalOutput")

    dint("dxn", (NSC, 128, NKC, 512), BF16)
    dint("dq", (NSC, 128, NKC, 512), BF16)
    dint("dk", (NSC, 128, NKC, 512), BF16)
    dint("dv", (NTC, 128, D), BF16)
    dint("dol", (NSC, NKC, 128, 512), BF16)
    dint("ds", (NSC, 128, NKC, 512), BF16)
    dint("dgrep", (H, L), BF16)
    dd["wg_nat"] = nc.dram_tensor("wg_nat", [INTER, D], BF16, kind="Internal")
    dd["wu_nat"] = nc.dram_tensor("wu_nat", [INTER, D], BF16, kind="Internal")
    dd["wd_nat"] = nc.dram_tensor("wd_nat", [D, INTER], BF16, kind="Internal")
    dint("wgT", (NOB, 128, NKC, 128), BF16)
    dint("wuT", (NOB, 128, NKC, 128), BF16)
    dint("wdT", (NKC, 128, NOB, 128), BF16)
    if DEBUG:
        dint("dbg_lqt", (128, NKC, 128), BF16)
        dint("dbg_lkt", (128, NKC, 128), BF16)
        dint("dbg_lvt", (128, D), BF16)
        dint("dbg_lot", (128, NKC, 128), BF16)
        dint("dbg_sge", (128, 512), F32)
        dint("dbg_scrB", (128, 512), F32)
        dint("dbg_mg", (128, 4, 128), BF16)
        dint("dbg_red", (128, 32), F32)
        dint("dbg_agb", (128, 512), BF16)
        dint("dbg_agt", (128, 512), BF16)
        dint("dbg_olt", (128, 512), BF16)
        dint("dbg_grep", (128, 512), BF16)
        dint("dbg_og", (128, 512), F32)
        dint("dbg_lmf", (128, NKC, NL), BF16)
        dint("dbg_lmf2", (128, NKC, NL), BF16)
        dint("dbg_lmf3", (128, NKC, NL), BF16)
        dint("dbg_gk", (128, NKC, NL), BF16)
        dint("dbg_gv", (128, D), BF16)
        dint("dbg_gated", (128, NKC, 512), BF16)

    with ExitStack() as stack:
        block = stack.enter_context(nc.Block())
        ctx = Ctx(nc, stack, block)
        ctx.d = dd
        ctx.PF = [stack.enter_context(nc.psum_tensor(f"pf{i}", [128, 512], F32))
                  for i in range(7)]
        ctx.PB = stack.enter_context(nc.psum_tensor("pbt", [128, 1024], BF16))
        emit(ctx)

    return nc, dd


def p_consts(ctx):
    block, d = ctx.block, ctx.d
    a = ctx.alloc
    a("mloc", [128, 64], BF16)
    a("lut_sb", [128, 16], F32)
    a("luth_sb", [128, 16], F32)
    a("ident_sb", [128, 128], BF16)
    a("identf", [128, 128], F32)
    a("ones_sb", [1, 128], BF16)
    a("bqkr", [1, 2 * D], BF16)
    a("wgate_sb", [128, NKC, H], BF16)
    a("bgate_sb", [H, 1], F32)
    a("bstage", [48, 128], F32)
    a("scrA", [128, 512], F32)
    a("scrB", [128, 512], F32)
    a("stat", [128, 8], F32)
    a("red", [128, 32], F32)
    a("epsc", [128, 1], F32)
    bias_c = {"blq": 16, "blk": 16, "blmout": 16, "bl2q": 16,
              "bl2out": 16, "bgk": 16, "bout": 16}
    for nm, c in bias_c.items():
        a("b_" + nm, [128, c], F32)
    a("b_bout_h", [128, 16], F32)

    sb = ctx.sb

    @block.sync
    def _(s):
        ctx.mark(s.dma_start(out=sb["mloc"][:], in_=d["mask_local"][:]), "sync", 16)
        ctx.mark(s.dma_start(out=sb["lut_sb"][:], in_=d["lutb"][:]), "sync", 16)
        ctx.mark(s.dma_start(out=sb["ident_sb"][:], in_=d["ident"][:]), "sync", 16)
        ctx.mark(s.dma_start(out=sb["ones_sb"][:], in_=d["ones"][:]), "sync", 16)
        ctx.mark(s.dma_start(out=sb["bgate_sb"][:], in_=d["bgate"][:]), "sync", 16)

    @block.gpsimd
    def _(g):
        ctx.mark(g.dma_start(out=sb["wgate_sb"][:], in_=d["wgate"][:].rearrange(
            "(c p) o -> p c o", p=128)), "gpsimd", 16)
        ctx.mark(g.dma_start(out=sb["bqkr"][:], in_=d["bqk_r"][:]), "gpsimd", 16)

    @block.vector
    def _(v):
        ctx.wait(v, "sync")
        ctx.mark(v.tensor_scalar_mul(sb["luth_sb"][:], sb["lut_sb"][:], 0.5), "vector")
        ctx.mark(v.tensor_copy(out=sb["identf"][:], in_=sb["ident_sb"][:]), "vector")
        ctx.mark(v.memset(sb["epsc"][:], 1e-5), "vector")

    for nm, C in bias_c.items():
        tgt = sb["b_" + nm]

        @block.sync
        def _(s, nm=nm, C=C):
            ctx.wait(s, "tensor")
            ctx.mark(s.dma_start(out=sb["bstage"][0:C, :], in_=d[nm][:]), "sync", 16)

        @block.tensor
        def _(t, C=C):
            ctx.wait(t, "sync")
            ctx.wait(t, "vector")
            ctx.mark(t.transpose(out=ctx.PF[6][:, 0:C], in_=sb["bstage"][0:C, :],
                                 identity=sb["identf"][0:C, 0:C]), "tensor")

        @block.vector
        def _(v, tgt=tgt, C=C):
            ctx.wait(v, "tensor")
            ctx.mark(v.tensor_copy(out=tgt[:], in_=ctx.PF[6][:, 0:C]), "vector")

    @block.vector
    def _(v):
        ctx.mark(v.tensor_scalar_mul(sb["b_bout_h"][:], sb["b_bout"][:], 0.5),
                 "vector")


class Expander:

    def __init__(self, ctx):
        a = ctx.alloc
        self.ctx = ctx
        self.idxbuf = [a(f"idx{i}", [128, 2752], I32) for i in range(2)]
        self.idxb = a("idxb", [128, 2752], BF16)
        self.sel = a("sel", [128, 16, 512], BF16)
        self.wt = [a(f"wt{i}", [128, 2752], BF16) for i in range(2)]
        d = ctx.d
        sb = ctx.sb
        jobs = []
        for b in range(NOB):
            jobs.append((d["gidx"], d["wg_nat"], b, 0, D, sb["lut_sb"]))
        for b in range(NOB):
            jobs.append((d["uidx"], d["wu_nat"], b, 0, D, sb["lut_sb"]))
        for b in range(NKC):
            jobs.append((d["didx"], d["wd_nat"], b, 0, 2752, sb["luth_sb"]))
            jobs.append((d["didx"], d["wd_nat"], b, 2752, 2752, sb["luth_sb"]))
        self.jobs = jobs
        self.li = 0
        self.ci = 0
        self.si = 0
        self.job_spill = {}
        self.conv_done = [0, 0]
        self.spill_done = [0, 0]
        self.sel_free = [0, 0]

    def _load(self):
        ctx = self.ctx
        if self.li >= len(self.jobs):
            return
        i = self.li
        (dsrc, ddst, b, c0, ncols, lsb) = self.jobs[i]
        slot = i % 2

        @ctx.block.sync
        def _(s):
            if i >= 2:
                ctx.wait(s, "vector", self.conv_done[slot])
            ctx.mark(s.dma_start(
                out=self.idxbuf[slot][:, 0:ncols],
                in_=dsrc[b * 128:(b + 1) * 128, c0:c0 + ncols]), "sync", 16)
        self.li += 1

    def pump(self):
        ctx = self.ctx
        if self.ci == 0 and self.si == 0:
            self._load()
            self._load()
        tr = getattr(ctx, "tr", None)
        self.pump_ct = getattr(self, "pump_ct", 0) + 1
        if tr is not None and (self.pump_ct % 2 == 0 or
                               self.ci >= len(self.jobs)):
            if tr.step():
                return
        if self.ci >= len(self.jobs):
            return
        i = self.ci
        (dsrc, ddst, b, c0, ncols, lsb) = self.jobs[i]
        slot = i % 2
        nsub = (ncols + 511) // 512
        sub = self.si

        if sub == 0:
            @ctx.block.vector
            def _(v):
                ctx.wait(v, "sync")
                if i >= 2:
                    ctx.wait(v, "sync", self.spill_done[slot])
                self.conv_done[slot] = ctx.mark(
                    v.tensor_copy(out=self.idxb[:, 0:ncols],
                                  in_=self.idxbuf[slot][:, 0:ncols]), "vector")

        @ctx.block.vector
        def _(v):
            sel = self.sel
            w = min(512, ncols - sub * 512)
            srcv = self.idxb[:, sub * 512: sub * 512 + w]
            for t in range(16):
                v.tensor_scalar(out=sel[:, t, 0:w], in0=srcv,
                                scalar1=float(t), scalar2=lsb[:, t:t + 1],
                                op0=ALU.is_equal, op1=ALU.mult)
            if w == 512:
                f = sel[:].rearrange("p c f -> p (c f)")
                v.tensor_add(f[:, 0:4096], f[:, 0:4096], f[:, 4096:8192])
                v.tensor_add(f[:, 0:2048], f[:, 0:2048], f[:, 2048:4096])
                v.tensor_add(f[:, 0:1024], f[:, 0:1024], f[:, 1024:2048])
                inst = v.tensor_add(
                    self.wt[slot][:, sub * 512: sub * 512 + 512],
                    f[:, 0:512], f[:, 512:1024])
            else:
                stride = 1
                while stride < 16:
                    for t0 in range(0, 16, 2 * stride):
                        dst = sel[:, t0, 0:w]
                        if 2 * stride == 16:
                            dst = self.wt[slot][:, sub * 512: sub * 512 + w]
                        inst = v.tensor_add(dst, sel[:, t0, 0:w],
                                            sel[:, t0 + stride, 0:w])
                    stride *= 2
            ctx.mark(inst, "vector")

        self.si += 1
        if self.si < nsub:
            return

        @ctx.block.sync
        def _(s):
            ctx.wait(s, "vector")
            self.spill_done[slot] = ctx.mark(
                s.dma_start(out=ddst[b * 128:(b + 1) * 128, c0:c0 + ncols],
                            in_=self.wt[slot][:, 0:ncols]), "sync", 16)
            self.job_spill[i] = self.spill_done[slot]
        self.si = 0
        self.ci += 1
        self._load()

    def finish(self):
        while self.ci < len(self.jobs):
            self.pump()


class TrRound:

    def __init__(self, ctx):
        a = ctx.alloc
        self.ctx = ctx
        self.tb = [a(f"trb{i}", [128, 512], BF16) for i in range(2)]
        d = ctx.d
        steps = []
        fams = [(d["wg_nat"], d["wgT"], INTER, NKC, 0),
                (d["wu_nat"], d["wuT"], INTER, NKC, 43),
                (d["wd_nat"], d["wdT"], D, NOB, 86)]
        for (src_, dst, O, HB, base) in fams:
            nog = (O + 511) // 512
            for hb in range(HB):
                for og in range(nog):
                    rows = min(512, O - og * 512)
                    ntile = (rows + 127) // 128
                    if base < 86:
                        ready = base + min(4 * og + 3, 42)
                    else:
                        ready = 86 + 8 * og + 7
                    steps.append((src_, dst, og, hb, rows, ntile, ready))
        self.steps = steps
        self.i = 0
        self.pending = None
        self.store_done = [0, 0]
        self.load_done = [0, 0]

    def _store(self):
        ctx = self.ctx
        k = self.pending
        (src_, dst, og, hb, rows, ntile, ready) = self.steps[k]
        slot = k % 2

        @ctx.block.scalar
        def _(sce):
            ctx.wait(sce, "sync", self.load_done[slot])
            for j in range(ntile):
                w = min(128, rows - j * 128)
                inst = sce.dma_start(
                    out=dst[og * 4 + j, :, hb, 0:w],
                    in_=self.tb[slot][:, j * 128: j * 128 + w])
                self.store_done[slot] = ctx.mark(inst, "scalar", 16)
        self.pending = None

    def step(self):
        ctx = self.ctx
        did = False
        if self.pending is not None:
            self._store()
            did = True
        if self.i >= len(self.steps):
            return did
        (src_, dst, og, hb, rows, ntile, ready) = self.steps[self.i]
        rmark = ctx.exp.job_spill.get(ready)
        if rmark is None:
            return did
        slot = self.i % 2

        @ctx.block.sync
        def _(s):
            ctx.wait(s, "sync", rmark)
            if self.store_done[slot]:
                ctx.wait(s, "scalar", self.store_done[slot])
            self.load_done[slot] = ctx.mark(s.dma_start_transpose(
                out=self.tb[slot][:, 0:rows],
                in_=src_[og * 512: og * 512 + rows,
                         hb * 128:(hb + 1) * 128]), "sync", 16)
        self.pending = self.i
        self.i += 1
        return True

    def finish(self):
        while self.i < len(self.steps) or self.pending is not None:
            if not self.step():
                raise RuntimeError("trround source not spilled at finish")


def p_lm_scope_alloc(ctx):
    block, d, sb = ctx.block, ctx.d, ctx.sb
    a = ctx.alloc
    a("lmfT", [128, NKC, NL], BF16)
    a("gz", [H, L], BF16)
    a("kbias", [1, 3 * D], BF16)
    for i in range(2):
        a(f"wqb{i}", [128, NKC, 128], BF16)
        a(f"wvb{i}", [128, NKC, 256], BF16)

    @block.gpsimd
    def _(g):
        ctx.mark(g.dma_start(out=sb["kbias"][:, 0:D], in_=d["bv"][:]), "gpsimd", 16)
        ctx.mark(g.dma_start(out=sb["kbias"][:, D:2 * D], in_=d["blv"][:]), "gpsimd", 16)
        ctx.mark(g.dma_start(out=sb["kbias"][:, 2 * D:3 * D], in_=d["bgv"][:]), "gpsimd", 16)


def p_post1_alloc(ctx):
    ctx.tr = TrRound(ctx)
    _a0 = ctx.alloc
    _a0("mlm", [128, 128], BF16)
    _a0("mra", [128, 512], BF16)

    @ctx.block.sync
    def _(s):
        ctx.mark(s.dma_start(out=ctx.sb["mlm"][:], in_=ctx.d["mask_lm"][:]),
                 "sync", 16)
        ctx.mark(s.dma_start(out=ctx.sb["mra"][:], in_=ctx.d["mask_ra"][:]),
                 "sync", 16)

    a = ctx.alloc
    a("lmf2T", [128, NKC, NL], BF16)
    a("lmf3T", [128, NKC, NL], BF16)
    a("l2qT", [128, NKC, NL], BF16)
    a("gkT", [128, NKC, NL], BF16)
    a("lqT", [128, NKC, NL], BF16)
    a("lkT", [128, NKC, NL], BF16)
    a("lvtok", [128, D], BF16)
    a("loT", [128, NKC, NL], BF16)
    a("roT", [128, NKC, NL], BF16)
    a("gv_sb", [128, D], BF16)


def p_sweep1(ctx):
    block, d, sb = ctx.block, ctx.d, ctx.sb
    a = ctx.alloc
    xbuf = [a(f"xb{i}", [128, D], F32) for i in range(2)]
    xnb = [a("xnb0", [128, D], BF16)] * 2
    xnT = a("xnT", [128, NKC, 512], BF16)
    qT = a("qT_s", [128, NKC, 512], BF16)
    kT = a("kT_s", [128, NKC, 512], BF16)
    vsc = a("vsc", [128, 4, D], BF16)
    cos_sb = a("cos_sb", [128, 512], BF16)
    sin_sb = a("sin_sb", [128, 512], BF16)
    pq_sb = a("pq_sb", [128, NKC, 128], BF16)
    wqb = [ctx.sb["wqb0"], ctx.sb["wqb1"]]
    wvb = [ctx.sb["wvb0"], ctx.sb["wvb1"]]
    scrA = sb["scrA"]
    scrB = sb["scrB"]
    stat = sb["stat"]
    red = sb["red"]
    A_sb = a("A_sb", [128, 4, 64], BF16)
    AT_sb = a("AT_sb", [128, 4, 128], BF16)
    pa_sb = a("pa_sb", [128, 512], BF16)
    paT_sb = a("paT_sb", [128, 4, 128], BF16)
    olst = a("olst", [128, 512], BF16)

    @block.gpsimd
    def _(g):
        ctx.mark(g.dma_start(out=pq_sb[:], in_=d["PQ"][:].rearrange(
            "(c p) o -> p c o", p=128)), "gpsimd", 16)

    xb_free = [0, 0]
    xnb_free = [0, 0]
    wq_free = ctx.wqb_free
    wv_free = ctx.wvb_free
    for sc in range(NSC):
        @block.gpsimd
        def _(g, sc=sc):
            ctx.wait(g, "vector", ctx.n["vector"])
            ctx.mark(g.dma_start(out=cos_sb[:],
                                 in_=d["cosT"][:, sc * 512:(sc + 1) * 512]),
                     "gpsimd", 16)
            ctx.mark(g.dma_start(out=sin_sb[:],
                                 in_=d["sinT"][:, sc * 512:(sc + 1) * 512]),
                     "gpsimd", 16)

        prev_lnstats = [ctx.n["vector"]]
        for tc_rel in range(4) if "a" in SW1 else []:
            tc = sc * 4 + tc_rel
            slot = tc % 2

            @block.sync
            def _(s, tc=tc, slot=slot):
                if tc >= 2:
                    ctx.wait(s, "vector", xb_free[slot])
                ctx.mark(s.dma_start(out=xbuf[slot][:],
                                     in_=d["x"][tc * 128:(tc + 1) * 128, :]),
                         "sync", 16)

            @block.scalar
            def _(sce, slot=slot):
                ctx.wait(sce, "sync")
                ctx.wait(sce, "vector", prev_lnstats[0])
                x = xbuf[slot][:]
                sce.activation(out=scrA[:], in_=x[:, 0:512], func=AF.Square)
                sce.activation(out=scrB[:], in_=x[:, 512:1024], func=AF.Square)
                ctx.mark(sce.activation(out=xnb[0][:, 0:512], in_=x[:, 0:512],
                                        func=AF.Square), "scalar")

            @block.vector
            def _(v):
                ctx.wait(v, "scalar")
                v.reduce_sum(red[:, 12:13], scrA[:], axis=mybir.AxisListType.X)
                ctx.mark(v.reduce_sum(red[:, 13:14], scrB[:],
                                      axis=mybir.AxisListType.X), "vector")

            @block.scalar
            def _(sce, slot=slot):
                ctx.wait(sce, "vector")
                x = xbuf[slot][:]
                sce.activation(out=scrA[:], in_=x[:, 1024:1536], func=AF.Square)
                sce.activation(out=scrB[:], in_=x[:, 1536:2048], func=AF.Square)
                ctx.mark(sce.activation(out=xnb[0][:, 0:512],
                                        in_=x[:, 1024:1536],
                                        func=AF.Square), "scalar")

            @block.vector
            def _(v, slot=slot):
                ctx.wait(v, "scalar")
                ctx.wait(v, "tensor")
                x = xbuf[slot][:]
                v.reduce_sum(red[:, 14:15], scrA[:], axis=mybir.AxisListType.X)
                v.reduce_sum(red[:, 15:16], scrB[:], axis=mybir.AxisListType.X)
                ctx.vguard(v, v.reduce_sum(stat[:, 1:2], x,
                                           axis=mybir.AxisListType.X))
                v.tensor_add(red[:, 8:9], red[:, 12:13], red[:, 13:14])
                ctx.vguard(v, v.tensor_add(red[:, 9:10], red[:, 14:15],
                                           red[:, 15:16]))
                v.tensor_add(red[:, 10:11], red[:, 8:9], red[:, 9:10])
                ctx.vguard(v, v.tensor_scalar_mul(stat[:, 2:3], stat[:, 1:2],
                                                  1.0 / D))
                v.tensor_scalar_mul(stat[:, 0:1], red[:, 10:11], 1.0 / D)
                ctx.vguard(v, v.tensor_scalar(out=stat[:, 3:4], in0=stat[:, 2:3],
                                              scalar1=stat[:, 2:3], scalar2=-1.0,
                                              op0=ALU.mult, op1=ALU.mult))
                v.tensor_add(stat[:, 4:5], stat[:, 0:1], stat[:, 3:4])
                ctx.mark(v.tensor_copy(out=red[:, 30:31], in_=stat[:, 4:5]),
                         "vector")
                prev_lnstats[0] = ctx.n["vector"]

            @block.scalar
            def _(sce):
                ctx.wait(sce, "vector")
                sce.activation(out=stat[:, 6:7], in_=stat[:, 4:5], func=AF.Ln,
                               bias=sb["epsc"][:], scale=1.0)
                ctx.mark(sce.activation(out=stat[:, 7:8], in_=stat[:, 4:5],
                                        func=AF.Ln, bias=sb["epsc"][:],
                                        scale=1.0), "scalar")
                ctx.wait(sce, "scalar")
                sce.activation(out=stat[:, 5:6], in_=stat[:, 6:7],
                               func=AF.Exp, scale=-0.5)
                ctx.mark(sce.activation(out=stat[:, 7:8], in_=stat[:, 6:7],
                                        func=AF.Exp, scale=-0.5), "scalar")

            @block.vector
            def _(v, slot=slot, tc=tc):
                ctx.wait(v, "scalar")
                xb_free[slot] = ctx.mark(
                    v.tensor_scalar(out=xnb[slot][:], in0=xbuf[slot][:],
                                    scalar1=stat[:, 2:3], scalar2=stat[:, 5:6],
                                    op0=ALU.subtract, op1=ALU.mult), "vector")

            if "s1a" in PUMP_PHASES:
                ctx.exp.pump()
                ctx.exp.pump()
            for half in range(2):
                @block.tensor
                def _(t, slot=slot, half=half):
                    ctx.wait(t, "vector", xb_free[slot])
                    ctx.bank_wait(t, "PB")
                    for j in range(8):
                        c = half * 8 + j
                        inst = t.transpose(out=ctx.PB[:, j * 128:(j + 1) * 128],
                                           in_=xnb[slot][:, c * 128:(c + 1) * 128],
                                           identity=sb["ident_sb"][:])
                    xnb_free[slot] = ctx.mark(inst, "tensor")

                @block.vector
                def _(v, tc_rel=tc_rel, half=half):
                    ctx.wait(v, "tensor")
                    ctx.mark(v.tensor_copy(
                        out=xnT[:, half * 8:(half + 1) * 8,
                                tc_rel * 128:(tc_rel + 1) * 128],
                        in_=ctx.PB[:].rearrange("p (c f) -> p c f", c=8)),
                        "vector")
                    ctx.bank_read("PB", "vector")



        xnT_ready = ctx.n["vector"]
        if "a" in SW1:
            @block.sync
            def _(s, sc=sc):
                ctx.wait(s, "vector")
                ctx.mark(s.dma_start(out=d["dxn"][sc], in_=xnT[:]), "sync", 16)

        rope_prev = [max(xnT_ready, prev_lnstats[0])]
        for grp in range(16) if "b" in SW1 else []:
            is_q = grp < 8
            j = grp % 8
            ce = j if is_q else 16 + j
            co = ce + 8
            dstT = qT if is_q else kT
            ddst = d["dq"] if is_q else d["dk"]
            for par, oc in ((0, ce), (1, co)):
                slot = par

                @block.gpsimd
                def _(g, oc=oc, slot=slot, grp=grp):
                    if wq_free[slot]:
                        ctx.wait(g, "tensor", wq_free[slot])
                    ctx.mark(g.dma_start(out=wqb[slot][:],
                                         in_=d["wqk"][oc]), "gpsimd", 16)

                @block.tensor
                def _(t, slot=slot, par=par, oc=oc):
                    ctx.wait(t, "gpsimd")
                    ctx.wait(t, "vector", xnT_ready)
                    ctx.bank_wait(t, par)
                    for kc in range(NKC):
                        t.matmul(out=ctx.PF[par][:],
                                 lhsT=wqb[slot][:, kc, :],
                                 rhs=xnT[:, kc, :],
                                 start=(kc == 0), stop=False)
                    for q4 in range(4):
                        inst = t.matmul(
                            out=ctx.PF[par][:, q4 * 128:(q4 + 1) * 128],
                            lhsT=sb["bqkr"][0:1, oc * 128:(oc + 1) * 128],
                            rhs=sb["ones_sb"][:],
                            start=False, stop=True)
                    wq_free[slot] = ctx.mark(inst, "tensor")

            @block.vector
            def _(v, j=j, dstT=dstT, sc=sc):
                ctx.wait(v, "tensor")
                ctx.wait(v, "gpsimd")
                ctx.wait(v, "scalar")
                cs = cos_sb[:]
                sn = sin_sb[:]
                v.tensor_mul(scrA[:], ctx.PF[0][:], cs)
                v.tensor_mul(scrB[:], ctx.PF[1][:], sn)
                v.tensor_sub(dstT[:, j, :], scrA[:], scrB[:])
                v.tensor_mul(scrA[:], ctx.PF[0][:], sn)
                v.tensor_mul(scrB[:], ctx.PF[1][:], cs)
                ctx.mark(v.tensor_add(dstT[:, j + 8, :], scrA[:], scrB[:]),
                         "vector")
                rope_prev[0] = ctx.n["vector"]
                ctx.bank_read(0, "vector")
                ctx.bank_read(1, "vector")

            @block.sync
            def _(s, j=j, dstT=dstT, ddst=ddst, sc=sc):
                ctx.wait(s, "vector")
                ctx.mark(s.dma_start(out=ddst[sc, :, j, :],
                                     in_=dstT[:, j, :]), "sync", 16)
                ctx.mark(s.dma_start(out=ddst[sc, :, j + 8, :],
                                     in_=dstT[:, j + 8, :]), "sync", 16)

            if "s1b" in PUMP_PHASES:
                ctx.exp.pump()
                ctx.exp.pump()

        for vb in range(8) if "v" in SW1 else []:
            slot = vb % 2

            @block.gpsimd
            def _(g, vb=vb, slot=slot, sc=sc):
                if wv_free[slot]:
                    ctx.wait(g, "tensor", wv_free[slot])
                ctx.mark(g.dma_start(out=wvb[slot][:],
                                     in_=d["wv"][vb]), "gpsimd", 16)

            for tc_rel in range(4):
                bank = 2 + (tc_rel % 2)

                @block.tensor
                def _(t, vb=vb, slot=slot, tc_rel=tc_rel, bank=bank):
                    if tc_rel == 0:
                        ctx.wait(t, "gpsimd")
                    ctx.wait(t, "vector", xnT_ready)
                    ctx.bank_wait(t, bank)
                    for kc in range(NKC):
                        t.matmul(out=ctx.PF[bank][:, 0:256],
                                 lhsT=xnT[:, kc, tc_rel * 128:(tc_rel + 1) * 128],
                                 rhs=wvb[slot][:, kc, :],
                                 start=(kc == 0), stop=False)
                    inst = t.matmul(out=ctx.PF[bank][:, 0:256], lhsT=sb["ones_sb"][:],
                                    rhs=sb["kbias"][0:1, vb * 256:(vb + 1) * 256],
                                    start=False, stop=True)
                    w = ctx.mark(inst, "tensor")
                    if tc_rel == 3:
                        wv_free[slot] = w

                @block.vector
                def _(v, vb=vb, tc_rel=tc_rel, bank=bank):
                    ctx.wait(v, "tensor")
                    ctx.mark(v.tensor_copy(
                        out=vsc[:, tc_rel, vb * 256:(vb + 1) * 256],
                        in_=ctx.PF[bank][:, 0:256]), "vector")
                    ctx.bank_read(bank, "vector")

        @block.sync
        def _(s, sc=sc):
            ctx.wait(s, "vector")
            for tc_rel in range(4):
                ctx.mark(s.dma_start(out=d["dv"][sc * 4 + tc_rel, :, :],
                                     in_=vsc[:, tc_rel, :]), "sync", 16)

        if "g" not in SW1:
            continue
        @block.tensor
        def _(t):
            ctx.wait(t, "vector", xnT_ready)
            ctx.wait(t, "gpsimd")
            ctx.bank_wait(t, 4)
            for kc in range(NKC):
                inst = t.matmul(out=ctx.PF[4][0:32, :],
                                lhsT=sb["wgate_sb"][:, kc, :],
                                rhs=xnT[:, kc, :],
                                start=(kc == 0), stop=(kc == NKC - 1))
            ctx.mark(inst, "tensor")

        @block.scalar
        def _(sce, sc=sc):
            ctx.wait(sce, "tensor")
            ctx.mark(sce.activation(out=sb["gz"][:, sc * 512:(sc + 1) * 512],
                                    in_=ctx.PF[4][0:32, :], func=AF.Identity,
                                    bias=sb["bgate_sb"][:], scale=1.0), "scalar")
            ctx.bank_read(4, "scalar")

        qk_done = ctx.n["vector"]
        if "psc" not in SW1:
            continue
        @block.tensor
        def _(t):
            ctx.wait(t, "vector", qk_done)
            ctx.wait(t, "gpsimd")
            ctx.bank_wait(t, 5)
            for kc in range(NKC):
                inst = t.matmul(out=ctx.PF[5][:], lhsT=pq_sb[:, kc, :],
                                rhs=kT[:, kc, :],
                                start=(kc == 0), stop=(kc == NKC - 1))
            ctx.mark(inst, "tensor")

        @block.vector
        def _(v):
            ctx.wait(v, "tensor")
            ctx.wait(v, "scalar")
            s3 = ctx.PF[5][:].rearrange("p (s f) -> p s f", s=8)
            ctx.vguard(v, v.reduce_max(red[:, 0:8], s3, axis=mybir.AxisListType.X))
            ctx.mark(v.tensor_sub(scrA[:].rearrange("p (s f) -> p s f", s=8), s3,
                                  red[:, 0:8].unsqueeze(2).broadcast_to(
                                      [128, 8, 64])), "vector")
            ctx.bank_read(5, "vector")
        pool_sub = ctx.n["vector"]

        @block.scalar
        def _(sce):
            ctx.wait(sce, "vector", pool_sub)
            ctx.mark(sce.activation(out=scrB[:], in_=scrA[:], func=AF.Exp,
                                    scale=SCALE), "scalar")

        @block.vector
        def _(v):
            ctx.wait(v, "scalar")
            e3 = scrB[:].rearrange("p (s f) -> p s f", s=8)
            ctx.vguard(v, v.reduce_sum(red[:, 8:16], e3, axis=mybir.AxisListType.X))
            ctx.vguard(v, v.reciprocal(red[:, 16:24], red[:, 8:16]))
            ctx.mark(v.tensor_mul(pa_sb[:].rearrange("p (s f) -> p s f", s=8),
                                  e3, red[:, 16:24].unsqueeze(2).broadcast_to(
                                      [128, 8, 64])), "vector")
        pa_mark = ctx.n["vector"]

        @block.tensor
        def _(t):
            ctx.wait(t, "vector", pa_mark)
            ctx.bank_wait(t, "PB")
            for tc_rel in range(4):
                inst = t.transpose(out=ctx.PB[:, tc_rel * 128:(tc_rel + 1) * 128],
                                   in_=pa_sb[:, tc_rel * 128:(tc_rel + 1) * 128],
                                   identity=sb["ident_sb"][:])
            ctx.mark(inst, "tensor")

        @block.vector
        def _(v):
            ctx.wait(v, "tensor")
            ctx.mark(v.tensor_copy(out=paT_sb[:],
                                   in_=ctx.PB[:, 0:512].rearrange(
                                       "p (c f) -> p c f", c=4)), "vector")
            ctx.bank_read("PB", "vector")
        paT_mark = ctx.n["vector"]

        nhp = 16 if "loc" in SW1 else (1 if "loc1" in SW1 else 0)
        for hp in range(nhp):
            for hh in range(2):
                h = 2 * hp + hh
                bank = hh

                @block.tensor
                def _(t, h=h, bank=bank):
                    ctx.wait(t, "vector", qk_done)
                    ctx.bank_wait(t, bank)
                    for nrel in range(8):
                        for par in range(2):
                            ch = par * 8 + h // 4
                            rb = (h % 4) * 32
                            inst = t.matmul(
                                out=ctx.PF[bank][(nrel % 2) * 64:(nrel % 2) * 64 + 64,
                                                 (nrel // 2) * 64:(nrel // 2) * 64 + 64],
                                lhsT=qT[rb:rb + 32, ch, nrel * 64:(nrel + 1) * 64],
                                rhs=kT[rb:rb + 32, ch, nrel * 64:(nrel + 1) * 64],
                                start=(par == 0), stop=(par == 1),
                                tile_position=(rb, (nrel % 2) * 64))
                    ctx.mark(inst, "tensor")

                @block.vector
                def _(v, bank=bank):
                    ctx.wait(v, "tensor")
                    ctx.wait(v, "scalar")
                    s3 = ctx.PF[bank][:, 0:256].rearrange("p (s f) -> p s f", s=4)
                    ctx.vguard(v, v.reduce_max(red[:, 0:4], s3,
                                               axis=mybir.AxisListType.X))
                    ctx.mark(v.tensor_sub(
                        scrA[:, 0:256].rearrange("p (s f) -> p s f", s=4), s3,
                        red[:, 0:4].unsqueeze(2).broadcast_to([128, 4, 64])),
                        "vector")
                    ctx.bank_read(bank, "vector")
                lsub_mark = ctx.n["vector"]

                @block.scalar
                def _(sce, lsub_mark=lsub_mark):
                    ctx.wait(sce, "vector", lsub_mark)
                    ctx.mark(sce.activation(out=scrB[:, 0:256], in_=scrA[:, 0:256],
                                            func=AF.Exp, scale=SCALE), "scalar")

                @block.vector
                def _(v):
                    ctx.wait(v, "scalar")
                    ctx.wait(v, "tensor")
                    e3 = scrB[:, 0:256].rearrange("p (s f) -> p s f", s=4)
                    v.tensor_mul(scrA[:, 0:256].rearrange("p (s f) -> p s f", s=4),
                                 e3, sb["mloc"][:].unsqueeze(1).broadcast_to(
                                     [128, 4, 64]))
                    m3 = scrA[:, 0:256].rearrange("p (s f) -> p s f", s=4)
                    ctx.vguard(v, v.reduce_sum(red[:, 4:8], m3,
                                               axis=mybir.AxisListType.X))
                    ctx.vguard(v, v.reciprocal(red[:, 16:20], red[:, 4:8]))
                    ctx.mark(v.tensor_mul(A_sb[:], m3,
                                          red[:, 16:20].unsqueeze(2).broadcast_to(
                                              [128, 4, 64])), "vector")
                A_mark = ctx.n["vector"]

                @block.tensor
                def _(t, A_mark=A_mark):
                    ctx.wait(t, "vector", A_mark)
                    ctx.bank_wait(t, "PB")
                    for g_ in range(4):
                        t.transpose(out=ctx.PB[0:64, g_ * 128:(g_ + 1) * 128],
                                    in_=A_sb[:, g_, :],
                                    identity=sb["ident_sb"][:])
                        inst = t.transpose(
                            out=ctx.PB[64:128, g_ * 128:(g_ + 1) * 128],
                            in_=A_sb[:, g_, :], identity=sb["ident_sb"][:],
                            tile_position=(0, 64))
                    ctx.mark(inst, "tensor")

                @block.vector
                def _(v):
                    ctx.wait(v, "tensor")
                    ctx.mark(v.tensor_copy(out=AT_sb[:],
                                           in_=ctx.PB[:, 0:512].rearrange(
                                               "p (c f) -> p c f", c=4)), "vector")
                    ctx.bank_read("PB", "vector")
                AT_mark = ctx.n["vector"]

                @block.tensor
                def _(t, h=h, hh=hh, AT_mark=AT_mark):
                    ctx.wait(t, "vector", AT_mark)
                    if hh == 0:
                        ctx.bank_wait(t, 2)
                        ctx.bank_wait(t, 6)
                    for nrel in range(8):
                        rb = (nrel % 2) * 64
                        bnk = 2 if nrel % 2 == 0 else 6
                        inst = t.matmul(
                            out=ctx.PF[bnk][hh * 64:hh * 64 + 64,
                                            (nrel // 2) * 64:(nrel // 2) * 64 + 64],
                            lhsT=vsc[rb:rb + 64, nrel // 2, h * 64:(h + 1) * 64],
                            rhs=AT_sb[rb:rb + 64, nrel // 2, rb:rb + 64],
                            start=True, stop=True,
                            tile_position=(rb, hh * 64))
                    ctx.mark(inst, "tensor")

            @block.vector
            def _(v):
                ctx.wait(v, "tensor")
                ctx.wait(v, "sync")
                oview = olst[:].rearrange("p (n2 two f) -> p n2 two f", two=2, f=64)
                v.tensor_copy(out=oview[:, :, 0, :],
                              in_=ctx.PF[2][:, 0:256].rearrange(
                                  "p (n f) -> p n f", f=64))
                ctx.mark(v.tensor_copy(out=oview[:, :, 1, :],
                                       in_=ctx.PF[6][:, 0:256].rearrange(
                                           "p (n f) -> p n f", f=64)), "vector")
                ctx.bank_read(2, "vector")
                ctx.bank_read(6, "vector")

            @block.sync
            def _(s, hp=hp, sc=sc):
                ctx.wait(s, "vector")
                ctx.mark(s.dma_start(out=d["dol"][sc, hp], in_=olst[:]),
                         "sync", 16)

            if "s1c" in PUMP_PHASES:
                ctx.exp.pump()
                ctx.exp.pump()
                ctx.exp.pump()

        if "lm2" not in SW1:
            continue
        @block.tensor
        def _(t):
            ctx.wait(t, "vector", paT_mark)
            ctx.bank_wait(t, 3)
            ctx.bank_wait(t, 5)
            for h in range(H):
                for nrel in range(8):
                    rb = (nrel % 2) * 64
                    bnk = 3 if nrel % 2 == 0 else 5
                    inst = t.matmul(
                        out=ctx.PF[bnk][(h % 2) * 64:(h % 2) * 64 + 64,
                                        (h // 2) * 16 + (nrel // 2) * 4:
                                        (h // 2) * 16 + (nrel // 2) * 4 + 4],
                        lhsT=vsc[rb:rb + 64, nrel // 2, h * 64:(h + 1) * 64],
                        rhs=paT_sb[rb:rb + 64, nrel // 2, 4 * h:4 * h + 4],
                        start=True, stop=True,
                        tile_position=(rb, (h % 2) * 64))
            ctx.mark(inst, "tensor")

        @block.vector
        def _(v, sc=sc):
            ctx.wait(v, "tensor")
            dview = sb["lmfT"][:, :, sc * 32:(sc + 1) * 32].rearrange(
                "p c (n2 two m) -> p c n2 two m", two=2, m=4)
            v.tensor_copy(out=dview[:, :, :, 0, :],
                          in_=ctx.PF[3][:, 0:256].rearrange(
                              "p (c n2 m) -> p c n2 m", n2=4, m=4))
            ctx.mark(v.tensor_copy(out=dview[:, :, :, 1, :],
                                   in_=ctx.PF[5][:, 0:256].rearrange(
                                       "p (c n2 m) -> p c n2 m", n2=4, m=4)),
                     "vector")
            ctx.bank_read(3, "vector")
            ctx.bank_read(5, "vector")

    barrier(ctx)
    if "grep" not in SW1:
        return

    for qq in range(4):
        @block.scalar
        def _(sce, qq=qq):
            ctx.wait(sce, "vector")
            ctx.mark(sce.activation(out=scrA[0:32, :],
                                    in_=sb["gz"][:, qq * 512:(qq + 1) * 512],
                                    func=AF.Tanh, scale=0.5), "scalar")

        @block.vector
        def _(v, qq=qq):
            ctx.wait(v, "scalar")
            ctx.mark(v.tensor_scalar(out=sb["gz"][:, qq * 512:(qq + 1) * 512],
                                     in0=scrA[0:32, :], scalar1=0.5,
                                     scalar2=0.5, op0=ALU.mult, op1=ALU.add),
                     "vector")

    @block.sync
    def _(s):
        ctx.wait(s, "vector")
        ctx.mark(s.dma_start(out=d["dgrep"][:], in_=sb["gz"][:]), "sync", 16)


def _wproj_to_chunks(ctx, wname, bias_sb, rhs_sb, out_sb, resid_sb=None,
                     tag=""):
    block, d, sb = ctx.block, ctx.d, ctx.sb
    wqb = [sb["wqb0"], sb["wqb1"]]
    wl_free = ctx.wqb_free
    for oc in range(NKC):
        slot = oc % 2

        @block.gpsimd
        def _(g, oc=oc, slot=slot):
            if wl_free[slot]:
                ctx.wait(g, "tensor", wl_free[slot])
            ctx.mark(g.dma_start(out=wqb[slot][:], in_=d[wname][oc]),
                     "gpsimd", 16)

        bank = oc % 2

        @block.tensor
        def _(t, slot=slot, bank=bank, oc=oc):
            ctx.wait(t, "gpsimd")
            if oc == 0:
                ctx.wait(t, "vector")
                ctx.wait(t, "scalar")
            ctx.bank_wait(t, bank)
            for kc in range(NKC):
                inst = t.matmul(out=ctx.PF[bank][:, 0:NL],
                                lhsT=wqb[slot][:, kc, :],
                                rhs=rhs_sb[:, kc, :],
                                start=(kc == 0), stop=(kc == NKC - 1))
            wl_free[slot] = ctx.mark(inst, "tensor")

        if resid_sb is None:
            @block.scalar
            def _(sce, oc=oc, bank=bank):
                ctx.wait(sce, "tensor")
                ctx.mark(sce.activation(out=out_sb[:, oc, :],
                                        in_=ctx.PF[bank][:, 0:NL],
                                        func=AF.Identity,
                                        bias=bias_sb[:, oc:oc + 1], scale=1.0),
                         "scalar")
                ctx.bank_read(bank, "scalar")
        else:
            @block.vector
            def _(v, oc=oc, bank=bank):
                ctx.wait(v, "tensor")
                ctx.mark(v.scalar_tensor_tensor(
                    out=out_sb[:, oc, :], in0=ctx.PF[bank][:, 0:NL],
                    scalar=bias_sb[:, oc:oc + 1], in1=resid_sb[:, oc, :],
                    op0=ALU.add, op1=ALU.add), "vector")
                ctx.bank_read(bank, "vector")


def _vproj_tok(ctx, wname, kbias_off, lhsT_sb, out_sb):
    block, d, sb = ctx.block, ctx.d, ctx.sb
    wvb = [sb["wvb0"], sb["wvb1"]]
    wv_free = ctx.wvb_free
    for vb in range(8):
        slot = vb % 2
        bank = 2 + vb % 2

        @block.gpsimd
        def _(g, vb=vb, slot=slot):
            if wv_free[slot]:
                ctx.wait(g, "tensor", wv_free[slot])
            ctx.mark(g.dma_start(out=wvb[slot][:], in_=d[wname][vb]),
                     "gpsimd", 16)

        @block.tensor
        def _(t, vb=vb, slot=slot, bank=bank):
            ctx.wait(t, "gpsimd")
            ctx.wait(t, "vector")
            ctx.bank_wait(t, bank)
            for kc in range(NKC):
                t.matmul(out=ctx.PF[bank][:, 0:256], lhsT=lhsT_sb[:, kc, :],
                         rhs=wvb[slot][:, kc, :],
                         start=(kc == 0), stop=False)
            inst = t.matmul(out=ctx.PF[bank][:, 0:256], lhsT=sb["ones_sb"][:],
                            rhs=sb["kbias"][0:1, kbias_off + vb * 256:
                                            kbias_off + (vb + 1) * 256],
                            start=False, stop=True)
            wv_free[slot] = ctx.mark(inst, "tensor")

        @block.vector
        def _(v, vb=vb, bank=bank):
            ctx.wait(v, "tensor")
            ctx.mark(v.tensor_copy(out=out_sb[:, vb * 256:(vb + 1) * 256],
                                   in_=ctx.PF[bank][:, 0:256]), "vector")
            ctx.bank_read(bank, "vector")


def p_landmark(ctx):
    block, d, sb = ctx.block, ctx.d, ctx.sb
    scrA, scrB = sb["scrA"], sb["scrB"]
    red = sb["red"]
    barrier(ctx)
    _wproj_to_chunks(ctx, "wlq", sb["b_blq"], sb["lmfT"], sb["lqT"])
    _wproj_to_chunks(ctx, "wlk", sb["b_blk"], sb["lmfT"], sb["lkT"])
    _vproj_tok(ctx, "wlv", D, sb["lmfT"], sb["lvtok"])
    la_sb = ctx.alloc("la_sb", [128, 128], BF16)
    laT_sb = ctx.alloc("laT_sb", [128, 128], BF16)

    for h in range(H):
        hb = (h % 2) * 64
        bank = h % 2

        @block.tensor
        def _(t, h=h, hb=hb, bank=bank):
            ctx.bank_wait(t, bank)
            ctx.mark(t.matmul(out=ctx.PF[bank][:, 0:128],
                              lhsT=sb["lqT"][hb:hb + 64, h // 2, :],
                              rhs=sb["lkT"][hb:hb + 64, h // 2, :],
                              start=True, stop=True,
                              tile_position=(hb, 0)), "tensor")

        @block.vector
        def _(v, bank=bank):
            ctx.wait(v, "tensor")
            ctx.wait(v, "scalar")
            ctx.vguard(v, v.reduce_max(red[:, 0:1], ctx.PF[bank][:, 0:128],
                                       axis=mybir.AxisListType.X))
            v.tensor_scalar_mul(red[:, 1:2], red[:, 0:1], -SCALE)
            ctx.mark(v.tensor_copy(out=red[:, 30:31], in_=red[:, 0:1]),
                     "vector")
        mneg_mark = ctx.n["vector"]

        @block.scalar
        def _(sce, bank=bank, mneg_mark=mneg_mark):
            ctx.wait(sce, "vector", mneg_mark)
            ctx.mark(sce.activation(out=scrA[:, 0:128], in_=ctx.PF[bank][:, 0:128],
                                    func=AF.Exp, scale=SCALE,
                                    bias=red[:, 1:2]), "scalar")
            ctx.bank_read(bank, "scalar")

        @block.vector
        def _(v):
            ctx.wait(v, "scalar")
            v.tensor_mul(scrB[:, 0:128], scrA[:, 0:128], sb["mlm"][:])
            ctx.vguard(v, v.reduce_sum(red[:, 2:3], scrB[:, 0:128],
                                       axis=mybir.AxisListType.X))
            ctx.vguard(v, v.reciprocal(red[:, 3:4], red[:, 2:3]))
            ctx.mark(v.tensor_scalar_mul(la_sb[:], scrB[:, 0:128],
                                         red[:, 3:4]), "vector")
        la_mark = ctx.n["vector"]

        @block.tensor
        def _(t, la_mark=la_mark):
            ctx.wait(t, "vector", la_mark)
            ctx.bank_wait(t, "PB")
            ctx.mark(t.transpose(out=ctx.PB[:, 0:128], in_=la_sb[:],
                                 identity=sb["ident_sb"][:]), "tensor")

        @block.vector
        def _(v):
            ctx.wait(v, "tensor")
            ctx.mark(v.tensor_copy(out=laT_sb[:], in_=ctx.PB[:, 0:128]), "vector")
            ctx.bank_read("PB", "vector")
        laT_mark = ctx.n["vector"]

        @block.tensor
        def _(t, h=h, hb=hb, laT_mark=laT_mark):
            ctx.wait(t, "vector", laT_mark)
            if hb == 0:
                ctx.bank_wait(t, 2)
            ctx.mark(t.matmul(out=ctx.PF[2][hb:hb + 64, 0:128],
                              lhsT=sb["lvtok"][:, h * 64:(h + 1) * 64],
                              rhs=laT_sb[:], start=True, stop=True,
                              tile_position=(0, hb)), "tensor")

        if h % 2 == 1:
            @block.vector
            def _(v, h=h):
                ctx.wait(v, "tensor")
                ctx.mark(v.tensor_copy(out=sb["loT"][:, h // 2, :],
                                       in_=ctx.PF[2][:, 0:128]), "vector")
                ctx.bank_read(2, "vector")
        if "lm" in PUMP_PHASES:
            ctx.exp.pump()
            ctx.exp.pump()
            ctx.exp.pump()

    _wproj_to_chunks(ctx, "wlmout", sb["b_blmout"], sb["loT"], sb["lmf2T"],
                     resid_sb=sb["lmfT"])
    _wproj_to_chunks(ctx, "wl2q", sb["b_bl2q"], sb["lmf2T"], sb["l2qT"])
    if DEBUG:
        @block.sync
        def _(s):
            ctx.wait(s, "vector")
            ctx.wait(s, "scalar")
            ctx.mark(s.dma_start(out=d["dbg_lmf"][:], in_=sb["lmfT"][:]), "sync", 16)
            ctx.mark(s.dma_start(out=d["dbg_lmf2"][:], in_=sb["lmf2T"][:]), "sync", 16)
            ctx.mark(s.dma_start(out=d["dbg_lqt"][:], in_=sb["lqT"][:]), "sync", 16)
            ctx.mark(s.dma_start(out=d["dbg_lkt"][:], in_=sb["lkT"][:]), "sync", 16)
            ctx.mark(s.dma_start(out=d["dbg_lvt"][:], in_=sb["lvtok"][:]), "sync", 16)
            ctx.mark(s.dma_start(out=d["dbg_lot"][:], in_=sb["loT"][:]), "sync", 16)


def p_sweep2a(ctx):
    block, d, sb = ctx.block, ctx.d, ctx.sb
    kT = ctx.alloc("kT2", [128, NKC, 512], BF16)
    vsc = ctx.alloc("vsc2", [128, 4, D], BF16)
    scrA, scrB, red = sb["scrA"], sb["scrB"], sb["red"]
    raA = ctx.alloc("raA", [128, 512], BF16)
    raT = ctx.alloc("raT", [128, 4, 128], BF16)
    for sc in range(NSC):
        barrier(ctx)

        @block.sync
        def _(s, sc=sc):
            ctx.mark(s.dma_start(out=kT[:], in_=d["dk"][sc]), "sync", 16)
            for tc_rel in range(4):
                ctx.mark(s.dma_start(out=vsc[:, tc_rel, :],
                                     in_=d["dv"][sc * 4 + tc_rel, :, :]),
                         "sync", 16)

        for hg in range(8):
            bank = hg % 2

            @block.tensor
            def _(t, hg=hg, sc=sc, bank=bank):
                ctx.wait(t, "sync")
                ctx.bank_wait(t, bank)
                for hsub in range(4):
                    h = 4 * hg + hsub
                    for par in range(2):
                        ch = par * 8 + h // 4
                        rb = (h % 4) * 32
                        inst = t.matmul(
                            out=ctx.PF[bank][hsub * 32:(hsub + 1) * 32, :],
                            lhsT=sb["l2qT"][rb:rb + 32, ch,
                                            sc * 32:(sc + 1) * 32],
                            rhs=kT[rb:rb + 32, ch, :],
                            start=(par == 0), stop=(par == 1),
                            tile_position=(rb, hsub * 32))
                ctx.mark(inst, "tensor")

            @block.vector
            def _(v, bank=bank):
                ctx.wait(v, "tensor")
                ctx.wait(v, "scalar")
                ctx.vguard(v, v.reduce_max(red[:, 0:1], ctx.PF[bank][:],
                                           axis=mybir.AxisListType.X))
                v.tensor_scalar_mul(red[:, 1:2], red[:, 0:1], -SCALE)
                ctx.mark(v.tensor_copy(out=red[:, 30:31], in_=red[:, 0:1]),
                         "vector")
            mneg2_mark = ctx.n["vector"]

            @block.scalar
            def _(sce, bank=bank, mneg2_mark=mneg2_mark):
                ctx.wait(sce, "vector", mneg2_mark)
                ctx.mark(sce.activation(out=scrA[:], in_=ctx.PF[bank][:],
                                        func=AF.Exp, scale=SCALE,
                                        bias=red[:, 1:2]), "scalar")
                ctx.bank_read(bank, "scalar")

            @block.vector
            def _(v):
                ctx.wait(v, "scalar")
                v.tensor_mul(scrB[:], scrA[:], sb["mra"][:])
                ctx.vguard(v, v.reduce_sum(red[:, 2:3], scrB[:],
                                           axis=mybir.AxisListType.X))
                ctx.vguard(v, v.reciprocal(red[:, 3:4], red[:, 2:3]))
                ctx.mark(v.tensor_scalar_mul(raA[:], scrB[:], red[:, 3:4]),
                         "vector")
            raA_mark = ctx.n["vector"]

            @block.tensor
            def _(t, raA_mark=raA_mark):
                ctx.wait(t, "vector", raA_mark)
                ctx.bank_wait(t, "PB")
                for pp in range(4):
                    inst = t.transpose(out=ctx.PB[:, pp * 128:(pp + 1) * 128],
                                       in_=raA[:, pp * 128:(pp + 1) * 128],
                                       identity=sb["ident_sb"][:])
                ctx.mark(inst, "tensor")

            @block.vector
            def _(v):
                ctx.wait(v, "tensor")
                ctx.mark(v.tensor_copy(out=raT[:],
                                       in_=ctx.PB[:, 0:512].rearrange(
                                           "p (c f) -> p c f", c=4)), "vector")
                ctx.bank_read("PB", "vector")
            raT_mark = ctx.n["vector"]

            @block.tensor
            def _(t, hg=hg, raT_mark=raT_mark):
                ctx.wait(t, "vector", raT_mark)
                if hg == 0:
                    ctx.bank_wait(t, 2)
                    ctx.bank_wait(t, 3)
                for hsub in range(4):
                    h = 4 * hg + hsub
                    for nrel in range(8):
                        rb = (nrel % 2) * 64
                        bnk = 2 if nrel % 2 == 0 else 3
                        inst = t.matmul(
                            out=ctx.PF[bnk][(h % 2) * 64:(h % 2) * 64 + 64,
                                            (h // 2) * 16 + (nrel // 2) * 4:
                                            (h // 2) * 16 + (nrel // 2) * 4 + 4],
                            lhsT=vsc[rb:rb + 64, nrel // 2, h * 64:(h + 1) * 64],
                            rhs=raT[rb:rb + 64, nrel // 2,
                                    hsub * 32 + nrel * 4:hsub * 32 + nrel * 4 + 4],
                            start=True, stop=True,
                            tile_position=(rb, (h % 2) * 64))
                ctx.mark(inst, "tensor")

            if "s2a" in PUMP_PHASES:
                ctx.exp.pump()
                ctx.exp.pump()
                ctx.exp.pump()
                ctx.exp.pump()

        @block.vector
        def _(v, sc=sc):
            ctx.wait(v, "tensor")
            dview = sb["roT"][:, :, sc * 32:(sc + 1) * 32].rearrange(
                "p c (n2 two m) -> p c n2 two m", two=2, m=4)
            v.tensor_copy(out=dview[:, :, :, 0, :],
                          in_=ctx.PF[2][:, 0:256].rearrange(
                              "p (c n2 m) -> p c n2 m", n2=4, m=4))
            ctx.mark(v.tensor_copy(out=dview[:, :, :, 1, :],
                                   in_=ctx.PF[3][:, 0:256].rearrange(
                                       "p (c n2 m) -> p c n2 m", n2=4, m=4)),
                     "vector")
            ctx.bank_read(2, "vector")
            ctx.bank_read(3, "vector")


def p_lmtail(ctx):
    block, d, sb = ctx.block, ctx.d, ctx.sb
    barrier(ctx)
    _wproj_to_chunks(ctx, "wl2out", sb["b_bl2out"], sb["roT"], sb["lmf3T"],
                     resid_sb=sb["lmf2T"])
    _wproj_to_chunks(ctx, "wgk", sb["b_bgk"], sb["lmf3T"], sb["gkT"])
    _vproj_tok(ctx, "wgv", 2 * D, sb["lmf3T"], sb["gv_sb"])
    if DEBUG:
        @block.sync
        def _(s):
            ctx.wait(s, "vector")
            ctx.wait(s, "scalar")
            ctx.mark(s.dma_start(out=d["dbg_lmf3"][:], in_=sb["lmf3T"][:]), "sync", 16)
            ctx.mark(s.dma_start(out=d["dbg_gk"][:], in_=sb["gkT"][:]), "sync", 16)
            ctx.mark(s.dma_start(out=d["dbg_gv"][:], in_=sb["gv_sb"][:]), "sync", 16)


def p_sweep2b(ctx):
    block, d, sb = ctx.block, ctx.d, ctx.sb
    qT = ctx.alloc("qT2", [128, NKC, 512], BF16)
    scrA, scrB, red = sb["scrA"], sb["scrB"], sb["red"]
    mg = ctx.alloc("mg", [128, NTC, 128], BF16)
    agb = ctx.alloc("agb", [128, 512], BF16)
    agT = ctx.alloc("agT", [128, 4, 128], BF16)

    @block.sync
    def _(s):
        ctx.mark(s.dma_start(out=mg[:], in_=d["mask_g"][:].rearrange(
            "(c p) l -> p c l", p=128)), "sync", 16)
    mg_mark = ctx.n["sync"]
    olT = [ctx.alloc(f"olT{i}", [128, 512], BF16) for i in range(2)]
    grep = [ctx.alloc(f"grep{i}", [128, 512], BF16) for i in range(2)]
    gated = ctx.alloc("gated", [128, NKC, 512], BF16)
    sstg = ctx.alloc("sstg", [128, 512], BF16)
    ol_used = [0, 0]

    for sc in range(NSC):
        barrier(ctx)

        @block.sync
        def _(s, sc=sc):
            ctx.mark(s.dma_start(out=qT[:], in_=d["dq"][sc]), "sync", 16)

        for hp in range(16):
            slot = hp % 2

            @block.sync
            def _(s, hp=hp, sc=sc, slot=slot):
                if hp >= 2:
                    ctx.wait(s, "vector", ol_used[slot])
                ctx.mark(s.dma_start(out=olT[slot][:],
                                     in_=d["dol"][sc, hp]), "sync", 16)
                gsrc = d["dgrep"][2 * hp:2 * hp + 2, sc * 512:(sc + 1) * 512]
                ctx.mark(s.dma_start(out=grep[slot][:],
                                     in_=gsrc.unsqueeze(1).broadcast_to(
                                         [2, 64, 512])), "sync", 16)

            for hh in range(2):
                h = 2 * hp + hh
                bank = hh

                @block.tensor
                def _(t, h=h, bank=bank):
                    ctx.wait(t, "sync")
                    ctx.bank_wait(t, bank)
                    for tc_rel in range(4):
                        for par in range(2):
                            ch = par * 8 + h // 4
                            rb = (h % 4) * 32
                            inst = t.matmul(
                                out=ctx.PF[bank][:, tc_rel * 128:(tc_rel + 1) * 128],
                                lhsT=qT[rb:rb + 32, ch,
                                        tc_rel * 128:(tc_rel + 1) * 128],
                                rhs=sb["gkT"][rb:rb + 32, ch, :],
                                start=(par == 0), stop=(par == 1),
                                tile_position=(rb, 0))
                    ctx.mark(inst, "tensor")

                @block.vector
                def _(v, bank=bank):
                    ctx.wait(v, "tensor")
                    ctx.wait(v, "scalar")
                    s3 = ctx.PF[bank][:].rearrange("p (s f) -> p s f", s=4)
                    ctx.vguard(v, v.reduce_max(red[:, 0:4], s3,
                                               axis=mybir.AxisListType.X))
                    v.tensor_scalar_mul(red[:, 4:8], red[:, 0:4], -SCALE)
                    ctx.mark(v.tensor_copy(out=red[:, 30:31], in_=red[:, 0:1]),
                             "vector")
                mneg3_mark = ctx.n["vector"]

                @block.scalar
                def _(sce, bank=bank, mneg3_mark=mneg3_mark):
                    ctx.wait(sce, "vector", mneg3_mark)
                    for tc_rel in range(4):
                        inst = sce.activation(
                            out=scrA[:, tc_rel * 128:(tc_rel + 1) * 128],
                            in_=ctx.PF[bank][:, tc_rel * 128:(tc_rel + 1) * 128],
                            func=AF.Exp, scale=SCALE,
                            bias=red[:, 4 + tc_rel:5 + tc_rel])
                    ctx.mark(inst, "scalar")
                    ctx.bank_read(bank, "scalar")

                @block.vector
                def _(v, sc=sc):
                    ctx.wait(v, "scalar")
                    ctx.wait(v, "sync", mg_mark)
                    e3 = scrA[:].rearrange("p (s f) -> p s f", s=4)
                    v.tensor_mul(scrB[:].rearrange("p (s f) -> p s f", s=4),
                                 e3, mg[:, sc * 4:(sc + 1) * 4, :])
                    m3 = scrB[:].rearrange("p (s f) -> p s f", s=4)
                    ctx.vguard(v, v.reduce_sum(red[:, 8:12], m3,
                                               axis=mybir.AxisListType.X))
                    ctx.vguard(v, v.tensor_scalar_max(red[:, 12:16],
                                                      red[:, 8:12], 1e-9))
                    ctx.vguard(v, v.reciprocal(red[:, 16:20], red[:, 12:16]))
                    ctx.mark(v.tensor_mul(agb[:].rearrange("p (s f) -> p s f", s=4),
                                          m3,
                                          red[:, 16:20].unsqueeze(2).broadcast_to(
                                              [128, 4, 128])), "vector")
                agb_mark = ctx.n["vector"]

                @block.tensor
                def _(t, agb_mark=agb_mark):
                    ctx.wait(t, "vector", agb_mark)
                    ctx.bank_wait(t, "PB")
                    for tc_rel in range(4):
                        inst = t.transpose(
                            out=ctx.PB[:, tc_rel * 128:(tc_rel + 1) * 128],
                            in_=agb[:, tc_rel * 128:(tc_rel + 1) * 128],
                            identity=sb["ident_sb"][:])
                    ctx.mark(inst, "tensor")

                @block.vector
                def _(v):
                    ctx.wait(v, "tensor")
                    ctx.mark(v.tensor_copy(out=agT[:],
                                           in_=ctx.PB[:, 0:512].rearrange(
                                               "p (c f) -> p c f", c=4)), "vector")
                    ctx.bank_read("PB", "vector")
                agT_mark = ctx.n["vector"]

                @block.tensor
                def _(t, h=h, hh=hh, agT_mark=agT_mark):
                    ctx.wait(t, "vector", agT_mark)
                    if hh == 0:
                        ctx.bank_wait(t, 2)
                    for tc_rel in range(4):
                        inst = t.matmul(
                            out=ctx.PF[2][hh * 64:hh * 64 + 64,
                                          tc_rel * 128:(tc_rel + 1) * 128],
                            lhsT=sb["gv_sb"][:, h * 64:(h + 1) * 64],
                            rhs=agT[:, tc_rel, :], start=True, stop=True,
                            tile_position=(0, hh * 64))
                    ctx.mark(inst, "tensor")

            if DEBUG and sc == 0 and hp == 0:
                @block.sync
                def _(s, slot=slot):
                    ctx.wait(s, "tensor")
                    ctx.wait(s, "vector")
                    ctx.mark(s.dma_start(out=d["dbg_sge"][:], in_=scrA[:]), "sync", 16)
                    ctx.mark(s.dma_start(out=d["dbg_scrB"][:], in_=scrB[:]), "sync", 16)
                    ctx.mark(s.dma_start(out=d["dbg_mg"][:], in_=mg[:, 0:4, :]), "sync", 16)
                    ctx.mark(s.dma_start(out=d["dbg_red"][:], in_=red[:]), "sync", 16)
                    ctx.mark(s.dma_start(out=d["dbg_agb"][:], in_=agb[:]), "sync", 16)
                    ctx.mark(s.dma_start(out=d["dbg_agt"][:],
                                         in_=agT[:].rearrange("p c f -> p (c f)")), "sync", 16)
                    ctx.mark(s.dma_start(out=d["dbg_olt"][:], in_=olT[slot][:]), "sync", 16)
                    ctx.mark(s.dma_start(out=d["dbg_grep"][:], in_=grep[slot][:]), "sync", 16)

            @block.vector
            def _(v, hp=hp, slot=slot):
                ctx.wait(v, "tensor")
                ctx.wait(v, "sync")
                v.tensor_sub(scrA[:], ctx.PF[2][:], olT[slot][:])
                v.tensor_mul(scrB[:], scrA[:], grep[slot][:])
                ol_used[slot] = ctx.mark(
                    v.tensor_add(gated[:, hp, :], scrB[:], olT[slot][:]),
                    "vector")
                ctx.bank_read(2, "vector")

            for _i in range(5):
                ctx.exp.pump()

        if DEBUG and sc == 0:
            @block.sync
            def _(s):
                ctx.wait(s, "vector")
                ctx.mark(s.dma_start(out=d["dbg_gated"][:], in_=gated[:]),
                         "sync", 16)

        gated_mark = ctx.n["vector"]
        wqb = [sb["wqb0"], sb["wqb1"]]
        wo_free = ctx.wqb_free
        for oc in range(NKC):
            slot = oc % 2
            bank = oc % 2

            @block.gpsimd
            def _(g, oc=oc, slot=slot):
                if wo_free[slot]:
                    ctx.wait(g, "tensor", wo_free[slot])
                ctx.mark(g.dma_start(out=wqb[slot][:], in_=d["wout"][oc]),
                         "gpsimd", 16)

            @block.tensor
            def _(t, slot=slot, bank=bank, oc=oc, gated_mark=gated_mark):
                ctx.wait(t, "gpsimd")
                ctx.bank_wait(t, bank)
                if oc < 2:
                    ctx.wait(t, "vector", gated_mark)
                for kc in range(NKC):
                    inst = t.matmul(out=ctx.PF[bank][:],
                                    lhsT=wqb[slot][:, kc, :],
                                    rhs=gated[:, kc, :],
                                    start=(kc == 0), stop=(kc == NKC - 1))
                wo_free[slot] = ctx.mark(inst, "tensor")

            @block.scalar
            def _(sce, oc=oc, bank=bank):
                ctx.wait(sce, "tensor")
                ctx.wait(sce, "sync")
                ctx.mark(sce.activation(out=sstg[:], in_=ctx.PF[bank][:],
                                        func=AF.Identity, scale=0.5,
                                        bias=sb["b_bout_h"][:, oc:oc + 1]),
                         "scalar")
                ctx.bank_read(bank, "scalar")

            @block.sync
            def _(s, oc=oc, sc=sc):
                ctx.wait(s, "scalar")
                ctx.mark(s.dma_start(out=d["ds"][sc, :, oc, :],
                                     in_=sstg[:]), "sync", 16)


def p_ffn(ctx):
    block, d, sb = ctx.block, ctx.d, ctx.sb
    a = ctx.alloc
    xnf = a("xnf", [128, NKC, 512], BF16)
    sTf = a("sTf", [128, NKC, 512], BF16)
    hT = a("hT", [128, NOB, 512], BF16)
    pT = sTf
    gbuf = [a(f"gbuf{i}", [128, NKC, 128], BF16) for i in range(2)]
    ubuf = [a(f"ubuf{i}", [128, NKC, 128], BF16) for i in range(2)]
    dbuf = [a(f"dbuf{i}", [128, NOB, 128], BF16) for i in range(2)]
    xbuf2 = [a("xbf0", [128, D], F32)] * 2
    ostg = [a("ostg0", [128, D], F32)] * 2
    scr2A = a("scr2A", [128, 512], F32)
    scr2B = a("scr2B", [128, 512], F32)
    scrp = [(sb["scrA"], sb["scrB"]), (scr2A, scr2B)]

    gu_free = [0, 0]
    dn_free = [0, 0]
    x2_free = [0, 0]
    os_free = [0, 0]
    for sc in range(NSC):
        barrier(ctx)

        @block.sync
        def _(s, sc=sc):
            ctx.mark(s.dma_start(out=xnf[:], in_=d["dxn"][sc]), "sync", 16)
            ctx.mark(s.dma_start(out=sTf[:], in_=d["ds"][sc]), "sync", 16)

        for oc in range(NOB):
            slot = oc % 2

            @block.sync
            def _(s, oc=oc, slot=slot, sc=sc):
                if oc >= 2 or sc > 0:
                    ctx.wait(s, "tensor", gu_free[slot])
                ctx.mark(s.dma_start(out=gbuf[slot][:], in_=d["wgT"][oc]),
                         "sync", 16)
                ctx.mark(s.dma_start(out=ubuf[slot][:], in_=d["wuT"][oc]),
                         "sync", 16)

            bg = 0 if oc % 2 == 0 else 4
            bu = 1 if oc % 2 == 0 else 5

            @block.tensor
            def _(t, oc=oc, slot=slot, bg=bg, bu=bu):
                ctx.wait(t, "sync")
                ctx.bank_wait(t, bg)
                ctx.bank_wait(t, bu)
                for kc in range(NKC):
                    t.matmul(out=ctx.PF[bg][:], lhsT=gbuf[slot][:, kc, :],
                             rhs=xnf[:, kc, :], start=(kc == 0),
                             stop=(kc == NKC - 1))
                for kc in range(NKC):
                    inst = t.matmul(out=ctx.PF[bu][:], lhsT=ubuf[slot][:, kc, :],
                                    rhs=xnf[:, kc, :], start=(kc == 0),
                                    stop=(kc == NKC - 1))
                gu_free[slot] = ctx.mark(inst, "tensor")

            sA, sB = scrp[oc % 2]

            @block.scalar
            def _(sce, bg=bg, sA=sA):
                ctx.wait(sce, "tensor")
                ctx.mark(sce.activation(out=sA[:], in_=ctx.PF[bg][:],
                                        func=AF.Tanh, scale=0.5), "scalar")

            @block.vector
            def _(v, oc=oc, bg=bg, bu=bu, sA=sA, sB=sB):
                ctx.wait(v, "scalar")
                v.scalar_tensor_tensor(out=sB[:], in0=sA[:], scalar=1.0,
                                       in1=ctx.PF[bg][:], op0=ALU.add,
                                       op1=ALU.mult)
                ctx.mark(v.tensor_mul(hT[:, oc, :], sB[:], ctx.PF[bu][:]),
                         "vector")
                ctx.bank_read(bg, "vector")
                ctx.bank_read(bu, "vector")

        for oc2 in range(NKC):
            slot = oc2 % 2
            bank = 2 + oc2 % 2

            @block.sync
            def _(s, oc2=oc2, slot=slot, sc=sc):
                if oc2 >= 2 or sc > 0:
                    ctx.wait(s, "tensor", dn_free[slot])
                ctx.mark(s.dma_start(out=dbuf[slot][:], in_=d["wdT"][oc2]),
                         "sync", 16)

            @block.tensor
            def _(t, oc2=oc2, slot=slot, bank=bank):
                ctx.wait(t, "sync")
                if oc2 == 0:
                    ctx.wait(t, "vector")
                ctx.bank_wait(t, bank)
                for ic in range(NOB):
                    inst = t.matmul(out=ctx.PF[bank][:],
                                    lhsT=dbuf[slot][:, ic, :],
                                    rhs=hT[:, ic, :],
                                    start=(ic == 0), stop=(ic == NOB - 1))
                dn_free[slot] = ctx.mark(inst, "tensor")

            sA, sB = scrp[oc2 % 2]

            @block.scalar
            def _(sce, bank=bank, sA=sA):
                ctx.wait(sce, "tensor")
                ctx.mark(sce.activation(out=sA[:], in_=ctx.PF[bank][:],
                                        func=AF.Tanh, scale=0.5), "scalar")

            @block.vector
            def _(v, oc2=oc2, bank=bank, sA=sA, sB=sB):
                ctx.wait(v, "scalar")
                ctx.wait(v, "sync")
                v.scalar_tensor_tensor(out=sB[:], in0=sA[:], scalar=1.0,
                                       in1=ctx.PF[bank][:], op0=ALU.add,
                                       op1=ALU.mult)
                ctx.mark(v.tensor_mul(pT[:, oc2, :], sB[:], sTf[:, oc2, :]),
                         "vector")
                ctx.bank_read(bank, "vector")

        for tc_rel in range(4):
            tc = sc * 4 + tc_rel
            slot = 0

            @block.sync
            def _(s, tc=tc, slot=slot):
                if tc >= 1:
                    ctx.wait(s, "vector", x2_free[slot])
                ctx.mark(s.dma_start(out=xbuf2[slot][:],
                                     in_=d["x"][tc * 128:(tc + 1) * 128, :]),
                         "sync", 16)

            for half in range(2):
                @block.tensor
                def _(t, tc_rel=tc_rel, half=half):
                    if half == 0 and tc_rel == 0:
                        ctx.wait(t, "vector")
                    ctx.bank_wait(t, "PB")
                    for j in range(8):
                        oc2 = half * 8 + j
                        inst = t.transpose(
                            out=ctx.PB[:, j * 128:(j + 1) * 128],
                            in_=pT[:, oc2, tc_rel * 128:(tc_rel + 1) * 128],
                            identity=sb["ident_sb"][:])
                    ctx.mark(inst, "tensor")

                @block.vector
                def _(v, half=half, slot=slot):
                    ctx.wait(v, "tensor")
                    if half == 0:
                        ctx.wait(v, "sync", os_free[slot])
                        ctx.wait(v, "sync")
                    inst = v.tensor_add(
                        ostg[slot][:, half * 1024:(half + 1) * 1024],
                        xbuf2[slot][:, half * 1024:(half + 1) * 1024],
                        ctx.PB[:, 0:1024])
                    w = ctx.mark(inst, "vector")
                    if half == 1:
                        x2_free[slot] = w
                    ctx.bank_read("PB", "vector")

            @block.sync
            def _(s, tc=tc, slot=slot):
                ctx.wait(s, "vector")
                os_free[slot] = ctx.mark(
                    s.dma_start(out=d["out"][tc * 128:(tc + 1) * 128, :],
                                in_=ostg[slot][:]), "sync", 16)


def emit(ctx):
    p_consts(ctx)
    ctx.push_scope()
    p_lm_scope_alloc(ctx)
    ctx.exp = Expander(ctx)
    ctx.push_scope()
    p_sweep1(ctx)
    barrier(ctx)
    ctx.pop_scope()
    ctx.push_scope()
    p_post1_alloc(ctx)
    p_landmark(ctx)
    ctx.push_scope()
    p_sweep2a(ctx)
    barrier(ctx)
    ctx.pop_scope()
    p_lmtail(ctx)
    ctx.push_scope()
    p_sweep2b(ctx)
    ctx.exp.finish()
    ctx.tr.finish()
    barrier(ctx)
    ctx.pop_scope()
    barrier(ctx)
    ctx.pop_scope()
    barrier(ctx)
    ctx.pop_scope()
    ctx.push_scope()
    p_ffn(ctx)
    ctx.pop_scope()
    block = ctx.block

    @block.sync
    def _(s):
        for e in ("tensor", "vector", "scalar", "gpsimd"):
            ctx.wait(s, e)
        ctx.wait(s, "sync")


def kernel(**inputs):
    cst = host_prep(inputs)
    x = np.asarray(inputs["L0_attn_x"])
    nc, dd = build_program()
    in_maps = []
    for c in range(8):
        m = dict(cst)
        m["x"] = np.ascontiguousarray(x[c])
        in_maps.append(m)
    res = run_bass_kernel_spmd(nc, in_maps, core_ids=list(range(8)))
    return np.stack([res.results[c]["out"] for c in range(8)], 0)


# version 23
# speedup vs baseline: 1.1023x; 1.0357x over previous
import numpy as np
import ml_dtypes
from contextlib import ExitStack

import concourse.bass as bass
import concourse.mybir as mybir
from concourse.bass_utils import run_bass_kernel_spmd
from concourse.dve_spec import Spec, Src0, Src1, C0, C1, C2, One, eq, lower
from concourse.dve_op import DveOp
from concourse.dve_ops import OPS as _DVE_OPS
from concourse.dve_table_gen import dve_ver_for


def _make_lut2acc():
    for op in _DVE_OPS:
        if op.name == "LUT2ACC":
            return op
    body = Src1 + eq(Src0, C2) * C0 + eq(Src0 - C2, One) * C1
    spec = Spec(body=body)
    ver = dve_ver_for("TRN2")
    sha = lower(spec, ver=ver, perf_en=False, name="LUT2ACC").sha(ver)
    op = DveOp("LUT2ACC", spec, subdim=False, uops_sha={ver: sha},
               reference=lambda in0, in1, s0, s1, imm2:
                   in1 + (in0 == imm2) * s0 + ((in0 - imm2) == 1.0) * s1)
    _DVE_OPS.append(op)
    from concourse import dve_ops as _dvo
    _dvo._SUB_OPCODE_FOR_NAME[op.name] = _dvo._CUSTOM_DVE_ROW_BASE + len(_DVE_OPS) - 1
    return op


LUT2ACC = _make_lut2acc()

F32 = mybir.dt.float32
BF16 = mybir.dt.bfloat16
I32 = mybir.dt.int32
AF = mybir.ActivationFunctionType
ALU = mybir.AluOpType
BF = ml_dtypes.bfloat16

D = 2048
H = 32
DH = 64
M = 4
S = 64
L = 2048
NB = 32
NL = 128
INTER = 5504
NKC = 16
NTC = 16
NSC = 4
SCALE = 0.125
ROPE_BASE = 10000.0
NOB = 43

DEBUG = False
BUILD_UPTO = 99
SW1 = {"a", "b", "v", "g", "psc", "loc", "lm2", "grep"}
PUMP_PHASES = {"s1a", "s1b", "s1c", "lm", "s2a", "s2b"}


def _ps_perm():
    perm = np.zeros(D, np.int64)
    for r in range(D):
        p = r // 1024
        rem = r % 1024
        h = rem // 32
        i = rem % 32
        perm[r] = h * DH + 2 * i + p
    return perm


def _ptile(W, ow):
    Din, O = W.shape
    return np.ascontiguousarray(
        W.reshape(Din // 128, 128, O // ow, ow).transpose(2, 1, 0, 3),
        dtype=np.float32)


def host_prep(inputs):
    g = lambda n: np.asarray(inputs["L0_attn_" + n])
    perm = _ps_perm()

    w_qkv = g("w_qkv")
    b_qkv = g("b_qkv")
    wq = w_qkv[:, 0:D][:, perm]
    wk = w_qkv[:, D:2 * D][:, perm]
    wqk = np.ascontiguousarray(np.concatenate([wq, wk], 1), dtype=np.float32)
    bqk = np.concatenate([b_qkv[0:D][perm], b_qkv[D:2 * D][perm]])
    wv = np.ascontiguousarray(w_qkv[:, 2 * D:3 * D], dtype=np.float32)
    bv = b_qkv[2 * D:3 * D]

    pq = g("pool_q")
    PQ = np.zeros((D, NL), np.float32)
    for r in range(D):
        p = r // 1024
        rem = r % 1024
        h = rem // 32
        i = rem % 32
        PQ[r, 4 * h: 4 * h + 4] = pq[:, h * DH + 2 * i + p]

    w_lm_qkv = g("w_lm_qkv")
    b_lm_qkv = g("b_lm_qkv")

    def bias_cp(b):
        return np.ascontiguousarray(b.reshape(-1, 128), dtype=np.float32)

    i_of_row = np.arange(128) % 32
    inv = ROPE_BASE ** (-(2.0 * i_of_row) / DH)
    ang = inv[:, None] * np.arange(L, dtype=np.float64)[None, :]
    cosT = np.cos(ang).astype(np.float32)
    sinT = np.sin(ang).astype(np.float32)

    tril = np.tril(np.ones((S, S), np.float32))
    mask_local = np.concatenate([tril, tril], 0).astype(BF)
    blk = np.repeat(np.arange(NB), M)
    mask_lm = (blk[:, None] >= blk[None, :]).astype(np.float32).astype(BF)
    mra = np.zeros((128, 512), np.float32)
    for r in range(128):
        nrel = (r % 32) // 4
        mra[r, nrel * 64:(nrel + 1) * 64] = 1.0
    mask_ra = mra.astype(BF)
    tok = np.arange(L)
    col = np.arange(NL)
    mask_g = ((col[None, :] // 4) < (tok[:, None] // 64)).astype(np.float32).astype(BF)

    lut = np.asarray(inputs["L0_ffn_lut"]).astype(np.float32)
    lutb = np.broadcast_to(lut, (128, 16)).copy()

    cst = {
        "wqk": _ptile(wqk, 128), "bqk": bias_cp(bqk),
        "wv": _ptile(wv, 256), "bv": np.ascontiguousarray(bv, np.float32).reshape(1, D),
        "PQ": PQ,
        "wlq": _ptile(np.ascontiguousarray(w_lm_qkv[:, 0:D]), 128),
        "wlk": _ptile(np.ascontiguousarray(w_lm_qkv[:, D:2 * D]), 128),
        "wlv": _ptile(np.ascontiguousarray(w_lm_qkv[:, 2 * D:3 * D]), 256),
        "blq": bias_cp(b_lm_qkv[0:D]), "blk": bias_cp(b_lm_qkv[D:2 * D]),
        "blv": np.ascontiguousarray(b_lm_qkv[2 * D:3 * D], np.float32).reshape(1, D),
        "wlmout": _ptile(np.asarray(g("w_lm_out"), np.float32), 128),
        "blmout": bias_cp(g("b_lm_out")),
        "wl2q": _ptile(np.ascontiguousarray(g("w_l2t_q")[:, perm]), 128),
        "bl2q": bias_cp(g("b_l2t_q")[perm]),
        "wl2out": _ptile(np.asarray(g("w_l2t_out"), np.float32), 128),
        "bl2out": bias_cp(g("b_l2t_out")),
        "wgk": _ptile(np.ascontiguousarray(g("w_lm_kv")[:, 0:D][:, perm]), 128),
        "bgk": bias_cp(g("b_lm_kv")[0:D][perm]),
        "wgv": _ptile(np.ascontiguousarray(g("w_lm_kv")[:, D:2 * D]), 256),
        "bgv": np.ascontiguousarray(g("b_lm_kv")[D:2 * D], np.float32).reshape(1, D),
        "wout": _ptile(np.asarray(g("w_out"), np.float32), 128),
        "bout": bias_cp(g("b_out")),
        "wgate": np.ascontiguousarray(g("w_gate"), np.float32),
        "bgate": np.ascontiguousarray(g("b_gate"), np.float32).reshape(H, 1),
        "cosT": cosT, "sinT": sinT,
        "mask_local": mask_local, "mask_lm": mask_lm,
        "mask_ra": mask_ra, "mask_g": mask_g,
        "lutb": lutb,
        "ident": np.eye(128).astype(BF),
        "ones": np.ones((1, 128)).astype(BF),
        "bqk_r": np.ascontiguousarray(bqk, np.float32).reshape(1, 2 * D),
        "gidx": np.ascontiguousarray(inputs["L0_ffn_gate_idx"]),
        "uidx": np.ascontiguousarray(inputs["L0_ffn_up_idx"]),
        "didx": np.ascontiguousarray(inputs["L0_ffn_down_idx"]),
    }
    return cst


class Ctx:
    ENGS = ("sync", "tensor", "vector", "scalar", "gpsimd")

    def __init__(self, nc, stack, block):
        self.nc = nc
        self.stack = stack
        self.block = block
        self.h = {}
        self.n = {}
        for e in self.ENGS:
            self.h[e] = stack.enter_context(nc.semaphore("prg_" + e))
            self.n[e] = 0
        self.sb = {}
        self.scopes = []
        self.pf_free = {}
        self.wqb_free = [0, 0]
        self.wvb_free = [0, 0]

    def mark(self, inst, eng, amt=1):
        inst.then_inc(self.h[eng], amt)
        self.n[eng] += amt
        return self.n[eng]

    def wait(self, engobj, eng, val=None):
        v = self.n[eng] if val is None else val
        if v and v > 0:
            engobj.wait_ge(self.h[eng], v)

    def snap(self):
        return dict(self.n)

    def alloc(self, name, shape, dtype):
        st = self.scopes[-1] if self.scopes else self.stack
        t = st.enter_context(self.nc.sbuf_tensor(name, shape, dtype))
        self.sb[name] = t
        return t

    def push_scope(self):
        self.scopes.append(ExitStack())

    def pop_scope(self):
        self.scopes.pop().close()

    def vguard(self, v, inst):
        self.mark(inst, "vector")
        self.wait(v, "vector")

    def bank_wait(self, engobj, bank):
        fr = self.pf_free.get(bank)
        if fr is not None:
            self.wait(engobj, fr[0], fr[1])

    def bank_read(self, bank, eng):
        self.pf_free[bank] = (eng, self.n[eng])


def barrier(ctx):
    snap = ctx.snap()
    for name in Ctx.ENGS:
        def _f(e, snap=snap, me=name):
            for other, v in snap.items():
                if v > 0 and other != me:
                    e.wait_ge(ctx.h[other], v)
        getattr(ctx.block, name)(_f)


def build_program():
    nc = bass.Bass("TRN2", target_bir_lowering=False)
    dd = {}

    def din(name, shape, dtype):
        dd[name] = nc.dram_tensor(name, list(shape), dtype, kind="ExternalInput")

    def dint(name, shape, dtype):
        kind = "ExternalOutput" if DEBUG else "Internal"
        dd[name] = nc.dram_tensor(name, list(shape), dtype, kind=kind)

    din("x", (L, D), F32)
    din("wqk", (32, 128, 16, 128), F32); din("bqk", (32, 128), F32)
    din("wv", (8, 128, 16, 256), F32); din("bv", (1, D), F32)
    din("PQ", (D, NL), F32)
    din("wlq", (16, 128, 16, 128), F32); din("wlk", (16, 128, 16, 128), F32)
    din("wlv", (8, 128, 16, 256), F32)
    din("blq", (16, 128), F32); din("blk", (16, 128), F32); din("blv", (1, D), F32)
    din("wlmout", (16, 128, 16, 128), F32); din("blmout", (16, 128), F32)
    din("wl2q", (16, 128, 16, 128), F32); din("bl2q", (16, 128), F32)
    din("wl2out", (16, 128, 16, 128), F32); din("bl2out", (16, 128), F32)
    din("wgk", (16, 128, 16, 128), F32); din("bgk", (16, 128), F32)
    din("wgv", (8, 128, 16, 256), F32); din("bgv", (1, D), F32)
    din("wout", (16, 128, 16, 128), F32); din("bout", (16, 128), F32)
    din("wgate", (D, H), F32); din("bgate", (H, 1), F32)
    din("cosT", (128, L), F32); din("sinT", (128, L), F32)
    din("mask_local", (128, 64), BF16)
    din("mask_lm", (128, 128), BF16)
    din("mask_ra", (128, 512), BF16)
    din("mask_g", (L, NL), BF16)
    din("lutb", (128, 16), F32)
    din("gidx", (INTER, D), I32)
    din("uidx", (INTER, D), I32)
    din("didx", (D, INTER), I32)
    din("ident", (128, 128), BF16)
    din("ones", (1, 128), BF16)
    din("bqk_r", (1, 2 * D), F32)

    dd["out"] = nc.dram_tensor("out", [L, D], F32, kind="ExternalOutput")

    dint("dxn", (NSC, 128, NKC, 512), BF16)
    dint("dq", (NSC, 128, NKC, 512), BF16)
    dint("dk", (NSC, 128, NKC, 512), BF16)
    dint("dv", (NTC, 128, D), BF16)
    dint("dol", (NSC, NKC, 128, 512), BF16)
    dint("ds", (NSC, 128, NKC, 512), BF16)
    dint("dgrep", (H, L), BF16)
    dd["wg_nat"] = nc.dram_tensor("wg_nat", [INTER, D], BF16, kind="Internal")
    dd["wu_nat"] = nc.dram_tensor("wu_nat", [INTER, D], BF16, kind="Internal")
    dd["wd_nat"] = nc.dram_tensor("wd_nat", [D, INTER], BF16, kind="Internal")
    dint("wgT", (NOB, 128, NKC, 128), BF16)
    dint("wuT", (NOB, 128, NKC, 128), BF16)
    dint("wdT", (NKC, 128, NOB, 128), BF16)
    if DEBUG:
        dint("dbg_lqt", (128, NKC, 128), BF16)
        dint("dbg_lkt", (128, NKC, 128), BF16)
        dint("dbg_lvt", (128, D), BF16)
        dint("dbg_lot", (128, NKC, 128), BF16)
        dint("dbg_sge", (128, 512), F32)
        dint("dbg_scrB", (128, 512), F32)
        dint("dbg_mg", (128, 4, 128), BF16)
        dint("dbg_red", (128, 32), F32)
        dint("dbg_agb", (128, 512), BF16)
        dint("dbg_agt", (128, 512), BF16)
        dint("dbg_olt", (128, 512), BF16)
        dint("dbg_grep", (128, 512), BF16)
        dint("dbg_og", (128, 512), F32)
        dint("dbg_lmf", (128, NKC, NL), BF16)
        dint("dbg_lmf2", (128, NKC, NL), BF16)
        dint("dbg_lmf3", (128, NKC, NL), BF16)
        dint("dbg_gk", (128, NKC, NL), BF16)
        dint("dbg_gv", (128, D), BF16)
        dint("dbg_gated", (128, NKC, 512), BF16)

    with ExitStack() as stack:
        block = stack.enter_context(nc.Block())
        ctx = Ctx(nc, stack, block)
        ctx.d = dd
        ctx.PF = [stack.enter_context(nc.psum_tensor(f"pf{i}", [128, 512], F32))
                  for i in range(7)]
        ctx.PB = stack.enter_context(nc.psum_tensor("pbt", [128, 1024], BF16))
        emit(ctx)

    return nc, dd


def p_consts(ctx):
    block, d = ctx.block, ctx.d
    a = ctx.alloc
    a("mloc", [128, 64], BF16)
    a("lut_sb", [128, 16], F32)
    a("luth_sb", [128, 16], F32)
    a("ident_sb", [128, 128], BF16)
    a("identf", [128, 128], F32)
    a("ones_sb", [1, 128], BF16)
    a("bqkr", [1, 2 * D], BF16)
    a("wgate_sb", [128, NKC, H], BF16)
    a("bgate_sb", [H, 1], F32)
    a("bstage", [48, 128], F32)
    a("scrA", [128, 512], F32)
    a("scrB", [128, 512], F32)
    a("stat", [128, 8], F32)
    a("red", [128, 32], F32)
    a("epsc", [128, 1], F32)
    bias_c = {"blq": 16, "blk": 16, "blmout": 16, "bl2q": 16,
              "bl2out": 16, "bgk": 16, "bout": 16}
    for nm, c in bias_c.items():
        a("b_" + nm, [128, c], F32)
    a("b_bout_h", [128, 16], F32)

    sb = ctx.sb

    @block.sync
    def _(s):
        ctx.mark(s.dma_start(out=sb["mloc"][:], in_=d["mask_local"][:]), "sync", 16)
        ctx.mark(s.dma_start(out=sb["lut_sb"][:], in_=d["lutb"][:]), "sync", 16)
        ctx.mark(s.dma_start(out=sb["ident_sb"][:], in_=d["ident"][:]), "sync", 16)
        ctx.mark(s.dma_start(out=sb["ones_sb"][:], in_=d["ones"][:]), "sync", 16)
        ctx.mark(s.dma_start(out=sb["bgate_sb"][:], in_=d["bgate"][:]), "sync", 16)

    @block.gpsimd
    def _(g):
        ctx.mark(g.dma_start(out=sb["wgate_sb"][:], in_=d["wgate"][:].rearrange(
            "(c p) o -> p c o", p=128)), "gpsimd", 16)
        ctx.mark(g.dma_start(out=sb["bqkr"][:], in_=d["bqk_r"][:]), "gpsimd", 16)

    @block.vector
    def _(v):
        ctx.wait(v, "sync")
        ctx.mark(v.tensor_scalar_mul(sb["luth_sb"][:], sb["lut_sb"][:], 0.5), "vector")
        ctx.mark(v.tensor_copy(out=sb["identf"][:], in_=sb["ident_sb"][:]), "vector")
        ctx.mark(v.memset(sb["epsc"][:], 1e-5), "vector")


def p_bias_stage(ctx):
    block, d, sb = ctx.block, ctx.d, ctx.sb
    bias_c = {"blq": 16, "blk": 16, "blmout": 16, "bl2q": 16,
              "bl2out": 16, "bgk": 16, "bout": 16}
    for nm, C in bias_c.items():
        tgt = sb["b_" + nm]

        @block.sync
        def _(s, nm=nm, C=C):
            ctx.wait(s, "tensor")
            ctx.mark(s.dma_start(out=sb["bstage"][0:C, :], in_=d[nm][:]), "sync", 16)

        @block.tensor
        def _(t, C=C):
            ctx.wait(t, "sync")
            ctx.wait(t, "vector")
            ctx.mark(t.transpose(out=ctx.PF[6][:, 0:C], in_=sb["bstage"][0:C, :],
                                 identity=sb["identf"][0:C, 0:C]), "tensor")

        @block.vector
        def _(v, tgt=tgt, C=C):
            ctx.wait(v, "tensor")
            ctx.mark(v.tensor_copy(out=tgt[:], in_=ctx.PF[6][:, 0:C]), "vector")

    @block.vector
    def _(v):
        ctx.mark(v.tensor_scalar_mul(sb["b_bout_h"][:], sb["b_bout"][:], 0.5),
                 "vector")


class Expander:

    def __init__(self, ctx):
        a = ctx.alloc
        self.ctx = ctx
        self.idxbuf = [a(f"idx{i}", [128, 2752], I32) for i in range(2)]
        self.idxb = a("idxb", [128, 2752], BF16)
        self.sel = a("sel", [128, 16, 512], BF16)
        self.wt = [a(f"wt{i}", [128, 2752], BF16) for i in range(2)]
        d = ctx.d
        sb = ctx.sb
        jobs = []
        for b in range(NOB):
            jobs.append((d["gidx"], d["wg_nat"], b, 0, D, sb["lut_sb"]))
        for b in range(NOB):
            jobs.append((d["uidx"], d["wu_nat"], b, 0, D, sb["lut_sb"]))
        for b in range(NKC):
            jobs.append((d["didx"], d["wd_nat"], b, 0, 2752, sb["luth_sb"]))
            jobs.append((d["didx"], d["wd_nat"], b, 2752, 2752, sb["luth_sb"]))
        self.jobs = jobs
        self.li = 0
        self.ci = 0
        self.si = 0
        self.job_spill = {}
        self.job_load = {}
        self.conv_done = [0, 0]
        self.spill_done = [0, 0]
        self.sel_free = [0, 0]

    def _load(self):
        ctx = self.ctx
        if self.li >= len(self.jobs):
            return
        i = self.li
        (dsrc, ddst, b, c0, ncols, lsb) = self.jobs[i]
        slot = i % 2

        @ctx.block.sync
        def _(s):
            if i >= 2:
                ctx.wait(s, "vector", self.conv_done[slot])
            self.job_load[i] = ctx.mark(s.dma_start(
                out=self.idxbuf[slot][:, 0:ncols],
                in_=dsrc[b * 128:(b + 1) * 128, c0:c0 + ncols]), "sync", 16)
        self.li += 1

    def pump(self):
        ctx = self.ctx
        if self.ci == 0 and self.si == 0:
            self._load()
            self._load()
        tr = getattr(ctx, "tr", None)
        self.pump_ct = getattr(self, "pump_ct", 0) + 1
        if tr is not None and (self.pump_ct % 2 == 0 or
                               self.ci >= len(self.jobs)):
            if tr.step():
                return
        if self.ci >= len(self.jobs):
            return
        i = self.ci
        (dsrc, ddst, b, c0, ncols, lsb) = self.jobs[i]
        slot = i % 2
        nsub = (ncols + 511) // 512
        sub = self.si

        if sub == 0:
            @ctx.block.vector
            def _(v):
                ctx.wait(v, "sync", self.job_load[i])
                if i >= 2:
                    ctx.wait(v, "sync", self.spill_done[slot])
                self.conv_done[slot] = ctx.mark(
                    v.tensor_copy(out=self.idxb[:, 0:ncols],
                                  in_=self.idxbuf[slot][:, 0:ncols]), "vector")

        @ctx.block.vector
        def _(v):
            sel = self.sel
            w = min(512, ncols - sub * 512)
            srcv = self.idxb[:, sub * 512: sub * 512 + w]
            for t in range(16):
                v.tensor_scalar(out=sel[:, t, 0:w], in0=srcv,
                                scalar1=float(t), scalar2=lsb[:, t:t + 1],
                                op0=ALU.is_equal, op1=ALU.mult)
            if w == 512:
                f = sel[:].rearrange("p c f -> p (c f)")
                v.tensor_add(f[:, 0:4096], f[:, 0:4096], f[:, 4096:8192])
                v.tensor_add(f[:, 0:2048], f[:, 0:2048], f[:, 2048:4096])
                v.tensor_add(f[:, 0:1024], f[:, 0:1024], f[:, 1024:2048])
                inst = v.tensor_add(
                    self.wt[slot][:, sub * 512: sub * 512 + 512],
                    f[:, 0:512], f[:, 512:1024])
            else:
                stride = 1
                while stride < 16:
                    for t0 in range(0, 16, 2 * stride):
                        dst = sel[:, t0, 0:w]
                        if 2 * stride == 16:
                            dst = self.wt[slot][:, sub * 512: sub * 512 + w]
                        inst = v.tensor_add(dst, sel[:, t0, 0:w],
                                            sel[:, t0 + stride, 0:w])
                    stride *= 2
            ctx.mark(inst, "vector")

        self.si += 1
        if self.si < nsub:
            return

        @ctx.block.sync
        def _(s):
            ctx.wait(s, "vector")
            self.spill_done[slot] = ctx.mark(
                s.dma_start(out=ddst[b * 128:(b + 1) * 128, c0:c0 + ncols],
                            in_=self.wt[slot][:, 0:ncols]), "sync", 16)
            self.job_spill[i] = self.spill_done[slot]
        self.si = 0
        self.ci += 1
        self._load()

    def finish(self):
        while self.ci < len(self.jobs):
            self.pump()


class TrRound:

    def __init__(self, ctx):
        a = ctx.alloc
        self.ctx = ctx
        self.tb = [a(f"trb{i}", [128, 512], BF16) for i in range(2)]
        d = ctx.d
        steps = []
        fams = [(d["wg_nat"], d["wgT"], INTER, NKC, 0),
                (d["wu_nat"], d["wuT"], INTER, NKC, 43),
                (d["wd_nat"], d["wdT"], D, NOB, 86)]
        for (src_, dst, O, HB, base) in fams:
            nog = (O + 511) // 512
            for hb in range(HB):
                for og in range(nog):
                    rows = min(512, O - og * 512)
                    ntile = (rows + 127) // 128
                    if base < 86:
                        ready = base + min(4 * og + 3, 42)
                    else:
                        ready = 86 + 8 * og + 7
                    steps.append((src_, dst, og, hb, rows, ntile, ready))
        self.steps = steps
        self.i = 0
        self.pending = None
        self.store_done = [0, 0]
        self.load_done = [0, 0]

    def _store(self):
        ctx = self.ctx
        k = self.pending
        (src_, dst, og, hb, rows, ntile, ready) = self.steps[k]
        slot = k % 2

        @ctx.block.scalar
        def _(sce):
            ctx.wait(sce, "sync", self.load_done[slot])
            for j in range(ntile):
                w = min(128, rows - j * 128)
                inst = sce.dma_start(
                    out=dst[og * 4 + j, :, hb, 0:w],
                    in_=self.tb[slot][:, j * 128: j * 128 + w])
                self.store_done[slot] = ctx.mark(inst, "scalar", 16)
        self.pending = None

    def step(self):
        ctx = self.ctx
        did = False
        if self.pending is not None:
            self._store()
            did = True
        if self.i >= len(self.steps):
            return did
        (src_, dst, og, hb, rows, ntile, ready) = self.steps[self.i]
        rmark = ctx.exp.job_spill.get(ready)
        if rmark is None:
            return did
        slot = self.i % 2

        @ctx.block.sync
        def _(s):
            ctx.wait(s, "sync", rmark)
            if self.store_done[slot]:
                ctx.wait(s, "scalar", self.store_done[slot])
            self.load_done[slot] = ctx.mark(s.dma_start_transpose(
                out=self.tb[slot][:, 0:rows],
                in_=src_[og * 512: og * 512 + rows,
                         hb * 128:(hb + 1) * 128]), "sync", 16)
        self.pending = self.i
        self.i += 1
        return True

    def finish(self):
        while self.i < len(self.steps) or self.pending is not None:
            if not self.step():
                raise RuntimeError("trround source not spilled at finish")


def p_lm_scope_alloc(ctx):
    block, d, sb = ctx.block, ctx.d, ctx.sb
    a = ctx.alloc
    a("lmfT", [128, NKC, NL], BF16)
    a("gz", [H, L], BF16)
    a("kbias", [1, 3 * D], BF16)
    for i in range(2):
        a(f"wqb{i}", [128, NKC, 128], BF16)
        a(f"wvb{i}", [128, NKC, 256], BF16)

    @block.gpsimd
    def _(g):
        ctx.mark(g.dma_start(out=sb["kbias"][:, 0:D], in_=d["bv"][:]), "gpsimd", 16)
        ctx.mark(g.dma_start(out=sb["kbias"][:, D:2 * D], in_=d["blv"][:]), "gpsimd", 16)
        ctx.mark(g.dma_start(out=sb["kbias"][:, 2 * D:3 * D], in_=d["bgv"][:]), "gpsimd", 16)


def p_post1_alloc(ctx):
    ctx.tr = TrRound(ctx)
    _a0 = ctx.alloc
    _a0("mlm", [128, 128], BF16)
    _a0("mra", [128, 512], BF16)

    @ctx.block.sync
    def _(s):
        ctx.mark(s.dma_start(out=ctx.sb["mlm"][:], in_=ctx.d["mask_lm"][:]),
                 "sync", 16)
        ctx.mark(s.dma_start(out=ctx.sb["mra"][:], in_=ctx.d["mask_ra"][:]),
                 "sync", 16)

    a = ctx.alloc
    a("lmf2T", [128, NKC, NL], BF16)
    a("lmf3T", [128, NKC, NL], BF16)
    a("l2qT", [128, NKC, NL], BF16)
    a("gkT", [128, NKC, NL], BF16)
    a("lqT", [128, NKC, NL], BF16)
    a("lkT", [128, NKC, NL], BF16)
    a("lvtok", [128, D], BF16)
    a("loT", [128, NKC, NL], BF16)
    a("roT", [128, NKC, NL], BF16)
    a("gv_sb", [128, D], BF16)


def p_sweep1(ctx):
    block, d, sb = ctx.block, ctx.d, ctx.sb
    a = ctx.alloc
    xbuf = [a(f"xb{i}", [128, D], F32) for i in range(2)]
    xnb = [a("xnb0", [128, D], BF16)] * 2
    xnT = a("xnT", [128, NKC, 512], BF16)
    qT = a("qT_s", [128, NKC, 512], BF16)
    kT = a("kT_s", [128, NKC, 512], BF16)
    vsc = a("vsc", [128, 4, D], BF16)
    cos_sb = a("cos_sb", [128, 512], BF16)
    sin_sb = a("sin_sb", [128, 512], BF16)
    pq_sb = a("pq_sb", [128, NKC, 128], BF16)
    wqb = [ctx.sb["wqb0"], ctx.sb["wqb1"]]
    wvb = [ctx.sb["wvb0"], ctx.sb["wvb1"]]
    scrA = sb["scrA"]
    scrB = sb["scrB"]
    stat = sb["stat"]
    red = sb["red"]
    A_sb = a("A_sb", [128, 4, 64], BF16)
    AT_sb = a("AT_sb", [128, 4, 128], BF16)
    pa_sb = a("pa_sb", [128, 512], BF16)
    paT_sb = a("paT_sb", [128, 4, 128], BF16)
    olst = a("olst", [128, 512], BF16)

    @block.gpsimd
    def _(g):
        ctx.mark(g.dma_start(out=pq_sb[:], in_=d["PQ"][:].rearrange(
            "(c p) o -> p c o", p=128)), "gpsimd", 16)

    xb_free = [0, 0]
    xnb_free = [0, 0]
    wq_free = ctx.wqb_free
    wv_free = ctx.wvb_free
    for sc in range(NSC):
        @block.gpsimd
        def _(g, sc=sc):
            ctx.wait(g, "vector", ctx.n["vector"])
            ctx.mark(g.dma_start(out=cos_sb[:],
                                 in_=d["cosT"][:, sc * 512:(sc + 1) * 512]),
                     "gpsimd", 16)
            ctx.mark(g.dma_start(out=sin_sb[:],
                                 in_=d["sinT"][:, sc * 512:(sc + 1) * 512]),
                     "gpsimd", 16)

        prev_lnstats = [ctx.n["vector"]]
        for tc_rel in range(4) if "a" in SW1 else []:
            tc = sc * 4 + tc_rel
            slot = tc % 2

            xld = [0]

            @block.sync
            def _(s, tc=tc, slot=slot):
                if tc >= 2:
                    ctx.wait(s, "vector", xb_free[slot])
                xld[0] = ctx.mark(
                    s.dma_start(out=xbuf[slot][:],
                                in_=d["x"][tc * 128:(tc + 1) * 128, :]),
                    "sync", 16)

            @block.scalar
            def _(sce, slot=slot):
                ctx.wait(sce, "sync", xld[0])
                ctx.wait(sce, "vector", prev_lnstats[0])
                x = xbuf[slot][:]
                sce.activation(out=scrA[:], in_=x[:, 0:512], func=AF.Square)
                sce.activation(out=scrB[:], in_=x[:, 512:1024], func=AF.Square)
                ctx.mark(sce.activation(out=xnb[0][:, 0:512], in_=x[:, 0:512],
                                        func=AF.Square), "scalar")

            @block.vector
            def _(v):
                ctx.wait(v, "scalar")
                v.reduce_sum(red[:, 12:13], scrA[:], axis=mybir.AxisListType.X)
                ctx.mark(v.reduce_sum(red[:, 13:14], scrB[:],
                                      axis=mybir.AxisListType.X), "vector")

            @block.scalar
            def _(sce, slot=slot):
                ctx.wait(sce, "vector")
                x = xbuf[slot][:]
                sce.activation(out=scrA[:], in_=x[:, 1024:1536], func=AF.Square)
                sce.activation(out=scrB[:], in_=x[:, 1536:2048], func=AF.Square)
                ctx.mark(sce.activation(out=xnb[0][:, 0:512],
                                        in_=x[:, 1024:1536],
                                        func=AF.Square), "scalar")

            @block.vector
            def _(v, slot=slot):
                ctx.wait(v, "scalar")
                ctx.wait(v, "tensor")
                x = xbuf[slot][:]
                v.reduce_sum(red[:, 14:15], scrA[:], axis=mybir.AxisListType.X)
                v.reduce_sum(red[:, 15:16], scrB[:], axis=mybir.AxisListType.X)
                ctx.vguard(v, v.reduce_sum(stat[:, 1:2], x,
                                           axis=mybir.AxisListType.X))
                v.tensor_add(red[:, 8:9], red[:, 12:13], red[:, 13:14])
                ctx.vguard(v, v.tensor_add(red[:, 9:10], red[:, 14:15],
                                           red[:, 15:16]))
                v.tensor_add(red[:, 10:11], red[:, 8:9], red[:, 9:10])
                ctx.vguard(v, v.tensor_scalar_mul(stat[:, 2:3], stat[:, 1:2],
                                                  1.0 / D))
                v.tensor_scalar_mul(stat[:, 0:1], red[:, 10:11], 1.0 / D)
                ctx.vguard(v, v.tensor_scalar(out=stat[:, 3:4], in0=stat[:, 2:3],
                                              scalar1=stat[:, 2:3], scalar2=-1.0,
                                              op0=ALU.mult, op1=ALU.mult))
                v.tensor_add(stat[:, 4:5], stat[:, 0:1], stat[:, 3:4])
                ctx.mark(v.tensor_copy(out=red[:, 30:31], in_=stat[:, 4:5]),
                         "vector")
                prev_lnstats[0] = ctx.n["vector"]

            @block.scalar
            def _(sce):
                ctx.wait(sce, "vector")
                sce.activation(out=stat[:, 6:7], in_=stat[:, 4:5], func=AF.Ln,
                               bias=sb["epsc"][:], scale=1.0)
                ctx.mark(sce.activation(out=stat[:, 7:8], in_=stat[:, 4:5],
                                        func=AF.Ln, bias=sb["epsc"][:],
                                        scale=1.0), "scalar")
                ctx.wait(sce, "scalar")
                sce.activation(out=stat[:, 5:6], in_=stat[:, 6:7],
                               func=AF.Exp, scale=-0.5)
                ctx.mark(sce.activation(out=stat[:, 7:8], in_=stat[:, 6:7],
                                        func=AF.Exp, scale=-0.5), "scalar")

            @block.vector
            def _(v, slot=slot, tc=tc):
                ctx.wait(v, "scalar")
                xb_free[slot] = ctx.mark(
                    v.tensor_scalar(out=xnb[slot][:], in0=xbuf[slot][:],
                                    scalar1=stat[:, 2:3], scalar2=stat[:, 5:6],
                                    op0=ALU.subtract, op1=ALU.mult), "vector")

            if "s1a" in PUMP_PHASES:
                ctx.exp.pump()
                ctx.exp.pump()
            for half in range(2):
                @block.tensor
                def _(t, slot=slot, half=half):
                    ctx.wait(t, "vector", xb_free[slot])
                    ctx.bank_wait(t, "PB")
                    for j in range(8):
                        c = half * 8 + j
                        inst = t.transpose(out=ctx.PB[:, j * 128:(j + 1) * 128],
                                           in_=xnb[slot][:, c * 128:(c + 1) * 128],
                                           identity=sb["ident_sb"][:])
                    xnb_free[slot] = ctx.mark(inst, "tensor")

                @block.vector
                def _(v, tc_rel=tc_rel, half=half):
                    ctx.wait(v, "tensor")
                    ctx.mark(v.tensor_copy(
                        out=xnT[:, half * 8:(half + 1) * 8,
                                tc_rel * 128:(tc_rel + 1) * 128],
                        in_=ctx.PB[:].rearrange("p (c f) -> p c f", c=8)),
                        "vector")
                    ctx.bank_read("PB", "vector")



        xnT_ready = ctx.n["vector"]
        if "a" in SW1:
            @block.sync
            def _(s, sc=sc):
                ctx.wait(s, "vector")
                ctx.mark(s.dma_start(out=d["dxn"][sc], in_=xnT[:]), "sync", 16)

        rope_prev = [max(xnT_ready, prev_lnstats[0])]
        for grp in range(16) if "b" in SW1 else []:
            is_q = grp < 8
            j = grp % 8
            ce = j if is_q else 16 + j
            co = ce + 8
            dstT = qT if is_q else kT
            ddst = d["dq"] if is_q else d["dk"]
            for par, oc in ((0, ce), (1, co)):
                slot = par

                @block.gpsimd
                def _(g, oc=oc, slot=slot, grp=grp):
                    if wq_free[slot]:
                        ctx.wait(g, "tensor", wq_free[slot])
                    ctx.mark(g.dma_start(out=wqb[slot][:],
                                         in_=d["wqk"][oc]), "gpsimd", 16)

                @block.tensor
                def _(t, slot=slot, par=par, oc=oc):
                    ctx.wait(t, "gpsimd")
                    ctx.wait(t, "vector", xnT_ready)
                    ctx.bank_wait(t, par)
                    for kc in range(NKC):
                        t.matmul(out=ctx.PF[par][:],
                                 lhsT=wqb[slot][:, kc, :],
                                 rhs=xnT[:, kc, :],
                                 start=(kc == 0), stop=False)
                    for q4 in range(4):
                        inst = t.matmul(
                            out=ctx.PF[par][:, q4 * 128:(q4 + 1) * 128],
                            lhsT=sb["bqkr"][0:1, oc * 128:(oc + 1) * 128],
                            rhs=sb["ones_sb"][:],
                            start=False, stop=True)
                    wq_free[slot] = ctx.mark(inst, "tensor")

            @block.vector
            def _(v, j=j, dstT=dstT, sc=sc):
                ctx.wait(v, "tensor")
                ctx.wait(v, "gpsimd")
                ctx.wait(v, "scalar")
                cs = cos_sb[:]
                sn = sin_sb[:]
                v.tensor_mul(scrA[:], ctx.PF[0][:], cs)
                v.tensor_mul(scrB[:], ctx.PF[1][:], sn)
                v.tensor_sub(dstT[:, j, :], scrA[:], scrB[:])
                v.tensor_mul(scrA[:], ctx.PF[0][:], sn)
                v.tensor_mul(scrB[:], ctx.PF[1][:], cs)
                ctx.mark(v.tensor_add(dstT[:, j + 8, :], scrA[:], scrB[:]),
                         "vector")
                rope_prev[0] = ctx.n["vector"]
                ctx.bank_read(0, "vector")
                ctx.bank_read(1, "vector")

            @block.sync
            def _(s, j=j, dstT=dstT, ddst=ddst, sc=sc):
                ctx.wait(s, "vector")
                ctx.mark(s.dma_start(out=ddst[sc, :, j, :],
                                     in_=dstT[:, j, :]), "sync", 16)
                ctx.mark(s.dma_start(out=ddst[sc, :, j + 8, :],
                                     in_=dstT[:, j + 8, :]), "sync", 16)

            if "s1b" in PUMP_PHASES:
                ctx.exp.pump()
                ctx.exp.pump()

        for vb in range(8) if "v" in SW1 else []:
            slot = vb % 2

            @block.gpsimd
            def _(g, vb=vb, slot=slot, sc=sc):
                if wv_free[slot]:
                    ctx.wait(g, "tensor", wv_free[slot])
                ctx.mark(g.dma_start(out=wvb[slot][:],
                                     in_=d["wv"][vb]), "gpsimd", 16)

            for tc_rel in range(4):
                bank = 2 + (tc_rel % 2)

                @block.tensor
                def _(t, vb=vb, slot=slot, tc_rel=tc_rel, bank=bank):
                    if tc_rel == 0:
                        ctx.wait(t, "gpsimd")
                    ctx.wait(t, "vector", xnT_ready)
                    ctx.bank_wait(t, bank)
                    for kc in range(NKC):
                        t.matmul(out=ctx.PF[bank][:, 0:256],
                                 lhsT=xnT[:, kc, tc_rel * 128:(tc_rel + 1) * 128],
                                 rhs=wvb[slot][:, kc, :],
                                 start=(kc == 0), stop=False)
                    inst = t.matmul(out=ctx.PF[bank][:, 0:256], lhsT=sb["ones_sb"][:],
                                    rhs=sb["kbias"][0:1, vb * 256:(vb + 1) * 256],
                                    start=False, stop=True)
                    w = ctx.mark(inst, "tensor")
                    if tc_rel == 3:
                        wv_free[slot] = w

                @block.vector
                def _(v, vb=vb, tc_rel=tc_rel, bank=bank):
                    ctx.wait(v, "tensor")
                    ctx.mark(v.tensor_copy(
                        out=vsc[:, tc_rel, vb * 256:(vb + 1) * 256],
                        in_=ctx.PF[bank][:, 0:256]), "vector")
                    ctx.bank_read(bank, "vector")

        @block.sync
        def _(s, sc=sc):
            ctx.wait(s, "vector")
            for tc_rel in range(4):
                ctx.mark(s.dma_start(out=d["dv"][sc * 4 + tc_rel, :, :],
                                     in_=vsc[:, tc_rel, :]), "sync", 16)

        if "g" not in SW1:
            continue
        @block.tensor
        def _(t):
            ctx.wait(t, "vector", xnT_ready)
            ctx.wait(t, "gpsimd")
            ctx.bank_wait(t, 4)
            for kc in range(NKC):
                inst = t.matmul(out=ctx.PF[4][0:32, :],
                                lhsT=sb["wgate_sb"][:, kc, :],
                                rhs=xnT[:, kc, :],
                                start=(kc == 0), stop=(kc == NKC - 1))
            ctx.mark(inst, "tensor")

        @block.scalar
        def _(sce, sc=sc):
            ctx.wait(sce, "tensor")
            ctx.mark(sce.activation(out=sb["gz"][:, sc * 512:(sc + 1) * 512],
                                    in_=ctx.PF[4][0:32, :], func=AF.Identity,
                                    bias=sb["bgate_sb"][:], scale=1.0), "scalar")
            ctx.bank_read(4, "scalar")

        qk_done = ctx.n["vector"]
        if "psc" not in SW1:
            continue
        @block.tensor
        def _(t):
            ctx.wait(t, "vector", qk_done)
            ctx.wait(t, "gpsimd")
            ctx.bank_wait(t, 5)
            for kc in range(NKC):
                inst = t.matmul(out=ctx.PF[5][:], lhsT=pq_sb[:, kc, :],
                                rhs=kT[:, kc, :],
                                start=(kc == 0), stop=(kc == NKC - 1))
            ctx.mark(inst, "tensor")

        @block.vector
        def _(v):
            ctx.wait(v, "tensor")
            ctx.wait(v, "scalar")
            s3 = ctx.PF[5][:].rearrange("p (s f) -> p s f", s=8)
            ctx.vguard(v, v.reduce_max(red[:, 0:8], s3, axis=mybir.AxisListType.X))
            ctx.mark(v.tensor_sub(scrA[:].rearrange("p (s f) -> p s f", s=8), s3,
                                  red[:, 0:8].unsqueeze(2).broadcast_to(
                                      [128, 8, 64])), "vector")
            ctx.bank_read(5, "vector")
        pool_sub = ctx.n["vector"]

        @block.scalar
        def _(sce):
            ctx.wait(sce, "vector", pool_sub)
            ctx.mark(sce.activation(out=scrB[:], in_=scrA[:], func=AF.Exp,
                                    scale=SCALE), "scalar")

        @block.vector
        def _(v):
            ctx.wait(v, "scalar")
            e3 = scrB[:].rearrange("p (s f) -> p s f", s=8)
            ctx.vguard(v, v.reduce_sum(red[:, 8:16], e3, axis=mybir.AxisListType.X))
            ctx.vguard(v, v.reciprocal(red[:, 16:24], red[:, 8:16]))
            ctx.mark(v.tensor_mul(pa_sb[:].rearrange("p (s f) -> p s f", s=8),
                                  e3, red[:, 16:24].unsqueeze(2).broadcast_to(
                                      [128, 8, 64])), "vector")
        pa_mark = ctx.n["vector"]

        @block.tensor
        def _(t):
            ctx.wait(t, "vector", pa_mark)
            ctx.bank_wait(t, "PB")
            for tc_rel in range(4):
                inst = t.transpose(out=ctx.PB[:, tc_rel * 128:(tc_rel + 1) * 128],
                                   in_=pa_sb[:, tc_rel * 128:(tc_rel + 1) * 128],
                                   identity=sb["ident_sb"][:])
            ctx.mark(inst, "tensor")

        @block.vector
        def _(v):
            ctx.wait(v, "tensor")
            ctx.mark(v.tensor_copy(out=paT_sb[:],
                                   in_=ctx.PB[:, 0:512].rearrange(
                                       "p (c f) -> p c f", c=4)), "vector")
            ctx.bank_read("PB", "vector")
        paT_mark = ctx.n["vector"]

        nhp = 16 if "loc" in SW1 else (1 if "loc1" in SW1 else 0)
        for hp in range(nhp):
            for hh in range(2):
                h = 2 * hp + hh
                bank = hh

                @block.tensor
                def _(t, h=h, bank=bank):
                    ctx.wait(t, "vector", qk_done)
                    ctx.bank_wait(t, bank)
                    for nrel in range(8):
                        for par in range(2):
                            ch = par * 8 + h // 4
                            rb = (h % 4) * 32
                            inst = t.matmul(
                                out=ctx.PF[bank][(nrel % 2) * 64:(nrel % 2) * 64 + 64,
                                                 (nrel // 2) * 64:(nrel // 2) * 64 + 64],
                                lhsT=qT[rb:rb + 32, ch, nrel * 64:(nrel + 1) * 64],
                                rhs=kT[rb:rb + 32, ch, nrel * 64:(nrel + 1) * 64],
                                start=(par == 0), stop=(par == 1),
                                tile_position=(rb, (nrel % 2) * 64))
                    ctx.mark(inst, "tensor")

                @block.vector
                def _(v, bank=bank):
                    ctx.wait(v, "tensor")
                    ctx.wait(v, "scalar")
                    s3 = ctx.PF[bank][:, 0:256].rearrange("p (s f) -> p s f", s=4)
                    ctx.vguard(v, v.reduce_max(red[:, 0:4], s3,
                                               axis=mybir.AxisListType.X))
                    ctx.mark(v.tensor_sub(
                        scrA[:, 0:256].rearrange("p (s f) -> p s f", s=4), s3,
                        red[:, 0:4].unsqueeze(2).broadcast_to([128, 4, 64])),
                        "vector")
                    ctx.bank_read(bank, "vector")
                lsub_mark = ctx.n["vector"]

                @block.scalar
                def _(sce, lsub_mark=lsub_mark):
                    ctx.wait(sce, "vector", lsub_mark)
                    ctx.mark(sce.activation(out=scrB[:, 0:256], in_=scrA[:, 0:256],
                                            func=AF.Exp, scale=SCALE), "scalar")

                @block.vector
                def _(v):
                    ctx.wait(v, "scalar")
                    ctx.wait(v, "tensor")
                    e3 = scrB[:, 0:256].rearrange("p (s f) -> p s f", s=4)
                    v.tensor_mul(scrA[:, 0:256].rearrange("p (s f) -> p s f", s=4),
                                 e3, sb["mloc"][:].unsqueeze(1).broadcast_to(
                                     [128, 4, 64]))
                    m3 = scrA[:, 0:256].rearrange("p (s f) -> p s f", s=4)
                    ctx.vguard(v, v.reduce_sum(red[:, 4:8], m3,
                                               axis=mybir.AxisListType.X))
                    ctx.vguard(v, v.reciprocal(red[:, 16:20], red[:, 4:8]))
                    ctx.mark(v.tensor_mul(A_sb[:], m3,
                                          red[:, 16:20].unsqueeze(2).broadcast_to(
                                              [128, 4, 64])), "vector")
                A_mark = ctx.n["vector"]

                @block.tensor
                def _(t, A_mark=A_mark):
                    ctx.wait(t, "vector", A_mark)
                    ctx.bank_wait(t, "PB")
                    for g_ in range(4):
                        t.transpose(out=ctx.PB[0:64, g_ * 128:(g_ + 1) * 128],
                                    in_=A_sb[:, g_, :],
                                    identity=sb["ident_sb"][:])
                        inst = t.transpose(
                            out=ctx.PB[64:128, g_ * 128:(g_ + 1) * 128],
                            in_=A_sb[:, g_, :], identity=sb["ident_sb"][:],
                            tile_position=(0, 64))
                    ctx.mark(inst, "tensor")

                @block.vector
                def _(v):
                    ctx.wait(v, "tensor")
                    ctx.mark(v.tensor_copy(out=AT_sb[:],
                                           in_=ctx.PB[:, 0:512].rearrange(
                                               "p (c f) -> p c f", c=4)), "vector")
                    ctx.bank_read("PB", "vector")
                AT_mark = ctx.n["vector"]

                @block.tensor
                def _(t, h=h, hh=hh, AT_mark=AT_mark):
                    ctx.wait(t, "vector", AT_mark)
                    if hh == 0:
                        ctx.bank_wait(t, 2)
                        ctx.bank_wait(t, 6)
                    for nrel in range(8):
                        rb = (nrel % 2) * 64
                        bnk = 2 if nrel % 2 == 0 else 6
                        inst = t.matmul(
                            out=ctx.PF[bnk][hh * 64:hh * 64 + 64,
                                            (nrel // 2) * 64:(nrel // 2) * 64 + 64],
                            lhsT=vsc[rb:rb + 64, nrel // 2, h * 64:(h + 1) * 64],
                            rhs=AT_sb[rb:rb + 64, nrel // 2, rb:rb + 64],
                            start=True, stop=True,
                            tile_position=(rb, hh * 64))
                    ctx.mark(inst, "tensor")

            @block.vector
            def _(v):
                ctx.wait(v, "tensor")
                ctx.wait(v, "sync")
                oview = olst[:].rearrange("p (n2 two f) -> p n2 two f", two=2, f=64)
                v.tensor_copy(out=oview[:, :, 0, :],
                              in_=ctx.PF[2][:, 0:256].rearrange(
                                  "p (n f) -> p n f", f=64))
                ctx.mark(v.tensor_copy(out=oview[:, :, 1, :],
                                       in_=ctx.PF[6][:, 0:256].rearrange(
                                           "p (n f) -> p n f", f=64)), "vector")
                ctx.bank_read(2, "vector")
                ctx.bank_read(6, "vector")

            @block.sync
            def _(s, hp=hp, sc=sc):
                ctx.wait(s, "vector")
                ctx.mark(s.dma_start(out=d["dol"][sc, hp], in_=olst[:]),
                         "sync", 16)

            if "s1c" in PUMP_PHASES:
                ctx.exp.pump()
                ctx.exp.pump()
                ctx.exp.pump()

        if "lm2" not in SW1:
            continue
        @block.tensor
        def _(t):
            ctx.wait(t, "vector", paT_mark)
            ctx.bank_wait(t, 3)
            ctx.bank_wait(t, 5)
            for h in range(H):
                for nrel in range(8):
                    rb = (nrel % 2) * 64
                    bnk = 3 if nrel % 2 == 0 else 5
                    inst = t.matmul(
                        out=ctx.PF[bnk][(h % 2) * 64:(h % 2) * 64 + 64,
                                        (h // 2) * 16 + (nrel // 2) * 4:
                                        (h // 2) * 16 + (nrel // 2) * 4 + 4],
                        lhsT=vsc[rb:rb + 64, nrel // 2, h * 64:(h + 1) * 64],
                        rhs=paT_sb[rb:rb + 64, nrel // 2, 4 * h:4 * h + 4],
                        start=True, stop=True,
                        tile_position=(rb, (h % 2) * 64))
            ctx.mark(inst, "tensor")

        @block.vector
        def _(v, sc=sc):
            ctx.wait(v, "tensor")
            dview = sb["lmfT"][:, :, sc * 32:(sc + 1) * 32].rearrange(
                "p c (n2 two m) -> p c n2 two m", two=2, m=4)
            v.tensor_copy(out=dview[:, :, :, 0, :],
                          in_=ctx.PF[3][:, 0:256].rearrange(
                              "p (c n2 m) -> p c n2 m", n2=4, m=4))
            ctx.mark(v.tensor_copy(out=dview[:, :, :, 1, :],
                                   in_=ctx.PF[5][:, 0:256].rearrange(
                                       "p (c n2 m) -> p c n2 m", n2=4, m=4)),
                     "vector")
            ctx.bank_read(3, "vector")
            ctx.bank_read(5, "vector")

    barrier(ctx)
    if "grep" not in SW1:
        return

    for qq in range(4):
        @block.scalar
        def _(sce, qq=qq):
            ctx.wait(sce, "vector")
            ctx.mark(sce.activation(out=scrA[0:32, :],
                                    in_=sb["gz"][:, qq * 512:(qq + 1) * 512],
                                    func=AF.Tanh, scale=0.5), "scalar")

        @block.vector
        def _(v, qq=qq):
            ctx.wait(v, "scalar")
            ctx.mark(v.tensor_scalar(out=sb["gz"][:, qq * 512:(qq + 1) * 512],
                                     in0=scrA[0:32, :], scalar1=0.5,
                                     scalar2=0.5, op0=ALU.mult, op1=ALU.add),
                     "vector")

    @block.sync
    def _(s):
        ctx.wait(s, "vector")
        ctx.mark(s.dma_start(out=d["dgrep"][:], in_=sb["gz"][:]), "sync", 16)


def _wproj_to_chunks(ctx, wname, bias_sb, rhs_sb, out_sb, resid_sb=None,
                     tag=""):
    block, d, sb = ctx.block, ctx.d, ctx.sb
    wqb = [sb["wqb0"], sb["wqb1"]]
    wl_free = ctx.wqb_free
    for oc in range(NKC):
        slot = oc % 2

        @block.gpsimd
        def _(g, oc=oc, slot=slot):
            if wl_free[slot]:
                ctx.wait(g, "tensor", wl_free[slot])
            ctx.mark(g.dma_start(out=wqb[slot][:], in_=d[wname][oc]),
                     "gpsimd", 16)

        bank = oc % 2

        @block.tensor
        def _(t, slot=slot, bank=bank, oc=oc):
            ctx.wait(t, "gpsimd")
            if oc == 0:
                ctx.wait(t, "vector")
                ctx.wait(t, "scalar")
            ctx.bank_wait(t, bank)
            for kc in range(NKC):
                inst = t.matmul(out=ctx.PF[bank][:, 0:NL],
                                lhsT=wqb[slot][:, kc, :],
                                rhs=rhs_sb[:, kc, :],
                                start=(kc == 0), stop=(kc == NKC - 1))
            wl_free[slot] = ctx.mark(inst, "tensor")

        if resid_sb is None:
            @block.scalar
            def _(sce, oc=oc, bank=bank):
                ctx.wait(sce, "tensor")
                ctx.mark(sce.activation(out=out_sb[:, oc, :],
                                        in_=ctx.PF[bank][:, 0:NL],
                                        func=AF.Identity,
                                        bias=bias_sb[:, oc:oc + 1], scale=1.0),
                         "scalar")
                ctx.bank_read(bank, "scalar")
        else:
            @block.vector
            def _(v, oc=oc, bank=bank):
                ctx.wait(v, "tensor")
                ctx.mark(v.scalar_tensor_tensor(
                    out=out_sb[:, oc, :], in0=ctx.PF[bank][:, 0:NL],
                    scalar=bias_sb[:, oc:oc + 1], in1=resid_sb[:, oc, :],
                    op0=ALU.add, op1=ALU.add), "vector")
                ctx.bank_read(bank, "vector")


def _vproj_tok(ctx, wname, kbias_off, lhsT_sb, out_sb):
    block, d, sb = ctx.block, ctx.d, ctx.sb
    wvb = [sb["wvb0"], sb["wvb1"]]
    wv_free = ctx.wvb_free
    for vb in range(8):
        slot = vb % 2
        bank = 2 + vb % 2

        @block.gpsimd
        def _(g, vb=vb, slot=slot):
            if wv_free[slot]:
                ctx.wait(g, "tensor", wv_free[slot])
            ctx.mark(g.dma_start(out=wvb[slot][:], in_=d[wname][vb]),
                     "gpsimd", 16)

        @block.tensor
        def _(t, vb=vb, slot=slot, bank=bank):
            ctx.wait(t, "gpsimd")
            ctx.wait(t, "vector")
            ctx.bank_wait(t, bank)
            for kc in range(NKC):
                t.matmul(out=ctx.PF[bank][:, 0:256], lhsT=lhsT_sb[:, kc, :],
                         rhs=wvb[slot][:, kc, :],
                         start=(kc == 0), stop=False)
            inst = t.matmul(out=ctx.PF[bank][:, 0:256], lhsT=sb["ones_sb"][:],
                            rhs=sb["kbias"][0:1, kbias_off + vb * 256:
                                            kbias_off + (vb + 1) * 256],
                            start=False, stop=True)
            wv_free[slot] = ctx.mark(inst, "tensor")

        @block.vector
        def _(v, vb=vb, bank=bank):
            ctx.wait(v, "tensor")
            ctx.mark(v.tensor_copy(out=out_sb[:, vb * 256:(vb + 1) * 256],
                                   in_=ctx.PF[bank][:, 0:256]), "vector")
            ctx.bank_read(bank, "vector")


def p_landmark(ctx):
    block, d, sb = ctx.block, ctx.d, ctx.sb
    scrA, scrB = sb["scrA"], sb["scrB"]
    red = sb["red"]
    barrier(ctx)
    _wproj_to_chunks(ctx, "wlq", sb["b_blq"], sb["lmfT"], sb["lqT"])
    _wproj_to_chunks(ctx, "wlk", sb["b_blk"], sb["lmfT"], sb["lkT"])
    _vproj_tok(ctx, "wlv", D, sb["lmfT"], sb["lvtok"])
    la_sb = ctx.alloc("la_sb", [128, 128], BF16)
    laT_sb = ctx.alloc("laT_sb", [128, 128], BF16)

    for h in range(H):
        hb = (h % 2) * 64
        bank = h % 2

        @block.tensor
        def _(t, h=h, hb=hb, bank=bank):
            ctx.bank_wait(t, bank)
            ctx.mark(t.matmul(out=ctx.PF[bank][:, 0:128],
                              lhsT=sb["lqT"][hb:hb + 64, h // 2, :],
                              rhs=sb["lkT"][hb:hb + 64, h // 2, :],
                              start=True, stop=True,
                              tile_position=(hb, 0)), "tensor")

        @block.vector
        def _(v, bank=bank):
            ctx.wait(v, "tensor")
            ctx.wait(v, "scalar")
            ctx.vguard(v, v.reduce_max(red[:, 0:1], ctx.PF[bank][:, 0:128],
                                       axis=mybir.AxisListType.X))
            v.tensor_scalar_mul(red[:, 1:2], red[:, 0:1], -SCALE)
            ctx.mark(v.tensor_copy(out=red[:, 30:31], in_=red[:, 0:1]),
                     "vector")
        mneg_mark = ctx.n["vector"]

        @block.scalar
        def _(sce, bank=bank, mneg_mark=mneg_mark):
            ctx.wait(sce, "vector", mneg_mark)
            ctx.mark(sce.activation(out=scrA[:, 0:128], in_=ctx.PF[bank][:, 0:128],
                                    func=AF.Exp, scale=SCALE,
                                    bias=red[:, 1:2]), "scalar")
            ctx.bank_read(bank, "scalar")

        @block.vector
        def _(v):
            ctx.wait(v, "scalar")
            v.tensor_mul(scrB[:, 0:128], scrA[:, 0:128], sb["mlm"][:])
            ctx.vguard(v, v.reduce_sum(red[:, 2:3], scrB[:, 0:128],
                                       axis=mybir.AxisListType.X))
            ctx.vguard(v, v.reciprocal(red[:, 3:4], red[:, 2:3]))
            ctx.mark(v.tensor_scalar_mul(la_sb[:], scrB[:, 0:128],
                                         red[:, 3:4]), "vector")
        la_mark = ctx.n["vector"]

        @block.tensor
        def _(t, la_mark=la_mark):
            ctx.wait(t, "vector", la_mark)
            ctx.bank_wait(t, "PB")
            ctx.mark(t.transpose(out=ctx.PB[:, 0:128], in_=la_sb[:],
                                 identity=sb["ident_sb"][:]), "tensor")

        @block.vector
        def _(v):
            ctx.wait(v, "tensor")
            ctx.mark(v.tensor_copy(out=laT_sb[:], in_=ctx.PB[:, 0:128]), "vector")
            ctx.bank_read("PB", "vector")
        laT_mark = ctx.n["vector"]

        @block.tensor
        def _(t, h=h, hb=hb, laT_mark=laT_mark):
            ctx.wait(t, "vector", laT_mark)
            if hb == 0:
                ctx.bank_wait(t, 2)
            ctx.mark(t.matmul(out=ctx.PF[2][hb:hb + 64, 0:128],
                              lhsT=sb["lvtok"][:, h * 64:(h + 1) * 64],
                              rhs=laT_sb[:], start=True, stop=True,
                              tile_position=(0, hb)), "tensor")

        if h % 2 == 1:
            @block.vector
            def _(v, h=h):
                ctx.wait(v, "tensor")
                ctx.mark(v.tensor_copy(out=sb["loT"][:, h // 2, :],
                                       in_=ctx.PF[2][:, 0:128]), "vector")
                ctx.bank_read(2, "vector")
        if "lm" in PUMP_PHASES:
            ctx.exp.pump()
            ctx.exp.pump()
            ctx.exp.pump()

    _wproj_to_chunks(ctx, "wlmout", sb["b_blmout"], sb["loT"], sb["lmf2T"],
                     resid_sb=sb["lmfT"])
    _wproj_to_chunks(ctx, "wl2q", sb["b_bl2q"], sb["lmf2T"], sb["l2qT"])
    if DEBUG:
        @block.sync
        def _(s):
            ctx.wait(s, "vector")
            ctx.wait(s, "scalar")
            ctx.mark(s.dma_start(out=d["dbg_lmf"][:], in_=sb["lmfT"][:]), "sync", 16)
            ctx.mark(s.dma_start(out=d["dbg_lmf2"][:], in_=sb["lmf2T"][:]), "sync", 16)
            ctx.mark(s.dma_start(out=d["dbg_lqt"][:], in_=sb["lqT"][:]), "sync", 16)
            ctx.mark(s.dma_start(out=d["dbg_lkt"][:], in_=sb["lkT"][:]), "sync", 16)
            ctx.mark(s.dma_start(out=d["dbg_lvt"][:], in_=sb["lvtok"][:]), "sync", 16)
            ctx.mark(s.dma_start(out=d["dbg_lot"][:], in_=sb["loT"][:]), "sync", 16)


def p_sweep2a(ctx):
    block, d, sb = ctx.block, ctx.d, ctx.sb
    kT = ctx.alloc("kT2", [128, NKC, 512], BF16)
    vsc = ctx.alloc("vsc2", [128, 4, D], BF16)
    scrA, scrB, red = sb["scrA"], sb["scrB"], sb["red"]
    raA = ctx.alloc("raA", [128, 512], BF16)
    raT = ctx.alloc("raT", [128, 4, 128], BF16)
    for sc in range(NSC):
        barrier(ctx)

        @block.sync
        def _(s, sc=sc):
            ctx.mark(s.dma_start(out=kT[:], in_=d["dk"][sc]), "sync", 16)
            for tc_rel in range(4):
                ctx.mark(s.dma_start(out=vsc[:, tc_rel, :],
                                     in_=d["dv"][sc * 4 + tc_rel, :, :]),
                         "sync", 16)

        for hg in range(8):
            bank = hg % 2

            @block.tensor
            def _(t, hg=hg, sc=sc, bank=bank):
                ctx.wait(t, "sync")
                ctx.bank_wait(t, bank)
                for hsub in range(4):
                    h = 4 * hg + hsub
                    for par in range(2):
                        ch = par * 8 + h // 4
                        rb = (h % 4) * 32
                        inst = t.matmul(
                            out=ctx.PF[bank][hsub * 32:(hsub + 1) * 32, :],
                            lhsT=sb["l2qT"][rb:rb + 32, ch,
                                            sc * 32:(sc + 1) * 32],
                            rhs=kT[rb:rb + 32, ch, :],
                            start=(par == 0), stop=(par == 1),
                            tile_position=(rb, hsub * 32))
                ctx.mark(inst, "tensor")

            @block.vector
            def _(v, bank=bank):
                ctx.wait(v, "tensor")
                ctx.wait(v, "scalar")
                ctx.vguard(v, v.reduce_max(red[:, 0:1], ctx.PF[bank][:],
                                           axis=mybir.AxisListType.X))
                v.tensor_scalar_mul(red[:, 1:2], red[:, 0:1], -SCALE)
                ctx.mark(v.tensor_copy(out=red[:, 30:31], in_=red[:, 0:1]),
                         "vector")
            mneg2_mark = ctx.n["vector"]

            @block.scalar
            def _(sce, bank=bank, mneg2_mark=mneg2_mark):
                ctx.wait(sce, "vector", mneg2_mark)
                ctx.mark(sce.activation(out=scrA[:], in_=ctx.PF[bank][:],
                                        func=AF.Exp, scale=SCALE,
                                        bias=red[:, 1:2]), "scalar")
                ctx.bank_read(bank, "scalar")

            @block.vector
            def _(v):
                ctx.wait(v, "scalar")
                v.tensor_mul(scrB[:], scrA[:], sb["mra"][:])
                ctx.vguard(v, v.reduce_sum(red[:, 2:3], scrB[:],
                                           axis=mybir.AxisListType.X))
                ctx.vguard(v, v.reciprocal(red[:, 3:4], red[:, 2:3]))
                ctx.mark(v.tensor_scalar_mul(raA[:], scrB[:], red[:, 3:4]),
                         "vector")
            raA_mark = ctx.n["vector"]

            @block.tensor
            def _(t, raA_mark=raA_mark):
                ctx.wait(t, "vector", raA_mark)
                ctx.bank_wait(t, "PB")
                for pp in range(4):
                    inst = t.transpose(out=ctx.PB[:, pp * 128:(pp + 1) * 128],
                                       in_=raA[:, pp * 128:(pp + 1) * 128],
                                       identity=sb["ident_sb"][:])
                ctx.mark(inst, "tensor")

            @block.vector
            def _(v):
                ctx.wait(v, "tensor")
                ctx.mark(v.tensor_copy(out=raT[:],
                                       in_=ctx.PB[:, 0:512].rearrange(
                                           "p (c f) -> p c f", c=4)), "vector")
                ctx.bank_read("PB", "vector")
            raT_mark = ctx.n["vector"]

            @block.tensor
            def _(t, hg=hg, raT_mark=raT_mark):
                ctx.wait(t, "vector", raT_mark)
                if hg == 0:
                    ctx.bank_wait(t, 2)
                    ctx.bank_wait(t, 3)
                for hsub in range(4):
                    h = 4 * hg + hsub
                    for nrel in range(8):
                        rb = (nrel % 2) * 64
                        bnk = 2 if nrel % 2 == 0 else 3
                        inst = t.matmul(
                            out=ctx.PF[bnk][(h % 2) * 64:(h % 2) * 64 + 64,
                                            (h // 2) * 16 + (nrel // 2) * 4:
                                            (h // 2) * 16 + (nrel // 2) * 4 + 4],
                            lhsT=vsc[rb:rb + 64, nrel // 2, h * 64:(h + 1) * 64],
                            rhs=raT[rb:rb + 64, nrel // 2,
                                    hsub * 32 + nrel * 4:hsub * 32 + nrel * 4 + 4],
                            start=True, stop=True,
                            tile_position=(rb, (h % 2) * 64))
                ctx.mark(inst, "tensor")

            if "s2a" in PUMP_PHASES:
                ctx.exp.pump()
                ctx.exp.pump()
                ctx.exp.pump()
                ctx.exp.pump()

        @block.vector
        def _(v, sc=sc):
            ctx.wait(v, "tensor")
            dview = sb["roT"][:, :, sc * 32:(sc + 1) * 32].rearrange(
                "p c (n2 two m) -> p c n2 two m", two=2, m=4)
            v.tensor_copy(out=dview[:, :, :, 0, :],
                          in_=ctx.PF[2][:, 0:256].rearrange(
                              "p (c n2 m) -> p c n2 m", n2=4, m=4))
            ctx.mark(v.tensor_copy(out=dview[:, :, :, 1, :],
                                   in_=ctx.PF[3][:, 0:256].rearrange(
                                       "p (c n2 m) -> p c n2 m", n2=4, m=4)),
                     "vector")
            ctx.bank_read(2, "vector")
            ctx.bank_read(3, "vector")


def p_lmtail(ctx):
    block, d, sb = ctx.block, ctx.d, ctx.sb
    barrier(ctx)
    _wproj_to_chunks(ctx, "wl2out", sb["b_bl2out"], sb["roT"], sb["lmf3T"],
                     resid_sb=sb["lmf2T"])
    _wproj_to_chunks(ctx, "wgk", sb["b_bgk"], sb["lmf3T"], sb["gkT"])
    _vproj_tok(ctx, "wgv", 2 * D, sb["lmf3T"], sb["gv_sb"])
    if DEBUG:
        @block.sync
        def _(s):
            ctx.wait(s, "vector")
            ctx.wait(s, "scalar")
            ctx.mark(s.dma_start(out=d["dbg_lmf3"][:], in_=sb["lmf3T"][:]), "sync", 16)
            ctx.mark(s.dma_start(out=d["dbg_gk"][:], in_=sb["gkT"][:]), "sync", 16)
            ctx.mark(s.dma_start(out=d["dbg_gv"][:], in_=sb["gv_sb"][:]), "sync", 16)


def p_sweep2b(ctx):
    block, d, sb = ctx.block, ctx.d, ctx.sb
    qT = ctx.alloc("qT2", [128, NKC, 512], BF16)
    scrA, scrB, red = sb["scrA"], sb["scrB"], sb["red"]
    mg = ctx.alloc("mg", [128, NTC, 128], BF16)
    agb = ctx.alloc("agb", [128, 512], BF16)
    agT = ctx.alloc("agT", [128, 4, 128], BF16)

    @block.sync
    def _(s):
        ctx.mark(s.dma_start(out=mg[:], in_=d["mask_g"][:].rearrange(
            "(c p) l -> p c l", p=128)), "sync", 16)
    mg_mark = ctx.n["sync"]
    olT = [ctx.alloc(f"olT{i}", [128, 512], BF16) for i in range(2)]
    grep = [ctx.alloc(f"grep{i}", [128, 512], BF16) for i in range(2)]
    gated = ctx.alloc("gated", [128, NKC, 512], BF16)
    sstg = ctx.alloc("sstg", [128, 512], BF16)
    ol_used = [0, 0]

    for sc in range(NSC):
        barrier(ctx)

        @block.sync
        def _(s, sc=sc):
            ctx.mark(s.dma_start(out=qT[:], in_=d["dq"][sc]), "sync", 16)

        for hp in range(16):
            slot = hp % 2

            @block.sync
            def _(s, hp=hp, sc=sc, slot=slot):
                if hp >= 2:
                    ctx.wait(s, "vector", ol_used[slot])
                ctx.mark(s.dma_start(out=olT[slot][:],
                                     in_=d["dol"][sc, hp]), "sync", 16)
                gsrc = d["dgrep"][2 * hp:2 * hp + 2, sc * 512:(sc + 1) * 512]
                ctx.mark(s.dma_start(out=grep[slot][:],
                                     in_=gsrc.unsqueeze(1).broadcast_to(
                                         [2, 64, 512])), "sync", 16)

            for hh in range(2):
                h = 2 * hp + hh
                bank = hh

                @block.tensor
                def _(t, h=h, bank=bank):
                    ctx.wait(t, "sync")
                    ctx.bank_wait(t, bank)
                    for tc_rel in range(4):
                        for par in range(2):
                            ch = par * 8 + h // 4
                            rb = (h % 4) * 32
                            inst = t.matmul(
                                out=ctx.PF[bank][:, tc_rel * 128:(tc_rel + 1) * 128],
                                lhsT=qT[rb:rb + 32, ch,
                                        tc_rel * 128:(tc_rel + 1) * 128],
                                rhs=sb["gkT"][rb:rb + 32, ch, :],
                                start=(par == 0), stop=(par == 1),
                                tile_position=(rb, 0))
                    ctx.mark(inst, "tensor")

                @block.vector
                def _(v, bank=bank):
                    ctx.wait(v, "tensor")
                    ctx.wait(v, "scalar")
                    s3 = ctx.PF[bank][:].rearrange("p (s f) -> p s f", s=4)
                    ctx.vguard(v, v.reduce_max(red[:, 0:4], s3,
                                               axis=mybir.AxisListType.X))
                    v.tensor_scalar_mul(red[:, 4:8], red[:, 0:4], -SCALE)
                    ctx.mark(v.tensor_copy(out=red[:, 30:31], in_=red[:, 0:1]),
                             "vector")
                mneg3_mark = ctx.n["vector"]

                @block.scalar
                def _(sce, bank=bank, mneg3_mark=mneg3_mark):
                    ctx.wait(sce, "vector", mneg3_mark)
                    for tc_rel in range(4):
                        inst = sce.activation(
                            out=scrA[:, tc_rel * 128:(tc_rel + 1) * 128],
                            in_=ctx.PF[bank][:, tc_rel * 128:(tc_rel + 1) * 128],
                            func=AF.Exp, scale=SCALE,
                            bias=red[:, 4 + tc_rel:5 + tc_rel])
                    ctx.mark(inst, "scalar")
                    ctx.bank_read(bank, "scalar")

                @block.vector
                def _(v, sc=sc):
                    ctx.wait(v, "scalar")
                    ctx.wait(v, "sync", mg_mark)
                    e3 = scrA[:].rearrange("p (s f) -> p s f", s=4)
                    v.tensor_mul(scrB[:].rearrange("p (s f) -> p s f", s=4),
                                 e3, mg[:, sc * 4:(sc + 1) * 4, :])
                    m3 = scrB[:].rearrange("p (s f) -> p s f", s=4)
                    ctx.vguard(v, v.reduce_sum(red[:, 8:12], m3,
                                               axis=mybir.AxisListType.X))
                    ctx.vguard(v, v.tensor_scalar_max(red[:, 12:16],
                                                      red[:, 8:12], 1e-9))
                    ctx.vguard(v, v.reciprocal(red[:, 16:20], red[:, 12:16]))
                    ctx.mark(v.tensor_mul(agb[:].rearrange("p (s f) -> p s f", s=4),
                                          m3,
                                          red[:, 16:20].unsqueeze(2).broadcast_to(
                                              [128, 4, 128])), "vector")
                agb_mark = ctx.n["vector"]

                @block.tensor
                def _(t, agb_mark=agb_mark):
                    ctx.wait(t, "vector", agb_mark)
                    ctx.bank_wait(t, "PB")
                    for tc_rel in range(4):
                        inst = t.transpose(
                            out=ctx.PB[:, tc_rel * 128:(tc_rel + 1) * 128],
                            in_=agb[:, tc_rel * 128:(tc_rel + 1) * 128],
                            identity=sb["ident_sb"][:])
                    ctx.mark(inst, "tensor")

                @block.vector
                def _(v):
                    ctx.wait(v, "tensor")
                    ctx.mark(v.tensor_copy(out=agT[:],
                                           in_=ctx.PB[:, 0:512].rearrange(
                                               "p (c f) -> p c f", c=4)), "vector")
                    ctx.bank_read("PB", "vector")
                agT_mark = ctx.n["vector"]

                @block.tensor
                def _(t, h=h, hh=hh, agT_mark=agT_mark):
                    ctx.wait(t, "vector", agT_mark)
                    if hh == 0:
                        ctx.bank_wait(t, 2)
                    for tc_rel in range(4):
                        inst = t.matmul(
                            out=ctx.PF[2][hh * 64:hh * 64 + 64,
                                          tc_rel * 128:(tc_rel + 1) * 128],
                            lhsT=sb["gv_sb"][:, h * 64:(h + 1) * 64],
                            rhs=agT[:, tc_rel, :], start=True, stop=True,
                            tile_position=(0, hh * 64))
                    ctx.mark(inst, "tensor")

            if DEBUG and sc == 0 and hp == 0:
                @block.sync
                def _(s, slot=slot):
                    ctx.wait(s, "tensor")
                    ctx.wait(s, "vector")
                    ctx.mark(s.dma_start(out=d["dbg_sge"][:], in_=scrA[:]), "sync", 16)
                    ctx.mark(s.dma_start(out=d["dbg_scrB"][:], in_=scrB[:]), "sync", 16)
                    ctx.mark(s.dma_start(out=d["dbg_mg"][:], in_=mg[:, 0:4, :]), "sync", 16)
                    ctx.mark(s.dma_start(out=d["dbg_red"][:], in_=red[:]), "sync", 16)
                    ctx.mark(s.dma_start(out=d["dbg_agb"][:], in_=agb[:]), "sync", 16)
                    ctx.mark(s.dma_start(out=d["dbg_agt"][:],
                                         in_=agT[:].rearrange("p c f -> p (c f)")), "sync", 16)
                    ctx.mark(s.dma_start(out=d["dbg_olt"][:], in_=olT[slot][:]), "sync", 16)
                    ctx.mark(s.dma_start(out=d["dbg_grep"][:], in_=grep[slot][:]), "sync", 16)

            @block.vector
            def _(v, hp=hp, slot=slot):
                ctx.wait(v, "tensor")
                ctx.wait(v, "sync")
                v.tensor_sub(scrA[:], ctx.PF[2][:], olT[slot][:])
                v.tensor_mul(scrB[:], scrA[:], grep[slot][:])
                ol_used[slot] = ctx.mark(
                    v.tensor_add(gated[:, hp, :], scrB[:], olT[slot][:]),
                    "vector")
                ctx.bank_read(2, "vector")

            for _i in range(5):
                ctx.exp.pump()

        if DEBUG and sc == 0:
            @block.sync
            def _(s):
                ctx.wait(s, "vector")
                ctx.mark(s.dma_start(out=d["dbg_gated"][:], in_=gated[:]),
                         "sync", 16)

        gated_mark = ctx.n["vector"]
        wqb = [sb["wqb0"], sb["wqb1"]]
        wo_free = ctx.wqb_free
        for oc in range(NKC):
            slot = oc % 2
            bank = oc % 2

            @block.gpsimd
            def _(g, oc=oc, slot=slot):
                if wo_free[slot]:
                    ctx.wait(g, "tensor", wo_free[slot])
                ctx.mark(g.dma_start(out=wqb[slot][:], in_=d["wout"][oc]),
                         "gpsimd", 16)

            @block.tensor
            def _(t, slot=slot, bank=bank, oc=oc, gated_mark=gated_mark):
                ctx.wait(t, "gpsimd")
                ctx.bank_wait(t, bank)
                if oc < 2:
                    ctx.wait(t, "vector", gated_mark)
                for kc in range(NKC):
                    inst = t.matmul(out=ctx.PF[bank][:],
                                    lhsT=wqb[slot][:, kc, :],
                                    rhs=gated[:, kc, :],
                                    start=(kc == 0), stop=(kc == NKC - 1))
                wo_free[slot] = ctx.mark(inst, "tensor")

            @block.scalar
            def _(sce, oc=oc, bank=bank):
                ctx.wait(sce, "tensor")
                ctx.wait(sce, "sync")
                ctx.mark(sce.activation(out=sstg[:], in_=ctx.PF[bank][:],
                                        func=AF.Identity, scale=0.5,
                                        bias=sb["b_bout_h"][:, oc:oc + 1]),
                         "scalar")
                ctx.bank_read(bank, "scalar")

            @block.sync
            def _(s, oc=oc, sc=sc):
                ctx.wait(s, "scalar")
                ctx.mark(s.dma_start(out=d["ds"][sc, :, oc, :],
                                     in_=sstg[:]), "sync", 16)


def p_ffn(ctx):
    block, d, sb = ctx.block, ctx.d, ctx.sb
    a = ctx.alloc
    xnf = a("xnf", [128, NKC, 512], BF16)
    sTf = a("sTf", [128, NKC, 512], BF16)
    hT = a("hT", [128, NOB, 512], BF16)
    pT = sTf
    gbuf = [a(f"gbuf{i}", [128, NKC, 128], BF16) for i in range(2)]
    ubuf = [a(f"ubuf{i}", [128, NKC, 128], BF16) for i in range(2)]
    dbuf = [a(f"dbuf{i}", [128, NOB, 128], BF16) for i in range(2)]
    xbuf2 = [a("xbf0", [128, D], F32)] * 2
    ostg = [a("ostg0", [128, D], F32)] * 2
    scr2A = a("scr2A", [128, 512], F32)
    scr2B = a("scr2B", [128, 512], F32)
    scrp = [(sb["scrA"], sb["scrB"]), (scr2A, scr2B)]

    gu_free = [0, 0]
    dn_free = [0, 0]
    x2_free = [0, 0]
    os_free = [0, 0]
    for sc in range(NSC):
        barrier(ctx)

        @block.sync
        def _(s, sc=sc):
            ctx.mark(s.dma_start(out=xnf[:], in_=d["dxn"][sc]), "sync", 16)
            ctx.mark(s.dma_start(out=sTf[:], in_=d["ds"][sc]), "sync", 16)

        for oc in range(NOB):
            slot = oc % 2

            @block.sync
            def _(s, oc=oc, slot=slot, sc=sc):
                if oc >= 2 or sc > 0:
                    ctx.wait(s, "tensor", gu_free[slot])
                ctx.mark(s.dma_start(out=gbuf[slot][:], in_=d["wgT"][oc]),
                         "sync", 16)
                ctx.mark(s.dma_start(out=ubuf[slot][:], in_=d["wuT"][oc]),
                         "sync", 16)

            bg = 0 if oc % 2 == 0 else 4
            bu = 1 if oc % 2 == 0 else 5

            @block.tensor
            def _(t, oc=oc, slot=slot, bg=bg, bu=bu):
                ctx.wait(t, "sync")
                ctx.bank_wait(t, bg)
                ctx.bank_wait(t, bu)
                for kc in range(NKC):
                    t.matmul(out=ctx.PF[bg][:], lhsT=gbuf[slot][:, kc, :],
                             rhs=xnf[:, kc, :], start=(kc == 0),
                             stop=(kc == NKC - 1))
                for kc in range(NKC):
                    inst = t.matmul(out=ctx.PF[bu][:], lhsT=ubuf[slot][:, kc, :],
                                    rhs=xnf[:, kc, :], start=(kc == 0),
                                    stop=(kc == NKC - 1))
                gu_free[slot] = ctx.mark(inst, "tensor")

            sA, sB = scrp[oc % 2]

            @block.scalar
            def _(sce, bg=bg, sA=sA):
                ctx.wait(sce, "tensor")
                ctx.mark(sce.activation(out=sA[:], in_=ctx.PF[bg][:],
                                        func=AF.Tanh, scale=0.5), "scalar")

            @block.vector
            def _(v, oc=oc, bg=bg, bu=bu, sA=sA, sB=sB):
                ctx.wait(v, "scalar")
                v.scalar_tensor_tensor(out=sB[:], in0=sA[:], scalar=1.0,
                                       in1=ctx.PF[bg][:], op0=ALU.add,
                                       op1=ALU.mult)
                ctx.mark(v.tensor_mul(hT[:, oc, :], sB[:], ctx.PF[bu][:]),
                         "vector")
                ctx.bank_read(bg, "vector")
                ctx.bank_read(bu, "vector")

        for oc2 in range(NKC):
            slot = oc2 % 2
            bank = 2 + oc2 % 2

            @block.sync
            def _(s, oc2=oc2, slot=slot, sc=sc):
                if oc2 >= 2 or sc > 0:
                    ctx.wait(s, "tensor", dn_free[slot])
                ctx.mark(s.dma_start(out=dbuf[slot][:], in_=d["wdT"][oc2]),
                         "sync", 16)

            @block.tensor
            def _(t, oc2=oc2, slot=slot, bank=bank):
                ctx.wait(t, "sync")
                if oc2 == 0:
                    ctx.wait(t, "vector")
                ctx.bank_wait(t, bank)
                for ic in range(NOB):
                    inst = t.matmul(out=ctx.PF[bank][:],
                                    lhsT=dbuf[slot][:, ic, :],
                                    rhs=hT[:, ic, :],
                                    start=(ic == 0), stop=(ic == NOB - 1))
                dn_free[slot] = ctx.mark(inst, "tensor")

            sA, sB = scrp[oc2 % 2]

            @block.scalar
            def _(sce, bank=bank, sA=sA):
                ctx.wait(sce, "tensor")
                ctx.mark(sce.activation(out=sA[:], in_=ctx.PF[bank][:],
                                        func=AF.Tanh, scale=0.5), "scalar")

            @block.vector
            def _(v, oc2=oc2, bank=bank, sA=sA, sB=sB):
                ctx.wait(v, "scalar")
                ctx.wait(v, "sync")
                v.scalar_tensor_tensor(out=sB[:], in0=sA[:], scalar=1.0,
                                       in1=ctx.PF[bank][:], op0=ALU.add,
                                       op1=ALU.mult)
                ctx.mark(v.tensor_mul(pT[:, oc2, :], sB[:], sTf[:, oc2, :]),
                         "vector")
                ctx.bank_read(bank, "vector")

        for tc_rel in range(4):
            tc = sc * 4 + tc_rel
            slot = 0

            @block.sync
            def _(s, tc=tc, slot=slot):
                if tc >= 1:
                    ctx.wait(s, "vector", x2_free[slot])
                ctx.mark(s.dma_start(out=xbuf2[slot][:],
                                     in_=d["x"][tc * 128:(tc + 1) * 128, :]),
                         "sync", 16)

            for half in range(2):
                @block.tensor
                def _(t, tc_rel=tc_rel, half=half):
                    if half == 0 and tc_rel == 0:
                        ctx.wait(t, "vector")
                    ctx.bank_wait(t, "PB")
                    for j in range(8):
                        oc2 = half * 8 + j
                        inst = t.transpose(
                            out=ctx.PB[:, j * 128:(j + 1) * 128],
                            in_=pT[:, oc2, tc_rel * 128:(tc_rel + 1) * 128],
                            identity=sb["ident_sb"][:])
                    ctx.mark(inst, "tensor")

                @block.vector
                def _(v, half=half, slot=slot):
                    ctx.wait(v, "tensor")
                    if half == 0:
                        ctx.wait(v, "sync", os_free[slot])
                        ctx.wait(v, "sync")
                    inst = v.tensor_add(
                        ostg[slot][:, half * 1024:(half + 1) * 1024],
                        xbuf2[slot][:, half * 1024:(half + 1) * 1024],
                        ctx.PB[:, 0:1024])
                    w = ctx.mark(inst, "vector")
                    if half == 1:
                        x2_free[slot] = w
                    ctx.bank_read("PB", "vector")

            @block.sync
            def _(s, tc=tc, slot=slot):
                ctx.wait(s, "vector")
                os_free[slot] = ctx.mark(
                    s.dma_start(out=d["out"][tc * 128:(tc + 1) * 128, :],
                                in_=ostg[slot][:]), "sync", 16)


def emit(ctx):
    p_consts(ctx)
    p_bias_stage(ctx)
    ctx.push_scope()
    p_lm_scope_alloc(ctx)
    ctx.exp = Expander(ctx)
    ctx.push_scope()
    p_sweep1(ctx)
    barrier(ctx)
    ctx.pop_scope()
    ctx.push_scope()
    p_post1_alloc(ctx)
    p_landmark(ctx)
    ctx.push_scope()
    p_sweep2a(ctx)
    barrier(ctx)
    ctx.pop_scope()
    p_lmtail(ctx)
    ctx.push_scope()
    p_sweep2b(ctx)
    ctx.exp.finish()
    ctx.tr.finish()
    barrier(ctx)
    ctx.pop_scope()
    barrier(ctx)
    ctx.pop_scope()
    barrier(ctx)
    ctx.pop_scope()
    ctx.push_scope()
    p_ffn(ctx)
    ctx.pop_scope()
    block = ctx.block

    @block.sync
    def _(s):
        for e in ("tensor", "vector", "scalar", "gpsimd"):
            ctx.wait(s, e)
        ctx.wait(s, "sync")


def kernel(**inputs):
    cst = host_prep(inputs)
    x = np.asarray(inputs["L0_attn_x"])
    nc, dd = build_program()
    in_maps = []
    for c in range(8):
        m = dict(cst)
        m["x"] = np.ascontiguousarray(x[c])
        in_maps.append(m)
    res = run_bass_kernel_spmd(nc, in_maps, core_ids=list(range(8)))
    return np.stack([res.results[c]["out"] for c in range(8)], 0)


# version 26
# speedup vs baseline: 1.1350x; 1.0297x over previous
import numpy as np
import ml_dtypes
from contextlib import ExitStack

import concourse.bass as bass
import concourse.mybir as mybir
from concourse.bass_utils import run_bass_kernel_spmd
from concourse.dve_spec import Spec, Src0, Src1, C0, C1, C2, One, eq, lower
from concourse.dve_op import DveOp
from concourse.dve_ops import OPS as _DVE_OPS
from concourse.dve_table_gen import dve_ver_for


def _make_lut2acc():
    for op in _DVE_OPS:
        if op.name == "LUT2ACC":
            return op
    body = Src1 + eq(Src0, C2) * C0 + eq(Src0 - C2, One) * C1
    spec = Spec(body=body)
    ver = dve_ver_for("TRN2")
    sha = lower(spec, ver=ver, perf_en=False, name="LUT2ACC").sha(ver)
    op = DveOp("LUT2ACC", spec, subdim=False, uops_sha={ver: sha},
               reference=lambda in0, in1, s0, s1, imm2:
                   in1 + (in0 == imm2) * s0 + ((in0 - imm2) == 1.0) * s1)
    _DVE_OPS.append(op)
    from concourse import dve_ops as _dvo
    _dvo._SUB_OPCODE_FOR_NAME[op.name] = _dvo._CUSTOM_DVE_ROW_BASE + len(_DVE_OPS) - 1
    return op


LUT2ACC = _make_lut2acc()

F32 = mybir.dt.float32
BF16 = mybir.dt.bfloat16
I32 = mybir.dt.int32
AF = mybir.ActivationFunctionType
ALU = mybir.AluOpType
BF = ml_dtypes.bfloat16

D = 2048
H = 32
DH = 64
M = 4
S = 64
L = 2048
NB = 32
NL = 128
INTER = 5504
NKC = 16
NTC = 16
NSC = 4
SCALE = 0.125
ROPE_BASE = 10000.0
NOB = 43

DEBUG = False
BUILD_UPTO = 99
SW1 = {"a", "b", "v", "g", "psc", "loc", "lm2", "grep"}
PUMP_PHASES = {"s1a", "s1b", "s1c", "lm", "s2a", "s2b"}


def _ps_perm():
    perm = np.zeros(D, np.int64)
    for r in range(D):
        p = r // 1024
        rem = r % 1024
        h = rem // 32
        i = rem % 32
        perm[r] = h * DH + 2 * i + p
    return perm


def _ptile(W, ow):
    Din, O = W.shape
    return np.ascontiguousarray(
        W.reshape(Din // 128, 128, O // ow, ow).transpose(2, 1, 0, 3),
        dtype=np.float32).astype(BF)


def host_prep(inputs):
    g = lambda n: np.asarray(inputs["L0_attn_" + n])
    perm = _ps_perm()

    w_qkv = g("w_qkv")
    b_qkv = g("b_qkv")
    wq = w_qkv[:, 0:D][:, perm]
    wk = w_qkv[:, D:2 * D][:, perm]
    wqk = np.ascontiguousarray(np.concatenate([wq, wk], 1), dtype=np.float32)
    bqk = np.concatenate([b_qkv[0:D][perm], b_qkv[D:2 * D][perm]])
    wv = np.ascontiguousarray(w_qkv[:, 2 * D:3 * D], dtype=np.float32)
    bv = b_qkv[2 * D:3 * D]

    pq = g("pool_q")
    PQ = np.zeros((D, NL), np.float32)
    for r in range(D):
        p = r // 1024
        rem = r % 1024
        h = rem // 32
        i = rem % 32
        PQ[r, 4 * h: 4 * h + 4] = pq[:, h * DH + 2 * i + p]
    PQ = PQ.astype(BF)

    w_lm_qkv = g("w_lm_qkv")
    b_lm_qkv = g("b_lm_qkv")

    def bias_cp(b):
        return np.ascontiguousarray(b.reshape(-1, 128), dtype=np.float32)

    i_of_row = np.arange(128) % 32
    inv = ROPE_BASE ** (-(2.0 * i_of_row) / DH)
    ang = inv[:, None] * np.arange(L, dtype=np.float64)[None, :]
    cosT = np.cos(ang).astype(np.float32).astype(BF)
    sinT = np.sin(ang).astype(np.float32).astype(BF)

    tril = np.tril(np.ones((S, S), np.float32))
    mask_local = np.concatenate([tril, tril], 0).astype(BF)
    blk = np.repeat(np.arange(NB), M)
    mask_lm = (blk[:, None] >= blk[None, :]).astype(np.float32).astype(BF)
    mra = np.zeros((128, 512), np.float32)
    for r in range(128):
        nrel = (r % 32) // 4
        mra[r, nrel * 64:(nrel + 1) * 64] = 1.0
    mask_ra = mra.astype(BF)
    tok = np.arange(L)
    col = np.arange(NL)
    mask_g = ((col[None, :] // 4) < (tok[:, None] // 64)).astype(np.float32).astype(BF)

    lut = np.asarray(inputs["L0_ffn_lut"]).astype(np.float32)
    lutb = np.broadcast_to(lut, (128, 16)).copy()

    cst = {
        "wqk": _ptile(wqk, 128), "bqk": bias_cp(bqk),
        "wv": _ptile(wv, 256), "bv": np.ascontiguousarray(bv, np.float32).reshape(1, D),
        "PQ": PQ,
        "wlq": _ptile(np.ascontiguousarray(w_lm_qkv[:, 0:D]), 128),
        "wlk": _ptile(np.ascontiguousarray(w_lm_qkv[:, D:2 * D]), 128),
        "wlv": _ptile(np.ascontiguousarray(w_lm_qkv[:, 2 * D:3 * D]), 256),
        "blq": bias_cp(b_lm_qkv[0:D]), "blk": bias_cp(b_lm_qkv[D:2 * D]),
        "blv": np.ascontiguousarray(b_lm_qkv[2 * D:3 * D], np.float32).reshape(1, D),
        "wlmout": _ptile(np.asarray(g("w_lm_out"), np.float32), 128),
        "blmout": bias_cp(g("b_lm_out")),
        "wl2q": _ptile(np.ascontiguousarray(g("w_l2t_q")[:, perm]), 128),
        "bl2q": bias_cp(g("b_l2t_q")[perm]),
        "wl2out": _ptile(np.asarray(g("w_l2t_out"), np.float32), 128),
        "bl2out": bias_cp(g("b_l2t_out")),
        "wgk": _ptile(np.ascontiguousarray(g("w_lm_kv")[:, 0:D][:, perm]), 128),
        "bgk": bias_cp(g("b_lm_kv")[0:D][perm]),
        "wgv": _ptile(np.ascontiguousarray(g("w_lm_kv")[:, D:2 * D]), 256),
        "bgv": np.ascontiguousarray(g("b_lm_kv")[D:2 * D], np.float32).reshape(1, D),
        "wout": _ptile(np.asarray(g("w_out"), np.float32), 128),
        "bout": bias_cp(g("b_out")),
        "wgate": np.ascontiguousarray(g("w_gate"), np.float32).astype(BF),
        "bgate": np.ascontiguousarray(g("b_gate"), np.float32).reshape(H, 1),
        "cosT": cosT, "sinT": sinT,
        "mask_local": mask_local, "mask_lm": mask_lm,
        "mask_ra": mask_ra, "mask_g": mask_g,
        "lutb": lutb,
        "ident": np.eye(128).astype(BF),
        "ones": np.ones((1, 128)).astype(BF),
        "bqk_r": np.ascontiguousarray(bqk, np.float32).reshape(1, 2 * D),
        "gidx": np.ascontiguousarray(
            np.asarray(inputs["L0_ffn_gate_idx"]).astype(BF)),
        "uidx": np.ascontiguousarray(
            np.asarray(inputs["L0_ffn_up_idx"]).astype(BF)),
        "didx": np.ascontiguousarray(
            np.asarray(inputs["L0_ffn_down_idx"]).astype(BF)),
    }
    return cst


class Ctx:
    ENGS = ("sync", "tensor", "vector", "scalar", "gpsimd")

    def __init__(self, nc, stack, block):
        self.nc = nc
        self.stack = stack
        self.block = block
        self.h = {}
        self.n = {}
        for e in self.ENGS:
            self.h[e] = stack.enter_context(nc.semaphore("prg_" + e))
            self.n[e] = 0
        self.sb = {}
        self.scopes = []
        self.pf_free = {}
        self.wqb_free = [0, 0]
        self.wvb_free = [0, 0]

    def mark(self, inst, eng, amt=1):
        inst.then_inc(self.h[eng], amt)
        self.n[eng] += amt
        return self.n[eng]

    def wait(self, engobj, eng, val=None):
        v = self.n[eng] if val is None else val
        if v and v > 0:
            engobj.wait_ge(self.h[eng], v)

    def snap(self):
        return dict(self.n)

    def alloc(self, name, shape, dtype):
        st = self.scopes[-1] if self.scopes else self.stack
        t = st.enter_context(self.nc.sbuf_tensor(name, shape, dtype))
        self.sb[name] = t
        return t

    def push_scope(self):
        self.scopes.append(ExitStack())

    def pop_scope(self):
        self.scopes.pop().close()

    def vguard(self, v, inst):
        self.mark(inst, "vector")
        self.wait(v, "vector")

    def bank_wait(self, engobj, bank):
        fr = self.pf_free.get(bank)
        if fr is not None:
            self.wait(engobj, fr[0], fr[1])

    def bank_read(self, bank, eng):
        self.pf_free[bank] = (eng, self.n[eng])


def barrier(ctx):
    snap = ctx.snap()
    for name in Ctx.ENGS:
        def _f(e, snap=snap, me=name):
            for other, v in snap.items():
                if v > 0 and other != me:
                    e.wait_ge(ctx.h[other], v)
        getattr(ctx.block, name)(_f)


def build_program():
    nc = bass.Bass("TRN2", target_bir_lowering=False)
    dd = {}

    def din(name, shape, dtype):
        dd[name] = nc.dram_tensor(name, list(shape), dtype, kind="ExternalInput")

    def dint(name, shape, dtype):
        kind = "ExternalOutput" if DEBUG else "Internal"
        dd[name] = nc.dram_tensor(name, list(shape), dtype, kind=kind)

    din("x", (L, D), F32)
    din("wqk", (32, 128, 16, 128), BF16); din("bqk", (32, 128), F32)
    din("wv", (8, 128, 16, 256), BF16); din("bv", (1, D), F32)
    din("PQ", (D, NL), BF16)
    din("wlq", (16, 128, 16, 128), BF16); din("wlk", (16, 128, 16, 128), BF16)
    din("wlv", (8, 128, 16, 256), BF16)
    din("blq", (16, 128), F32); din("blk", (16, 128), F32); din("blv", (1, D), F32)
    din("wlmout", (16, 128, 16, 128), BF16); din("blmout", (16, 128), F32)
    din("wl2q", (16, 128, 16, 128), BF16); din("bl2q", (16, 128), F32)
    din("wl2out", (16, 128, 16, 128), BF16); din("bl2out", (16, 128), F32)
    din("wgk", (16, 128, 16, 128), BF16); din("bgk", (16, 128), F32)
    din("wgv", (8, 128, 16, 256), BF16); din("bgv", (1, D), F32)
    din("wout", (16, 128, 16, 128), BF16); din("bout", (16, 128), F32)
    din("wgate", (D, H), BF16); din("bgate", (H, 1), F32)
    din("cosT", (128, L), BF16); din("sinT", (128, L), BF16)
    din("mask_local", (128, 64), BF16)
    din("mask_lm", (128, 128), BF16)
    din("mask_ra", (128, 512), BF16)
    din("mask_g", (L, NL), BF16)
    din("lutb", (128, 16), F32)
    din("gidx", (INTER, D), BF16)
    din("uidx", (INTER, D), BF16)
    din("didx", (D, INTER), BF16)
    din("ident", (128, 128), BF16)
    din("ones", (1, 128), BF16)
    din("bqk_r", (1, 2 * D), F32)

    dd["out"] = nc.dram_tensor("out", [L, D], F32, kind="ExternalOutput")

    dint("dxn", (NSC, 128, NKC, 512), BF16)
    dint("dq", (NSC, 128, NKC, 512), BF16)
    dint("dk", (NSC, 128, NKC, 512), BF16)
    dint("dv", (NTC, 128, D), BF16)
    dint("dol", (NSC, NKC, 128, 512), BF16)
    dint("ds", (NSC, 128, NKC, 512), BF16)
    dint("dgrep", (H, L), BF16)
    dd["wg_nat"] = nc.dram_tensor("wg_nat", [INTER, D], BF16, kind="Internal")
    dd["wu_nat"] = nc.dram_tensor("wu_nat", [INTER, D], BF16, kind="Internal")
    dd["wd_nat"] = nc.dram_tensor("wd_nat", [D, INTER], BF16, kind="Internal")
    dint("wgT", (NOB, 128, NKC, 128), BF16)
    dint("wuT", (NOB, 128, NKC, 128), BF16)
    dint("wdT", (NKC, 128, NOB, 128), BF16)
    if DEBUG:
        dint("dbg_lqt", (128, NKC, 128), BF16)
        dint("dbg_lkt", (128, NKC, 128), BF16)
        dint("dbg_lvt", (128, D), BF16)
        dint("dbg_lot", (128, NKC, 128), BF16)
        dint("dbg_sge", (128, 512), F32)
        dint("dbg_scrB", (128, 512), F32)
        dint("dbg_mg", (128, 4, 128), BF16)
        dint("dbg_red", (128, 32), F32)
        dint("dbg_agb", (128, 512), BF16)
        dint("dbg_agt", (128, 512), BF16)
        dint("dbg_olt", (128, 512), BF16)
        dint("dbg_grep", (128, 512), BF16)
        dint("dbg_og", (128, 512), F32)
        dint("dbg_lmf", (128, NKC, NL), BF16)
        dint("dbg_lmf2", (128, NKC, NL), BF16)
        dint("dbg_lmf3", (128, NKC, NL), BF16)
        dint("dbg_gk", (128, NKC, NL), BF16)
        dint("dbg_gv", (128, D), BF16)
        dint("dbg_gated", (128, NKC, 512), BF16)

    with ExitStack() as stack:
        block = stack.enter_context(nc.Block())
        ctx = Ctx(nc, stack, block)
        ctx.d = dd
        ctx.PF = [stack.enter_context(nc.psum_tensor(f"pf{i}", [128, 512], F32))
                  for i in range(7)]
        ctx.PB = stack.enter_context(nc.psum_tensor("pbt", [128, 1024], BF16))
        emit(ctx)

    return nc, dd


def p_consts(ctx):
    block, d = ctx.block, ctx.d
    a = ctx.alloc
    a("mloc", [128, 64], BF16)
    a("lut_sb", [128, 16], F32)
    a("luth_sb", [128, 16], F32)
    a("ident_sb", [128, 128], BF16)
    a("identf", [128, 128], F32)
    a("ones_sb", [1, 128], BF16)
    a("bqkr", [1, 2 * D], BF16)
    a("wgate_sb", [128, NKC, H], BF16)
    a("bgate_sb", [H, 1], F32)
    a("bstage", [48, 128], F32)
    a("scrA", [128, 512], F32)
    a("scrB", [128, 512], F32)
    a("stat", [128, 8], F32)
    a("red", [128, 32], F32)
    a("epsc", [128, 1], F32)
    bias_c = {"blq": 16, "blk": 16, "blmout": 16, "bl2q": 16,
              "bl2out": 16, "bgk": 16, "bout": 16}
    for nm, c in bias_c.items():
        a("b_" + nm, [128, c], F32)
    a("b_bout_h", [128, 16], F32)

    sb = ctx.sb

    @block.sync
    def _(s):
        ctx.mark(s.dma_start(out=sb["mloc"][:], in_=d["mask_local"][:]), "sync", 16)
        ctx.mark(s.dma_start(out=sb["lut_sb"][:], in_=d["lutb"][:]), "sync", 16)
        ctx.mark(s.dma_start(out=sb["ident_sb"][:], in_=d["ident"][:]), "sync", 16)
        ctx.mark(s.dma_start(out=sb["ones_sb"][:], in_=d["ones"][:]), "sync", 16)
        ctx.mark(s.dma_start(out=sb["bgate_sb"][:], in_=d["bgate"][:]), "sync", 16)

    @block.gpsimd
    def _(g):
        ctx.mark(g.dma_start(out=sb["wgate_sb"][:], in_=d["wgate"][:].rearrange(
            "(c p) o -> p c o", p=128)), "gpsimd", 16)
        ctx.mark(g.dma_start(out=sb["bqkr"][:], in_=d["bqk_r"][:]), "gpsimd", 16)

    @block.vector
    def _(v):
        ctx.wait(v, "sync")
        ctx.mark(v.tensor_scalar_mul(sb["luth_sb"][:], sb["lut_sb"][:], 0.5), "vector")
        ctx.mark(v.tensor_copy(out=sb["identf"][:], in_=sb["ident_sb"][:]), "vector")
        ctx.mark(v.memset(sb["epsc"][:], 1e-5), "vector")


def p_bias_stage(ctx):
    block, d, sb = ctx.block, ctx.d, ctx.sb
    bias_c = {"blq": 16, "blk": 16, "blmout": 16, "bl2q": 16,
              "bl2out": 16, "bgk": 16, "bout": 16}
    for nm, C in bias_c.items():
        tgt = sb["b_" + nm]

        @block.sync
        def _(s, nm=nm, C=C):
            ctx.wait(s, "tensor")
            ctx.mark(s.dma_start(out=sb["bstage"][0:C, :], in_=d[nm][:]), "sync", 16)

        @block.tensor
        def _(t, C=C):
            ctx.wait(t, "sync")
            ctx.wait(t, "vector")
            ctx.mark(t.transpose(out=ctx.PF[6][:, 0:C], in_=sb["bstage"][0:C, :],
                                 identity=sb["identf"][0:C, 0:C]), "tensor")

        @block.vector
        def _(v, tgt=tgt, C=C):
            ctx.wait(v, "tensor")
            ctx.mark(v.tensor_copy(out=tgt[:], in_=ctx.PF[6][:, 0:C]), "vector")

    @block.vector
    def _(v):
        ctx.mark(v.tensor_scalar_mul(sb["b_bout_h"][:], sb["b_bout"][:], 0.5),
                 "vector")


class Expander:

    def __init__(self, ctx):
        a = ctx.alloc
        self.ctx = ctx
        self.idxbuf = [a(f"idx{i}", [128, 2752], BF16) for i in range(2)]
        self.sel = a("sel", [128, 16, 512], BF16)
        self.wt = [a(f"wt{i}", [128, 2752], BF16) for i in range(2)]
        d = ctx.d
        sb = ctx.sb
        jobs = []
        for b in range(NOB):
            jobs.append((d["gidx"], d["wg_nat"], b, 0, D, sb["lut_sb"]))
        for b in range(NOB):
            jobs.append((d["uidx"], d["wu_nat"], b, 0, D, sb["lut_sb"]))
        for b in range(NKC):
            jobs.append((d["didx"], d["wd_nat"], b, 0, 2752, sb["luth_sb"]))
            jobs.append((d["didx"], d["wd_nat"], b, 2752, 2752, sb["luth_sb"]))
        self.jobs = jobs
        self.li = 0
        self.ci = 0
        self.si = 0
        self.job_load = {}
        self.job_spill = {}
        self.read_done = [0, 0]
        self.spill_done = [0, 0]

    def _load(self):
        ctx = self.ctx
        if self.li >= len(self.jobs):
            return
        i = self.li
        (dsrc, ddst, b, c0, ncols, lsb) = self.jobs[i]
        slot = i % 2

        @ctx.block.sync
        def _(s):
            if i >= 2:
                ctx.wait(s, "vector", self.read_done[slot])
            self.job_load[i] = ctx.mark(s.dma_start(
                out=self.idxbuf[slot][:, 0:ncols],
                in_=dsrc[b * 128:(b + 1) * 128, c0:c0 + ncols]), "sync", 16)
        self.li += 1

    def pump(self):
        ctx = self.ctx
        if self.ci == 0 and self.si == 0:
            self._load()
            self._load()
        tr = getattr(ctx, "tr", None)
        self.pump_ct = getattr(self, "pump_ct", 0) + 1
        if tr is not None and (self.pump_ct % 2 == 0 or
                               self.ci >= len(self.jobs)):
            if tr.step():
                return
        if self.ci >= len(self.jobs):
            return
        i = self.ci
        (dsrc, ddst, b, c0, ncols, lsb) = self.jobs[i]
        slot = i % 2
        nsub = (ncols + 511) // 512
        sub = self.si

        @ctx.block.vector
        def _(v):
            if sub == 0:
                ctx.wait(v, "sync", self.job_load[i])
                if i >= 2:
                    ctx.wait(v, "sync", self.spill_done[slot])
            sel = self.sel
            w = min(512, ncols - sub * 512)
            srcv = self.idxbuf[slot][:, sub * 512: sub * 512 + w]
            for t in range(16):
                v.tensor_scalar(out=sel[:, t, 0:w], in0=srcv,
                                scalar1=float(t), scalar2=lsb[:, t:t + 1],
                                op0=ALU.is_equal, op1=ALU.mult)
            if w == 512:
                f = sel[:].rearrange("p c f -> p (c f)")
                v.tensor_add(f[:, 0:4096], f[:, 0:4096], f[:, 4096:8192])
                v.tensor_add(f[:, 0:2048], f[:, 0:2048], f[:, 2048:4096])
                v.tensor_add(f[:, 0:1024], f[:, 0:1024], f[:, 1024:2048])
                inst = v.tensor_add(
                    self.wt[slot][:, sub * 512: sub * 512 + 512],
                    f[:, 0:512], f[:, 512:1024])
            else:
                stride = 1
                while stride < 16:
                    for t0 in range(0, 16, 2 * stride):
                        dst = sel[:, t0, 0:w]
                        if 2 * stride == 16:
                            dst = self.wt[slot][:, sub * 512: sub * 512 + w]
                        inst = v.tensor_add(dst, sel[:, t0, 0:w],
                                            sel[:, t0 + stride, 0:w])
                    stride *= 2
            m = ctx.mark(inst, "vector")
            if sub == nsub - 1:
                self.read_done[slot] = m

        self.si += 1
        if self.si < nsub:
            return

        @ctx.block.sync
        def _(s):
            ctx.wait(s, "vector", self.read_done[slot])
            self.spill_done[slot] = ctx.mark(
                s.dma_start(out=ddst[b * 128:(b + 1) * 128, c0:c0 + ncols],
                            in_=self.wt[slot][:, 0:ncols]), "sync", 16)
            self.job_spill[i] = self.spill_done[slot]
        self.si = 0
        self.ci += 1
        self._load()

    def finish(self):
        while self.ci < len(self.jobs):
            self.pump()


class TrRound:

    def __init__(self, ctx):
        a = ctx.alloc
        self.ctx = ctx
        self.tb = [a(f"trb{i}", [128, 512], BF16) for i in range(2)]
        d = ctx.d
        steps = []
        fams = [(d["wg_nat"], d["wgT"], INTER, NKC, 0),
                (d["wu_nat"], d["wuT"], INTER, NKC, 43),
                (d["wd_nat"], d["wdT"], D, NOB, 86)]
        for (src_, dst, O, HB, base) in fams:
            nog = (O + 511) // 512
            for hb in range(HB):
                for og in range(nog):
                    rows = min(512, O - og * 512)
                    ntile = (rows + 127) // 128
                    if base < 86:
                        ready = base + min(4 * og + 3, 42)
                    else:
                        ready = 86 + 8 * og + 7
                    steps.append((src_, dst, og, hb, rows, ntile, ready))
        self.steps = steps
        self.i = 0
        self.pending = None
        self.store_done = [0, 0]
        self.load_done = [0, 0]

    def _store(self):
        ctx = self.ctx
        k = self.pending
        (src_, dst, og, hb, rows, ntile, ready) = self.steps[k]
        slot = k % 2

        @ctx.block.scalar
        def _(sce):
            ctx.wait(sce, "sync", self.load_done[slot])
            for j in range(ntile):
                w = min(128, rows - j * 128)
                inst = sce.dma_start(
                    out=dst[og * 4 + j, :, hb, 0:w],
                    in_=self.tb[slot][:, j * 128: j * 128 + w])
                self.store_done[slot] = ctx.mark(inst, "scalar", 16)
        self.pending = None

    def step(self):
        ctx = self.ctx
        did = False
        if self.pending is not None:
            self._store()
            did = True
        if self.i >= len(self.steps):
            return did
        (src_, dst, og, hb, rows, ntile, ready) = self.steps[self.i]
        rmark = ctx.exp.job_spill.get(ready)
        if rmark is None:
            return did
        slot = self.i % 2

        @ctx.block.sync
        def _(s):
            ctx.wait(s, "sync", rmark)
            if self.store_done[slot]:
                ctx.wait(s, "scalar", self.store_done[slot])
            self.load_done[slot] = ctx.mark(s.dma_start_transpose(
                out=self.tb[slot][:, 0:rows],
                in_=src_[og * 512: og * 512 + rows,
                         hb * 128:(hb + 1) * 128]), "sync", 16)
        self.pending = self.i
        self.i += 1
        return True

    def finish(self):
        while self.i < len(self.steps) or self.pending is not None:
            if not self.step():
                raise RuntimeError("trround source not spilled at finish")


def p_lm_scope_alloc(ctx):
    block, d, sb = ctx.block, ctx.d, ctx.sb
    a = ctx.alloc
    a("lmfT", [128, NKC, NL], BF16)
    a("gz", [H, L], BF16)
    a("kbias", [1, 3 * D], BF16)
    for i in range(2):
        a(f"wqb{i}", [128, NKC, 128], BF16)
        a(f"wvb{i}", [128, NKC, 256], BF16)

    @block.gpsimd
    def _(g):
        ctx.mark(g.dma_start(out=sb["kbias"][:, 0:D], in_=d["bv"][:]), "gpsimd", 16)
        ctx.mark(g.dma_start(out=sb["kbias"][:, D:2 * D], in_=d["blv"][:]), "gpsimd", 16)
        ctx.mark(g.dma_start(out=sb["kbias"][:, 2 * D:3 * D], in_=d["bgv"][:]), "gpsimd", 16)


def p_post1_alloc(ctx):
    ctx.tr = TrRound(ctx)
    _a0 = ctx.alloc
    _a0("mlm", [128, 128], BF16)
    _a0("mra", [128, 512], BF16)

    @ctx.block.sync
    def _(s):
        ctx.mark(s.dma_start(out=ctx.sb["mlm"][:], in_=ctx.d["mask_lm"][:]),
                 "sync", 16)
        ctx.mark(s.dma_start(out=ctx.sb["mra"][:], in_=ctx.d["mask_ra"][:]),
                 "sync", 16)

    a = ctx.alloc
    a("lmf2T", [128, NKC, NL], BF16)
    a("lmf3T", [128, NKC, NL], BF16)
    a("l2qT", [128, NKC, NL], BF16)
    a("gkT", [128, NKC, NL], BF16)
    a("lqT", [128, NKC, NL], BF16)
    a("lkT", [128, NKC, NL], BF16)
    a("lvtok", [128, D], BF16)
    a("loT", [128, NKC, NL], BF16)
    a("roT", [128, NKC, NL], BF16)
    a("gv_sb", [128, D], BF16)


def p_sweep1(ctx):
    block, d, sb = ctx.block, ctx.d, ctx.sb
    a = ctx.alloc
    xbuf = [a(f"xb{i}", [128, D], F32) for i in range(2)]
    xnb = [a("xnb0", [128, D], BF16)] * 2
    xnT = a("xnT", [128, NKC, 512], BF16)
    qT = a("qT_s", [128, NKC, 512], BF16)
    kT = a("kT_s", [128, NKC, 512], BF16)
    vsc = a("vsc", [128, 4, D], BF16)
    cos_sb = a("cos_sb", [128, 512], BF16)
    sin_sb = a("sin_sb", [128, 512], BF16)
    pq_sb = a("pq_sb", [128, NKC, 128], BF16)
    wqb = [ctx.sb["wqb0"], ctx.sb["wqb1"]]
    wvb = [ctx.sb["wvb0"], ctx.sb["wvb1"]]
    scrA = sb["scrA"]
    scrB = sb["scrB"]
    stat = sb["stat"]
    red = sb["red"]
    A_sb = a("A_sb", [128, 4, 64], BF16)
    AT_sb = a("AT_sb", [128, 4, 128], BF16)
    pa_sb = a("pa_sb", [128, 512], BF16)
    paT_sb = a("paT_sb", [128, 4, 128], BF16)
    olst = a("olst", [128, 512], BF16)

    @block.gpsimd
    def _(g):
        ctx.mark(g.dma_start(out=pq_sb[:], in_=d["PQ"][:].rearrange(
            "(c p) o -> p c o", p=128)), "gpsimd", 16)

    xb_free = [0, 0]
    xnb_free = [0, 0]
    wq_free = ctx.wqb_free
    wv_free = ctx.wvb_free
    for sc in range(NSC):
        @block.gpsimd
        def _(g, sc=sc):
            ctx.wait(g, "vector", ctx.n["vector"])
            ctx.mark(g.dma_start(out=cos_sb[:],
                                 in_=d["cosT"][:, sc * 512:(sc + 1) * 512]),
                     "gpsimd", 16)
            ctx.mark(g.dma_start(out=sin_sb[:],
                                 in_=d["sinT"][:, sc * 512:(sc + 1) * 512]),
                     "gpsimd", 16)

        prev_lnstats = [ctx.n["vector"]]
        for tc_rel in range(4) if "a" in SW1 else []:
            tc = sc * 4 + tc_rel
            slot = tc % 2

            xld = [0]

            @block.sync
            def _(s, tc=tc, slot=slot):
                if tc >= 2:
                    ctx.wait(s, "vector", xb_free[slot])
                xld[0] = ctx.mark(
                    s.dma_start(out=xbuf[slot][:],
                                in_=d["x"][tc * 128:(tc + 1) * 128, :]),
                    "sync", 16)

            @block.scalar
            def _(sce, slot=slot):
                ctx.wait(sce, "sync", xld[0])
                ctx.wait(sce, "vector", prev_lnstats[0])
                x = xbuf[slot][:]
                sce.activation(out=scrA[:], in_=x[:, 0:512], func=AF.Square)
                sce.activation(out=scrB[:], in_=x[:, 512:1024], func=AF.Square)
                ctx.mark(sce.activation(out=xnb[0][:, 0:512], in_=x[:, 0:512],
                                        func=AF.Square), "scalar")

            @block.vector
            def _(v):
                ctx.wait(v, "scalar")
                v.reduce_sum(red[:, 12:13], scrA[:], axis=mybir.AxisListType.X)
                ctx.mark(v.reduce_sum(red[:, 13:14], scrB[:],
                                      axis=mybir.AxisListType.X), "vector")

            @block.scalar
            def _(sce, slot=slot):
                ctx.wait(sce, "vector")
                x = xbuf[slot][:]
                sce.activation(out=scrA[:], in_=x[:, 1024:1536], func=AF.Square)
                sce.activation(out=scrB[:], in_=x[:, 1536:2048], func=AF.Square)
                ctx.mark(sce.activation(out=xnb[0][:, 0:512],
                                        in_=x[:, 1024:1536],
                                        func=AF.Square), "scalar")

            @block.vector
            def _(v, slot=slot):
                ctx.wait(v, "scalar")
                ctx.wait(v, "tensor")
                x = xbuf[slot][:]
                v.reduce_sum(red[:, 14:15], scrA[:], axis=mybir.AxisListType.X)
                v.reduce_sum(red[:, 15:16], scrB[:], axis=mybir.AxisListType.X)
                ctx.vguard(v, v.reduce_sum(stat[:, 1:2], x,
                                           axis=mybir.AxisListType.X))
                v.tensor_add(red[:, 8:9], red[:, 12:13], red[:, 13:14])
                ctx.vguard(v, v.tensor_add(red[:, 9:10], red[:, 14:15],
                                           red[:, 15:16]))
                v.tensor_add(red[:, 10:11], red[:, 8:9], red[:, 9:10])
                ctx.vguard(v, v.tensor_scalar_mul(stat[:, 2:3], stat[:, 1:2],
                                                  1.0 / D))
                v.tensor_scalar_mul(stat[:, 0:1], red[:, 10:11], 1.0 / D)
                ctx.vguard(v, v.tensor_scalar(out=stat[:, 3:4], in0=stat[:, 2:3],
                                              scalar1=stat[:, 2:3], scalar2=-1.0,
                                              op0=ALU.mult, op1=ALU.mult))
                v.tensor_add(stat[:, 4:5], stat[:, 0:1], stat[:, 3:4])
                ctx.mark(v.tensor_copy(out=red[:, 30:31], in_=stat[:, 4:5]),
                         "vector")
                prev_lnstats[0] = ctx.n["vector"]

            @block.scalar
            def _(sce):
                ctx.wait(sce, "vector")
                sce.activation(out=stat[:, 6:7], in_=stat[:, 4:5], func=AF.Ln,
                               bias=sb["epsc"][:], scale=1.0)
                ctx.mark(sce.activation(out=stat[:, 7:8], in_=stat[:, 4:5],
                                        func=AF.Ln, bias=sb["epsc"][:],
                                        scale=1.0), "scalar")
                ctx.wait(sce, "scalar")
                sce.activation(out=stat[:, 5:6], in_=stat[:, 6:7],
                               func=AF.Exp, scale=-0.5)
                ctx.mark(sce.activation(out=stat[:, 7:8], in_=stat[:, 6:7],
                                        func=AF.Exp, scale=-0.5), "scalar")

            @block.vector
            def _(v, slot=slot, tc=tc):
                ctx.wait(v, "scalar")
                xb_free[slot] = ctx.mark(
                    v.tensor_scalar(out=xnb[slot][:], in0=xbuf[slot][:],
                                    scalar1=stat[:, 2:3], scalar2=stat[:, 5:6],
                                    op0=ALU.subtract, op1=ALU.mult), "vector")

            if "s1a" in PUMP_PHASES:
                ctx.exp.pump()
                ctx.exp.pump()
            for half in range(2):
                @block.tensor
                def _(t, slot=slot, half=half):
                    ctx.wait(t, "vector", xb_free[slot])
                    ctx.bank_wait(t, "PB")
                    for j in range(8):
                        c = half * 8 + j
                        inst = t.transpose(out=ctx.PB[:, j * 128:(j + 1) * 128],
                                           in_=xnb[slot][:, c * 128:(c + 1) * 128],
                                           identity=sb["ident_sb"][:])
                    xnb_free[slot] = ctx.mark(inst, "tensor")

                @block.vector
                def _(v, tc_rel=tc_rel, half=half):
                    ctx.wait(v, "tensor")
                    ctx.mark(v.tensor_copy(
                        out=xnT[:, half * 8:(half + 1) * 8,
                                tc_rel * 128:(tc_rel + 1) * 128],
                        in_=ctx.PB[:].rearrange("p (c f) -> p c f", c=8)),
                        "vector")
                    ctx.bank_read("PB", "vector")



        xnT_ready = ctx.n["vector"]
        if "a" in SW1:
            @block.sync
            def _(s, sc=sc):
                ctx.wait(s, "vector")
                ctx.mark(s.dma_start(out=d["dxn"][sc], in_=xnT[:]), "sync", 16)

        rope_prev = [max(xnT_ready, prev_lnstats[0])]
        for grp in range(16) if "b" in SW1 else []:
            is_q = grp < 8
            j = grp % 8
            ce = j if is_q else 16 + j
            co = ce + 8
            dstT = qT if is_q else kT
            ddst = d["dq"] if is_q else d["dk"]
            for par, oc in ((0, ce), (1, co)):
                slot = par

                @block.gpsimd
                def _(g, oc=oc, slot=slot, grp=grp):
                    if wq_free[slot]:
                        ctx.wait(g, "tensor", wq_free[slot])
                    ctx.mark(g.dma_start(out=wqb[slot][:],
                                         in_=d["wqk"][oc]), "gpsimd", 16)

                @block.tensor
                def _(t, slot=slot, par=par, oc=oc):
                    ctx.wait(t, "gpsimd")
                    ctx.wait(t, "vector", xnT_ready)
                    ctx.bank_wait(t, par)
                    for kc in range(NKC):
                        t.matmul(out=ctx.PF[par][:],
                                 lhsT=wqb[slot][:, kc, :],
                                 rhs=xnT[:, kc, :],
                                 start=(kc == 0), stop=False)
                    for q4 in range(4):
                        inst = t.matmul(
                            out=ctx.PF[par][:, q4 * 128:(q4 + 1) * 128],
                            lhsT=sb["bqkr"][0:1, oc * 128:(oc + 1) * 128],
                            rhs=sb["ones_sb"][:],
                            start=False, stop=True)
                    wq_free[slot] = ctx.mark(inst, "tensor")

            @block.vector
            def _(v, j=j, dstT=dstT, sc=sc):
                ctx.wait(v, "tensor")
                ctx.wait(v, "gpsimd")
                ctx.wait(v, "scalar")
                cs = cos_sb[:]
                sn = sin_sb[:]
                v.tensor_mul(scrA[:], ctx.PF[0][:], cs)
                v.tensor_mul(scrB[:], ctx.PF[1][:], sn)
                v.tensor_sub(dstT[:, j, :], scrA[:], scrB[:])
                v.tensor_mul(scrA[:], ctx.PF[0][:], sn)
                v.tensor_mul(scrB[:], ctx.PF[1][:], cs)
                ctx.mark(v.tensor_add(dstT[:, j + 8, :], scrA[:], scrB[:]),
                         "vector")
                rope_prev[0] = ctx.n["vector"]
                ctx.bank_read(0, "vector")
                ctx.bank_read(1, "vector")

            @block.sync
            def _(s, j=j, dstT=dstT, ddst=ddst, sc=sc):
                ctx.wait(s, "vector")
                ctx.mark(s.dma_start(out=ddst[sc, :, j, :],
                                     in_=dstT[:, j, :]), "sync", 16)
                ctx.mark(s.dma_start(out=ddst[sc, :, j + 8, :],
                                     in_=dstT[:, j + 8, :]), "sync", 16)

            if "s1b" in PUMP_PHASES:
                ctx.exp.pump()
                ctx.exp.pump()

        for vb in range(8) if "v" in SW1 else []:
            slot = vb % 2

            @block.gpsimd
            def _(g, vb=vb, slot=slot, sc=sc):
                if wv_free[slot]:
                    ctx.wait(g, "tensor", wv_free[slot])
                ctx.mark(g.dma_start(out=wvb[slot][:],
                                     in_=d["wv"][vb]), "gpsimd", 16)

            for tc_rel in range(4):
                bank = 2 + (tc_rel % 2)

                @block.tensor
                def _(t, vb=vb, slot=slot, tc_rel=tc_rel, bank=bank):
                    if tc_rel == 0:
                        ctx.wait(t, "gpsimd")
                    ctx.wait(t, "vector", xnT_ready)
                    ctx.bank_wait(t, bank)
                    for kc in range(NKC):
                        t.matmul(out=ctx.PF[bank][:, 0:256],
                                 lhsT=xnT[:, kc, tc_rel * 128:(tc_rel + 1) * 128],
                                 rhs=wvb[slot][:, kc, :],
                                 start=(kc == 0), stop=False)
                    inst = t.matmul(out=ctx.PF[bank][:, 0:256], lhsT=sb["ones_sb"][:],
                                    rhs=sb["kbias"][0:1, vb * 256:(vb + 1) * 256],
                                    start=False, stop=True)
                    w = ctx.mark(inst, "tensor")
                    if tc_rel == 3:
                        wv_free[slot] = w

                @block.vector
                def _(v, vb=vb, tc_rel=tc_rel, bank=bank):
                    ctx.wait(v, "tensor")
                    ctx.mark(v.tensor_copy(
                        out=vsc[:, tc_rel, vb * 256:(vb + 1) * 256],
                        in_=ctx.PF[bank][:, 0:256]), "vector")
                    ctx.bank_read(bank, "vector")

        @block.sync
        def _(s, sc=sc):
            ctx.wait(s, "vector")
            for tc_rel in range(4):
                ctx.mark(s.dma_start(out=d["dv"][sc * 4 + tc_rel, :, :],
                                     in_=vsc[:, tc_rel, :]), "sync", 16)

        if "g" not in SW1:
            continue
        @block.tensor
        def _(t):
            ctx.wait(t, "vector", xnT_ready)
            ctx.wait(t, "gpsimd")
            ctx.bank_wait(t, 4)
            for kc in range(NKC):
                inst = t.matmul(out=ctx.PF[4][0:32, :],
                                lhsT=sb["wgate_sb"][:, kc, :],
                                rhs=xnT[:, kc, :],
                                start=(kc == 0), stop=(kc == NKC - 1))
            ctx.mark(inst, "tensor")

        @block.scalar
        def _(sce, sc=sc):
            ctx.wait(sce, "tensor")
            ctx.mark(sce.activation(out=sb["gz"][:, sc * 512:(sc + 1) * 512],
                                    in_=ctx.PF[4][0:32, :], func=AF.Identity,
                                    bias=sb["bgate_sb"][:], scale=1.0), "scalar")
            ctx.bank_read(4, "scalar")

        qk_done = ctx.n["vector"]
        if "psc" not in SW1:
            continue
        @block.tensor
        def _(t):
            ctx.wait(t, "vector", qk_done)
            ctx.wait(t, "gpsimd")
            ctx.bank_wait(t, 5)
            for kc in range(NKC):
                inst = t.matmul(out=ctx.PF[5][:], lhsT=pq_sb[:, kc, :],
                                rhs=kT[:, kc, :],
                                start=(kc == 0), stop=(kc == NKC - 1))
            ctx.mark(inst, "tensor")

        @block.vector
        def _(v):
            ctx.wait(v, "tensor")
            ctx.wait(v, "scalar")
            s3 = ctx.PF[5][:].rearrange("p (s f) -> p s f", s=8)
            ctx.vguard(v, v.reduce_max(red[:, 0:8], s3, axis=mybir.AxisListType.X))
            ctx.mark(v.tensor_sub(scrA[:].rearrange("p (s f) -> p s f", s=8), s3,
                                  red[:, 0:8].unsqueeze(2).broadcast_to(
                                      [128, 8, 64])), "vector")
            ctx.bank_read(5, "vector")
        pool_sub = ctx.n["vector"]

        @block.scalar
        def _(sce):
            ctx.wait(sce, "vector", pool_sub)
            ctx.mark(sce.activation(out=scrB[:], in_=scrA[:], func=AF.Exp,
                                    scale=SCALE), "scalar")

        @block.vector
        def _(v):
            ctx.wait(v, "scalar")
            e3 = scrB[:].rearrange("p (s f) -> p s f", s=8)
            ctx.vguard(v, v.reduce_sum(red[:, 8:16], e3, axis=mybir.AxisListType.X))
            ctx.vguard(v, v.reciprocal(red[:, 16:24], red[:, 8:16]))
            ctx.mark(v.tensor_mul(pa_sb[:].rearrange("p (s f) -> p s f", s=8),
                                  e3, red[:, 16:24].unsqueeze(2).broadcast_to(
                                      [128, 8, 64])), "vector")
        pa_mark = ctx.n["vector"]

        @block.tensor
        def _(t):
            ctx.wait(t, "vector", pa_mark)
            ctx.bank_wait(t, "PB")
            for tc_rel in range(4):
                inst = t.transpose(out=ctx.PB[:, tc_rel * 128:(tc_rel + 1) * 128],
                                   in_=pa_sb[:, tc_rel * 128:(tc_rel + 1) * 128],
                                   identity=sb["ident_sb"][:])
            ctx.mark(inst, "tensor")

        @block.vector
        def _(v):
            ctx.wait(v, "tensor")
            ctx.mark(v.tensor_copy(out=paT_sb[:],
                                   in_=ctx.PB[:, 0:512].rearrange(
                                       "p (c f) -> p c f", c=4)), "vector")
            ctx.bank_read("PB", "vector")
        paT_mark = ctx.n["vector"]

        nhp = 16 if "loc" in SW1 else (1 if "loc1" in SW1 else 0)
        for hp in range(nhp):
            for hh in range(2):
                h = 2 * hp + hh
                bank = hh

                @block.tensor
                def _(t, h=h, bank=bank):
                    ctx.wait(t, "vector", qk_done)
                    ctx.bank_wait(t, bank)
                    for nrel in range(8):
                        for par in range(2):
                            ch = par * 8 + h // 4
                            rb = (h % 4) * 32
                            inst = t.matmul(
                                out=ctx.PF[bank][(nrel % 2) * 64:(nrel % 2) * 64 + 64,
                                                 (nrel // 2) * 64:(nrel // 2) * 64 + 64],
                                lhsT=qT[rb:rb + 32, ch, nrel * 64:(nrel + 1) * 64],
                                rhs=kT[rb:rb + 32, ch, nrel * 64:(nrel + 1) * 64],
                                start=(par == 0), stop=(par == 1),
                                tile_position=(rb, (nrel % 2) * 64))
                    ctx.mark(inst, "tensor")

                @block.vector
                def _(v, bank=bank):
                    ctx.wait(v, "tensor")
                    ctx.wait(v, "scalar")
                    s3 = ctx.PF[bank][:, 0:256].rearrange("p (s f) -> p s f", s=4)
                    ctx.vguard(v, v.reduce_max(red[:, 0:4], s3,
                                               axis=mybir.AxisListType.X))
                    ctx.mark(v.tensor_sub(
                        scrA[:, 0:256].rearrange("p (s f) -> p s f", s=4), s3,
                        red[:, 0:4].unsqueeze(2).broadcast_to([128, 4, 64])),
                        "vector")
                    ctx.bank_read(bank, "vector")
                lsub_mark = ctx.n["vector"]

                @block.scalar
                def _(sce, lsub_mark=lsub_mark):
                    ctx.wait(sce, "vector", lsub_mark)
                    ctx.mark(sce.activation(out=scrB[:, 0:256], in_=scrA[:, 0:256],
                                            func=AF.Exp, scale=SCALE), "scalar")

                @block.vector
                def _(v):
                    ctx.wait(v, "scalar")
                    ctx.wait(v, "tensor")
                    e3 = scrB[:, 0:256].rearrange("p (s f) -> p s f", s=4)
                    v.tensor_mul(scrA[:, 0:256].rearrange("p (s f) -> p s f", s=4),
                                 e3, sb["mloc"][:].unsqueeze(1).broadcast_to(
                                     [128, 4, 64]))
                    m3 = scrA[:, 0:256].rearrange("p (s f) -> p s f", s=4)
                    ctx.vguard(v, v.reduce_sum(red[:, 4:8], m3,
                                               axis=mybir.AxisListType.X))
                    ctx.vguard(v, v.reciprocal(red[:, 16:20], red[:, 4:8]))
                    ctx.mark(v.tensor_mul(A_sb[:], m3,
                                          red[:, 16:20].unsqueeze(2).broadcast_to(
                                              [128, 4, 64])), "vector")
                A_mark = ctx.n["vector"]

                @block.tensor
                def _(t, A_mark=A_mark):
                    ctx.wait(t, "vector", A_mark)
                    ctx.bank_wait(t, "PB")
                    for g_ in range(4):
                        t.transpose(out=ctx.PB[0:64, g_ * 128:(g_ + 1) * 128],
                                    in_=A_sb[:, g_, :],
                                    identity=sb["ident_sb"][:])
                        inst = t.transpose(
                            out=ctx.PB[64:128, g_ * 128:(g_ + 1) * 128],
                            in_=A_sb[:, g_, :], identity=sb["ident_sb"][:],
                            tile_position=(0, 64))
                    ctx.mark(inst, "tensor")

                @block.vector
                def _(v):
                    ctx.wait(v, "tensor")
                    ctx.mark(v.tensor_copy(out=AT_sb[:],
                                           in_=ctx.PB[:, 0:512].rearrange(
                                               "p (c f) -> p c f", c=4)), "vector")
                    ctx.bank_read("PB", "vector")
                AT_mark = ctx.n["vector"]

                @block.tensor
                def _(t, h=h, hh=hh, AT_mark=AT_mark):
                    ctx.wait(t, "vector", AT_mark)
                    if hh == 0:
                        ctx.bank_wait(t, 2)
                        ctx.bank_wait(t, 6)
                    for nrel in range(8):
                        rb = (nrel % 2) * 64
                        bnk = 2 if nrel % 2 == 0 else 6
                        inst = t.matmul(
                            out=ctx.PF[bnk][hh * 64:hh * 64 + 64,
                                            (nrel // 2) * 64:(nrel // 2) * 64 + 64],
                            lhsT=vsc[rb:rb + 64, nrel // 2, h * 64:(h + 1) * 64],
                            rhs=AT_sb[rb:rb + 64, nrel // 2, rb:rb + 64],
                            start=True, stop=True,
                            tile_position=(rb, hh * 64))
                    ctx.mark(inst, "tensor")

            @block.vector
            def _(v):
                ctx.wait(v, "tensor")
                ctx.wait(v, "sync")
                oview = olst[:].rearrange("p (n2 two f) -> p n2 two f", two=2, f=64)
                v.tensor_copy(out=oview[:, :, 0, :],
                              in_=ctx.PF[2][:, 0:256].rearrange(
                                  "p (n f) -> p n f", f=64))
                ctx.mark(v.tensor_copy(out=oview[:, :, 1, :],
                                       in_=ctx.PF[6][:, 0:256].rearrange(
                                           "p (n f) -> p n f", f=64)), "vector")
                ctx.bank_read(2, "vector")
                ctx.bank_read(6, "vector")

            @block.sync
            def _(s, hp=hp, sc=sc):
                ctx.wait(s, "vector")
                ctx.mark(s.dma_start(out=d["dol"][sc, hp], in_=olst[:]),
                         "sync", 16)

            if "s1c" in PUMP_PHASES:
                ctx.exp.pump()
                ctx.exp.pump()
                ctx.exp.pump()

        if "lm2" not in SW1:
            continue
        @block.tensor
        def _(t):
            ctx.wait(t, "vector", paT_mark)
            ctx.bank_wait(t, 3)
            ctx.bank_wait(t, 5)
            for h in range(H):
                for nrel in range(8):
                    rb = (nrel % 2) * 64
                    bnk = 3 if nrel % 2 == 0 else 5
                    inst = t.matmul(
                        out=ctx.PF[bnk][(h % 2) * 64:(h % 2) * 64 + 64,
                                        (h // 2) * 16 + (nrel // 2) * 4:
                                        (h // 2) * 16 + (nrel // 2) * 4 + 4],
                        lhsT=vsc[rb:rb + 64, nrel // 2, h * 64:(h + 1) * 64],
                        rhs=paT_sb[rb:rb + 64, nrel // 2, 4 * h:4 * h + 4],
                        start=True, stop=True,
                        tile_position=(rb, (h % 2) * 64))
            ctx.mark(inst, "tensor")

        @block.vector
        def _(v, sc=sc):
            ctx.wait(v, "tensor")
            dview = sb["lmfT"][:, :, sc * 32:(sc + 1) * 32].rearrange(
                "p c (n2 two m) -> p c n2 two m", two=2, m=4)
            v.tensor_copy(out=dview[:, :, :, 0, :],
                          in_=ctx.PF[3][:, 0:256].rearrange(
                              "p (c n2 m) -> p c n2 m", n2=4, m=4))
            ctx.mark(v.tensor_copy(out=dview[:, :, :, 1, :],
                                   in_=ctx.PF[5][:, 0:256].rearrange(
                                       "p (c n2 m) -> p c n2 m", n2=4, m=4)),
                     "vector")
            ctx.bank_read(3, "vector")
            ctx.bank_read(5, "vector")

    barrier(ctx)
    if "grep" not in SW1:
        return

    for qq in range(4):
        @block.scalar
        def _(sce, qq=qq):
            ctx.wait(sce, "vector")
            ctx.mark(sce.activation(out=scrA[0:32, :],
                                    in_=sb["gz"][:, qq * 512:(qq + 1) * 512],
                                    func=AF.Tanh, scale=0.5), "scalar")

        @block.vector
        def _(v, qq=qq):
            ctx.wait(v, "scalar")
            ctx.mark(v.tensor_scalar(out=sb["gz"][:, qq * 512:(qq + 1) * 512],
                                     in0=scrA[0:32, :], scalar1=0.5,
                                     scalar2=0.5, op0=ALU.mult, op1=ALU.add),
                     "vector")

    @block.sync
    def _(s):
        ctx.wait(s, "vector")
        ctx.mark(s.dma_start(out=d["dgrep"][:], in_=sb["gz"][:]), "sync", 16)


def _wproj_to_chunks(ctx, wname, bias_sb, rhs_sb, out_sb, resid_sb=None,
                     tag=""):
    block, d, sb = ctx.block, ctx.d, ctx.sb
    wqb = [sb["wqb0"], sb["wqb1"]]
    wl_free = ctx.wqb_free
    for oc in range(NKC):
        slot = oc % 2

        @block.gpsimd
        def _(g, oc=oc, slot=slot):
            if wl_free[slot]:
                ctx.wait(g, "tensor", wl_free[slot])
            ctx.mark(g.dma_start(out=wqb[slot][:], in_=d[wname][oc]),
                     "gpsimd", 16)

        bank = oc % 2

        @block.tensor
        def _(t, slot=slot, bank=bank, oc=oc):
            ctx.wait(t, "gpsimd")
            if oc == 0:
                ctx.wait(t, "vector")
                ctx.wait(t, "scalar")
            ctx.bank_wait(t, bank)
            for kc in range(NKC):
                inst = t.matmul(out=ctx.PF[bank][:, 0:NL],
                                lhsT=wqb[slot][:, kc, :],
                                rhs=rhs_sb[:, kc, :],
                                start=(kc == 0), stop=(kc == NKC - 1))
            wl_free[slot] = ctx.mark(inst, "tensor")

        if resid_sb is None:
            @block.scalar
            def _(sce, oc=oc, bank=bank):
                ctx.wait(sce, "tensor")
                ctx.mark(sce.activation(out=out_sb[:, oc, :],
                                        in_=ctx.PF[bank][:, 0:NL],
                                        func=AF.Identity,
                                        bias=bias_sb[:, oc:oc + 1], scale=1.0),
                         "scalar")
                ctx.bank_read(bank, "scalar")
        else:
            @block.vector
            def _(v, oc=oc, bank=bank):
                ctx.wait(v, "tensor")
                ctx.mark(v.scalar_tensor_tensor(
                    out=out_sb[:, oc, :], in0=ctx.PF[bank][:, 0:NL],
                    scalar=bias_sb[:, oc:oc + 1], in1=resid_sb[:, oc, :],
                    op0=ALU.add, op1=ALU.add), "vector")
                ctx.bank_read(bank, "vector")


def _vproj_tok(ctx, wname, kbias_off, lhsT_sb, out_sb):
    block, d, sb = ctx.block, ctx.d, ctx.sb
    wvb = [sb["wvb0"], sb["wvb1"]]
    wv_free = ctx.wvb_free
    for vb in range(8):
        slot = vb % 2
        bank = 2 + vb % 2

        @block.gpsimd
        def _(g, vb=vb, slot=slot):
            if wv_free[slot]:
                ctx.wait(g, "tensor", wv_free[slot])
            ctx.mark(g.dma_start(out=wvb[slot][:], in_=d[wname][vb]),
                     "gpsimd", 16)

        @block.tensor
        def _(t, vb=vb, slot=slot, bank=bank):
            ctx.wait(t, "gpsimd")
            ctx.wait(t, "vector")
            ctx.bank_wait(t, bank)
            for kc in range(NKC):
                t.matmul(out=ctx.PF[bank][:, 0:256], lhsT=lhsT_sb[:, kc, :],
                         rhs=wvb[slot][:, kc, :],
                         start=(kc == 0), stop=False)
            inst = t.matmul(out=ctx.PF[bank][:, 0:256], lhsT=sb["ones_sb"][:],
                            rhs=sb["kbias"][0:1, kbias_off + vb * 256:
                                            kbias_off + (vb + 1) * 256],
                            start=False, stop=True)
            wv_free[slot] = ctx.mark(inst, "tensor")

        @block.vector
        def _(v, vb=vb, bank=bank):
            ctx.wait(v, "tensor")
            ctx.mark(v.tensor_copy(out=out_sb[:, vb * 256:(vb + 1) * 256],
                                   in_=ctx.PF[bank][:, 0:256]), "vector")
            ctx.bank_read(bank, "vector")


def p_landmark(ctx):
    block, d, sb = ctx.block, ctx.d, ctx.sb
    scrA, scrB = sb["scrA"], sb["scrB"]
    red = sb["red"]
    barrier(ctx)
    _wproj_to_chunks(ctx, "wlq", sb["b_blq"], sb["lmfT"], sb["lqT"])
    _wproj_to_chunks(ctx, "wlk", sb["b_blk"], sb["lmfT"], sb["lkT"])
    _vproj_tok(ctx, "wlv", D, sb["lmfT"], sb["lvtok"])
    la_sb = ctx.alloc("la_sb", [128, 128], BF16)
    laT_sb = ctx.alloc("laT_sb", [128, 128], BF16)

    for h in range(H):
        hb = (h % 2) * 64
        bank = h % 2

        @block.tensor
        def _(t, h=h, hb=hb, bank=bank):
            ctx.bank_wait(t, bank)
            ctx.mark(t.matmul(out=ctx.PF[bank][:, 0:128],
                              lhsT=sb["lqT"][hb:hb + 64, h // 2, :],
                              rhs=sb["lkT"][hb:hb + 64, h // 2, :],
                              start=True, stop=True,
                              tile_position=(hb, 0)), "tensor")

        @block.vector
        def _(v, bank=bank):
            ctx.wait(v, "tensor")
            ctx.wait(v, "scalar")
            ctx.vguard(v, v.reduce_max(red[:, 0:1], ctx.PF[bank][:, 0:128],
                                       axis=mybir.AxisListType.X))
            v.tensor_scalar_mul(red[:, 1:2], red[:, 0:1], -SCALE)
            ctx.mark(v.tensor_copy(out=red[:, 30:31], in_=red[:, 0:1]),
                     "vector")
        mneg_mark = ctx.n["vector"]

        @block.scalar
        def _(sce, bank=bank, mneg_mark=mneg_mark):
            ctx.wait(sce, "vector", mneg_mark)
            ctx.mark(sce.activation(out=scrA[:, 0:128], in_=ctx.PF[bank][:, 0:128],
                                    func=AF.Exp, scale=SCALE,
                                    bias=red[:, 1:2]), "scalar")
            ctx.bank_read(bank, "scalar")

        @block.vector
        def _(v):
            ctx.wait(v, "scalar")
            v.tensor_mul(scrB[:, 0:128], scrA[:, 0:128], sb["mlm"][:])
            ctx.vguard(v, v.reduce_sum(red[:, 2:3], scrB[:, 0:128],
                                       axis=mybir.AxisListType.X))
            ctx.vguard(v, v.reciprocal(red[:, 3:4], red[:, 2:3]))
            ctx.mark(v.tensor_scalar_mul(la_sb[:], scrB[:, 0:128],
                                         red[:, 3:4]), "vector")
        la_mark = ctx.n["vector"]

        @block.tensor
        def _(t, la_mark=la_mark):
            ctx.wait(t, "vector", la_mark)
            ctx.bank_wait(t, "PB")
            ctx.mark(t.transpose(out=ctx.PB[:, 0:128], in_=la_sb[:],
                                 identity=sb["ident_sb"][:]), "tensor")

        @block.vector
        def _(v):
            ctx.wait(v, "tensor")
            ctx.mark(v.tensor_copy(out=laT_sb[:], in_=ctx.PB[:, 0:128]), "vector")
            ctx.bank_read("PB", "vector")
        laT_mark = ctx.n["vector"]

        @block.tensor
        def _(t, h=h, hb=hb, laT_mark=laT_mark):
            ctx.wait(t, "vector", laT_mark)
            if hb == 0:
                ctx.bank_wait(t, 2)
            ctx.mark(t.matmul(out=ctx.PF[2][hb:hb + 64, 0:128],
                              lhsT=sb["lvtok"][:, h * 64:(h + 1) * 64],
                              rhs=laT_sb[:], start=True, stop=True,
                              tile_position=(0, hb)), "tensor")

        if h % 2 == 1:
            @block.vector
            def _(v, h=h):
                ctx.wait(v, "tensor")
                ctx.mark(v.tensor_copy(out=sb["loT"][:, h // 2, :],
                                       in_=ctx.PF[2][:, 0:128]), "vector")
                ctx.bank_read(2, "vector")
        if "lm" in PUMP_PHASES:
            ctx.exp.pump()
            ctx.exp.pump()
            ctx.exp.pump()

    _wproj_to_chunks(ctx, "wlmout", sb["b_blmout"], sb["loT"], sb["lmf2T"],
                     resid_sb=sb["lmfT"])
    _wproj_to_chunks(ctx, "wl2q", sb["b_bl2q"], sb["lmf2T"], sb["l2qT"])
    if DEBUG:
        @block.sync
        def _(s):
            ctx.wait(s, "vector")
            ctx.wait(s, "scalar")
            ctx.mark(s.dma_start(out=d["dbg_lmf"][:], in_=sb["lmfT"][:]), "sync", 16)
            ctx.mark(s.dma_start(out=d["dbg_lmf2"][:], in_=sb["lmf2T"][:]), "sync", 16)
            ctx.mark(s.dma_start(out=d["dbg_lqt"][:], in_=sb["lqT"][:]), "sync", 16)
            ctx.mark(s.dma_start(out=d["dbg_lkt"][:], in_=sb["lkT"][:]), "sync", 16)
            ctx.mark(s.dma_start(out=d["dbg_lvt"][:], in_=sb["lvtok"][:]), "sync", 16)
            ctx.mark(s.dma_start(out=d["dbg_lot"][:], in_=sb["loT"][:]), "sync", 16)


def p_sweep2a(ctx):
    block, d, sb = ctx.block, ctx.d, ctx.sb
    kT = ctx.alloc("kT2", [128, NKC, 512], BF16)
    vsc = ctx.alloc("vsc2", [128, 4, D], BF16)
    scrA, scrB, red = sb["scrA"], sb["scrB"], sb["red"]
    raA = ctx.alloc("raA", [128, 512], BF16)
    raT = ctx.alloc("raT", [128, 4, 128], BF16)
    for sc in range(NSC):
        barrier(ctx)

        @block.sync
        def _(s, sc=sc):
            ctx.mark(s.dma_start(out=kT[:], in_=d["dk"][sc]), "sync", 16)
            for tc_rel in range(4):
                ctx.mark(s.dma_start(out=vsc[:, tc_rel, :],
                                     in_=d["dv"][sc * 4 + tc_rel, :, :]),
                         "sync", 16)

        for hg in range(8):
            bank = hg % 2

            @block.tensor
            def _(t, hg=hg, sc=sc, bank=bank):
                ctx.wait(t, "sync")
                ctx.bank_wait(t, bank)
                for hsub in range(4):
                    h = 4 * hg + hsub
                    for par in range(2):
                        ch = par * 8 + h // 4
                        rb = (h % 4) * 32
                        inst = t.matmul(
                            out=ctx.PF[bank][hsub * 32:(hsub + 1) * 32, :],
                            lhsT=sb["l2qT"][rb:rb + 32, ch,
                                            sc * 32:(sc + 1) * 32],
                            rhs=kT[rb:rb + 32, ch, :],
                            start=(par == 0), stop=(par == 1),
                            tile_position=(rb, hsub * 32))
                ctx.mark(inst, "tensor")

            @block.vector
            def _(v, bank=bank):
                ctx.wait(v, "tensor")
                ctx.wait(v, "scalar")
                ctx.vguard(v, v.reduce_max(red[:, 0:1], ctx.PF[bank][:],
                                           axis=mybir.AxisListType.X))
                v.tensor_scalar_mul(red[:, 1:2], red[:, 0:1], -SCALE)
                ctx.mark(v.tensor_copy(out=red[:, 30:31], in_=red[:, 0:1]),
                         "vector")
            mneg2_mark = ctx.n["vector"]

            @block.scalar
            def _(sce, bank=bank, mneg2_mark=mneg2_mark):
                ctx.wait(sce, "vector", mneg2_mark)
                ctx.mark(sce.activation(out=scrA[:], in_=ctx.PF[bank][:],
                                        func=AF.Exp, scale=SCALE,
                                        bias=red[:, 1:2]), "scalar")
                ctx.bank_read(bank, "scalar")

            @block.vector
            def _(v):
                ctx.wait(v, "scalar")
                v.tensor_mul(scrB[:], scrA[:], sb["mra"][:])
                ctx.vguard(v, v.reduce_sum(red[:, 2:3], scrB[:],
                                           axis=mybir.AxisListType.X))
                ctx.vguard(v, v.reciprocal(red[:, 3:4], red[:, 2:3]))
                ctx.mark(v.tensor_scalar_mul(raA[:], scrB[:], red[:, 3:4]),
                         "vector")
            raA_mark = ctx.n["vector"]

            @block.tensor
            def _(t, raA_mark=raA_mark):
                ctx.wait(t, "vector", raA_mark)
                ctx.bank_wait(t, "PB")
                for pp in range(4):
                    inst = t.transpose(out=ctx.PB[:, pp * 128:(pp + 1) * 128],
                                       in_=raA[:, pp * 128:(pp + 1) * 128],
                                       identity=sb["ident_sb"][:])
                ctx.mark(inst, "tensor")

            @block.vector
            def _(v):
                ctx.wait(v, "tensor")
                ctx.mark(v.tensor_copy(out=raT[:],
                                       in_=ctx.PB[:, 0:512].rearrange(
                                           "p (c f) -> p c f", c=4)), "vector")
                ctx.bank_read("PB", "vector")
            raT_mark = ctx.n["vector"]

            @block.tensor
            def _(t, hg=hg, raT_mark=raT_mark):
                ctx.wait(t, "vector", raT_mark)
                if hg == 0:
                    ctx.bank_wait(t, 2)
                    ctx.bank_wait(t, 3)
                for hsub in range(4):
                    h = 4 * hg + hsub
                    for nrel in range(8):
                        rb = (nrel % 2) * 64
                        bnk = 2 if nrel % 2 == 0 else 3
                        inst = t.matmul(
                            out=ctx.PF[bnk][(h % 2) * 64:(h % 2) * 64 + 64,
                                            (h // 2) * 16 + (nrel // 2) * 4:
                                            (h // 2) * 16 + (nrel // 2) * 4 + 4],
                            lhsT=vsc[rb:rb + 64, nrel // 2, h * 64:(h + 1) * 64],
                            rhs=raT[rb:rb + 64, nrel // 2,
                                    hsub * 32 + nrel * 4:hsub * 32 + nrel * 4 + 4],
                            start=True, stop=True,
                            tile_position=(rb, (h % 2) * 64))
                ctx.mark(inst, "tensor")

            if "s2a" in PUMP_PHASES:
                ctx.exp.pump()
                ctx.exp.pump()
                ctx.exp.pump()
                ctx.exp.pump()

        @block.vector
        def _(v, sc=sc):
            ctx.wait(v, "tensor")
            dview = sb["roT"][:, :, sc * 32:(sc + 1) * 32].rearrange(
                "p c (n2 two m) -> p c n2 two m", two=2, m=4)
            v.tensor_copy(out=dview[:, :, :, 0, :],
                          in_=ctx.PF[2][:, 0:256].rearrange(
                              "p (c n2 m) -> p c n2 m", n2=4, m=4))
            ctx.mark(v.tensor_copy(out=dview[:, :, :, 1, :],
                                   in_=ctx.PF[3][:, 0:256].rearrange(
                                       "p (c n2 m) -> p c n2 m", n2=4, m=4)),
                     "vector")
            ctx.bank_read(2, "vector")
            ctx.bank_read(3, "vector")


def p_lmtail(ctx):
    block, d, sb = ctx.block, ctx.d, ctx.sb
    barrier(ctx)
    _wproj_to_chunks(ctx, "wl2out", sb["b_bl2out"], sb["roT"], sb["lmf3T"],
                     resid_sb=sb["lmf2T"])
    _wproj_to_chunks(ctx, "wgk", sb["b_bgk"], sb["lmf3T"], sb["gkT"])
    _vproj_tok(ctx, "wgv", 2 * D, sb["lmf3T"], sb["gv_sb"])
    if DEBUG:
        @block.sync
        def _(s):
            ctx.wait(s, "vector")
            ctx.wait(s, "scalar")
            ctx.mark(s.dma_start(out=d["dbg_lmf3"][:], in_=sb["lmf3T"][:]), "sync", 16)
            ctx.mark(s.dma_start(out=d["dbg_gk"][:], in_=sb["gkT"][:]), "sync", 16)
            ctx.mark(s.dma_start(out=d["dbg_gv"][:], in_=sb["gv_sb"][:]), "sync", 16)


def p_sweep2b(ctx):
    block, d, sb = ctx.block, ctx.d, ctx.sb
    qT = ctx.alloc("qT2", [128, NKC, 512], BF16)
    scrA, scrB, red = sb["scrA"], sb["scrB"], sb["red"]
    mg = ctx.alloc("mg", [128, NTC, 128], BF16)
    agb = ctx.alloc("agb", [128, 512], BF16)
    agT = ctx.alloc("agT", [128, 4, 128], BF16)

    @block.sync
    def _(s):
        ctx.mark(s.dma_start(out=mg[:], in_=d["mask_g"][:].rearrange(
            "(c p) l -> p c l", p=128)), "sync", 16)
    mg_mark = ctx.n["sync"]
    olT = [ctx.alloc(f"olT{i}", [128, 512], BF16) for i in range(2)]
    grep = [ctx.alloc(f"grep{i}", [128, 512], BF16) for i in range(2)]
    gated = ctx.alloc("gated", [128, NKC, 512], BF16)
    sstg = ctx.alloc("sstg", [128, 512], BF16)
    ol_used = [0, 0]

    for sc in range(NSC):
        barrier(ctx)

        @block.sync
        def _(s, sc=sc):
            ctx.mark(s.dma_start(out=qT[:], in_=d["dq"][sc]), "sync", 16)

        for hp in range(16):
            slot = hp % 2

            @block.sync
            def _(s, hp=hp, sc=sc, slot=slot):
                if hp >= 2:
                    ctx.wait(s, "vector", ol_used[slot])
                ctx.mark(s.dma_start(out=olT[slot][:],
                                     in_=d["dol"][sc, hp]), "sync", 16)
                gsrc = d["dgrep"][2 * hp:2 * hp + 2, sc * 512:(sc + 1) * 512]
                ctx.mark(s.dma_start(out=grep[slot][:],
                                     in_=gsrc.unsqueeze(1).broadcast_to(
                                         [2, 64, 512])), "sync", 16)

            for hh in range(2):
                h = 2 * hp + hh
                bank = hh

                @block.tensor
                def _(t, h=h, bank=bank):
                    ctx.wait(t, "sync")
                    ctx.bank_wait(t, bank)
                    for tc_rel in range(4):
                        for par in range(2):
                            ch = par * 8 + h // 4
                            rb = (h % 4) * 32
                            inst = t.matmul(
                                out=ctx.PF[bank][:, tc_rel * 128:(tc_rel + 1) * 128],
                                lhsT=qT[rb:rb + 32, ch,
                                        tc_rel * 128:(tc_rel + 1) * 128],
                                rhs=sb["gkT"][rb:rb + 32, ch, :],
                                start=(par == 0), stop=(par == 1),
                                tile_position=(rb, 0))
                    ctx.mark(inst, "tensor")

                @block.vector
                def _(v, bank=bank):
                    ctx.wait(v, "tensor")
                    ctx.wait(v, "scalar")
                    s3 = ctx.PF[bank][:].rearrange("p (s f) -> p s f", s=4)
                    ctx.vguard(v, v.reduce_max(red[:, 0:4], s3,
                                               axis=mybir.AxisListType.X))
                    v.tensor_scalar_mul(red[:, 4:8], red[:, 0:4], -SCALE)
                    ctx.mark(v.tensor_copy(out=red[:, 30:31], in_=red[:, 0:1]),
                             "vector")
                mneg3_mark = ctx.n["vector"]

                @block.scalar
                def _(sce, bank=bank, mneg3_mark=mneg3_mark):
                    ctx.wait(sce, "vector", mneg3_mark)
                    for tc_rel in range(4):
                        inst = sce.activation(
                            out=scrA[:, tc_rel * 128:(tc_rel + 1) * 128],
                            in_=ctx.PF[bank][:, tc_rel * 128:(tc_rel + 1) * 128],
                            func=AF.Exp, scale=SCALE,
                            bias=red[:, 4 + tc_rel:5 + tc_rel])
                    ctx.mark(inst, "scalar")
                    ctx.bank_read(bank, "scalar")

                @block.vector
                def _(v, sc=sc):
                    ctx.wait(v, "scalar")
                    ctx.wait(v, "sync", mg_mark)
                    e3 = scrA[:].rearrange("p (s f) -> p s f", s=4)
                    v.tensor_mul(scrB[:].rearrange("p (s f) -> p s f", s=4),
                                 e3, mg[:, sc * 4:(sc + 1) * 4, :])
                    m3 = scrB[:].rearrange("p (s f) -> p s f", s=4)
                    ctx.vguard(v, v.reduce_sum(red[:, 8:12], m3,
                                               axis=mybir.AxisListType.X))
                    ctx.vguard(v, v.tensor_scalar_max(red[:, 12:16],
                                                      red[:, 8:12], 1e-9))
                    ctx.vguard(v, v.reciprocal(red[:, 16:20], red[:, 12:16]))
                    ctx.mark(v.tensor_mul(agb[:].rearrange("p (s f) -> p s f", s=4),
                                          m3,
                                          red[:, 16:20].unsqueeze(2).broadcast_to(
                                              [128, 4, 128])), "vector")
                agb_mark = ctx.n["vector"]

                @block.tensor
                def _(t, agb_mark=agb_mark):
                    ctx.wait(t, "vector", agb_mark)
                    ctx.bank_wait(t, "PB")
                    for tc_rel in range(4):
                        inst = t.transpose(
                            out=ctx.PB[:, tc_rel * 128:(tc_rel + 1) * 128],
                            in_=agb[:, tc_rel * 128:(tc_rel + 1) * 128],
                            identity=sb["ident_sb"][:])
                    ctx.mark(inst, "tensor")

                @block.vector
                def _(v):
                    ctx.wait(v, "tensor")
                    ctx.mark(v.tensor_copy(out=agT[:],
                                           in_=ctx.PB[:, 0:512].rearrange(
                                               "p (c f) -> p c f", c=4)), "vector")
                    ctx.bank_read("PB", "vector")
                agT_mark = ctx.n["vector"]

                @block.tensor
                def _(t, h=h, hh=hh, agT_mark=agT_mark):
                    ctx.wait(t, "vector", agT_mark)
                    if hh == 0:
                        ctx.bank_wait(t, 2)
                    for tc_rel in range(4):
                        inst = t.matmul(
                            out=ctx.PF[2][hh * 64:hh * 64 + 64,
                                          tc_rel * 128:(tc_rel + 1) * 128],
                            lhsT=sb["gv_sb"][:, h * 64:(h + 1) * 64],
                            rhs=agT[:, tc_rel, :], start=True, stop=True,
                            tile_position=(0, hh * 64))
                    ctx.mark(inst, "tensor")

            if DEBUG and sc == 0 and hp == 0:
                @block.sync
                def _(s, slot=slot):
                    ctx.wait(s, "tensor")
                    ctx.wait(s, "vector")
                    ctx.mark(s.dma_start(out=d["dbg_sge"][:], in_=scrA[:]), "sync", 16)
                    ctx.mark(s.dma_start(out=d["dbg_scrB"][:], in_=scrB[:]), "sync", 16)
                    ctx.mark(s.dma_start(out=d["dbg_mg"][:], in_=mg[:, 0:4, :]), "sync", 16)
                    ctx.mark(s.dma_start(out=d["dbg_red"][:], in_=red[:]), "sync", 16)
                    ctx.mark(s.dma_start(out=d["dbg_agb"][:], in_=agb[:]), "sync", 16)
                    ctx.mark(s.dma_start(out=d["dbg_agt"][:],
                                         in_=agT[:].rearrange("p c f -> p (c f)")), "sync", 16)
                    ctx.mark(s.dma_start(out=d["dbg_olt"][:], in_=olT[slot][:]), "sync", 16)
                    ctx.mark(s.dma_start(out=d["dbg_grep"][:], in_=grep[slot][:]), "sync", 16)

            @block.vector
            def _(v, hp=hp, slot=slot):
                ctx.wait(v, "tensor")
                ctx.wait(v, "sync")
                v.tensor_sub(scrA[:], ctx.PF[2][:], olT[slot][:])
                v.tensor_mul(scrB[:], scrA[:], grep[slot][:])
                ol_used[slot] = ctx.mark(
                    v.tensor_add(gated[:, hp, :], scrB[:], olT[slot][:]),
                    "vector")
                ctx.bank_read(2, "vector")

            for _i in range(5):
                ctx.exp.pump()

        if DEBUG and sc == 0:
            @block.sync
            def _(s):
                ctx.wait(s, "vector")
                ctx.mark(s.dma_start(out=d["dbg_gated"][:], in_=gated[:]),
                         "sync", 16)

        gated_mark = ctx.n["vector"]
        wqb = [sb["wqb0"], sb["wqb1"]]
        wo_free = ctx.wqb_free
        for oc in range(NKC):
            slot = oc % 2
            bank = oc % 2

            @block.gpsimd
            def _(g, oc=oc, slot=slot):
                if wo_free[slot]:
                    ctx.wait(g, "tensor", wo_free[slot])
                ctx.mark(g.dma_start(out=wqb[slot][:], in_=d["wout"][oc]),
                         "gpsimd", 16)

            @block.tensor
            def _(t, slot=slot, bank=bank, oc=oc, gated_mark=gated_mark):
                ctx.wait(t, "gpsimd")
                ctx.bank_wait(t, bank)
                if oc < 2:
                    ctx.wait(t, "vector", gated_mark)
                for kc in range(NKC):
                    inst = t.matmul(out=ctx.PF[bank][:],
                                    lhsT=wqb[slot][:, kc, :],
                                    rhs=gated[:, kc, :],
                                    start=(kc == 0), stop=(kc == NKC - 1))
                wo_free[slot] = ctx.mark(inst, "tensor")

            @block.scalar
            def _(sce, oc=oc, bank=bank):
                ctx.wait(sce, "tensor")
                ctx.wait(sce, "sync")
                ctx.mark(sce.activation(out=sstg[:], in_=ctx.PF[bank][:],
                                        func=AF.Identity, scale=0.5,
                                        bias=sb["b_bout_h"][:, oc:oc + 1]),
                         "scalar")
                ctx.bank_read(bank, "scalar")

            @block.sync
            def _(s, oc=oc, sc=sc):
                ctx.wait(s, "scalar")
                ctx.mark(s.dma_start(out=d["ds"][sc, :, oc, :],
                                     in_=sstg[:]), "sync", 16)


def p_ffn(ctx):
    block, d, sb = ctx.block, ctx.d, ctx.sb
    a = ctx.alloc
    xnf = a("xnf", [128, NKC, 512], BF16)
    sTf = a("sTf", [128, NKC, 512], BF16)
    hT = a("hT", [128, NOB, 512], BF16)
    pT = sTf
    gbuf = [a(f"gbuf{i}", [128, NKC, 128], BF16) for i in range(2)]
    ubuf = [a(f"ubuf{i}", [128, NKC, 128], BF16) for i in range(2)]
    dbuf = [a(f"dbuf{i}", [128, NOB, 128], BF16) for i in range(2)]
    xbuf2 = [a("xbf0", [128, D], F32)] * 2
    ostg = [a("ostg0", [128, D], F32)] * 2
    scr2A = a("scr2A", [128, 512], F32)
    scr2B = a("scr2B", [128, 512], F32)
    scrp = [(sb["scrA"], sb["scrB"]), (scr2A, scr2B)]

    gu_free = [0, 0]
    dn_free = [0, 0]
    x2_free = [0, 0]
    os_free = [0, 0]
    for sc in range(NSC):
        barrier(ctx)

        @block.sync
        def _(s, sc=sc):
            ctx.mark(s.dma_start(out=xnf[:], in_=d["dxn"][sc]), "sync", 16)
            ctx.mark(s.dma_start(out=sTf[:], in_=d["ds"][sc]), "sync", 16)

        for oc in range(NOB):
            slot = oc % 2

            @block.sync
            def _(s, oc=oc, slot=slot, sc=sc):
                if oc >= 2 or sc > 0:
                    ctx.wait(s, "tensor", gu_free[slot])
                ctx.mark(s.dma_start(out=gbuf[slot][:], in_=d["wgT"][oc]),
                         "sync", 16)
                ctx.mark(s.dma_start(out=ubuf[slot][:], in_=d["wuT"][oc]),
                         "sync", 16)

            bg = 0 if oc % 2 == 0 else 4
            bu = 1 if oc % 2 == 0 else 5

            @block.tensor
            def _(t, oc=oc, slot=slot, bg=bg, bu=bu):
                ctx.wait(t, "sync")
                ctx.bank_wait(t, bg)
                ctx.bank_wait(t, bu)
                for kc in range(NKC):
                    t.matmul(out=ctx.PF[bg][:], lhsT=gbuf[slot][:, kc, :],
                             rhs=xnf[:, kc, :], start=(kc == 0),
                             stop=(kc == NKC - 1))
                for kc in range(NKC):
                    inst = t.matmul(out=ctx.PF[bu][:], lhsT=ubuf[slot][:, kc, :],
                                    rhs=xnf[:, kc, :], start=(kc == 0),
                                    stop=(kc == NKC - 1))
                gu_free[slot] = ctx.mark(inst, "tensor")

            sA, sB = scrp[oc % 2]

            @block.scalar
            def _(sce, bg=bg, sA=sA):
                ctx.wait(sce, "tensor")
                ctx.mark(sce.activation(out=sA[:], in_=ctx.PF[bg][:],
                                        func=AF.Tanh, scale=0.5), "scalar")

            @block.vector
            def _(v, oc=oc, bg=bg, bu=bu, sA=sA, sB=sB):
                ctx.wait(v, "scalar")
                v.scalar_tensor_tensor(out=sB[:], in0=sA[:], scalar=1.0,
                                       in1=ctx.PF[bg][:], op0=ALU.add,
                                       op1=ALU.mult)
                ctx.mark(v.tensor_mul(hT[:, oc, :], sB[:], ctx.PF[bu][:]),
                         "vector")
                ctx.bank_read(bg, "vector")
                ctx.bank_read(bu, "vector")

        for oc2 in range(NKC):
            slot = oc2 % 2
            bank = 2 + oc2 % 2

            @block.sync
            def _(s, oc2=oc2, slot=slot, sc=sc):
                if oc2 >= 2 or sc > 0:
                    ctx.wait(s, "tensor", dn_free[slot])
                ctx.mark(s.dma_start(out=dbuf[slot][:], in_=d["wdT"][oc2]),
                         "sync", 16)

            @block.tensor
            def _(t, oc2=oc2, slot=slot, bank=bank):
                ctx.wait(t, "sync")
                if oc2 == 0:
                    ctx.wait(t, "vector")
                ctx.bank_wait(t, bank)
                for ic in range(NOB):
                    inst = t.matmul(out=ctx.PF[bank][:],
                                    lhsT=dbuf[slot][:, ic, :],
                                    rhs=hT[:, ic, :],
                                    start=(ic == 0), stop=(ic == NOB - 1))
                dn_free[slot] = ctx.mark(inst, "tensor")

            sA, sB = scrp[oc2 % 2]

            @block.scalar
            def _(sce, bank=bank, sA=sA):
                ctx.wait(sce, "tensor")
                ctx.mark(sce.activation(out=sA[:], in_=ctx.PF[bank][:],
                                        func=AF.Tanh, scale=0.5), "scalar")

            @block.vector
            def _(v, oc2=oc2, bank=bank, sA=sA, sB=sB):
                ctx.wait(v, "scalar")
                ctx.wait(v, "sync")
                v.scalar_tensor_tensor(out=sB[:], in0=sA[:], scalar=1.0,
                                       in1=ctx.PF[bank][:], op0=ALU.add,
                                       op1=ALU.mult)
                ctx.mark(v.tensor_mul(pT[:, oc2, :], sB[:], sTf[:, oc2, :]),
                         "vector")
                ctx.bank_read(bank, "vector")

        for tc_rel in range(4):
            tc = sc * 4 + tc_rel
            slot = 0

            @block.sync
            def _(s, tc=tc, slot=slot):
                if tc >= 1:
                    ctx.wait(s, "vector", x2_free[slot])
                ctx.mark(s.dma_start(out=xbuf2[slot][:],
                                     in_=d["x"][tc * 128:(tc + 1) * 128, :]),
                         "sync", 16)

            for half in range(2):
                @block.tensor
                def _(t, tc_rel=tc_rel, half=half):
                    if half == 0 and tc_rel == 0:
                        ctx.wait(t, "vector")
                    ctx.bank_wait(t, "PB")
                    for j in range(8):
                        oc2 = half * 8 + j
                        inst = t.transpose(
                            out=ctx.PB[:, j * 128:(j + 1) * 128],
                            in_=pT[:, oc2, tc_rel * 128:(tc_rel + 1) * 128],
                            identity=sb["ident_sb"][:])
                    ctx.mark(inst, "tensor")

                @block.vector
                def _(v, half=half, slot=slot):
                    ctx.wait(v, "tensor")
                    if half == 0:
                        ctx.wait(v, "sync", os_free[slot])
                        ctx.wait(v, "sync")
                    inst = v.tensor_add(
                        ostg[slot][:, half * 1024:(half + 1) * 1024],
                        xbuf2[slot][:, half * 1024:(half + 1) * 1024],
                        ctx.PB[:, 0:1024])
                    w = ctx.mark(inst, "vector")
                    if half == 1:
                        x2_free[slot] = w
                    ctx.bank_read("PB", "vector")

            @block.sync
            def _(s, tc=tc, slot=slot):
                ctx.wait(s, "vector")
                os_free[slot] = ctx.mark(
                    s.dma_start(out=d["out"][tc * 128:(tc + 1) * 128, :],
                                in_=ostg[slot][:]), "sync", 16)


def emit(ctx):
    p_consts(ctx)
    p_bias_stage(ctx)
    ctx.push_scope()
    p_lm_scope_alloc(ctx)
    ctx.exp = Expander(ctx)
    ctx.push_scope()
    p_sweep1(ctx)
    barrier(ctx)
    ctx.pop_scope()
    ctx.push_scope()
    p_post1_alloc(ctx)
    p_landmark(ctx)
    ctx.push_scope()
    p_sweep2a(ctx)
    barrier(ctx)
    ctx.pop_scope()
    p_lmtail(ctx)
    ctx.push_scope()
    p_sweep2b(ctx)
    ctx.exp.finish()
    ctx.tr.finish()
    barrier(ctx)
    ctx.pop_scope()
    barrier(ctx)
    ctx.pop_scope()
    barrier(ctx)
    ctx.pop_scope()
    ctx.push_scope()
    p_ffn(ctx)
    ctx.pop_scope()
    block = ctx.block

    @block.sync
    def _(s):
        for e in ("tensor", "vector", "scalar", "gpsimd"):
            ctx.wait(s, e)
        ctx.wait(s, "sync")


def kernel(**inputs):
    cst = host_prep(inputs)
    x = np.asarray(inputs["L0_attn_x"])
    nc, dd = build_program()
    in_maps = []
    for c in range(8):
        m = dict(cst)
        m["x"] = np.ascontiguousarray(x[c])
        in_maps.append(m)
    res = run_bass_kernel_spmd(nc, in_maps, core_ids=list(range(8)))
    return np.stack([res.results[c]["out"] for c in range(8)], 0)


# version 28 (final)
# speedup vs baseline: 1.1651x; 1.0265x over previous
import numpy as np
import ml_dtypes
from contextlib import ExitStack

import concourse.bass as bass
import concourse.mybir as mybir
from concourse.bass_utils import run_bass_kernel_spmd
from concourse.dve_spec import Spec, Src0, Src1, C0, C1, C2, One, eq, lower
from concourse.dve_op import DveOp
from concourse.dve_ops import OPS as _DVE_OPS
from concourse.dve_table_gen import dve_ver_for


def _make_lut2acc():
    for op in _DVE_OPS:
        if op.name == "LUT2ACC":
            return op
    body = Src1 + eq(Src0, C2) * C0 + eq(Src0 - C2, One) * C1
    spec = Spec(body=body)
    ver = dve_ver_for("TRN2")
    sha = lower(spec, ver=ver, perf_en=False, name="LUT2ACC").sha(ver)
    op = DveOp("LUT2ACC", spec, subdim=False, uops_sha={ver: sha},
               reference=lambda in0, in1, s0, s1, imm2:
                   in1 + (in0 == imm2) * s0 + ((in0 - imm2) == 1.0) * s1)
    _DVE_OPS.append(op)
    from concourse import dve_ops as _dvo
    _dvo._SUB_OPCODE_FOR_NAME[op.name] = _dvo._CUSTOM_DVE_ROW_BASE + len(_DVE_OPS) - 1
    return op


LUT2ACC = _make_lut2acc()

F32 = mybir.dt.float32
BF16 = mybir.dt.bfloat16
I32 = mybir.dt.int32
AF = mybir.ActivationFunctionType
ALU = mybir.AluOpType
BF = ml_dtypes.bfloat16

D = 2048
H = 32
DH = 64
M = 4
S = 64
L = 2048
NB = 32
NL = 128
INTER = 5504
NKC = 16
NTC = 16
NSC = 4
SCALE = 0.125
ROPE_BASE = 10000.0
NOB = 43

DEBUG = False
BUILD_UPTO = 99
SW1 = {"a", "b", "v", "g", "psc", "loc", "lm2", "grep"}
PUMP_PHASES = {"s1a", "s1b", "s1c", "lm", "s2a", "s2b"}


def _ps_perm():
    perm = np.zeros(D, np.int64)
    for r in range(D):
        p = r // 1024
        rem = r % 1024
        h = rem // 32
        i = rem % 32
        perm[r] = h * DH + 2 * i + p
    return perm


def _ptile(W, ow):
    Din, O = W.shape
    return np.ascontiguousarray(
        W.reshape(Din // 128, 128, O // ow, ow).transpose(2, 1, 0, 3),
        dtype=np.float32).astype(BF)


def host_prep(inputs):
    g = lambda n: np.asarray(inputs["L0_attn_" + n])
    perm = _ps_perm()

    w_qkv = g("w_qkv")
    b_qkv = g("b_qkv")
    wq = w_qkv[:, 0:D][:, perm]
    wk = w_qkv[:, D:2 * D][:, perm]
    wqk = np.ascontiguousarray(np.concatenate([wq, wk], 1), dtype=np.float32)
    bqk = np.concatenate([b_qkv[0:D][perm], b_qkv[D:2 * D][perm]])
    wv = np.ascontiguousarray(w_qkv[:, 2 * D:3 * D], dtype=np.float32)
    bv = b_qkv[2 * D:3 * D]

    pq = g("pool_q")
    PQ = np.zeros((D, NL), np.float32)
    for r in range(D):
        p = r // 1024
        rem = r % 1024
        h = rem // 32
        i = rem % 32
        PQ[r, 4 * h: 4 * h + 4] = pq[:, h * DH + 2 * i + p]
    PQ = PQ.astype(BF)

    w_lm_qkv = g("w_lm_qkv")
    b_lm_qkv = g("b_lm_qkv")

    def bias_cp(b):
        return np.ascontiguousarray(b.reshape(-1, 128), dtype=np.float32)

    i_of_row = np.arange(128) % 32
    inv = ROPE_BASE ** (-(2.0 * i_of_row) / DH)
    ang = inv[:, None] * np.arange(L, dtype=np.float64)[None, :]
    cosT = np.cos(ang).astype(np.float32).astype(BF)
    sinT = np.sin(ang).astype(np.float32).astype(BF)

    tril = np.tril(np.ones((S, S), np.float32))
    mask_local = np.concatenate([tril, tril], 0).astype(BF)
    blk = np.repeat(np.arange(NB), M)
    mask_lm = (blk[:, None] >= blk[None, :]).astype(np.float32).astype(BF)
    mra = np.zeros((128, 512), np.float32)
    for r in range(128):
        nrel = (r % 32) // 4
        mra[r, nrel * 64:(nrel + 1) * 64] = 1.0
    mask_ra = mra.astype(BF)
    tok = np.arange(L)
    col = np.arange(NL)
    mask_g = ((col[None, :] // 4) < (tok[:, None] // 64)).astype(np.float32).astype(BF)

    lut = np.asarray(inputs["L0_ffn_lut"]).astype(np.float32)
    lutb = np.broadcast_to(lut, (128, 16)).copy()

    cst = {
        "wqk": _ptile(wqk, 128), "bqk": bias_cp(bqk),
        "wv": _ptile(wv, 256), "bv": np.ascontiguousarray(bv, np.float32).reshape(1, D),
        "PQ": PQ,
        "wlq": _ptile(np.ascontiguousarray(w_lm_qkv[:, 0:D]), 128),
        "wlk": _ptile(np.ascontiguousarray(w_lm_qkv[:, D:2 * D]), 128),
        "wlv": _ptile(np.ascontiguousarray(w_lm_qkv[:, 2 * D:3 * D]), 256),
        "blq": bias_cp(b_lm_qkv[0:D]), "blk": bias_cp(b_lm_qkv[D:2 * D]),
        "blv": np.ascontiguousarray(b_lm_qkv[2 * D:3 * D], np.float32).reshape(1, D),
        "wlmout": _ptile(np.asarray(g("w_lm_out"), np.float32), 128),
        "blmout": bias_cp(g("b_lm_out")),
        "wl2q": _ptile(np.ascontiguousarray(g("w_l2t_q")[:, perm]), 128),
        "bl2q": bias_cp(g("b_l2t_q")[perm]),
        "wl2out": _ptile(np.asarray(g("w_l2t_out"), np.float32), 128),
        "bl2out": bias_cp(g("b_l2t_out")),
        "wgk": _ptile(np.ascontiguousarray(g("w_lm_kv")[:, 0:D][:, perm]), 128),
        "bgk": bias_cp(g("b_lm_kv")[0:D][perm]),
        "wgv": _ptile(np.ascontiguousarray(g("w_lm_kv")[:, D:2 * D]), 256),
        "bgv": np.ascontiguousarray(g("b_lm_kv")[D:2 * D], np.float32).reshape(1, D),
        "wout": _ptile(np.asarray(g("w_out"), np.float32), 128),
        "bout": bias_cp(g("b_out")),
        "wgate": np.ascontiguousarray(g("w_gate"), np.float32).astype(BF),
        "bgate": np.ascontiguousarray(g("b_gate"), np.float32).reshape(H, 1),
        "cosT": cosT, "sinT": sinT,
        "mask_local": mask_local, "mask_lm": mask_lm,
        "mask_ra": mask_ra, "mask_g": mask_g,
        "lutb": lutb,
        "ident": np.eye(128).astype(BF),
        "ones": np.ones((1, 128)).astype(BF),
        "bqk_r": np.ascontiguousarray(bqk, np.float32).reshape(1, 2 * D),
        "gidx": np.ascontiguousarray(
            np.asarray(inputs["L0_ffn_gate_idx"]).astype(BF)),
        "uidx": np.ascontiguousarray(
            np.asarray(inputs["L0_ffn_up_idx"]).astype(BF)),
        "didx": np.ascontiguousarray(
            np.asarray(inputs["L0_ffn_down_idx"]).astype(BF)),
    }
    return cst


class Ctx:
    ENGS = ("sync", "tensor", "vector", "scalar", "gpsimd")

    def __init__(self, nc, stack, block):
        self.nc = nc
        self.stack = stack
        self.block = block
        self.h = {}
        self.n = {}
        for e in self.ENGS:
            self.h[e] = stack.enter_context(nc.semaphore("prg_" + e))
            self.n[e] = 0
        self.sb = {}
        self.scopes = []
        self.pf_free = {}
        self.wqb_free = [0, 0]
        self.wvb_free = [0, 0]

    def mark(self, inst, eng, amt=1):
        inst.then_inc(self.h[eng], amt)
        self.n[eng] += amt
        return self.n[eng]

    def wait(self, engobj, eng, val=None):
        v = self.n[eng] if val is None else val
        if v and v > 0:
            engobj.wait_ge(self.h[eng], v)

    def snap(self):
        return dict(self.n)

    def alloc(self, name, shape, dtype):
        st = self.scopes[-1] if self.scopes else self.stack
        t = st.enter_context(self.nc.sbuf_tensor(name, shape, dtype))
        self.sb[name] = t
        return t

    def push_scope(self):
        self.scopes.append(ExitStack())

    def pop_scope(self):
        self.scopes.pop().close()

    def vguard(self, v, inst):
        self.mark(inst, "vector")
        self.wait(v, "vector")

    def bank_wait(self, engobj, bank):
        fr = self.pf_free.get(bank)
        if fr is not None:
            self.wait(engobj, fr[0], fr[1])

    def bank_read(self, bank, eng):
        self.pf_free[bank] = (eng, self.n[eng])


def barrier(ctx):
    snap = ctx.snap()
    for name in Ctx.ENGS:
        def _f(e, snap=snap, me=name):
            for other, v in snap.items():
                if v > 0 and other != me:
                    e.wait_ge(ctx.h[other], v)
        getattr(ctx.block, name)(_f)


def build_program():
    nc = bass.Bass("TRN2", target_bir_lowering=False)
    dd = {}

    def din(name, shape, dtype):
        dd[name] = nc.dram_tensor(name, list(shape), dtype, kind="ExternalInput")

    def dint(name, shape, dtype):
        kind = "ExternalOutput" if DEBUG else "Internal"
        dd[name] = nc.dram_tensor(name, list(shape), dtype, kind=kind)

    din("x", (L, D), F32)
    din("wqk", (32, 128, 16, 128), BF16); din("bqk", (32, 128), F32)
    din("wv", (8, 128, 16, 256), BF16); din("bv", (1, D), F32)
    din("PQ", (D, NL), BF16)
    din("wlq", (16, 128, 16, 128), BF16); din("wlk", (16, 128, 16, 128), BF16)
    din("wlv", (8, 128, 16, 256), BF16)
    din("blq", (16, 128), F32); din("blk", (16, 128), F32); din("blv", (1, D), F32)
    din("wlmout", (16, 128, 16, 128), BF16); din("blmout", (16, 128), F32)
    din("wl2q", (16, 128, 16, 128), BF16); din("bl2q", (16, 128), F32)
    din("wl2out", (16, 128, 16, 128), BF16); din("bl2out", (16, 128), F32)
    din("wgk", (16, 128, 16, 128), BF16); din("bgk", (16, 128), F32)
    din("wgv", (8, 128, 16, 256), BF16); din("bgv", (1, D), F32)
    din("wout", (16, 128, 16, 128), BF16); din("bout", (16, 128), F32)
    din("wgate", (D, H), BF16); din("bgate", (H, 1), F32)
    din("cosT", (128, L), BF16); din("sinT", (128, L), BF16)
    din("mask_local", (128, 64), BF16)
    din("mask_lm", (128, 128), BF16)
    din("mask_ra", (128, 512), BF16)
    din("mask_g", (L, NL), BF16)
    din("lutb", (128, 16), F32)
    din("gidx", (INTER, D), BF16)
    din("uidx", (INTER, D), BF16)
    din("didx", (D, INTER), BF16)
    din("ident", (128, 128), BF16)
    din("ones", (1, 128), BF16)
    din("bqk_r", (1, 2 * D), F32)

    dd["out"] = nc.dram_tensor("out", [L, D], F32, kind="ExternalOutput")

    dint("dxn", (NSC, 128, NKC, 512), BF16)
    dint("dq", (NSC, 128, NKC, 512), BF16)
    dint("dk", (NSC, 128, NKC, 512), BF16)
    dint("dv", (NTC, 128, D), BF16)
    dint("dol", (NSC, NKC, 128, 512), BF16)
    dint("ds", (NSC, 128, NKC, 512), BF16)
    dint("dgrep", (H, L), BF16)
    dd["wg_nat"] = nc.dram_tensor("wg_nat", [INTER, D], BF16, kind="Internal")
    dd["wu_nat"] = nc.dram_tensor("wu_nat", [INTER, D], BF16, kind="Internal")
    dd["wd_nat"] = nc.dram_tensor("wd_nat", [D, INTER], BF16, kind="Internal")
    dint("wgT", (NOB, 128, NKC, 128), BF16)
    dint("wuT", (NOB, 128, NKC, 128), BF16)
    dint("wdT", (NKC, 128, NOB, 128), BF16)
    if DEBUG:
        dint("dbg_lqt", (128, NKC, 128), BF16)
        dint("dbg_lkt", (128, NKC, 128), BF16)
        dint("dbg_lvt", (128, D), BF16)
        dint("dbg_lot", (128, NKC, 128), BF16)
        dint("dbg_sge", (128, 512), F32)
        dint("dbg_scrB", (128, 512), F32)
        dint("dbg_mg", (128, 4, 128), BF16)
        dint("dbg_red", (128, 32), F32)
        dint("dbg_agb", (128, 512), BF16)
        dint("dbg_agt", (128, 512), BF16)
        dint("dbg_olt", (128, 512), BF16)
        dint("dbg_grep", (128, 512), BF16)
        dint("dbg_og", (128, 512), F32)
        dint("dbg_lmf", (128, NKC, NL), BF16)
        dint("dbg_lmf2", (128, NKC, NL), BF16)
        dint("dbg_lmf3", (128, NKC, NL), BF16)
        dint("dbg_gk", (128, NKC, NL), BF16)
        dint("dbg_gv", (128, D), BF16)
        dint("dbg_gated", (128, NKC, 512), BF16)

    with ExitStack() as stack:
        block = stack.enter_context(nc.Block())
        ctx = Ctx(nc, stack, block)
        ctx.d = dd
        ctx.PF = [stack.enter_context(nc.psum_tensor(f"pf{i}", [128, 512], F32))
                  for i in range(7)]
        ctx.PB = stack.enter_context(nc.psum_tensor("pbt", [128, 1024], BF16))
        emit(ctx)

    return nc, dd


def p_consts(ctx):
    block, d = ctx.block, ctx.d
    a = ctx.alloc
    a("mloc", [128, 64], BF16)
    a("lut_sb", [128, 16], F32)
    a("luth_sb", [128, 16], F32)
    a("ident_sb", [128, 128], BF16)
    a("identf", [128, 128], F32)
    a("ones_sb", [1, 128], BF16)
    a("bqkr", [1, 2 * D], BF16)
    a("wgate_sb", [128, NKC, H], BF16)
    a("bgate_sb", [H, 1], F32)
    a("bstage", [48, 128], F32)
    a("scrA", [128, 512], F32)
    a("scrB", [128, 512], F32)
    a("stat", [128, 8], F32)
    a("red", [128, 32], F32)
    a("epsc", [128, 1], F32)
    bias_c = {"blq": 16, "blk": 16, "blmout": 16, "bl2q": 16,
              "bl2out": 16, "bgk": 16, "bout": 16}
    for nm, c in bias_c.items():
        a("b_" + nm, [128, c], F32)
    a("b_bout_h", [128, 16], F32)

    sb = ctx.sb

    @block.sync
    def _(s):
        ctx.mark(s.dma_start(out=sb["mloc"][:], in_=d["mask_local"][:]), "sync", 16)
        ctx.mark(s.dma_start(out=sb["lut_sb"][:], in_=d["lutb"][:]), "sync", 16)
        ctx.mark(s.dma_start(out=sb["ident_sb"][:], in_=d["ident"][:]), "sync", 16)
        ctx.mark(s.dma_start(out=sb["ones_sb"][:], in_=d["ones"][:]), "sync", 16)
        ctx.mark(s.dma_start(out=sb["bgate_sb"][:], in_=d["bgate"][:]), "sync", 16)

    @block.gpsimd
    def _(g):
        ctx.mark(g.dma_start(out=sb["wgate_sb"][:], in_=d["wgate"][:].rearrange(
            "(c p) o -> p c o", p=128)), "gpsimd", 16)
        ctx.mark(g.dma_start(out=sb["bqkr"][:], in_=d["bqk_r"][:]), "gpsimd", 16)

    @block.vector
    def _(v):
        ctx.wait(v, "sync")
        ctx.mark(v.tensor_scalar_mul(sb["luth_sb"][:], sb["lut_sb"][:], 0.5), "vector")
        ctx.mark(v.tensor_copy(out=sb["identf"][:], in_=sb["ident_sb"][:]), "vector")
        ctx.mark(v.memset(sb["epsc"][:], 1e-5), "vector")


def p_bias_stage(ctx):
    block, d, sb = ctx.block, ctx.d, ctx.sb
    bias_c = {"blq": 16, "blk": 16, "blmout": 16, "bl2q": 16,
              "bl2out": 16, "bgk": 16, "bout": 16}
    for nm, C in bias_c.items():
        tgt = sb["b_" + nm]

        @block.sync
        def _(s, nm=nm, C=C):
            ctx.wait(s, "tensor")
            ctx.wait(s, "sync")
            ctx.mark(s.dma_start(out=sb["bstage"][0:C, :], in_=d[nm][:]), "sync", 16)

        @block.tensor
        def _(t, C=C):
            ctx.wait(t, "sync")
            ctx.wait(t, "vector")
            ctx.mark(t.transpose(out=ctx.PF[6][:, 0:C], in_=sb["bstage"][0:C, :],
                                 identity=sb["identf"][0:C, 0:C]), "tensor")

        @block.vector
        def _(v, tgt=tgt, C=C):
            ctx.wait(v, "tensor")
            ctx.mark(v.tensor_copy(out=tgt[:], in_=ctx.PF[6][:, 0:C]), "vector")

    @block.vector
    def _(v):
        ctx.mark(v.tensor_scalar_mul(sb["b_bout_h"][:], sb["b_bout"][:], 0.5),
                 "vector")


class Expander:

    def __init__(self, ctx):
        a = ctx.alloc
        self.ctx = ctx
        self.idxbuf = [a(f"idx{i}", [128, 2752], BF16) for i in range(2)]
        self.sel = a("sel", [128, 16, 512], BF16)
        self.wt = [a(f"wt{i}", [128, 2752], BF16) for i in range(2)]
        d = ctx.d
        sb = ctx.sb
        jobs = []
        for b in range(NOB):
            jobs.append((d["gidx"], d["wg_nat"], b, 0, D, sb["lut_sb"]))
        for b in range(NOB):
            jobs.append((d["uidx"], d["wu_nat"], b, 0, D, sb["lut_sb"]))
        for b in range(NKC):
            jobs.append((d["didx"], d["wd_nat"], b, 0, 2752, sb["luth_sb"]))
            jobs.append((d["didx"], d["wd_nat"], b, 2752, 2752, sb["luth_sb"]))
        self.jobs = jobs
        self.li = 0
        self.ci = 0
        self.si = 0
        self.job_load = {}
        self.job_spill = {}
        self.read_done = [0, 0]
        self.spill_done = [0, 0]

    def _load(self):
        ctx = self.ctx
        if self.li >= len(self.jobs):
            return
        i = self.li
        (dsrc, ddst, b, c0, ncols, lsb) = self.jobs[i]
        slot = i % 2

        @ctx.block.sync
        def _(s):
            if i >= 2:
                ctx.wait(s, "vector", self.read_done[slot])
            self.job_load[i] = ctx.mark(s.dma_start(
                out=self.idxbuf[slot][:, 0:ncols],
                in_=dsrc[b * 128:(b + 1) * 128, c0:c0 + ncols]), "sync", 16)
        self.li += 1

    def pump(self):
        ctx = self.ctx
        if self.ci == 0 and self.si == 0:
            self._load()
            self._load()
        tr = getattr(ctx, "tr", None)
        self.pump_ct = getattr(self, "pump_ct", 0) + 1
        if tr is not None and (self.pump_ct % 2 == 0 or
                               self.ci >= len(self.jobs)):
            if tr.step():
                return
        if self.ci >= len(self.jobs):
            return
        i = self.ci
        (dsrc, ddst, b, c0, ncols, lsb) = self.jobs[i]
        slot = i % 2
        nsub = (ncols + 511) // 512
        sub = self.si

        @ctx.block.vector
        def _(v):
            if sub == 0:
                ctx.wait(v, "sync", self.job_load[i])
                if i >= 2:
                    ctx.wait(v, "sync", self.spill_done[slot])
            sel = self.sel
            w = min(512, ncols - sub * 512)
            srcv = self.idxbuf[slot][:, sub * 512: sub * 512 + w]
            for t in range(16):
                v.tensor_scalar(out=sel[:, t, 0:w], in0=srcv,
                                scalar1=float(t), scalar2=lsb[:, t:t + 1],
                                op0=ALU.is_equal, op1=ALU.mult)
            if w == 512:
                f = sel[:].rearrange("p c f -> p (c f)")
                v.tensor_add(f[:, 0:4096], f[:, 0:4096], f[:, 4096:8192])
                v.tensor_add(f[:, 0:2048], f[:, 0:2048], f[:, 2048:4096])
                v.tensor_add(f[:, 0:1024], f[:, 0:1024], f[:, 1024:2048])
                inst = v.tensor_add(
                    self.wt[slot][:, sub * 512: sub * 512 + 512],
                    f[:, 0:512], f[:, 512:1024])
            else:
                stride = 1
                while stride < 16:
                    for t0 in range(0, 16, 2 * stride):
                        dst = sel[:, t0, 0:w]
                        if 2 * stride == 16:
                            dst = self.wt[slot][:, sub * 512: sub * 512 + w]
                        inst = v.tensor_add(dst, sel[:, t0, 0:w],
                                            sel[:, t0 + stride, 0:w])
                    stride *= 2
            m = ctx.mark(inst, "vector")
            if sub == nsub - 1:
                self.read_done[slot] = m

        self.si += 1
        if self.si < nsub:
            return

        @ctx.block.sync
        def _(s):
            ctx.wait(s, "vector", self.read_done[slot])
            self.spill_done[slot] = ctx.mark(
                s.dma_start(out=ddst[b * 128:(b + 1) * 128, c0:c0 + ncols],
                            in_=self.wt[slot][:, 0:ncols]), "sync", 16)
            self.job_spill[i] = self.spill_done[slot]
        self.si = 0
        self.ci += 1
        self._load()

    def finish(self):
        while self.ci < len(self.jobs):
            self.pump()


class TrRound:

    def __init__(self, ctx):
        a = ctx.alloc
        self.ctx = ctx
        self.tb = [a(f"trb{i}", [128, 512], BF16) for i in range(2)]
        d = ctx.d
        steps = []
        fams = [(d["wg_nat"], d["wgT"], INTER, NKC, 0),
                (d["wu_nat"], d["wuT"], INTER, NKC, 43),
                (d["wd_nat"], d["wdT"], D, NOB, 86)]
        for (src_, dst, O, HB, base) in fams:
            nog = (O + 511) // 512
            for hb in range(HB):
                for og in range(nog):
                    rows = min(512, O - og * 512)
                    ntile = (rows + 127) // 128
                    if base < 86:
                        ready = base + min(4 * og + 3, 42)
                    else:
                        ready = 86 + 8 * og + 7
                    steps.append((src_, dst, og, hb, rows, ntile, ready))
        self.steps = steps
        self.i = 0
        self.pending = None
        self.store_done = [0, 0]
        self.load_done = [0, 0]

    def _store(self):
        ctx = self.ctx
        k = self.pending
        (src_, dst, og, hb, rows, ntile, ready) = self.steps[k]
        slot = k % 2

        @ctx.block.scalar
        def _(sce):
            ctx.wait(sce, "sync", self.load_done[slot])
            for j in range(ntile):
                w = min(128, rows - j * 128)
                inst = sce.dma_start(
                    out=dst[og * 4 + j, :, hb, 0:w],
                    in_=self.tb[slot][:, j * 128: j * 128 + w])
                self.store_done[slot] = ctx.mark(inst, "scalar", 16)
        self.pending = None

    def step(self):
        ctx = self.ctx
        did = False
        if self.pending is not None:
            self._store()
            did = True
        if self.i >= len(self.steps):
            return did
        (src_, dst, og, hb, rows, ntile, ready) = self.steps[self.i]
        rmark = ctx.exp.job_spill.get(ready)
        if rmark is None:
            return did
        slot = self.i % 2

        @ctx.block.sync
        def _(s):
            ctx.wait(s, "sync", rmark)
            if self.store_done[slot]:
                ctx.wait(s, "scalar", self.store_done[slot])
            self.load_done[slot] = ctx.mark(s.dma_start_transpose(
                out=self.tb[slot][:, 0:rows],
                in_=src_[og * 512: og * 512 + rows,
                         hb * 128:(hb + 1) * 128]), "sync", 16)
        self.pending = self.i
        self.i += 1
        return True

    def finish(self):
        while self.i < len(self.steps) or self.pending is not None:
            if not self.step():
                raise RuntimeError("trround source not spilled at finish")


def p_lm_scope_alloc(ctx):
    block, d, sb = ctx.block, ctx.d, ctx.sb
    a = ctx.alloc
    a("lmfT", [128, NKC, NL], BF16)
    a("gz", [H, L], BF16)
    a("kbias", [1, 3 * D], BF16)
    for i in range(2):
        a(f"wqb{i}", [128, NKC, 128], BF16)
        a(f"wvb{i}", [128, NKC, 256], BF16)

    @block.gpsimd
    def _(g):
        ctx.mark(g.dma_start(out=sb["kbias"][:, 0:D], in_=d["bv"][:]), "gpsimd", 16)
        ctx.mark(g.dma_start(out=sb["kbias"][:, D:2 * D], in_=d["blv"][:]), "gpsimd", 16)
        ctx.mark(g.dma_start(out=sb["kbias"][:, 2 * D:3 * D], in_=d["bgv"][:]), "gpsimd", 16)


def p_post1_alloc(ctx):
    ctx.tr = TrRound(ctx)
    _a0 = ctx.alloc
    _a0("mlm", [128, 128], BF16)
    _a0("mra", [128, 512], BF16)

    @ctx.block.sync
    def _(s):
        ctx.mark(s.dma_start(out=ctx.sb["mlm"][:], in_=ctx.d["mask_lm"][:]),
                 "sync", 16)
        ctx.mark(s.dma_start(out=ctx.sb["mra"][:], in_=ctx.d["mask_ra"][:]),
                 "sync", 16)

    a = ctx.alloc
    a("lmf2T", [128, NKC, NL], BF16)
    a("lmf3T", [128, NKC, NL], BF16)
    a("l2qT", [128, NKC, NL], BF16)
    a("gkT", [128, NKC, NL], BF16)
    a("lqT", [128, NKC, NL], BF16)
    a("lkT", [128, NKC, NL], BF16)
    a("lvtok", [128, D], BF16)
    a("loT", [128, NKC, NL], BF16)
    a("roT", [128, NKC, NL], BF16)
    a("gv_sb", [128, D], BF16)


def p_sweep1(ctx):
    block, d, sb = ctx.block, ctx.d, ctx.sb
    a = ctx.alloc
    xbuf = [a(f"xb{i}", [128, D], F32) for i in range(2)]
    xnb = [a("xnb0", [128, D], BF16)] * 2
    xnT = a("xnT", [128, NKC, 512], BF16)
    qT = a("qT_s", [128, NKC, 512], BF16)
    kT = a("kT_s", [128, NKC, 512], BF16)
    vsc = a("vsc", [128, 4, D], BF16)
    cos_sb = a("cos_sb", [128, 512], BF16)
    sin_sb = a("sin_sb", [128, 512], BF16)
    pq_sb = a("pq_sb", [128, NKC, 128], BF16)
    wqb = [ctx.sb["wqb0"], ctx.sb["wqb1"]]
    wvb = [ctx.sb["wvb0"], ctx.sb["wvb1"]]
    scrA = sb["scrA"]
    scrB = sb["scrB"]
    stat = sb["stat"]
    red = sb["red"]
    A_sb = a("A_sb", [128, 4, 64], BF16)
    AT_sb = a("AT_sb", [128, 4, 128], BF16)
    pa_sb = a("pa_sb", [128, 512], BF16)
    paT_sb = a("paT_sb", [128, 4, 128], BF16)
    olst = a("olst", [128, 512], BF16)

    @block.gpsimd
    def _(g):
        ctx.mark(g.dma_start(out=pq_sb[:], in_=d["PQ"][:].rearrange(
            "(c p) o -> p c o", p=128)), "gpsimd", 16)

    xb_free = [0, 0]
    xnb_free = [0, 0]
    wq_free = ctx.wqb_free
    wv_free = ctx.wvb_free
    for sc in range(NSC):
        @block.gpsimd
        def _(g, sc=sc):
            ctx.wait(g, "vector", ctx.n["vector"])
            ctx.mark(g.dma_start(out=cos_sb[:],
                                 in_=d["cosT"][:, sc * 512:(sc + 1) * 512]),
                     "gpsimd", 16)
            ctx.mark(g.dma_start(out=sin_sb[:],
                                 in_=d["sinT"][:, sc * 512:(sc + 1) * 512]),
                     "gpsimd", 16)

        prev_lnstats = [ctx.n["vector"]]
        for tc_rel in range(4) if "a" in SW1 else []:
            tc = sc * 4 + tc_rel
            slot = tc % 2

            xld = [0]

            @block.sync
            def _(s, tc=tc, slot=slot):
                if tc >= 2:
                    ctx.wait(s, "vector", xb_free[slot])
                xld[0] = ctx.mark(
                    s.dma_start(out=xbuf[slot][:],
                                in_=d["x"][tc * 128:(tc + 1) * 128, :]),
                    "sync", 16)

            @block.scalar
            def _(sce, slot=slot):
                ctx.wait(sce, "sync", xld[0])
                ctx.wait(sce, "vector", prev_lnstats[0])
                x = xbuf[slot][:]
                sce.activation(out=scrA[:], in_=x[:, 0:512], func=AF.Square)
                sce.activation(out=scrB[:], in_=x[:, 512:1024], func=AF.Square)
                ctx.mark(sce.activation(out=xnb[0][:, 0:512], in_=x[:, 0:512],
                                        func=AF.Square), "scalar")

            @block.vector
            def _(v):
                ctx.wait(v, "scalar")
                v.reduce_sum(red[:, 12:13], scrA[:], axis=mybir.AxisListType.X)
                ctx.mark(v.reduce_sum(red[:, 13:14], scrB[:],
                                      axis=mybir.AxisListType.X), "vector")

            @block.scalar
            def _(sce, slot=slot):
                ctx.wait(sce, "vector")
                x = xbuf[slot][:]
                sce.activation(out=scrA[:], in_=x[:, 1024:1536], func=AF.Square)
                sce.activation(out=scrB[:], in_=x[:, 1536:2048], func=AF.Square)
                ctx.mark(sce.activation(out=xnb[0][:, 0:512],
                                        in_=x[:, 1024:1536],
                                        func=AF.Square), "scalar")

            @block.vector
            def _(v, slot=slot):
                ctx.wait(v, "scalar")
                ctx.wait(v, "tensor")
                x = xbuf[slot][:]
                v.reduce_sum(red[:, 14:15], scrA[:], axis=mybir.AxisListType.X)
                v.reduce_sum(red[:, 15:16], scrB[:], axis=mybir.AxisListType.X)
                ctx.vguard(v, v.reduce_sum(stat[:, 1:2], x,
                                           axis=mybir.AxisListType.X))
                v.tensor_add(red[:, 8:9], red[:, 12:13], red[:, 13:14])
                ctx.vguard(v, v.tensor_add(red[:, 9:10], red[:, 14:15],
                                           red[:, 15:16]))
                v.tensor_add(red[:, 10:11], red[:, 8:9], red[:, 9:10])
                ctx.vguard(v, v.tensor_scalar_mul(stat[:, 2:3], stat[:, 1:2],
                                                  1.0 / D))
                v.tensor_scalar_mul(stat[:, 0:1], red[:, 10:11], 1.0 / D)
                ctx.vguard(v, v.tensor_scalar(out=stat[:, 3:4], in0=stat[:, 2:3],
                                              scalar1=stat[:, 2:3], scalar2=-1.0,
                                              op0=ALU.mult, op1=ALU.mult))
                v.tensor_add(stat[:, 4:5], stat[:, 0:1], stat[:, 3:4])
                ctx.mark(v.tensor_copy(out=red[:, 30:31], in_=stat[:, 4:5]),
                         "vector")
                prev_lnstats[0] = ctx.n["vector"]

            @block.scalar
            def _(sce):
                ctx.wait(sce, "vector")
                sce.activation(out=stat[:, 6:7], in_=stat[:, 4:5], func=AF.Ln,
                               bias=sb["epsc"][:], scale=1.0)
                ctx.mark(sce.activation(out=stat[:, 7:8], in_=stat[:, 4:5],
                                        func=AF.Ln, bias=sb["epsc"][:],
                                        scale=1.0), "scalar")
                ctx.wait(sce, "scalar")
                sce.activation(out=stat[:, 5:6], in_=stat[:, 6:7],
                               func=AF.Exp, scale=-0.5)
                ctx.mark(sce.activation(out=stat[:, 7:8], in_=stat[:, 6:7],
                                        func=AF.Exp, scale=-0.5), "scalar")

            @block.vector
            def _(v, slot=slot, tc=tc):
                ctx.wait(v, "scalar")
                xb_free[slot] = ctx.mark(
                    v.tensor_scalar(out=xnb[slot][:], in0=xbuf[slot][:],
                                    scalar1=stat[:, 2:3], scalar2=stat[:, 5:6],
                                    op0=ALU.subtract, op1=ALU.mult), "vector")

            if "s1a" in PUMP_PHASES:
                ctx.exp.pump()
                ctx.exp.pump()
            for half in range(2):
                @block.tensor
                def _(t, slot=slot, half=half):
                    ctx.wait(t, "vector", xb_free[slot])
                    ctx.bank_wait(t, "PB")
                    for j in range(8):
                        c = half * 8 + j
                        inst = t.transpose(out=ctx.PB[:, j * 128:(j + 1) * 128],
                                           in_=xnb[slot][:, c * 128:(c + 1) * 128],
                                           identity=sb["ident_sb"][:])
                    xnb_free[slot] = ctx.mark(inst, "tensor")

                @block.vector
                def _(v, tc_rel=tc_rel, half=half):
                    ctx.wait(v, "tensor")
                    ctx.mark(v.tensor_copy(
                        out=xnT[:, half * 8:(half + 1) * 8,
                                tc_rel * 128:(tc_rel + 1) * 128],
                        in_=ctx.PB[:].rearrange("p (c f) -> p c f", c=8)),
                        "vector")
                    ctx.bank_read("PB", "vector")



        xnT_ready = ctx.n["vector"]
        if "a" in SW1:
            @block.sync
            def _(s, sc=sc):
                ctx.wait(s, "vector")
                ctx.mark(s.dma_start(out=d["dxn"][sc], in_=xnT[:]), "sync", 16)

        rope_prev = [max(xnT_ready, prev_lnstats[0])]
        for grp in range(16) if "b" in SW1 else []:
            is_q = grp < 8
            j = grp % 8
            ce = j if is_q else 16 + j
            co = ce + 8
            dstT = qT if is_q else kT
            ddst = d["dq"] if is_q else d["dk"]
            for par, oc in ((0, ce), (1, co)):
                slot = par

                @block.gpsimd
                def _(g, oc=oc, slot=slot, grp=grp):
                    if wq_free[slot]:
                        ctx.wait(g, "tensor", wq_free[slot])
                    ctx.mark(g.dma_start(out=wqb[slot][:],
                                         in_=d["wqk"][oc]), "gpsimd", 16)

                @block.tensor
                def _(t, slot=slot, par=par, oc=oc):
                    ctx.wait(t, "gpsimd")
                    ctx.wait(t, "vector", xnT_ready)
                    ctx.bank_wait(t, par)
                    for kc in range(NKC):
                        t.matmul(out=ctx.PF[par][:],
                                 lhsT=wqb[slot][:, kc, :],
                                 rhs=xnT[:, kc, :],
                                 start=(kc == 0), stop=False)
                    for q4 in range(4):
                        inst = t.matmul(
                            out=ctx.PF[par][:, q4 * 128:(q4 + 1) * 128],
                            lhsT=sb["bqkr"][0:1, oc * 128:(oc + 1) * 128],
                            rhs=sb["ones_sb"][:],
                            start=False, stop=True)
                    wq_free[slot] = ctx.mark(inst, "tensor")

            @block.vector
            def _(v, j=j, dstT=dstT, sc=sc):
                ctx.wait(v, "tensor")
                ctx.wait(v, "gpsimd")
                ctx.wait(v, "scalar")
                cs = cos_sb[:]
                sn = sin_sb[:]
                v.tensor_mul(scrA[:], ctx.PF[0][:], cs)
                v.tensor_mul(scrB[:], ctx.PF[1][:], sn)
                v.tensor_sub(dstT[:, j, :], scrA[:], scrB[:])
                v.tensor_mul(scrA[:], ctx.PF[0][:], sn)
                v.tensor_mul(scrB[:], ctx.PF[1][:], cs)
                ctx.mark(v.tensor_add(dstT[:, j + 8, :], scrA[:], scrB[:]),
                         "vector")
                rope_prev[0] = ctx.n["vector"]
                ctx.bank_read(0, "vector")
                ctx.bank_read(1, "vector")

            @block.sync
            def _(s, j=j, dstT=dstT, ddst=ddst, sc=sc):
                ctx.wait(s, "vector")
                ctx.mark(s.dma_start(out=ddst[sc, :, j, :],
                                     in_=dstT[:, j, :]), "sync", 16)
                ctx.mark(s.dma_start(out=ddst[sc, :, j + 8, :],
                                     in_=dstT[:, j + 8, :]), "sync", 16)

            if "s1b" in PUMP_PHASES:
                ctx.exp.pump()
                ctx.exp.pump()

        for vb in range(8) if "v" in SW1 else []:
            slot = vb % 2

            @block.gpsimd
            def _(g, vb=vb, slot=slot, sc=sc):
                if wv_free[slot]:
                    ctx.wait(g, "tensor", wv_free[slot])
                ctx.mark(g.dma_start(out=wvb[slot][:],
                                     in_=d["wv"][vb]), "gpsimd", 16)

            for tc_rel in range(4):
                bank = 2 + (tc_rel % 2)

                @block.tensor
                def _(t, vb=vb, slot=slot, tc_rel=tc_rel, bank=bank):
                    if tc_rel == 0:
                        ctx.wait(t, "gpsimd")
                    ctx.wait(t, "vector", xnT_ready)
                    ctx.bank_wait(t, bank)
                    for kc in range(NKC):
                        t.matmul(out=ctx.PF[bank][:, 0:256],
                                 lhsT=xnT[:, kc, tc_rel * 128:(tc_rel + 1) * 128],
                                 rhs=wvb[slot][:, kc, :],
                                 start=(kc == 0), stop=False)
                    inst = t.matmul(out=ctx.PF[bank][:, 0:256], lhsT=sb["ones_sb"][:],
                                    rhs=sb["kbias"][0:1, vb * 256:(vb + 1) * 256],
                                    start=False, stop=True)
                    w = ctx.mark(inst, "tensor")
                    if tc_rel == 3:
                        wv_free[slot] = w

                @block.vector
                def _(v, vb=vb, tc_rel=tc_rel, bank=bank):
                    ctx.wait(v, "tensor")
                    ctx.mark(v.tensor_copy(
                        out=vsc[:, tc_rel, vb * 256:(vb + 1) * 256],
                        in_=ctx.PF[bank][:, 0:256]), "vector")
                    ctx.bank_read(bank, "vector")

        @block.sync
        def _(s, sc=sc):
            ctx.wait(s, "vector")
            for tc_rel in range(4):
                ctx.mark(s.dma_start(out=d["dv"][sc * 4 + tc_rel, :, :],
                                     in_=vsc[:, tc_rel, :]), "sync", 16)

        if "g" not in SW1:
            continue
        @block.tensor
        def _(t):
            ctx.wait(t, "vector", xnT_ready)
            ctx.wait(t, "gpsimd")
            ctx.bank_wait(t, 4)
            for kc in range(NKC):
                inst = t.matmul(out=ctx.PF[4][0:32, :],
                                lhsT=sb["wgate_sb"][:, kc, :],
                                rhs=xnT[:, kc, :],
                                start=(kc == 0), stop=(kc == NKC - 1))
            ctx.mark(inst, "tensor")

        @block.scalar
        def _(sce, sc=sc):
            ctx.wait(sce, "tensor")
            ctx.mark(sce.activation(out=sb["gz"][:, sc * 512:(sc + 1) * 512],
                                    in_=ctx.PF[4][0:32, :], func=AF.Identity,
                                    bias=sb["bgate_sb"][:], scale=1.0), "scalar")
            ctx.bank_read(4, "scalar")

        qk_done = ctx.n["vector"]
        if "psc" not in SW1:
            continue
        @block.tensor
        def _(t):
            ctx.wait(t, "vector", qk_done)
            ctx.wait(t, "gpsimd")
            ctx.bank_wait(t, 5)
            for kc in range(NKC):
                inst = t.matmul(out=ctx.PF[5][:], lhsT=pq_sb[:, kc, :],
                                rhs=kT[:, kc, :],
                                start=(kc == 0), stop=(kc == NKC - 1))
            ctx.mark(inst, "tensor")

        @block.vector
        def _(v):
            ctx.wait(v, "tensor")
            ctx.wait(v, "scalar")
            s3 = ctx.PF[5][:].rearrange("p (s f) -> p s f", s=8)
            ctx.vguard(v, v.reduce_max(red[:, 0:8], s3, axis=mybir.AxisListType.X))
            ctx.mark(v.tensor_sub(scrA[:].rearrange("p (s f) -> p s f", s=8), s3,
                                  red[:, 0:8].unsqueeze(2).broadcast_to(
                                      [128, 8, 64])), "vector")
            ctx.bank_read(5, "vector")
        pool_sub = ctx.n["vector"]

        @block.scalar
        def _(sce):
            ctx.wait(sce, "vector", pool_sub)
            ctx.mark(sce.activation(out=scrB[:], in_=scrA[:], func=AF.Exp,
                                    scale=SCALE), "scalar")

        @block.vector
        def _(v):
            ctx.wait(v, "scalar")
            e3 = scrB[:].rearrange("p (s f) -> p s f", s=8)
            ctx.vguard(v, v.reduce_sum(red[:, 8:16], e3, axis=mybir.AxisListType.X))
            ctx.vguard(v, v.reciprocal(red[:, 16:24], red[:, 8:16]))
            ctx.mark(v.tensor_mul(pa_sb[:].rearrange("p (s f) -> p s f", s=8),
                                  e3, red[:, 16:24].unsqueeze(2).broadcast_to(
                                      [128, 8, 64])), "vector")
        pa_mark = ctx.n["vector"]

        @block.tensor
        def _(t):
            ctx.wait(t, "vector", pa_mark)
            ctx.bank_wait(t, "PB")
            for tc_rel in range(4):
                inst = t.transpose(out=ctx.PB[:, tc_rel * 128:(tc_rel + 1) * 128],
                                   in_=pa_sb[:, tc_rel * 128:(tc_rel + 1) * 128],
                                   identity=sb["ident_sb"][:])
            ctx.mark(inst, "tensor")

        @block.vector
        def _(v):
            ctx.wait(v, "tensor")
            ctx.mark(v.tensor_copy(out=paT_sb[:],
                                   in_=ctx.PB[:, 0:512].rearrange(
                                       "p (c f) -> p c f", c=4)), "vector")
            ctx.bank_read("PB", "vector")
        paT_mark = ctx.n["vector"]

        nhp = 16 if "loc" in SW1 else (1 if "loc1" in SW1 else 0)
        for hp in range(nhp):
            for hh in range(2):
                h = 2 * hp + hh
                bank = hh

                @block.tensor
                def _(t, h=h, bank=bank):
                    ctx.wait(t, "vector", qk_done)
                    ctx.bank_wait(t, bank)
                    for nrel in range(8):
                        for par in range(2):
                            ch = par * 8 + h // 4
                            rb = (h % 4) * 32
                            inst = t.matmul(
                                out=ctx.PF[bank][(nrel % 2) * 64:(nrel % 2) * 64 + 64,
                                                 (nrel // 2) * 64:(nrel // 2) * 64 + 64],
                                lhsT=qT[rb:rb + 32, ch, nrel * 64:(nrel + 1) * 64],
                                rhs=kT[rb:rb + 32, ch, nrel * 64:(nrel + 1) * 64],
                                start=(par == 0), stop=(par == 1),
                                tile_position=(rb, (nrel % 2) * 64))
                    ctx.mark(inst, "tensor")

                @block.vector
                def _(v, bank=bank):
                    ctx.wait(v, "tensor")
                    ctx.wait(v, "scalar")
                    s3 = ctx.PF[bank][:, 0:256].rearrange("p (s f) -> p s f", s=4)
                    ctx.vguard(v, v.reduce_max(red[:, 0:4], s3,
                                               axis=mybir.AxisListType.X))
                    ctx.mark(v.tensor_sub(
                        scrA[:, 0:256].rearrange("p (s f) -> p s f", s=4), s3,
                        red[:, 0:4].unsqueeze(2).broadcast_to([128, 4, 64])),
                        "vector")
                    ctx.bank_read(bank, "vector")
                lsub_mark = ctx.n["vector"]

                @block.scalar
                def _(sce, lsub_mark=lsub_mark):
                    ctx.wait(sce, "vector", lsub_mark)
                    ctx.mark(sce.activation(out=scrB[:, 0:256], in_=scrA[:, 0:256],
                                            func=AF.Exp, scale=SCALE), "scalar")

                @block.vector
                def _(v):
                    ctx.wait(v, "scalar")
                    ctx.wait(v, "tensor")
                    e3 = scrB[:, 0:256].rearrange("p (s f) -> p s f", s=4)
                    v.tensor_mul(scrA[:, 0:256].rearrange("p (s f) -> p s f", s=4),
                                 e3, sb["mloc"][:].unsqueeze(1).broadcast_to(
                                     [128, 4, 64]))
                    m3 = scrA[:, 0:256].rearrange("p (s f) -> p s f", s=4)
                    ctx.vguard(v, v.reduce_sum(red[:, 4:8], m3,
                                               axis=mybir.AxisListType.X))
                    ctx.vguard(v, v.reciprocal(red[:, 16:20], red[:, 4:8]))
                    ctx.mark(v.tensor_mul(A_sb[:], m3,
                                          red[:, 16:20].unsqueeze(2).broadcast_to(
                                              [128, 4, 64])), "vector")
                A_mark = ctx.n["vector"]

                @block.tensor
                def _(t, A_mark=A_mark):
                    ctx.wait(t, "vector", A_mark)
                    ctx.bank_wait(t, "PB")
                    for g_ in range(4):
                        t.transpose(out=ctx.PB[0:64, g_ * 128:(g_ + 1) * 128],
                                    in_=A_sb[:, g_, :],
                                    identity=sb["ident_sb"][:])
                        inst = t.transpose(
                            out=ctx.PB[64:128, g_ * 128:(g_ + 1) * 128],
                            in_=A_sb[:, g_, :], identity=sb["ident_sb"][:],
                            tile_position=(0, 64))
                    ctx.mark(inst, "tensor")

                @block.vector
                def _(v):
                    ctx.wait(v, "tensor")
                    ctx.mark(v.tensor_copy(out=AT_sb[:],
                                           in_=ctx.PB[:, 0:512].rearrange(
                                               "p (c f) -> p c f", c=4)), "vector")
                    ctx.bank_read("PB", "vector")
                AT_mark = ctx.n["vector"]

                @block.tensor
                def _(t, h=h, hh=hh, AT_mark=AT_mark):
                    ctx.wait(t, "vector", AT_mark)
                    if hh == 0:
                        ctx.bank_wait(t, 2)
                        ctx.bank_wait(t, 6)
                    for nrel in range(8):
                        rb = (nrel % 2) * 64
                        bnk = 2 if nrel % 2 == 0 else 6
                        inst = t.matmul(
                            out=ctx.PF[bnk][hh * 64:hh * 64 + 64,
                                            (nrel // 2) * 64:(nrel // 2) * 64 + 64],
                            lhsT=vsc[rb:rb + 64, nrel // 2, h * 64:(h + 1) * 64],
                            rhs=AT_sb[rb:rb + 64, nrel // 2, rb:rb + 64],
                            start=True, stop=True,
                            tile_position=(rb, hh * 64))
                    ctx.mark(inst, "tensor")

            @block.vector
            def _(v):
                ctx.wait(v, "tensor")
                ctx.wait(v, "sync")
                oview = olst[:].rearrange("p (n2 two f) -> p n2 two f", two=2, f=64)
                v.tensor_copy(out=oview[:, :, 0, :],
                              in_=ctx.PF[2][:, 0:256].rearrange(
                                  "p (n f) -> p n f", f=64))
                ctx.mark(v.tensor_copy(out=oview[:, :, 1, :],
                                       in_=ctx.PF[6][:, 0:256].rearrange(
                                           "p (n f) -> p n f", f=64)), "vector")
                ctx.bank_read(2, "vector")
                ctx.bank_read(6, "vector")

            @block.sync
            def _(s, hp=hp, sc=sc):
                ctx.wait(s, "vector")
                ctx.mark(s.dma_start(out=d["dol"][sc, hp], in_=olst[:]),
                         "sync", 16)

            if "s1c" in PUMP_PHASES:
                ctx.exp.pump()
                ctx.exp.pump()
                ctx.exp.pump()

        if "lm2" not in SW1:
            continue
        @block.tensor
        def _(t):
            ctx.wait(t, "vector", paT_mark)
            ctx.bank_wait(t, 3)
            ctx.bank_wait(t, 5)
            for h in range(H):
                for nrel in range(8):
                    rb = (nrel % 2) * 64
                    bnk = 3 if nrel % 2 == 0 else 5
                    inst = t.matmul(
                        out=ctx.PF[bnk][(h % 2) * 64:(h % 2) * 64 + 64,
                                        (h // 2) * 16 + (nrel // 2) * 4:
                                        (h // 2) * 16 + (nrel // 2) * 4 + 4],
                        lhsT=vsc[rb:rb + 64, nrel // 2, h * 64:(h + 1) * 64],
                        rhs=paT_sb[rb:rb + 64, nrel // 2, 4 * h:4 * h + 4],
                        start=True, stop=True,
                        tile_position=(rb, (h % 2) * 64))
            ctx.mark(inst, "tensor")

        @block.vector
        def _(v, sc=sc):
            ctx.wait(v, "tensor")
            dview = sb["lmfT"][:, :, sc * 32:(sc + 1) * 32].rearrange(
                "p c (n2 two m) -> p c n2 two m", two=2, m=4)
            v.tensor_copy(out=dview[:, :, :, 0, :],
                          in_=ctx.PF[3][:, 0:256].rearrange(
                              "p (c n2 m) -> p c n2 m", n2=4, m=4))
            ctx.mark(v.tensor_copy(out=dview[:, :, :, 1, :],
                                   in_=ctx.PF[5][:, 0:256].rearrange(
                                       "p (c n2 m) -> p c n2 m", n2=4, m=4)),
                     "vector")
            ctx.bank_read(3, "vector")
            ctx.bank_read(5, "vector")

    barrier(ctx)
    if "grep" not in SW1:
        return

    for qq in range(4):
        @block.scalar
        def _(sce, qq=qq):
            ctx.wait(sce, "vector")
            ctx.mark(sce.activation(out=scrA[0:32, :],
                                    in_=sb["gz"][:, qq * 512:(qq + 1) * 512],
                                    func=AF.Tanh, scale=0.5), "scalar")

        @block.vector
        def _(v, qq=qq):
            ctx.wait(v, "scalar")
            ctx.mark(v.tensor_scalar(out=sb["gz"][:, qq * 512:(qq + 1) * 512],
                                     in0=scrA[0:32, :], scalar1=0.5,
                                     scalar2=0.5, op0=ALU.mult, op1=ALU.add),
                     "vector")

    @block.sync
    def _(s):
        ctx.wait(s, "vector")
        ctx.mark(s.dma_start(out=d["dgrep"][:], in_=sb["gz"][:]), "sync", 16)


def _wproj_to_chunks(ctx, wname, bias_sb, rhs_sb, out_sb, resid_sb=None,
                     tag=""):
    block, d, sb = ctx.block, ctx.d, ctx.sb
    wqb = [sb["wqb0"], sb["wqb1"]]
    wl_free = ctx.wqb_free
    for oc in range(NKC):
        slot = oc % 2

        @block.gpsimd
        def _(g, oc=oc, slot=slot):
            if wl_free[slot]:
                ctx.wait(g, "tensor", wl_free[slot])
            ctx.mark(g.dma_start(out=wqb[slot][:], in_=d[wname][oc]),
                     "gpsimd", 16)

        bank = oc % 2

        @block.tensor
        def _(t, slot=slot, bank=bank, oc=oc):
            ctx.wait(t, "gpsimd")
            if oc == 0:
                ctx.wait(t, "vector")
                ctx.wait(t, "scalar")
            ctx.bank_wait(t, bank)
            for kc in range(NKC):
                inst = t.matmul(out=ctx.PF[bank][:, 0:NL],
                                lhsT=wqb[slot][:, kc, :],
                                rhs=rhs_sb[:, kc, :],
                                start=(kc == 0), stop=(kc == NKC - 1))
            wl_free[slot] = ctx.mark(inst, "tensor")

        if resid_sb is None:
            @block.vector
            def _(v, oc=oc, bank=bank):
                ctx.wait(v, "tensor")
                ctx.mark(v.tensor_scalar(out=out_sb[:, oc, :],
                                         in0=ctx.PF[bank][:, 0:NL],
                                         scalar1=bias_sb[:, oc:oc + 1],
                                         scalar2=None, op0=ALU.add),
                         "vector")
                ctx.bank_read(bank, "vector")
        else:
            @block.vector
            def _(v, oc=oc, bank=bank):
                ctx.wait(v, "tensor")
                ctx.mark(v.scalar_tensor_tensor(
                    out=out_sb[:, oc, :], in0=ctx.PF[bank][:, 0:NL],
                    scalar=bias_sb[:, oc:oc + 1], in1=resid_sb[:, oc, :],
                    op0=ALU.add, op1=ALU.add), "vector")
                ctx.bank_read(bank, "vector")


def _vproj_tok(ctx, wname, kbias_off, lhsT_sb, out_sb):
    block, d, sb = ctx.block, ctx.d, ctx.sb
    wvb = [sb["wvb0"], sb["wvb1"]]
    wv_free = ctx.wvb_free
    for vb in range(8):
        slot = vb % 2
        bank = 2 + vb % 2

        @block.gpsimd
        def _(g, vb=vb, slot=slot):
            if wv_free[slot]:
                ctx.wait(g, "tensor", wv_free[slot])
            ctx.mark(g.dma_start(out=wvb[slot][:], in_=d[wname][vb]),
                     "gpsimd", 16)

        @block.tensor
        def _(t, vb=vb, slot=slot, bank=bank):
            ctx.wait(t, "gpsimd")
            ctx.wait(t, "vector")
            ctx.bank_wait(t, bank)
            for kc in range(NKC):
                t.matmul(out=ctx.PF[bank][:, 0:256], lhsT=lhsT_sb[:, kc, :],
                         rhs=wvb[slot][:, kc, :],
                         start=(kc == 0), stop=False)
            inst = t.matmul(out=ctx.PF[bank][:, 0:256], lhsT=sb["ones_sb"][:],
                            rhs=sb["kbias"][0:1, kbias_off + vb * 256:
                                            kbias_off + (vb + 1) * 256],
                            start=False, stop=True)
            wv_free[slot] = ctx.mark(inst, "tensor")

        @block.vector
        def _(v, vb=vb, bank=bank):
            ctx.wait(v, "tensor")
            ctx.mark(v.tensor_copy(out=out_sb[:, vb * 256:(vb + 1) * 256],
                                   in_=ctx.PF[bank][:, 0:256]), "vector")
            ctx.bank_read(bank, "vector")


def p_landmark(ctx):
    block, d, sb = ctx.block, ctx.d, ctx.sb
    scrA, scrB = sb["scrA"], sb["scrB"]
    red = sb["red"]
    barrier(ctx)
    _wproj_to_chunks(ctx, "wlq", sb["b_blq"], sb["lmfT"], sb["lqT"])
    _wproj_to_chunks(ctx, "wlk", sb["b_blk"], sb["lmfT"], sb["lkT"])
    _vproj_tok(ctx, "wlv", D, sb["lmfT"], sb["lvtok"])
    la_sb = ctx.alloc("la_sb", [128, 128], BF16)
    laT_sb = ctx.alloc("laT_sb", [128, 128], BF16)

    for h in range(H):
        hb = (h % 2) * 64
        bank = h % 2

        @block.tensor
        def _(t, h=h, hb=hb, bank=bank):
            ctx.bank_wait(t, bank)
            ctx.mark(t.matmul(out=ctx.PF[bank][:, 0:128],
                              lhsT=sb["lqT"][hb:hb + 64, h // 2, :],
                              rhs=sb["lkT"][hb:hb + 64, h // 2, :],
                              start=True, stop=True,
                              tile_position=(hb, 0)), "tensor")

        @block.vector
        def _(v, bank=bank):
            ctx.wait(v, "tensor")
            ctx.wait(v, "scalar")
            ctx.vguard(v, v.reduce_max(red[:, 0:1], ctx.PF[bank][:, 0:128],
                                       axis=mybir.AxisListType.X))
            v.tensor_scalar_mul(red[:, 1:2], red[:, 0:1], -SCALE)
            ctx.mark(v.tensor_copy(out=red[:, 30:31], in_=red[:, 0:1]),
                     "vector")
        mneg_mark = ctx.n["vector"]

        @block.scalar
        def _(sce, bank=bank, mneg_mark=mneg_mark):
            ctx.wait(sce, "vector", mneg_mark)
            ctx.mark(sce.activation(out=scrA[:, 0:128], in_=ctx.PF[bank][:, 0:128],
                                    func=AF.Exp, scale=SCALE,
                                    bias=red[:, 1:2]), "scalar")
            ctx.bank_read(bank, "scalar")

        @block.vector
        def _(v):
            ctx.wait(v, "scalar")
            v.tensor_mul(scrB[:, 0:128], scrA[:, 0:128], sb["mlm"][:])
            ctx.vguard(v, v.reduce_sum(red[:, 2:3], scrB[:, 0:128],
                                       axis=mybir.AxisListType.X))
            ctx.vguard(v, v.reciprocal(red[:, 3:4], red[:, 2:3]))
            ctx.mark(v.tensor_scalar_mul(la_sb[:], scrB[:, 0:128],
                                         red[:, 3:4]), "vector")
        la_mark = ctx.n["vector"]

        @block.tensor
        def _(t, la_mark=la_mark):
            ctx.wait(t, "vector", la_mark)
            ctx.bank_wait(t, "PB")
            ctx.mark(t.transpose(out=ctx.PB[:, 0:128], in_=la_sb[:],
                                 identity=sb["ident_sb"][:]), "tensor")

        @block.vector
        def _(v):
            ctx.wait(v, "tensor")
            ctx.mark(v.tensor_copy(out=laT_sb[:], in_=ctx.PB[:, 0:128]), "vector")
            ctx.bank_read("PB", "vector")
        laT_mark = ctx.n["vector"]

        @block.tensor
        def _(t, h=h, hb=hb, laT_mark=laT_mark):
            ctx.wait(t, "vector", laT_mark)
            if hb == 0:
                ctx.bank_wait(t, 2)
            ctx.mark(t.matmul(out=ctx.PF[2][hb:hb + 64, 0:128],
                              lhsT=sb["lvtok"][:, h * 64:(h + 1) * 64],
                              rhs=laT_sb[:], start=True, stop=True,
                              tile_position=(0, hb)), "tensor")

        if h % 2 == 1:
            @block.vector
            def _(v, h=h):
                ctx.wait(v, "tensor")
                ctx.mark(v.tensor_copy(out=sb["loT"][:, h // 2, :],
                                       in_=ctx.PF[2][:, 0:128]), "vector")
                ctx.bank_read(2, "vector")
        if "lm" in PUMP_PHASES:
            ctx.exp.pump()
            ctx.exp.pump()
            ctx.exp.pump()

    _wproj_to_chunks(ctx, "wlmout", sb["b_blmout"], sb["loT"], sb["lmf2T"],
                     resid_sb=sb["lmfT"])
    _wproj_to_chunks(ctx, "wl2q", sb["b_bl2q"], sb["lmf2T"], sb["l2qT"])
    if DEBUG:
        @block.sync
        def _(s):
            ctx.wait(s, "vector")
            ctx.wait(s, "scalar")
            ctx.mark(s.dma_start(out=d["dbg_lmf"][:], in_=sb["lmfT"][:]), "sync", 16)
            ctx.mark(s.dma_start(out=d["dbg_lmf2"][:], in_=sb["lmf2T"][:]), "sync", 16)
            ctx.mark(s.dma_start(out=d["dbg_lqt"][:], in_=sb["lqT"][:]), "sync", 16)
            ctx.mark(s.dma_start(out=d["dbg_lkt"][:], in_=sb["lkT"][:]), "sync", 16)
            ctx.mark(s.dma_start(out=d["dbg_lvt"][:], in_=sb["lvtok"][:]), "sync", 16)
            ctx.mark(s.dma_start(out=d["dbg_lot"][:], in_=sb["loT"][:]), "sync", 16)


def p_sweep2a(ctx):
    block, d, sb = ctx.block, ctx.d, ctx.sb
    kT = ctx.alloc("kT2", [128, NKC, 512], BF16)
    vsc = ctx.alloc("vsc2", [128, 4, D], BF16)
    scrA, scrB, red = sb["scrA"], sb["scrB"], sb["red"]
    raA = ctx.alloc("raA", [128, 512], BF16)
    raT = ctx.alloc("raT", [128, 4, 128], BF16)
    for sc in range(NSC):
        barrier(ctx)

        @block.sync
        def _(s, sc=sc):
            ctx.mark(s.dma_start(out=kT[:], in_=d["dk"][sc]), "sync", 16)
            for tc_rel in range(4):
                ctx.mark(s.dma_start(out=vsc[:, tc_rel, :],
                                     in_=d["dv"][sc * 4 + tc_rel, :, :]),
                         "sync", 16)

        for hg in range(8):
            bank = hg % 2

            @block.tensor
            def _(t, hg=hg, sc=sc, bank=bank):
                ctx.wait(t, "sync")
                ctx.bank_wait(t, bank)
                for hsub in range(4):
                    h = 4 * hg + hsub
                    for par in range(2):
                        ch = par * 8 + h // 4
                        rb = (h % 4) * 32
                        inst = t.matmul(
                            out=ctx.PF[bank][hsub * 32:(hsub + 1) * 32, :],
                            lhsT=sb["l2qT"][rb:rb + 32, ch,
                                            sc * 32:(sc + 1) * 32],
                            rhs=kT[rb:rb + 32, ch, :],
                            start=(par == 0), stop=(par == 1),
                            tile_position=(rb, hsub * 32))
                ctx.mark(inst, "tensor")

            @block.vector
            def _(v, bank=bank):
                ctx.wait(v, "tensor")
                ctx.wait(v, "scalar")
                ctx.vguard(v, v.reduce_max(red[:, 0:1], ctx.PF[bank][:],
                                           axis=mybir.AxisListType.X))
                v.tensor_scalar_mul(red[:, 1:2], red[:, 0:1], -SCALE)
                ctx.mark(v.tensor_copy(out=red[:, 30:31], in_=red[:, 0:1]),
                         "vector")
            mneg2_mark = ctx.n["vector"]

            @block.scalar
            def _(sce, bank=bank, mneg2_mark=mneg2_mark):
                ctx.wait(sce, "vector", mneg2_mark)
                ctx.mark(sce.activation(out=scrA[:], in_=ctx.PF[bank][:],
                                        func=AF.Exp, scale=SCALE,
                                        bias=red[:, 1:2]), "scalar")
                ctx.bank_read(bank, "scalar")

            @block.vector
            def _(v):
                ctx.wait(v, "scalar")
                v.tensor_mul(scrB[:], scrA[:], sb["mra"][:])
                ctx.vguard(v, v.reduce_sum(red[:, 2:3], scrB[:],
                                           axis=mybir.AxisListType.X))
                ctx.vguard(v, v.reciprocal(red[:, 3:4], red[:, 2:3]))
                ctx.mark(v.tensor_scalar_mul(raA[:], scrB[:], red[:, 3:4]),
                         "vector")
            raA_mark = ctx.n["vector"]

            @block.tensor
            def _(t, raA_mark=raA_mark):
                ctx.wait(t, "vector", raA_mark)
                ctx.bank_wait(t, "PB")
                for pp in range(4):
                    inst = t.transpose(out=ctx.PB[:, pp * 128:(pp + 1) * 128],
                                       in_=raA[:, pp * 128:(pp + 1) * 128],
                                       identity=sb["ident_sb"][:])
                ctx.mark(inst, "tensor")

            @block.vector
            def _(v):
                ctx.wait(v, "tensor")
                ctx.mark(v.tensor_copy(out=raT[:],
                                       in_=ctx.PB[:, 0:512].rearrange(
                                           "p (c f) -> p c f", c=4)), "vector")
                ctx.bank_read("PB", "vector")
            raT_mark = ctx.n["vector"]

            @block.tensor
            def _(t, hg=hg, raT_mark=raT_mark):
                ctx.wait(t, "vector", raT_mark)
                if hg == 0:
                    ctx.bank_wait(t, 2)
                    ctx.bank_wait(t, 3)
                for hsub in range(4):
                    h = 4 * hg + hsub
                    for nrel in range(8):
                        rb = (nrel % 2) * 64
                        bnk = 2 if nrel % 2 == 0 else 3
                        inst = t.matmul(
                            out=ctx.PF[bnk][(h % 2) * 64:(h % 2) * 64 + 64,
                                            (h // 2) * 16 + (nrel // 2) * 4:
                                            (h // 2) * 16 + (nrel // 2) * 4 + 4],
                            lhsT=vsc[rb:rb + 64, nrel // 2, h * 64:(h + 1) * 64],
                            rhs=raT[rb:rb + 64, nrel // 2,
                                    hsub * 32 + nrel * 4:hsub * 32 + nrel * 4 + 4],
                            start=True, stop=True,
                            tile_position=(rb, (h % 2) * 64))
                ctx.mark(inst, "tensor")

            if "s2a" in PUMP_PHASES:
                for _i in range(6):
                    ctx.exp.pump()

        @block.vector
        def _(v, sc=sc):
            ctx.wait(v, "tensor")
            dview = sb["roT"][:, :, sc * 32:(sc + 1) * 32].rearrange(
                "p c (n2 two m) -> p c n2 two m", two=2, m=4)
            v.tensor_copy(out=dview[:, :, :, 0, :],
                          in_=ctx.PF[2][:, 0:256].rearrange(
                              "p (c n2 m) -> p c n2 m", n2=4, m=4))
            ctx.mark(v.tensor_copy(out=dview[:, :, :, 1, :],
                                   in_=ctx.PF[3][:, 0:256].rearrange(
                                       "p (c n2 m) -> p c n2 m", n2=4, m=4)),
                     "vector")
            ctx.bank_read(2, "vector")
            ctx.bank_read(3, "vector")


def p_lmtail(ctx):
    block, d, sb = ctx.block, ctx.d, ctx.sb
    barrier(ctx)
    _wproj_to_chunks(ctx, "wl2out", sb["b_bl2out"], sb["roT"], sb["lmf3T"],
                     resid_sb=sb["lmf2T"])
    _wproj_to_chunks(ctx, "wgk", sb["b_bgk"], sb["lmf3T"], sb["gkT"])
    _vproj_tok(ctx, "wgv", 2 * D, sb["lmf3T"], sb["gv_sb"])
    if DEBUG:
        @block.sync
        def _(s):
            ctx.wait(s, "vector")
            ctx.wait(s, "scalar")
            ctx.mark(s.dma_start(out=d["dbg_lmf3"][:], in_=sb["lmf3T"][:]), "sync", 16)
            ctx.mark(s.dma_start(out=d["dbg_gk"][:], in_=sb["gkT"][:]), "sync", 16)
            ctx.mark(s.dma_start(out=d["dbg_gv"][:], in_=sb["gv_sb"][:]), "sync", 16)


def p_sweep2b(ctx):
    block, d, sb = ctx.block, ctx.d, ctx.sb
    qT = ctx.alloc("qT2", [128, NKC, 512], BF16)
    scrA, scrB, red = sb["scrA"], sb["scrB"], sb["red"]
    mg = ctx.alloc("mg", [128, NTC, 128], BF16)
    agb = ctx.alloc("agb", [128, 512], BF16)
    agT = ctx.alloc("agT", [128, 4, 128], BF16)

    @block.sync
    def _(s):
        ctx.mark(s.dma_start(out=mg[:], in_=d["mask_g"][:].rearrange(
            "(c p) l -> p c l", p=128)), "sync", 16)
    mg_mark = ctx.n["sync"]
    olT = [ctx.alloc(f"olT{i}", [128, 512], BF16) for i in range(2)]
    grep = [ctx.alloc(f"grep{i}", [128, 512], BF16) for i in range(2)]
    gated = ctx.alloc("gated", [128, NKC, 512], BF16)
    sstg = ctx.alloc("sstg", [128, 512], BF16)
    ol_used = [0, 0]

    for sc in range(NSC):
        barrier(ctx)

        @block.sync
        def _(s, sc=sc):
            ctx.mark(s.dma_start(out=qT[:], in_=d["dq"][sc]), "sync", 16)

        for hp in range(16):
            slot = hp % 2

            @block.sync
            def _(s, hp=hp, sc=sc, slot=slot):
                if hp >= 2:
                    ctx.wait(s, "vector", ol_used[slot])
                ctx.mark(s.dma_start(out=olT[slot][:],
                                     in_=d["dol"][sc, hp]), "sync", 16)
                gsrc = d["dgrep"][2 * hp:2 * hp + 2, sc * 512:(sc + 1) * 512]
                ctx.mark(s.dma_start(out=grep[slot][:],
                                     in_=gsrc.unsqueeze(1).broadcast_to(
                                         [2, 64, 512])), "sync", 16)

            for hh in range(2):
                h = 2 * hp + hh
                bank = hh

                @block.tensor
                def _(t, h=h, bank=bank):
                    ctx.wait(t, "sync")
                    ctx.bank_wait(t, bank)
                    for tc_rel in range(4):
                        for par in range(2):
                            ch = par * 8 + h // 4
                            rb = (h % 4) * 32
                            inst = t.matmul(
                                out=ctx.PF[bank][:, tc_rel * 128:(tc_rel + 1) * 128],
                                lhsT=qT[rb:rb + 32, ch,
                                        tc_rel * 128:(tc_rel + 1) * 128],
                                rhs=sb["gkT"][rb:rb + 32, ch, :],
                                start=(par == 0), stop=(par == 1),
                                tile_position=(rb, 0))
                    ctx.mark(inst, "tensor")

                @block.vector
                def _(v, bank=bank):
                    ctx.wait(v, "tensor")
                    ctx.wait(v, "scalar")
                    s3 = ctx.PF[bank][:].rearrange("p (s f) -> p s f", s=4)
                    ctx.vguard(v, v.reduce_max(red[:, 0:4], s3,
                                               axis=mybir.AxisListType.X))
                    v.tensor_scalar_mul(red[:, 4:8], red[:, 0:4], -SCALE)
                    ctx.mark(v.tensor_copy(out=red[:, 30:31], in_=red[:, 0:1]),
                             "vector")
                mneg3_mark = ctx.n["vector"]

                @block.scalar
                def _(sce, bank=bank, mneg3_mark=mneg3_mark):
                    ctx.wait(sce, "vector", mneg3_mark)
                    for tc_rel in range(4):
                        inst = sce.activation(
                            out=scrA[:, tc_rel * 128:(tc_rel + 1) * 128],
                            in_=ctx.PF[bank][:, tc_rel * 128:(tc_rel + 1) * 128],
                            func=AF.Exp, scale=SCALE,
                            bias=red[:, 4 + tc_rel:5 + tc_rel])
                    ctx.mark(inst, "scalar")
                    ctx.bank_read(bank, "scalar")

                @block.vector
                def _(v, sc=sc):
                    ctx.wait(v, "scalar")
                    ctx.wait(v, "sync", mg_mark)
                    e3 = scrA[:].rearrange("p (s f) -> p s f", s=4)
                    v.tensor_mul(scrB[:].rearrange("p (s f) -> p s f", s=4),
                                 e3, mg[:, sc * 4:(sc + 1) * 4, :])
                    m3 = scrB[:].rearrange("p (s f) -> p s f", s=4)
                    ctx.vguard(v, v.reduce_sum(red[:, 8:12], m3,
                                               axis=mybir.AxisListType.X))
                    ctx.vguard(v, v.tensor_scalar_max(red[:, 12:16],
                                                      red[:, 8:12], 1e-9))
                    ctx.vguard(v, v.reciprocal(red[:, 16:20], red[:, 12:16]))
                    ctx.mark(v.tensor_mul(agb[:].rearrange("p (s f) -> p s f", s=4),
                                          m3,
                                          red[:, 16:20].unsqueeze(2).broadcast_to(
                                              [128, 4, 128])), "vector")
                agb_mark = ctx.n["vector"]

                @block.tensor
                def _(t, agb_mark=agb_mark):
                    ctx.wait(t, "vector", agb_mark)
                    ctx.bank_wait(t, "PB")
                    for tc_rel in range(4):
                        inst = t.transpose(
                            out=ctx.PB[:, tc_rel * 128:(tc_rel + 1) * 128],
                            in_=agb[:, tc_rel * 128:(tc_rel + 1) * 128],
                            identity=sb["ident_sb"][:])
                    ctx.mark(inst, "tensor")

                @block.vector
                def _(v):
                    ctx.wait(v, "tensor")
                    ctx.mark(v.tensor_copy(out=agT[:],
                                           in_=ctx.PB[:, 0:512].rearrange(
                                               "p (c f) -> p c f", c=4)), "vector")
                    ctx.bank_read("PB", "vector")
                agT_mark = ctx.n["vector"]

                @block.tensor
                def _(t, h=h, hh=hh, agT_mark=agT_mark):
                    ctx.wait(t, "vector", agT_mark)
                    if hh == 0:
                        ctx.bank_wait(t, 2)
                    for tc_rel in range(4):
                        inst = t.matmul(
                            out=ctx.PF[2][hh * 64:hh * 64 + 64,
                                          tc_rel * 128:(tc_rel + 1) * 128],
                            lhsT=sb["gv_sb"][:, h * 64:(h + 1) * 64],
                            rhs=agT[:, tc_rel, :], start=True, stop=True,
                            tile_position=(0, hh * 64))
                    ctx.mark(inst, "tensor")

            if DEBUG and sc == 0 and hp == 0:
                @block.sync
                def _(s, slot=slot):
                    ctx.wait(s, "tensor")
                    ctx.wait(s, "vector")
                    ctx.mark(s.dma_start(out=d["dbg_sge"][:], in_=scrA[:]), "sync", 16)
                    ctx.mark(s.dma_start(out=d["dbg_scrB"][:], in_=scrB[:]), "sync", 16)
                    ctx.mark(s.dma_start(out=d["dbg_mg"][:], in_=mg[:, 0:4, :]), "sync", 16)
                    ctx.mark(s.dma_start(out=d["dbg_red"][:], in_=red[:]), "sync", 16)
                    ctx.mark(s.dma_start(out=d["dbg_agb"][:], in_=agb[:]), "sync", 16)
                    ctx.mark(s.dma_start(out=d["dbg_agt"][:],
                                         in_=agT[:].rearrange("p c f -> p (c f)")), "sync", 16)
                    ctx.mark(s.dma_start(out=d["dbg_olt"][:], in_=olT[slot][:]), "sync", 16)
                    ctx.mark(s.dma_start(out=d["dbg_grep"][:], in_=grep[slot][:]), "sync", 16)

            @block.vector
            def _(v, hp=hp, slot=slot):
                ctx.wait(v, "tensor")
                ctx.wait(v, "sync")
                v.tensor_sub(scrA[:], ctx.PF[2][:], olT[slot][:])
                v.tensor_mul(scrB[:], scrA[:], grep[slot][:])
                ol_used[slot] = ctx.mark(
                    v.tensor_add(gated[:, hp, :], scrB[:], olT[slot][:]),
                    "vector")
                ctx.bank_read(2, "vector")

            for _i in range(9):
                ctx.exp.pump()

        if DEBUG and sc == 0:
            @block.sync
            def _(s):
                ctx.wait(s, "vector")
                ctx.mark(s.dma_start(out=d["dbg_gated"][:], in_=gated[:]),
                         "sync", 16)

        gated_mark = ctx.n["vector"]
        wqb = [sb["wqb0"], sb["wqb1"]]
        wo_free = ctx.wqb_free
        for oc in range(NKC):
            slot = oc % 2
            bank = oc % 2

            @block.gpsimd
            def _(g, oc=oc, slot=slot):
                if wo_free[slot]:
                    ctx.wait(g, "tensor", wo_free[slot])
                ctx.mark(g.dma_start(out=wqb[slot][:], in_=d["wout"][oc]),
                         "gpsimd", 16)

            @block.tensor
            def _(t, slot=slot, bank=bank, oc=oc, gated_mark=gated_mark):
                ctx.wait(t, "gpsimd")
                ctx.bank_wait(t, bank)
                if oc < 2:
                    ctx.wait(t, "vector", gated_mark)
                for kc in range(NKC):
                    inst = t.matmul(out=ctx.PF[bank][:],
                                    lhsT=wqb[slot][:, kc, :],
                                    rhs=gated[:, kc, :],
                                    start=(kc == 0), stop=(kc == NKC - 1))
                wo_free[slot] = ctx.mark(inst, "tensor")

            @block.vector
            def _(v, oc=oc, bank=bank):
                ctx.wait(v, "tensor")
                ctx.wait(v, "sync")
                ctx.mark(v.tensor_scalar(out=sstg[:], in0=ctx.PF[bank][:],
                                         scalar1=0.5,
                                         scalar2=sb["b_bout_h"][:, oc:oc + 1],
                                         op0=ALU.mult, op1=ALU.add),
                         "vector")
                ctx.bank_read(bank, "vector")

            @block.sync
            def _(s, oc=oc, sc=sc):
                ctx.wait(s, "vector")
                ctx.mark(s.dma_start(out=d["ds"][sc, :, oc, :],
                                     in_=sstg[:]), "sync", 16)


def p_ffn(ctx):
    block, d, sb = ctx.block, ctx.d, ctx.sb
    a = ctx.alloc
    xnf = a("xnf", [128, NKC, 512], BF16)
    sTf = a("sTf", [128, NKC, 512], BF16)
    hT = a("hT", [128, NOB, 512], BF16)
    pT = sTf
    gbuf = [a(f"gbuf{i}", [128, NKC, 128], BF16) for i in range(2)]
    ubuf = [a(f"ubuf{i}", [128, NKC, 128], BF16) for i in range(2)]
    dbuf = [a(f"dbuf{i}", [128, NOB, 128], BF16) for i in range(2)]
    xbuf2 = [a("xbf0", [128, D], F32)] * 2
    ostg = [a("ostg0", [128, D], F32)] * 2
    scr2A = a("scr2A", [128, 512], F32)
    scr2B = a("scr2B", [128, 512], F32)
    scrp = [(sb["scrA"], sb["scrB"]), (scr2A, scr2B)]

    gu_free = [0, 0]
    dn_free = [0, 0]
    x2_free = [0, 0]
    os_free = [0, 0]
    for sc in range(NSC):
        barrier(ctx)

        @block.sync
        def _(s, sc=sc):
            ctx.mark(s.dma_start(out=xnf[:], in_=d["dxn"][sc]), "sync", 16)
            ctx.mark(s.dma_start(out=sTf[:], in_=d["ds"][sc]), "sync", 16)

        for oc in range(NOB):
            slot = oc % 2

            @block.sync
            def _(s, oc=oc, slot=slot, sc=sc):
                if oc >= 2 or sc > 0:
                    ctx.wait(s, "tensor", gu_free[slot])
                ctx.mark(s.dma_start(out=gbuf[slot][:], in_=d["wgT"][oc]),
                         "sync", 16)
                ctx.mark(s.dma_start(out=ubuf[slot][:], in_=d["wuT"][oc]),
                         "sync", 16)

            bg = 0 if oc % 2 == 0 else 4
            bu = 1 if oc % 2 == 0 else 5

            @block.tensor
            def _(t, oc=oc, slot=slot, bg=bg, bu=bu):
                ctx.wait(t, "sync")
                ctx.bank_wait(t, bg)
                ctx.bank_wait(t, bu)
                for kc in range(NKC):
                    t.matmul(out=ctx.PF[bg][:], lhsT=gbuf[slot][:, kc, :],
                             rhs=xnf[:, kc, :], start=(kc == 0),
                             stop=(kc == NKC - 1))
                for kc in range(NKC):
                    inst = t.matmul(out=ctx.PF[bu][:], lhsT=ubuf[slot][:, kc, :],
                                    rhs=xnf[:, kc, :], start=(kc == 0),
                                    stop=(kc == NKC - 1))
                gu_free[slot] = ctx.mark(inst, "tensor")

            sA, sB = scrp[oc % 2]

            @block.scalar
            def _(sce, bg=bg, sA=sA):
                ctx.wait(sce, "tensor")
                ctx.mark(sce.activation(out=sA[:], in_=ctx.PF[bg][:],
                                        func=AF.Tanh, scale=0.5), "scalar")

            @block.vector
            def _(v, oc=oc, bg=bg, bu=bu, sA=sA, sB=sB):
                ctx.wait(v, "scalar")
                v.scalar_tensor_tensor(out=sB[:], in0=sA[:], scalar=1.0,
                                       in1=ctx.PF[bg][:], op0=ALU.add,
                                       op1=ALU.mult)
                ctx.mark(v.tensor_mul(hT[:, oc, :], sB[:], ctx.PF[bu][:]),
                         "vector")
                ctx.bank_read(bg, "vector")
                ctx.bank_read(bu, "vector")

        for oc2 in range(NKC):
            slot = oc2 % 2
            bank = 2 + oc2 % 2

            @block.sync
            def _(s, oc2=oc2, slot=slot, sc=sc):
                if oc2 >= 2 or sc > 0:
                    ctx.wait(s, "tensor", dn_free[slot])
                ctx.mark(s.dma_start(out=dbuf[slot][:], in_=d["wdT"][oc2]),
                         "sync", 16)

            @block.tensor
            def _(t, oc2=oc2, slot=slot, bank=bank):
                ctx.wait(t, "sync")
                if oc2 == 0:
                    ctx.wait(t, "vector")
                ctx.bank_wait(t, bank)
                for ic in range(NOB):
                    inst = t.matmul(out=ctx.PF[bank][:],
                                    lhsT=dbuf[slot][:, ic, :],
                                    rhs=hT[:, ic, :],
                                    start=(ic == 0), stop=(ic == NOB - 1))
                dn_free[slot] = ctx.mark(inst, "tensor")

            sA, sB = scrp[oc2 % 2]

            @block.scalar
            def _(sce, bank=bank, sA=sA):
                ctx.wait(sce, "tensor")
                ctx.mark(sce.activation(out=sA[:], in_=ctx.PF[bank][:],
                                        func=AF.Tanh, scale=0.5), "scalar")

            @block.vector
            def _(v, oc2=oc2, bank=bank, sA=sA, sB=sB):
                ctx.wait(v, "scalar")
                ctx.wait(v, "sync")
                v.scalar_tensor_tensor(out=sB[:], in0=sA[:], scalar=1.0,
                                       in1=ctx.PF[bank][:], op0=ALU.add,
                                       op1=ALU.mult)
                ctx.mark(v.tensor_mul(pT[:, oc2, :], sB[:], sTf[:, oc2, :]),
                         "vector")
                ctx.bank_read(bank, "vector")

        for tc_rel in range(4):
            tc = sc * 4 + tc_rel
            slot = 0

            @block.sync
            def _(s, tc=tc, slot=slot):
                if tc >= 1:
                    ctx.wait(s, "vector", x2_free[slot])
                ctx.mark(s.dma_start(out=xbuf2[slot][:],
                                     in_=d["x"][tc * 128:(tc + 1) * 128, :]),
                         "sync", 16)

            for half in range(2):
                @block.tensor
                def _(t, tc_rel=tc_rel, half=half):
                    if half == 0 and tc_rel == 0:
                        ctx.wait(t, "vector")
                    ctx.bank_wait(t, "PB")
                    for j in range(8):
                        oc2 = half * 8 + j
                        inst = t.transpose(
                            out=ctx.PB[:, j * 128:(j + 1) * 128],
                            in_=pT[:, oc2, tc_rel * 128:(tc_rel + 1) * 128],
                            identity=sb["ident_sb"][:])
                    ctx.mark(inst, "tensor")

                @block.vector
                def _(v, half=half, slot=slot):
                    ctx.wait(v, "tensor")
                    if half == 0:
                        ctx.wait(v, "sync", os_free[slot])
                        ctx.wait(v, "sync")
                    inst = v.tensor_add(
                        ostg[slot][:, half * 1024:(half + 1) * 1024],
                        xbuf2[slot][:, half * 1024:(half + 1) * 1024],
                        ctx.PB[:, 0:1024])
                    w = ctx.mark(inst, "vector")
                    if half == 1:
                        x2_free[slot] = w
                    ctx.bank_read("PB", "vector")

            @block.sync
            def _(s, tc=tc, slot=slot):
                ctx.wait(s, "vector")
                os_free[slot] = ctx.mark(
                    s.dma_start(out=d["out"][tc * 128:(tc + 1) * 128, :],
                                in_=ostg[slot][:]), "sync", 16)


def emit(ctx):
    p_consts(ctx)
    p_bias_stage(ctx)
    ctx.push_scope()
    p_lm_scope_alloc(ctx)
    ctx.exp = Expander(ctx)
    ctx.push_scope()
    p_sweep1(ctx)
    barrier(ctx)
    ctx.pop_scope()
    ctx.push_scope()
    p_post1_alloc(ctx)
    p_landmark(ctx)
    ctx.push_scope()
    p_sweep2a(ctx)
    barrier(ctx)
    ctx.pop_scope()
    p_lmtail(ctx)
    ctx.push_scope()
    p_sweep2b(ctx)
    ctx.exp.finish()
    ctx.tr.finish()
    barrier(ctx)
    ctx.pop_scope()
    barrier(ctx)
    ctx.pop_scope()
    barrier(ctx)
    ctx.pop_scope()
    ctx.push_scope()
    p_ffn(ctx)
    ctx.pop_scope()
    block = ctx.block

    @block.sync
    def _(s):
        for e in ("tensor", "vector", "scalar", "gpsimd"):
            ctx.wait(s, e)
        ctx.wait(s, "sync")


def kernel(**inputs):
    cst = host_prep(inputs)
    x = np.asarray(inputs["L0_attn_x"])
    nc, dd = build_program()
    in_maps = []
    for c in range(8):
        m = dict(cst)
        m["x"] = np.ascontiguousarray(x[c])
        in_maps.append(m)
    res = run_bass_kernel_spmd(nc, in_maps, core_ids=list(range(8)))
    return np.stack([res.results[c]["out"] for c in range(8)], 0)
